# version 03
# baseline (speedup 1.0000x reference)
import contextlib
import os
import numpy as np
import concourse.bass as bass
import concourse.mybir as mybir
from concourse.bass_utils import run_bass_kernel_spmd

F32 = mybir.dt.float32
BF16 = mybir.dt.bfloat16
I32 = mybir.dt.int32
AF = mybir.ActivationFunctionType
ALU = mybir.AluOpType
AX = mybir.AxisListType

NB, LSEQ, DM = 4, 4096, 2048
HALF, NQC = 2048, 16
NH, EH = 16, 128
FH = 8192
PROJ, DSSM, CONVD, HS, PS, SS = 8512, 4096, 4352, 64, 64, 128
LN_EPS = 1e-5
RMS_EPS = float(np.finfo(np.float32).eps)
SSM_EPS = 1e-5
INV_SQRT_EH = 1.0 / float(np.sqrt(128.0))
NEG_BIG = -100000.0
ALL_STAGES = ("attn", "ffn1", "ssm", "ffn2")
DEV_SKIP_QUANT = bool(int(os.environ.get("KDEV_SKIP_QUANT", "0")))
DEV_FFN_TILES = int(os.environ.get("KDEV_FFN_TILES", "8"))


class Sem:
    def __init__(self, nc, es, name):
        self.h = es.enter_context(nc.semaphore(name))
        self.n = 0
        self.name = name

    def bump(self, k=1):
        self.n += k
        return (self, k)

    def at(self):
        return (self, self.n)


class Ops:
    ENGS = ("sync", "tensor", "vector", "scalar", "gpsimd")

    def __init__(self):
        self.q = {e: [] for e in self.ENGS}

    def add(self, eng, fn, wait=(), inc=None):
        wait = tuple((s, v) for (s, v) in wait if v > 0)

        def run(e, fn=fn, wait=wait, inc=inc):
            for s, v in wait:
                e.wait_ge(s.h, v)
            r = fn(e)
            if inc is not None:
                r.then_inc(inc[0].h, inc[1])
        self.q[eng].append(run)

    def barrier(self, eng, waits):
        self.add(eng, lambda e: e.nop(), wait=waits)


class B:
    def __init__(self, stages=ALL_STAGES):
        self.stages = stages
        self.nc = bass.Bass("TRN2", target_bir_lowering=False)
        self.es = contextlib.ExitStack()
        self.ops = Ops()
        self.sems = {}

    def sem(self, name):
        if name not in self.sems:
            self.sems[name] = Sem(self.nc, self.es, name)
        return self.sems[name]

    def sb(self, name, shape, dtype, es=None):
        return (es or self.es).enter_context(self.nc.sbuf_tensor(name, shape, dtype))

    def ps(self, name, shape, dtype=F32, es=None):
        return (es or self.es).enter_context(self.nc.psum_tensor(name, shape, dtype))



def build_program(stages=ALL_STAGES):
    b = B(stages)
    nc = b.nc

    b.in_x = nc.dram_tensor("x", [LSEQ, DM], F32, kind="ExternalInput")
    b.in_wq = nc.dram_tensor("w_q", [NH, DM, EH], F32, kind="ExternalInput")
    b.in_wk = nc.dram_tensor("w_k", [NH, DM, EH], F32, kind="ExternalInput")
    b.in_wv = nc.dram_tensor("w_v", [NH, DM, EH], F32, kind="ExternalInput")
    b.in_wo = nc.dram_tensor("w_o", [NH, EH, DM], F32, kind="ExternalInput")
    b.in_bq = nc.dram_tensor("b_q", [NH, EH], F32, kind="ExternalInput")
    b.in_bk = nc.dram_tensor("b_k", [NH, EH], F32, kind="ExternalInput")
    b.in_bv = nc.dram_tensor("b_v", [NH, EH], F32, kind="ExternalInput")
    b.in_bo = nc.dram_tensor("b_o", [1, DM], F32, kind="ExternalInput")
    b.in_wgu = nc.dram_tensor("w_gu", [2 * FH, DM], F32, kind="ExternalInput")
    b.in_wp = nc.dram_tensor("w_p", [DM, FH], F32, kind="ExternalInput")
    b.in_win = nc.dram_tensor("w_in", [DM, PROJ], F32, kind="ExternalInput")
    b.in_wout = nc.dram_tensor("w_out", [DSSM, DM], F32, kind="ExternalInput")
    b.in_convw = nc.dram_tensor("conv_w_fm", [128, 34 * 4], F32, kind="ExternalInput")
    b.in_convb = nc.dram_tensor("conv_b_fm", [128, 34], F32, kind="ExternalInput")
    b.in_ssmv = nc.dram_tensor("ssm_vec", [HS, 4], F32, kind="ExternalInput")
    b.in_ssmrow = nc.dram_tensor("ssm_row", [1, 192], F32, kind="ExternalInput")
    b.in_normw = nc.dram_tensor("norm_w_fm", [128, 32], F32, kind="ExternalInput")
    b.in_mup = nc.dram_tensor("mup_fm", [128, 67], F32, kind="ExternalInput")
    b.in_qpos = nc.dram_tensor("qpos", [128, NQC], I32, kind="ExternalInput")
    b.in_bqkv = nc.dram_tensor("bqkv", [128, 3 * NH], F32, kind="ExternalInput")
    b.in_mtiles = nc.dram_tensor("mtiles", [128, 6 * 512], F32, kind="ExternalInput")
    b.in_gate = nc.dram_tensor("gate", [1, 4], F32, kind="ExternalInput")

    b.out_x = nc.dram_tensor("xo", [HALF, DM], F32, kind="ExternalOutput")

    b.hb_wguT = nc.dram_tensor("hb_wguT", [DM, 2 * FH], BF16, kind="Internal")
    b.hb_wpT = nc.dram_tensor("hb_wpT", [FH, DM], BF16, kind="Internal")
    b.hb_win = nc.dram_tensor("hb_win", [DM, PROJ], BF16, kind="Internal")
    b.hb_wout = nc.dram_tensor("hb_wout", [DSSM, DM], BF16, kind="Internal")
    b.hb_xn = nc.dram_tensor("hb_xn", [LSEQ, DM], BF16, kind="Internal")
    b.hb_sz = nc.dram_tensor("hb_sz", [8 * 128, 2 * DSSM], BF16, kind="Internal")
    b.hb_xbc = nc.dram_tensor("hb_xbc", [8 * 128, 34 * 256], BF16, kind="Internal")
    b.hb_dtf = nc.dram_tensor("hb_dtf", [8 * 128, 3 * 256], F32, kind="Internal")
    b.hb_wq = nc.dram_tensor("hb_wq", [NH * DM, EH], BF16, kind="Internal")
    b.hb_wk = nc.dram_tensor("hb_wk", [NH * DM, EH], BF16, kind="Internal")
    b.hb_wv = nc.dram_tensor("hb_wv", [NH * DM, EH], BF16, kind="Internal")
    b.cc_tail_in = nc.dram_tensor("cc_tail_in", [3, DM], F32, kind="Internal")
    b.cc_tail_out = nc.dram_tensor("cc_tail_out", [6, DM], F32, kind="Internal")
    b.cc_st_in = nc.dram_tensor("cc_st_in", [128, 64 * PS], BF16, kind="Internal")
    b.cc_st_out = nc.dram_tensor("cc_st_out", [256, 64 * PS], BF16, kind="Internal")

    b.ident = b.sb("g_ident", [128, 128], BF16)
    b.colA = b.sb("g_colA", [128, 192], F32)
    b.gate_sb = b.sb("g_gate", [1, 4], F32)

    _emit_setup(b)
    if DEV_SKIP_QUANT:
        b.quant_done = list(b.setup_done)
    else:
        _emit_quant(b)
    if "attn" in stages:
        _emit_attn(b)
    else:
        b.resid = b.sb("g_resid", [128, NQC, DM], F32)
        _emit_resid_init(b)
    for st in stages:
        if st == "ffn1" or st == "ffn2":
            _emit_ffn(b, st)
        elif st == "ssm":
            _emit_ssm(b)
    _emit_output(b)

    with b.es:
        with nc.Block() as block:
            @block.sync
            def _(e):
                for fn in b.ops.q["sync"]:
                    fn(e)

            @block.tensor
            def _(e):
                for fn in b.ops.q["tensor"]:
                    fn(e)

            @block.vector
            def _(e):
                for fn in b.ops.q["vector"]:
                    fn(e)

            @block.scalar
            def _(e):
                for fn in b.ops.q["scalar"]:
                    fn(e)

            @block.gpsimd
            def _(e):
                for fn in b.ops.q["gpsimd"]:
                    fn(e)
    return b



def _emit_setup(b):
    ops = b.ops
    sG = b.sem("su_g")
    sV = b.sem("su_v")
    sLd = b.sem("su_ld")
    ph = contextlib.ExitStack()
    iota_r = b.sb("su_ir", [128, 128], F32, ph)
    iota_c = b.sb("su_ic", [128, 128], F32, ph)
    ops.add("gpsimd", lambda e: e.iota(iota_r[:, :], [[1, 128]], channel_multiplier=0,
                                       allow_small_or_imprecise_dtypes=True))
    ops.add("gpsimd", lambda e: e.iota(iota_c[:, :], [[0, 128]], channel_multiplier=1,
                                       allow_small_or_imprecise_dtypes=True),
            inc=sG.bump())
    ops.add("vector", lambda e: e.tensor_tensor(b.ident[:, :], iota_r[:, :],
                                                iota_c[:, :], ALU.is_equal),
            wait=[sG.at()], inc=sV.bump())
    ops.add("sync", lambda e: e.dma_start(out=b.gate_sb[:, :], in_=b.in_gate[:, :]),
            inc=sLd.bump(16))
    b.setup_done = [sV.at(), sLd.at()]
    ph.close()



def _emit_quant(b):
    ops = b.ops
    sV = b.sem("q_v")
    sA = b.sem("q_a")
    sT = b.sem("q_t")
    sLd = [b.sem("q_ld0"), b.sem("q_ld1")]
    sSt = [b.sem("q_st0"), b.sem("q_st1")]
    sCb = [b.sem("q_cb0"), b.sem("q_cb1")]

    ph = contextlib.ExitStack()
    wbuf = [b.sb(f"q_w{i}", [128, FH], BF16, ph) for i in range(2)]
    trs = [b.sb(f"q_tr{i}", [128, FH], BF16, ph) for i in range(2)]
    absf = b.sb("q_abs", [128, FH], F32, ph)
    sums = b.sb("q_sums", [128, 128], F32, ph)
    scal = b.sb("q_scal", [128, 128], F32, ph)
    scab = b.sb("q_scab", [128, 128], BF16, ph)
    rcp = b.sb("q_rcp", [128, 128], F32, ph)
    rb = b.sb("q_rb", [128, FH], BF16, ph)
    pst = b.ps("q_pst", [128, 512], BF16, ph)

    war_w = [None, None]
    war_tr = [None, None]

    jobs = [
        (b.in_wgu, DM, 2 * FH // 128, b.hb_wguT, 2 * FH),
        (b.in_wp, FH, DM // 128, b.hb_wpT, DM),
    ]
    gi = 0
    for src_t, C, nch, dst, dstw in jobs:
        G = C // 64
        nj = C // 128
        for i in range(nch):
            pb = gi % 2
            gi += 1
            wb_t, tr_t = wbuf[pb], trs[pb]
            w = [b.setup_done[0]] if gi == 1 else []
            if war_w[pb] is not None:
                w.append(war_w[pb])
            ops.add("gpsimd", lambda e, t=wb_t, s=src_t, i=i, C=C: e.dma_start(
                out=t[:, 0:C], in_=s[i * 128:(i + 1) * 128, :]),
                wait=w, inc=sLd[pb].bump(16))
            ld = sLd[pb].at()
            ops.add("scalar", lambda e, t=wb_t, C=C: e.activation(
                absf[:, 0:C], t[:, 0:C], AF.Abs), wait=[ld, sV.at()], inc=sA.bump())
            ops.add("vector", lambda e, C=C, G=G: e.tensor_reduce(
                sums[:, 0:G].rearrange("p (g o) -> p g o", o=1),
                absf[:, 0:C].rearrange("p (g o) -> p g o", o=64),
                AX.X, ALU.add), wait=[sA.at()], inc=sV.bump())
            ops.add("vector", lambda e, G=G: e.tensor_scalar(
                scal[:, 0:G], sums[:, 0:G], 1.0 / 64.0, 1e-8, ALU.mult, ALU.max),
                wait=[sV.at()], inc=sV.bump())
            ops.add("vector", lambda e, G=G: e.tensor_copy(
                scab[:, 0:G], scal[:, 0:G]), wait=[sV.at()], inc=sV.bump())
            ops.add("vector", lambda e, G=G: e.reciprocal(
                rcp[:, 0:G], scab[:, 0:G]), wait=[sV.at()], inc=sV.bump())

            def _r(e, t=wb_t, C=C, G=G):
                bcast = bass.AP(rcp, 0, [[128, 128], [1, G], [0, 64]])
                return e.tensor_tensor(
                    rb[:, 0:C].rearrange("p (g o) -> p g o", o=64),
                    t[:, 0:C].rearrange("p (g o) -> p g o", o=64), bcast, ALU.mult)
            ops.add("vector", _r, wait=[sV.at()], inc=sV.bump())
            ops.add("vector", lambda e, C=C: e.tensor_scalar(
                rb[:, 0:C], rb[:, 0:C], 192.0, None, ALU.add), wait=[sV.at()], inc=sV.bump())
            ops.add("vector", lambda e, C=C: e.tensor_scalar(
                rb[:, 0:C], rb[:, 0:C], -192.0, None, ALU.add), wait=[sV.at()], inc=sV.bump())
            ops.add("vector", lambda e, C=C: e.tensor_scalar(
                rb[:, 0:C], rb[:, 0:C], 1.0, -1.0, ALU.min, ALU.max), wait=[sV.at()], inc=sV.bump())

            def _wh(e, t=wb_t, C=C, G=G):
                bcast = bass.AP(scab, 0, [[128, 128], [1, G], [0, 64]])
                return e.tensor_tensor(
                    t[:, 0:C].rearrange("p (g o) -> p g o", o=64),
                    rb[:, 0:C].rearrange("p (g o) -> p g o", o=64), bcast, ALU.mult)
            ops.add("vector", _wh, wait=[sV.at()], inc=sV.bump())
            wh_done = sV.at()

            trw = [war_tr[pb]] if war_tr[pb] is not None else []
            for j0 in range(0, nj, 4):
                jn = min(j0 + 4, nj)
                for j in range(j0, jn):
                    w2 = ([wh_done, sV.at()] + trw) if j == j0 else []
                    ops.add("tensor", lambda e, t=wb_t, j=j, j0=j0: e.transpose(
                        pst[:, (j - j0) * 128:(j - j0 + 1) * 128],
                        t[:, j * 128:(j + 1) * 128], b.ident[:, :]),
                        wait=w2, inc=sT.bump())
                ops.add("vector", lambda e, t=tr_t, j0=j0, jn=jn: e.tensor_copy(
                    t[:, j0 * 128:jn * 128], pst[:, 0:(jn - j0) * 128]),
                    wait=[sT.at()], inc=sV.bump())
            war_w[pb] = sT.at()

            def _st(e, t=tr_t, dst=dst, dstw=dstw, i=i, nj=nj):
                out_ap = bass.AP(dst, i * 128, [[dstw, 128], [128 * dstw, nj], [1, 128]])
                in_ap = bass.AP(t, 0, [[FH, 128], [128, nj], [1, 128]])
                return e.dma_start(out=out_ap, in_=in_ap)
            ops.add("gpsimd", _st, wait=[sV.at()], inc=sSt[pb].bump(16))
            war_tr[pb] = sSt[pb].at()

    cbuf = [b.sb(f"q_cb{i}", [128, PROJ], BF16, ph) for i in range(2)]
    first_c = [True, True]
    flat_jobs = []
    for wsrc, wdst in [(b.in_wq, b.hb_wq), (b.in_wk, b.hb_wk), (b.in_wv, b.hb_wv)]:
        flat_jobs.append((bass.AP(wsrc, 0, [[4096, 1024], [1, 4096]]),
                          bass.AP(wdst, 0, [[4096, 1024], [1, 4096]])))
    for fsrc, fdst in flat_jobs:
        for i in range(8):
            pb = gi % 2
            gi += 1
            w = [] if first_c[pb] else [sCb[pb].at()]
            first_c[pb] = False
            ops.add("gpsimd", lambda e, t=cbuf[pb], fsrc=fsrc, i=i: e.dma_start(
                out=t[:, 0:4096],
                in_=bass.AP(fsrc.tensor, i * 128 * 4096, [[4096, 128], [1, 4096]])),
                wait=w, inc=sCb[pb].bump(16))
            ops.add("gpsimd", lambda e, t=cbuf[pb], fdst=fdst, i=i: e.dma_start(
                out=bass.AP(fdst.tensor, i * 128 * 4096, [[4096, 128], [1, 4096]]),
                in_=t[:, 0:4096]), wait=[sCb[pb].at()], inc=sCb[pb].bump(16))
    for src_t, nch, C, dst in [(b.in_win, DM // 128, PROJ, b.hb_win),
                               (b.in_wout, DSSM // 128, DM, b.hb_wout)]:
        for i in range(nch):
            pb = gi % 2
            gi += 1
            w = [] if first_c[pb] else [sCb[pb].at()]
            first_c[pb] = False
            ops.add("gpsimd", lambda e, t=cbuf[pb], s=src_t, i=i, C=C: e.dma_start(
                out=t[:, 0:C], in_=s[i * 128:(i + 1) * 128, :]), wait=w,
                inc=sCb[pb].bump(16))
            ops.add("gpsimd", lambda e, t=cbuf[pb], d=dst, i=i, C=C: e.dma_start(
                out=d[i * 128:(i + 1) * 128, :], in_=t[:, 0:C]),
                wait=[sCb[pb].at()], inc=sCb[pb].bump(16))

    b.quant_done = [sSt[0].at(), sSt[1].at(), sCb[0].at(), sCb[1].at(),
                    sV.at(), sA.at(), sT.at()]
    ph.close()



def _emit_resid_init(b):
    ops = b.ops
    sLd = b.sem("ri_ld")
    qd = list(b.quant_done)
    half_rows = bass.AP(b.in_x, 0, [[1, 1]])
    for qc in range(NQC):
        ops.add("sync", lambda e, qc=qc: e.dma_start(
            out=b.resid[:, qc, :], in_=b.in_x[qc * 128:(qc + 1) * 128, :]),
            wait=qd if qc == 0 else [], inc=sLd.bump(16))
    b.resid_ready = [sLd.at()]



def _emit_ln_stats(b, tag, sq_scr):
    ops = b.ops
    c = b.colA
    sV = b.sem(f"{tag}_lv")
    sA = b.sem(f"{tag}_la")
    rw = b.resid_ready
    for qc in range(NQC):
        ops.add("scalar", lambda e, qc=qc: e.activation(
            sq_scr, b.resid[:, qc, :], AF.Square,
            accum_out=c[:, 48 + qc:49 + qc]), wait=rw + [sA.at()], inc=sA.bump())
    ops.add("vector", lambda e: e.tensor_reduce(
        c[:, 64:80].rearrange("p (g o) -> p g o", o=1),
        bass.AP(b.resid, 0, [[NQC * DM, 128], [DM, NQC], [1, DM]]),
        AX.X, ALU.add), wait=rw, inc=sV.bump())
    ops.add("vector", lambda e: e.tensor_scalar(
        c[:, 0:16], c[:, 64:80], 1.0 / DM, None, ALU.mult),
        wait=[sV.at()], inc=sV.bump())
    ops.add("vector", lambda e: e.tensor_scalar(
        c[:, 80:96], c[:, 48:64], 1.0 / DM, None, ALU.mult),
        wait=[sA.at(), sV.at()], inc=sV.bump())
    ops.add("vector", lambda e: e.scalar_tensor_tensor(
        c[:, 16:32], c[:, 0:16], 1.0, c[:, 0:16], ALU.mult, ALU.mult),
        wait=[sV.at()], inc=sV.bump())
    ops.add("vector", lambda e: e.tensor_sub(
        c[:, 80:96], c[:, 80:96], c[:, 16:32]), wait=[sV.at()], inc=sV.bump())
    ops.add("vector", lambda e: e.tensor_scalar(
        c[:, 80:96], c[:, 80:96], LN_EPS, None, ALU.add), wait=[sV.at()], inc=sV.bump())
    ops.add("scalar", lambda e: e.activation(
        c[:, 96:112], c[:, 80:96], AF.Sqrt), wait=[sV.at()], inc=sA.bump())
    ops.add("vector", lambda e: e.reciprocal(
        c[:, 16:32], c[:, 96:112]), wait=[sA.at()], inc=sV.bump())
    ops.add("vector", lambda e: e.scalar_tensor_tensor(
        c[:, 32:48], c[:, 0:16], -1.0, c[:, 16:32], ALU.mult, ALU.mult),
        wait=[sV.at()], inc=sV.bump())
    return [sV.at()]



def _emit_ffn(b, tag):
    ops = b.ops
    sW = b.sem(f"{tag}_w")
    sV = b.sem(f"{tag}_v")
    sA = b.sem(f"{tag}_a")
    sT = b.sem(f"{tag}_t")
    qd = b.quant_done

    ph = contextlib.ExitStack()
    hbuf = b.sb(f"{tag}_h", [128, 2, FH], BF16, ph)
    xnT = b.sb(f"{tag}_xnT", [128, 16, 256], BF16, ph)
    wg = [b.sb(f"{tag}_wg{i}", [128, 8, 512], BF16, ph) for i in range(2)]
    wu = [b.sb(f"{tag}_wu{i}", [128, 8, 512], BF16, ph) for i in range(2)]
    wpq = b.sb(f"{tag}_wpq", [128, 4, 512], BF16, ph)
    htf = [b.sb(f"{tag}_htf{i}", [128, 256], BF16, ph) for i in range(2)]
    sg = b.sb(f"{tag}_sg", [128, 512], BF16, ph)
    ssqp = b.sb(f"{tag}_ssqp", [128, 2, 16], F32, ph)
    inv_c = b.sb(f"{tag}_inv", [128, 8], F32, ph)
    ps_t = b.ps(f"{tag}_pst", [128, 1024], BF16, ph)
    A4 = b.ps(f"{tag}_a4", [128, 2048], F32, ph)
    c = b.colA
    NT = DEV_FFN_TILES
    NHG = FH // 512

    sq_scr = hbuf[:, 1, 0:DM]
    rw0 = list(b.resid_ready)
    for eng in ("sync", "gpsimd", "tensor", "vector", "scalar"):
        ops.barrier(eng, rw0)
    ln_done = _emit_ln_stats(b, tag, sq_scr)

    war_wg = [None, None]
    war_wp = [None, None]
    prev_tile_done = []

    for tt in range(NT):
        xnT_marks = []
        for tc, qc in enumerate((2 * tt, 2 * tt + 1)):
            xv = hbuf[:, 0, 0:DM]
            ops.add("scalar", lambda e, qc=qc, xv=xv: e.activation(
                xv, b.resid[:, qc, :], AF.Identity,
                bias=c[:, 32 + qc:33 + qc], scale=c[:, 16 + qc:17 + qc]),
                wait=ln_done + [sA.at(), sT.at()] + prev_tile_done, inc=sA.bump())
            for j0 in range(0, 16, 4):
                for j in range(j0, j0 + 4):
                    ops.add("tensor", lambda e, j=j, j0=j0, xv=xv: e.transpose(
                        ps_t[:, (j - j0) * 128:(j - j0 + 1) * 128],
                        xv[:, j * 128:(j + 1) * 128], b.ident[:, :]),
                        wait=[sA.at(), sV.at()] if j == j0 else [], inc=sT.bump())
                ops.add("vector", lambda e, j0=j0, tc=tc: e.tensor_copy(
                    bass.AP(xnT, tc * 128 + j0 * 256,
                            [[16 * 256, 128], [256, 4], [1, 128]]),
                    ps_t[:, 0:512]), wait=[sT.at()], inc=sV.bump())
        xnT_done = sV.at()

        for hg in range(NHG):
            slotm = []
            for half_i in range(2):

                def _ldg(e, hi=half_i, hg=hg):
                    out_ap = bass.AP(wg[hi], 0, [[8 * 512, 128], [512, 8], [1, 512]])
                    in_ap = bass.AP(b.hb_wguT, hg * 512 + hi * 8 * 128 * 2 * FH,
                                    [[2 * FH, 128], [128 * 2 * FH, 8], [1, 512]])
                    return e.dma_start(out=out_ap, in_=in_ap)

                def _ldu(e, hi=half_i, hg=hg):
                    out_ap = bass.AP(wu[hi], 0, [[8 * 512, 128], [512, 8], [1, 512]])
                    in_ap = bass.AP(b.hb_wguT, FH + hg * 512 + hi * 8 * 128 * 2 * FH,
                                    [[2 * FH, 128], [128 * 2 * FH, 8], [1, 512]])
                    return e.dma_start(out=out_ap, in_=in_ap)
                w = [qd[0]] if (tt == 0 and hg == 0) else []
                if war_wg[half_i] is not None:
                    w = w + [war_wg[half_i]]
                sg_s = b.sem(f"{tag}_wg{half_i}")
                su_s = b.sem(f"{tag}_wu{half_i}")
                ops.add("sync", _ldg, wait=w, inc=sg_s.bump(16))
                ops.add("sync", _ldu, inc=su_s.bump(16))
                slotm.append((sg_s.at(), su_s.at()))

            pb = hg % 2
            g_ps = A4[:, pb * 512:(pb + 1) * 512]
            u_ps = A4[:, 1024 + pb * 512:1024 + (pb + 1) * 512]
            for tc in range(2):
                for d in range(16):
                    hi, dd = (0, d) if d < 8 else (1, d - 8)
                    wt = []
                    if d == 0:
                        wt = [xnT_done, sA.at(), sV.at()]
                    if tc == 0 and (d == 0 or d == 8):
                        wt = wt + [slotm[hi][0], slotm[hi][1]]
                    ops.add("tensor", lambda e, dd=dd, hi=hi, tc=tc, d=d, g_ps=g_ps: e.matmul(
                        g_ps, xnT[:, d, tc * 128:(tc + 1) * 128],
                        wg[hi][:, dd, :], start=(d == 0), stop=(d == 15)),
                        wait=wt)
                    ops.add("tensor", lambda e, dd=dd, hi=hi, tc=tc, d=d, u_ps=u_ps: e.matmul(
                        u_ps, xnT[:, d, tc * 128:(tc + 1) * 128],
                        wu[hi][:, dd, :], start=(d == 0), stop=(d == 15)),
                        inc=sT.bump() if d == 15 else None)
                mm_done = sT.at()
                ops.add("scalar", lambda e, g_ps=g_ps: e.activation(
                    sg[:, :], g_ps, AF.Silu),
                    wait=[mm_done, sV.at()], inc=sA.bump())
                ops.add("vector", lambda e, u_ps=u_ps, tc=tc, hg=hg: e.tensor_tensor(
                    hbuf[:, tc, hg * 512:(hg + 1) * 512], sg[:, :],
                    u_ps, ALU.mult), wait=[sA.at()], inc=sV.bump())
                ops.add("scalar", lambda e, tc=tc, hg=hg: e.activation(
                    sg[:, :], hbuf[:, tc, hg * 512:(hg + 1) * 512], AF.Square,
                    accum_out=ssqp[:, tc, hg:hg + 1]), wait=[sV.at()], inc=sA.bump())
            war_wg[0] = war_wg[1] = sT.at()
        h_done_V = sV.at()
        h_done_A = sA.at()

        ops.add("vector", lambda e: e.tensor_reduce(
            inv_c[:, 0:2].rearrange("p (g o) -> p g o", o=1),
            ssqp[:, :, :], AX.X, ALU.add), wait=[h_done_A, sV.at()], inc=sV.bump())
        ops.add("vector", lambda e: e.tensor_scalar(
            inv_c[:, 2:4], inv_c[:, 0:2], 1.0 / FH, RMS_EPS, ALU.mult, ALU.add),
            wait=[sV.at()], inc=sV.bump())
        ops.add("scalar", lambda e: e.activation(
            inv_c[:, 4:6], inv_c[:, 2:4], AF.Sqrt), wait=[sV.at()], inc=sA.bump())
        ops.add("vector", lambda e: e.reciprocal(
            inv_c[:, 6:8], inv_c[:, 4:6]), wait=[sA.at()], inc=sV.bump())
        inv_done = sV.at()

        NHC = FH // 128
        for dh in range(2):
            for hc in range(NHC):
                pb = hc % 2
                qmarks = []
                for q in range(2):
                    slot = pb * 2 + q
                    w = []
                    if tt == 0 and dh == 0 and hc == 0 and q == 0:
                        w = [qd[0]]
                    if war_wp[pb] is not None and q == 0:
                        w = w + [war_wp[pb]]
                    sq_s = b.sem(f"{tag}_wq{slot}")
                    ops.add("sync", lambda e, hc=hc, dh=dh, q=q, slot=slot: e.dma_start(
                        out=wpq[:, slot, :],
                        in_=b.hb_wpT[hc * 128:(hc + 1) * 128,
                                     dh * 1024 + q * 512:dh * 1024 + (q + 1) * 512]),
                        wait=w, inc=sq_s.bump(16))
                    qmarks.append(sq_s.at())
                for tc in range(2):
                    tw = [sV.at()]
                    if hc == 0 and dh == 0:
                        tw += [h_done_V]
                    ops.add("tensor", lambda e, tc=tc, hc=hc: e.transpose(
                        ps_t[:, tc * 512:tc * 512 + 128],
                        hbuf[:, tc, hc * 128:(hc + 1) * 128], b.ident[:, :]),
                        wait=tw, inc=sT.bump())
                    ops.add("vector", lambda e, tc=tc: e.tensor_copy(
                        htf[tc][:, 0:128], ps_t[:, tc * 512:tc * 512 + 128]),
                        wait=[sT.at()], inc=sV.bump())
                    tr_done = sV.at()
                    for q in range(2):
                        wt = [qmarks[q]] if tc == 0 else []
                        if q == 0:
                            wt += [tr_done]
                            if hc == 0 and tc == 0:
                                wt += [h_done_A, sV.at()]
                                if dh == 1 or tt > 0:
                                    wt += [prev_evac_done]
                        ops.add("tensor", lambda e, q=q, hc=hc, tc=tc,
                                slot=pb * 2 + q: e.matmul(
                            A4[:, tc * 1024 + q * 512:tc * 1024 + (q + 1) * 512],
                            htf[tc][:, 0:128], wpq[:, slot, :],
                            start=(hc == 0), stop=(hc == NHC - 1)),
                            wait=wt, inc=sT.bump() if (tc == 1 and q == 1) else None)
                war_wp[pb] = sT.at()
            mm2_done = sT.at()
            for tc in range(2):
                for q in range(2):
                    ops.add("vector", lambda e, tc=tc, tt=tt, dh=dh, q=q: e.scalar_tensor_tensor(
                        b.resid[:, 2 * tt + tc,
                                dh * 1024 + q * 512:dh * 1024 + (q + 1) * 512],
                        A4[:, tc * 1024 + q * 512:tc * 1024 + (q + 1) * 512],
                        inv_c[:, 6 + tc:7 + tc],
                        b.resid[:, 2 * tt + tc,
                                dh * 1024 + q * 512:dh * 1024 + (q + 1) * 512],
                        ALU.mult, ALU.add),
                        wait=[mm2_done, inv_done] if (tc == 0 and q == 0) else [],
                        inc=sV.bump())
            prev_evac_done = sV.at()
        prev_tile_done = [sV.at()]
    b.resid_ready = [sV.at()]
    ph.close()



def _emit_attn(b):
    ops = b.ops
    sV = b.sem("a_v")
    sA = b.sem("a_a")
    sT = b.sem("a_t")
    sG = b.sem("a_g")
    sX = [b.sem("a_x0"), b.sem("a_x1")]
    sP2 = [b.sem("a_p20"), b.sem("a_p21")]
    sI0 = [b.sem("a_i00"), b.sem("a_i01")]
    sXq = [b.sem("a_xq0"), b.sem("a_xq1")]
    sQs = [b.sem("a_qs0"), b.sem("a_qs1")]
    sWc = b.sem("a_wc")
    sO = b.sem("a_wo")
    sXn = b.sem("a_xn")
    sKp = b.sem("a_kp")
    c = b.colA

    ph1 = contextlib.ExitStack()
    xA = b.sb("a_xA", [128, 2, DM], F32, ph1)
    xnO = b.sb("a_xnO", [128, 2, DM], BF16, ph1)
    sq1 = b.sb("a_sq1", [128, DM], BF16, ph1)
    ac = b.sb("a_ac", [128, 6, 32], F32, ph1)
    NCK = LSEQ // 128
    for i in range(NCK):
        pb = i % 2
        w = (list(b.setup_done) + list(b.quant_done)) if i < 2 else [sV.at(), sA.at()]
        ops.add("sync", lambda e, i=i, pb=pb: e.dma_start(
            out=xA[:, pb, :], in_=b.in_x[i * 128:(i + 1) * 128, :]),
            wait=w, inc=sX[pb].bump(16))
        ld = sX[pb].at()
        ops.add("vector", lambda e, i=i, pb=pb: e.tensor_reduce(
            ac[:, 0, i:i + 1].rearrange("p (g o) -> p g o", o=1),
            xA[:, pb, :].rearrange("p (g o) -> p g o", o=DM), AX.X, ALU.add),
            wait=[ld], inc=sV.bump())
        ops.add("scalar", lambda e, i=i, pb=pb: e.activation(
            sq1[:, :], xA[:, pb, :], AF.Square, accum_out=ac[:, 1, i:i + 1]),
            wait=[ld, sA.at()], inc=sA.bump())
    st1 = [sV.at(), sA.at()]
    ops.add("vector", lambda e: e.tensor_scalar(
        ac[:, 2, :], ac[:, 0, :], 1.0 / DM, None, ALU.mult), wait=st1, inc=sV.bump())
    ops.add("vector", lambda e: e.tensor_scalar(
        ac[:, 4, :], ac[:, 1, :], 1.0 / DM, None, ALU.mult), wait=st1, inc=sV.bump())
    ops.add("vector", lambda e: e.scalar_tensor_tensor(
        ac[:, 5, :], ac[:, 2, :], 1.0, ac[:, 2, :], ALU.mult, ALU.mult),
        wait=[sV.at()], inc=sV.bump())
    ops.add("vector", lambda e: e.tensor_sub(
        ac[:, 4, :], ac[:, 4, :], ac[:, 5, :]), wait=[sV.at()], inc=sV.bump())
    ops.add("vector", lambda e: e.tensor_scalar(
        ac[:, 4, :], ac[:, 4, :], LN_EPS, None, ALU.add), wait=[sV.at()], inc=sV.bump())
    ops.add("scalar", lambda e: e.activation(
        ac[:, 5, :], ac[:, 4, :], AF.Sqrt), wait=[sV.at(), sA.at()], inc=sA.bump())
    ops.add("vector", lambda e: e.reciprocal(
        ac[:, 3, :], ac[:, 5, :]), wait=[sA.at()], inc=sV.bump())
    ops.add("vector", lambda e: e.scalar_tensor_tensor(
        ac[:, 5, :], ac[:, 2, :], -1.0, ac[:, 3, :], ALU.mult, ALU.mult),
        wait=[sV.at()], inc=sV.bump())
    stats_done = [sV.at()]
    for i in range(NCK):
        pb = i % 2
        ops.add("sync", lambda e, i=i, pb=pb: e.dma_start(
            out=xA[:, pb, :], in_=b.in_x[i * 128:(i + 1) * 128, :]),
            wait=[sA.at()] if i >= 2 else stats_done, inc=sP2[pb].bump(16))
        ld = sP2[pb].at()
        ops.add("scalar", lambda e, i=i, pb=pb: e.activation(
            xnO[:, pb, :], xA[:, pb, :], AF.Identity,
            bias=ac[:, 5, i:i + 1], scale=ac[:, 3, i:i + 1]),
            wait=[ld] + stats_done + [sXn.at()], inc=sA.bump())
        ops.add("gpsimd", lambda e, i=i, pb=pb: e.dma_start(
            out=b.hb_xn[i * 128:(i + 1) * 128, :], in_=xnO[:, pb, :]),
            wait=[sA.at()], inc=sXn.bump(16))
    ph1.close()
    xn_done = [sXn.at()]

    b.resid = b.sb("g_resid", [128, NQC, DM], F32)
    php = contextlib.ExitStack()
    psA = b.ps("a_psA", [128, 3072], F32, php)
    ps_t = b.ps("a_pst", [128, 1024], BF16, php)
    ph0 = contextlib.ExitStack()
    onesb = b.sb("a_onesb", [1, 128], BF16, ph0)
    bor = b.sb("a_bor", [1, DM], BF16, ph0)
    x0b = b.sb("a_x0b", [128, 2, DM], F32, ph0)
    ops.add("vector", lambda e: e.memset(onesb[:, :], 1.0), inc=sV.bump())
    sBo = b.sem("a_bo")
    ops.add("gpsimd", lambda e: e.dma_start(out=bor[0:1, :], in_=b.in_bo[0:1, :]),
            inc=sBo.bump(16))
    for s4 in range(4):
        ops.add("tensor", lambda e, s4=s4: e.matmul(
            psA[:, s4 * 512:(s4 + 1) * 512],
            onesb[0:1, :], bor[0:1, s4 * 512:(s4 + 1) * 512],
            start=True, stop=True), wait=[sV.at(), sBo.at()] if s4 == 0 else [],
            inc=sT.bump() if s4 == 3 else None)
    bob_done = sT.at()
    for qc in range(NQC):
        pb = qc % 2
        ops.add("sync", lambda e, qc=qc, pb=pb: e.dma_start(
            out=x0b[:, pb, :],
            in_=b.in_x[HALF + qc * 128:HALF + (qc + 1) * 128, :]),
            wait=(xn_done + [sA.at(), sV.at()]) if qc < 2 else [sV.at()],
            inc=sI0[pb].bump(16))
        x0ld = sI0[pb].at()
        for ds in range(4):
            ops.add("vector", lambda e, qc=qc, ds=ds, pb=pb: e.tensor_add(
                b.resid[:, qc, ds * 512:(ds + 1) * 512],
                x0b[:, pb, ds * 512:(ds + 1) * 512],
                psA[:, ds * 512:(ds + 1) * 512]),
                wait=[x0ld, bob_done] if ds == 0 else [], inc=sV.bump())
    ph0.close()
    init_done = [sV.at()]

    ph2 = contextlib.ExitStack()
    K_sb = b.sb("a_K", [128, LSEQ], BF16, ph2)
    Vf = b.sb("a_Vf", [128, 1024], BF16, ph2)
    Vt = b.sb("a_Vt", [128, 32, EH], BF16, ph2)
    Q_sb = b.sb("a_Q", [128, HALF], BF16, ph2)
    zh = b.sb("a_zh", [128, HALF], BF16, ph2)
    xq = b.sb("a_xq", [128, 2, 1024], BF16, ph2)
    wqkv = b.sb("a_wqkv", [128, 16, 2, EH], BF16, ph2)
    wq_s = b.sb("a_wqs", [128, 2, EH], BF16, ph2)
    wo_sb = b.sb("a_wosb", [128, DM], BF16, ph2)
    scs = b.sb("a_scs", [128, LSEQ], F32, ph2)
    prs = b.sb("a_prs", [128, 2, 512], BF16, ph2)
    prT = b.sb("a_prT", [128, 32, 128], BF16, ph2)
    mtiles = b.sb("a_mtiles", [128, 6, 512], BF16, ph2)
    bqkv = b.sb("a_bqkv", [128, 3 * NH], F32, ph2)
    lparts = b.sb("a_lp", [128, 8], F32, ph2)
    linv = b.sb("a_linv", [128, NQC], F32, ph2)
    ops.add("sync", lambda e: e.dma_start(out=bqkv[:, :], in_=b.in_bqkv[:, :]),
            inc=sKp.bump(16))
    ops.add("sync", lambda e: e.dma_start(out=scs[:, 0:3072], in_=b.in_mtiles[:, :]),
            inc=sKp.bump(16))
    ops.add("vector", lambda e: e.tensor_copy(
        bass.AP(mtiles, 0, [[6 * 512, 128], [1, 3072]]), scs[:, 0:3072]),
        wait=[sKp.at()], inc=sV.bump())
    kp_done = [sKp.at(), sV.at()]

    NQG = 4
    for h in range(NH):
        for wi, wsrc in ((0, b.hb_wk), (1, b.hb_wv)):
            ops.add("sync", lambda e, h=h, wi=wi, wsrc=wsrc: e.dma_start(
                out=bass.AP(wqkv, wi * EH, [[16 * 2 * EH, 128], [2 * EH, 16], [1, EH]]),
                in_=bass.AP(wsrc, h * DM * EH, [[EH, 128], [128 * EH, 16], [1, EH]])),
                wait=[sT.at()], inc=sWc.bump(16))
        ops.add("gpsimd", lambda e, h=h: e.dma_start(
            out=wo_sb[:, :], in_=b.in_wo[h, :, :]),
            wait=[sT.at()], inc=sO.bump(16))
        wld = sWc.at()
        for qt in range(4):
            for ci in range(16):
                pb = ci % 2
                ops.add("sync", lambda e, h=h, qt=qt, ci=ci, pb=pb: e.dma_start(
                    out=xq[:, pb, :],
                    in_=b.hb_xn[qt * 1024:(qt + 1) * 1024, ci * 128:(ci + 1) * 128],
                    transpose=True),
                    wait=(xn_done + init_done if (h == 0 and qt == 0 and ci < 2)
                          else [sT.at()]),
                    inc=sXq[pb].bump(16))
                xld = sXq[pb].at()
                wt0 = [xld, wld]
                if ci == 0:
                    wt0 = wt0 + [sV.at()]
                if qt >= 2:
                    ops.add("sync", lambda e, h=h, ci=ci, pb=pb: e.dma_start(
                        out=wq_s[:, pb, :],
                        in_=b.hb_wq[h * DM + ci * 128:h * DM + (ci + 1) * 128, :]),
                        wait=[sT.at()], inc=sQs[pb].bump(16))
                    wt0 = wt0 + [sQs[pb].at()]
                for nn in range(2):
                    ops.add("tensor", lambda e, h=h, ci=ci, nn=nn, pb=pb: e.matmul(
                        psA[:, nn * 512:(nn + 1) * 512],
                        wqkv[:, ci, 0, :], xq[:, pb, nn * 512:(nn + 1) * 512],
                        start=(ci == 0), stop=(ci == 15)),
                        wait=wt0 if nn == 0 else [])
                    ops.add("tensor", lambda e, h=h, ci=ci, nn=nn, pb=pb: e.matmul(
                        psA[:, 1024 + nn * 512:1024 + (nn + 1) * 512],
                        wqkv[:, ci, 1, :], xq[:, pb, nn * 512:(nn + 1) * 512],
                        start=(ci == 0), stop=(ci == 15)),
                        inc=None if qt >= 2 else (sT.bump() if nn == 1 else None))
                    if qt >= 2:
                        ops.add("tensor", lambda e, h=h, ci=ci, nn=nn, pb=pb: e.matmul(
                            psA[:, 2048 + nn * 512:2048 + (nn + 1) * 512],
                            wq_s[:, pb, :], xq[:, pb, nn * 512:(nn + 1) * 512],
                            start=(ci == 0), stop=(ci == 15)),
                            inc=sT.bump() if nn == 1 else None)
            mm_done = sT.at()
            ops.add("vector", lambda e, qt=qt, h=h: e.tensor_scalar(
                K_sb[:, qt * 1024:(qt + 1) * 1024], psA[:, 0:1024],
                bqkv[:, 3 * h + 1:3 * h + 2], None, ALU.add),
                wait=[mm_done] + kp_done + [sV.at()], inc=sV.bump())
            ops.add("vector", lambda e, qt=qt, h=h: e.tensor_scalar(
                Vf[:, :], psA[:, 1024:2048],
                bqkv[:, 3 * h + 2:3 * h + 3], None, ALU.add),
                wait=[sV.at()], inc=sV.bump())
            if qt >= 2:
                ops.add("vector", lambda e, qt=qt, h=h: e.tensor_scalar(
                    Q_sb[:, (qt - 2) * 1024:(qt - 1) * 1024], psA[:, 2048:3072],
                    bqkv[:, 3 * h:3 * h + 1], None, ALU.add),
                    wait=[sV.at()], inc=sV.bump())
            ev_done = sV.at()
            for v4 in range(2):
                for vj in range(4):
                    ops.add("tensor", lambda e, v4=v4, vj=vj: e.transpose(
                        ps_t[:, vj * 128:(vj + 1) * 128],
                        Vf[:, v4 * 512 + vj * 128:v4 * 512 + (vj + 1) * 128],
                        b.ident[:, :]),
                        wait=[ev_done, sV.at()] if vj == 0 else [], inc=sT.bump())
                ops.add("vector", lambda e, qt=qt, v4=v4: e.tensor_copy(
                    bass.AP(Vt, (qt * 8 + v4 * 4) * EH,
                            [[32 * EH, 128], [EH, 4], [1, 128]]),
                    ps_t[:, 0:512]), wait=[sT.at()], inc=sV.bump())
        kvq_done = [sV.at(), sT.at()]

        for g4 in range(NQG):
            nseg = 5 + g4
            kext = 512 * nseg
            for qi in range(4):
                    qc = g4 * 4 + qi
                    for sg in range(nseg):
                        sp = sg % 2
                        if sg < 4:
                            msel = 0
                        elif sg == nseg - 1:
                            msel = 2 + (qc % 4)
                        else:
                            msel = 1
                        wt = (kvq_done + kp_done + [sV.at()]) if sg < 2 else [sV.at()]
                        ops.add("tensor", lambda e, qc=qc, sg=sg, sp=sp: e.matmul(
                            psA[:, sp * 512:(sp + 1) * 512],
                            Q_sb[:, qc * 128:(qc + 1) * 128],
                            K_sb[:, sg * 512:(sg + 1) * 512],
                            start=True, stop=True), wait=wt, inc=sT.bump())
                        ops.add("vector", lambda e, sg=sg, sp=sp, msel=msel: e.scalar_tensor_tensor(
                            scs[:, sg * 512:(sg + 1) * 512],
                            psA[:, sp * 512:(sp + 1) * 512],
                            INV_SQRT_EH, mtiles[:, msel, :],
                            ALU.mult, ALU.add), wait=[sT.at()],
                            inc=sV.bump())
                    ops.add("vector", lambda e, kext=kext: e.reduce_max(
                        c[:, 176:177], scs[:, 0:kext], AX.X, negate=True),
                        wait=[sV.at()], inc=sV.bump())
                    mx_done = sV.at()
                    for sg in range(nseg):
                        sp = sg % 2
                        ops.add("scalar", lambda e, sg=sg, sp=sp: e.activation(
                            prs[:, sp, :], scs[:, sg * 512:(sg + 1) * 512],
                            AF.Exp, bias=c[:, 176:177],
                            accum_out=lparts[:, sg:sg + 1]),
                            wait=[mx_done, sA.at(), sT.at()], inc=sA.bump())
                        exp_done = sA.at()
                        for kc4 in range(4):
                            ops.add("tensor", lambda e, sp=sp, kc4=kc4: e.transpose(
                                ps_t[:, kc4 * 128:(kc4 + 1) * 128],
                                prs[:, sp, kc4 * 128:(kc4 + 1) * 128],
                                b.ident[:, :]),
                                wait=[exp_done, sV.at()] if kc4 == 0 else [],
                                inc=sT.bump())
                        ops.add("vector", lambda e, sg=sg: e.tensor_copy(
                            bass.AP(prT, sg * 4 * 128,
                                    [[32 * 128, 128], [128, 4], [1, 128]]),
                            ps_t[:, 0:512]), wait=[sT.at()], inc=sV.bump())
                    ops.add("vector", lambda e, nseg=nseg: e.tensor_reduce(
                        c[:, 177:178].rearrange("p (g o) -> p g o", o=1),
                        lparts[:, 0:nseg].rearrange("p (g o) -> p g o", o=nseg),
                        AX.X, ALU.add), wait=[sA.at(), sV.at()], inc=sV.bump())
                    ops.add("vector", lambda e, qc=qc: e.reciprocal(
                        linv[:, qc:qc + 1], c[:, 177:178]),
                        wait=[sV.at()], inc=sV.bump())
                    trp_done = sV.at()
                    for kc in range(4 * nseg):
                        wt = [trp_done, sV.at()] if kc == 0 else []
                        ops.add("tensor", lambda e, kc=kc, nseg=nseg: e.matmul(
                            psA[:, 1024:1152], Vt[:, kc, :], prT[:, kc, :],
                            start=(kc == 0), stop=(kc == 4 * nseg - 1)),
                            wait=wt, inc=sT.bump() if kc == 4 * nseg - 1 else None)
                    ops.add("vector", lambda e, qc=qc: e.tensor_copy(
                        zh[:, qc * 128:(qc + 1) * 128],
                        psA[:, 1024:1152]), wait=[sT.at()], inc=sV.bump())
        z_done = [sV.at(), sT.at()]

        wo_ld = sO.at()
        for qc in range(NQC):
            for ds in range(4):
                wt = [sV.at()] if ds == 0 else []
                if qc == 0 and ds == 0:
                    wt = z_done + [wo_ld, sV.at()]
                ops.add("tensor", lambda e, h=h, qc=qc, ds=ds: e.matmul(
                    psA[:, 1024 + ds * 512:1024 + (ds + 1) * 512],
                    zh[:, qc * 128:(qc + 1) * 128],
                    wo_sb[:, ds * 512:(ds + 1) * 512],
                    start=True, stop=True), wait=wt,
                    inc=sT.bump() if ds == 3 else None)
            wmm = sT.at()
            for ds in range(4):
                ops.add("vector", lambda e, qc=qc, ds=ds: e.scalar_tensor_tensor(
                    b.resid[:, qc, ds * 512:(ds + 1) * 512],
                    psA[:, 1024 + ds * 512:1024 + (ds + 1) * 512],
                    linv[:, qc:qc + 1],
                    b.resid[:, qc, ds * 512:(ds + 1) * 512],
                    ALU.mult, ALU.add), wait=[wmm], inc=sV.bump())
    ph2.close()
    php.close()
    b.resid_ready = [sV.at()]
def _emit_ln_stats(b, tag, sq_scr):
    ops = b.ops
    c = b.colA
    sV = b.sem(f"{tag}_lv")
    sA = b.sem(f"{tag}_la")
    rw = b.resid_ready
    for qc in range(NQC):
        ops.add("scalar", lambda e, qc=qc: e.activation(
            sq_scr, b.resid[:, qc, :], AF.Square,
            accum_out=c[:, 48 + qc:49 + qc]), wait=rw + [sA.at()], inc=sA.bump())
    ops.add("vector", lambda e: e.tensor_reduce(
        c[:, 64:80].rearrange("p (g o) -> p g o", o=1),
        bass.AP(b.resid, 0, [[NQC * DM, 128], [DM, NQC], [1, DM]]),
        AX.X, ALU.add), wait=rw, inc=sV.bump())
    ops.add("vector", lambda e: e.tensor_scalar(
        c[:, 0:16], c[:, 64:80], 1.0 / DM, None, ALU.mult),
        wait=[sV.at()], inc=sV.bump())
    ops.add("vector", lambda e: e.tensor_scalar(
        c[:, 80:96], c[:, 48:64], 1.0 / DM, None, ALU.mult),
        wait=[sA.at(), sV.at()], inc=sV.bump())
    ops.add("vector", lambda e: e.scalar_tensor_tensor(
        c[:, 16:32], c[:, 0:16], 1.0, c[:, 0:16], ALU.mult, ALU.mult),
        wait=[sV.at()], inc=sV.bump())
    ops.add("vector", lambda e: e.tensor_sub(
        c[:, 80:96], c[:, 80:96], c[:, 16:32]), wait=[sV.at()], inc=sV.bump())
    ops.add("vector", lambda e: e.tensor_scalar(
        c[:, 80:96], c[:, 80:96], LN_EPS, None, ALU.add), wait=[sV.at()], inc=sV.bump())
    ops.add("scalar", lambda e: e.activation(
        c[:, 96:112], c[:, 80:96], AF.Sqrt), wait=[sV.at()], inc=sA.bump())
    ops.add("vector", lambda e: e.reciprocal(
        c[:, 16:32], c[:, 96:112]), wait=[sA.at()], inc=sV.bump())
    ops.add("vector", lambda e: e.scalar_tensor_tensor(
        c[:, 32:48], c[:, 0:16], -1.0, c[:, 16:32], ALU.mult, ALU.mult),
        wait=[sV.at()], inc=sV.bump())
    return [sV.at()]



def _emit_ffn(b, tag):
    ops = b.ops
    sW = b.sem(f"{tag}_w")
    sV = b.sem(f"{tag}_v")
    sA = b.sem(f"{tag}_a")
    sT = b.sem(f"{tag}_t")
    qd = b.quant_done

    ph = contextlib.ExitStack()
    hbuf = b.sb(f"{tag}_h", [128, 2, FH], BF16, ph)
    xnT = b.sb(f"{tag}_xnT", [128, 16, 256], BF16, ph)
    wg = [b.sb(f"{tag}_wg{i}", [128, 8, 512], BF16, ph) for i in range(2)]
    wu = [b.sb(f"{tag}_wu{i}", [128, 8, 512], BF16, ph) for i in range(2)]
    wpq = b.sb(f"{tag}_wpq", [128, 4, 512], BF16, ph)
    htf = [b.sb(f"{tag}_htf{i}", [128, 256], BF16, ph) for i in range(2)]
    sg = b.sb(f"{tag}_sg", [128, 512], BF16, ph)
    ssqp = b.sb(f"{tag}_ssqp", [128, 2, 16], F32, ph)
    inv_c = b.sb(f"{tag}_inv", [128, 8], F32, ph)
    ps_t = b.ps(f"{tag}_pst", [128, 1024], BF16, ph)
    A4 = b.ps(f"{tag}_a4", [128, 2048], F32, ph)
    c = b.colA
    NT = DEV_FFN_TILES
    NHG = FH // 512

    sq_scr = hbuf[:, 1, 0:DM]
    rw0 = list(b.resid_ready)
    for eng in ("sync", "gpsimd", "tensor", "vector", "scalar"):
        ops.barrier(eng, rw0)
    ln_done = _emit_ln_stats(b, tag, sq_scr)

    war_wg = [None, None]
    war_wp = [None, None]
    prev_tile_done = []

    for tt in range(NT):
        xnT_marks = []
        for tc, qc in enumerate((2 * tt, 2 * tt + 1)):
            xv = hbuf[:, 0, 0:DM]
            ops.add("scalar", lambda e, qc=qc, xv=xv: e.activation(
                xv, b.resid[:, qc, :], AF.Identity,
                bias=c[:, 32 + qc:33 + qc], scale=c[:, 16 + qc:17 + qc]),
                wait=ln_done + [sA.at(), sT.at()] + prev_tile_done, inc=sA.bump())
            for j0 in range(0, 16, 4):
                for j in range(j0, j0 + 4):
                    ops.add("tensor", lambda e, j=j, j0=j0, xv=xv: e.transpose(
                        ps_t[:, (j - j0) * 128:(j - j0 + 1) * 128],
                        xv[:, j * 128:(j + 1) * 128], b.ident[:, :]),
                        wait=[sA.at(), sV.at()] if j == j0 else [], inc=sT.bump())
                ops.add("vector", lambda e, j0=j0, tc=tc: e.tensor_copy(
                    bass.AP(xnT, tc * 128 + j0 * 256,
                            [[16 * 256, 128], [256, 4], [1, 128]]),
                    ps_t[:, 0:512]), wait=[sT.at()], inc=sV.bump())
        xnT_done = sV.at()

        for hg in range(NHG):
            slotm = []
            for half_i in range(2):

                def _ldg(e, hi=half_i, hg=hg):
                    out_ap = bass.AP(wg[hi], 0, [[8 * 512, 128], [512, 8], [1, 512]])
                    in_ap = bass.AP(b.hb_wguT, hg * 512 + hi * 8 * 128 * 2 * FH,
                                    [[2 * FH, 128], [128 * 2 * FH, 8], [1, 512]])
                    return e.dma_start(out=out_ap, in_=in_ap)

                def _ldu(e, hi=half_i, hg=hg):
                    out_ap = bass.AP(wu[hi], 0, [[8 * 512, 128], [512, 8], [1, 512]])
                    in_ap = bass.AP(b.hb_wguT, FH + hg * 512 + hi * 8 * 128 * 2 * FH,
                                    [[2 * FH, 128], [128 * 2 * FH, 8], [1, 512]])
                    return e.dma_start(out=out_ap, in_=in_ap)
                w = [qd[0]] if (tt == 0 and hg == 0) else []
                if war_wg[half_i] is not None:
                    w = w + [war_wg[half_i]]
                sg_s = b.sem(f"{tag}_wg{half_i}")
                su_s = b.sem(f"{tag}_wu{half_i}")
                ops.add("sync", _ldg, wait=w, inc=sg_s.bump(16))
                ops.add("sync", _ldu, inc=su_s.bump(16))
                slotm.append((sg_s.at(), su_s.at()))

            pb = hg % 2
            g_ps = A4[:, pb * 512:(pb + 1) * 512]
            u_ps = A4[:, 1024 + pb * 512:1024 + (pb + 1) * 512]
            for tc in range(2):
                for d in range(16):
                    hi, dd = (0, d) if d < 8 else (1, d - 8)
                    wt = []
                    if d == 0:
                        wt = [xnT_done, sA.at(), sV.at()]
                    if tc == 0 and (d == 0 or d == 8):
                        wt = wt + [slotm[hi][0], slotm[hi][1]]
                    ops.add("tensor", lambda e, dd=dd, hi=hi, tc=tc, d=d, g_ps=g_ps: e.matmul(
                        g_ps, xnT[:, d, tc * 128:(tc + 1) * 128],
                        wg[hi][:, dd, :], start=(d == 0), stop=(d == 15)),
                        wait=wt)
                    ops.add("tensor", lambda e, dd=dd, hi=hi, tc=tc, d=d, u_ps=u_ps: e.matmul(
                        u_ps, xnT[:, d, tc * 128:(tc + 1) * 128],
                        wu[hi][:, dd, :], start=(d == 0), stop=(d == 15)),
                        inc=sT.bump() if d == 15 else None)
                mm_done = sT.at()
                ops.add("scalar", lambda e, g_ps=g_ps: e.activation(
                    sg[:, :], g_ps, AF.Silu),
                    wait=[mm_done, sV.at()], inc=sA.bump())
                ops.add("vector", lambda e, u_ps=u_ps, tc=tc, hg=hg: e.tensor_tensor(
                    hbuf[:, tc, hg * 512:(hg + 1) * 512], sg[:, :],
                    u_ps, ALU.mult), wait=[sA.at()], inc=sV.bump())
                ops.add("scalar", lambda e, tc=tc, hg=hg: e.activation(
                    sg[:, :], hbuf[:, tc, hg * 512:(hg + 1) * 512], AF.Square,
                    accum_out=ssqp[:, tc, hg:hg + 1]), wait=[sV.at()], inc=sA.bump())
            war_wg[0] = war_wg[1] = sT.at()
        h_done_V = sV.at()
        h_done_A = sA.at()

        ops.add("vector", lambda e: e.tensor_reduce(
            inv_c[:, 0:2].rearrange("p (g o) -> p g o", o=1),
            ssqp[:, :, :], AX.X, ALU.add), wait=[h_done_A, sV.at()], inc=sV.bump())
        ops.add("vector", lambda e: e.tensor_scalar(
            inv_c[:, 2:4], inv_c[:, 0:2], 1.0 / FH, RMS_EPS, ALU.mult, ALU.add),
            wait=[sV.at()], inc=sV.bump())
        ops.add("scalar", lambda e: e.activation(
            inv_c[:, 4:6], inv_c[:, 2:4], AF.Sqrt), wait=[sV.at()], inc=sA.bump())
        ops.add("vector", lambda e: e.reciprocal(
            inv_c[:, 6:8], inv_c[:, 4:6]), wait=[sA.at()], inc=sV.bump())
        inv_done = sV.at()

        NHC = FH // 128
        for dh in range(2):
            for hc in range(NHC):
                pb = hc % 2
                qmarks = []
                for q in range(2):
                    slot = pb * 2 + q
                    w = []
                    if tt == 0 and dh == 0 and hc == 0 and q == 0:
                        w = [qd[0]]
                    if war_wp[pb] is not None and q == 0:
                        w = w + [war_wp[pb]]
                    sq_s = b.sem(f"{tag}_wq{slot}")
                    ops.add("sync", lambda e, hc=hc, dh=dh, q=q, slot=slot: e.dma_start(
                        out=wpq[:, slot, :],
                        in_=b.hb_wpT[hc * 128:(hc + 1) * 128,
                                     dh * 1024 + q * 512:dh * 1024 + (q + 1) * 512]),
                        wait=w, inc=sq_s.bump(16))
                    qmarks.append(sq_s.at())
                for tc in range(2):
                    tw = [sV.at()]
                    if hc == 0 and dh == 0:
                        tw += [h_done_V]
                    ops.add("tensor", lambda e, tc=tc, hc=hc: e.transpose(
                        ps_t[:, tc * 512:tc * 512 + 128],
                        hbuf[:, tc, hc * 128:(hc + 1) * 128], b.ident[:, :]),
                        wait=tw, inc=sT.bump())
                    ops.add("vector", lambda e, tc=tc: e.tensor_copy(
                        htf[tc][:, 0:128], ps_t[:, tc * 512:tc * 512 + 128]),
                        wait=[sT.at()], inc=sV.bump())
                    tr_done = sV.at()
                    for q in range(2):
                        wt = [qmarks[q]] if tc == 0 else []
                        if q == 0:
                            wt += [tr_done]
                            if hc == 0 and tc == 0:
                                wt += [h_done_A, sV.at()]
                                if dh == 1 or tt > 0:
                                    wt += [prev_evac_done]
                        ops.add("tensor", lambda e, q=q, hc=hc, tc=tc,
                                slot=pb * 2 + q: e.matmul(
                            A4[:, tc * 1024 + q * 512:tc * 1024 + (q + 1) * 512],
                            htf[tc][:, 0:128], wpq[:, slot, :],
                            start=(hc == 0), stop=(hc == NHC - 1)),
                            wait=wt, inc=sT.bump() if (tc == 1 and q == 1) else None)
                war_wp[pb] = sT.at()
            mm2_done = sT.at()
            for tc in range(2):
                for q in range(2):
                    ops.add("vector", lambda e, tc=tc, tt=tt, dh=dh, q=q: e.scalar_tensor_tensor(
                        b.resid[:, 2 * tt + tc,
                                dh * 1024 + q * 512:dh * 1024 + (q + 1) * 512],
                        A4[:, tc * 1024 + q * 512:tc * 1024 + (q + 1) * 512],
                        inv_c[:, 6 + tc:7 + tc],
                        b.resid[:, 2 * tt + tc,
                                dh * 1024 + q * 512:dh * 1024 + (q + 1) * 512],
                        ALU.mult, ALU.add),
                        wait=[mm2_done, inv_done] if (tc == 0 and q == 0) else [],
                        inc=sV.bump())
            prev_evac_done = sV.at()
        prev_tile_done = [sV.at()]
    b.resid_ready = [sV.at()]
    ph.close()



def _emit_ssm(b):
    ops = b.ops
    sV = b.sem("s_v")
    sA = b.sem("s_a")
    sT = b.sem("s_t")
    sWi = [b.sem("s_wi0"), b.sem("s_wi1")]
    sWo = [b.sem(f"s_wo{i}") for i in range(4)]
    sCt = b.sem("s_ctail")
    sCtC = b.sem("s_ctailc")
    sCs = b.sem("s_cstate")
    sCsC = b.sem("s_cstatec")
    sSm = b.sem("s_sm")
    sCz = b.sem("s_czst")
    sZl = b.sem("s_czld")
    sSm2 = b.sem("s_sm2")

    ph = contextlib.ExitStack()
    zg = b.sb("s_zg", [128, 2, DSSM], BF16, ph)
    xbc = b.sb("s_xbc", [128, 34, 256], BF16, ph)
    xnT = b.sb("s_xnT", [128, 16, 256], BF16, ph)
    tails = b.sb("s_tails", [128, 34, 4], BF16, ph)
    raw = b.sb("s_raw", [128, 1, 260], BF16, ph)
    st = b.sb("s_st", [128, 64, PS], BF16, ph)
    win_sb = [b.sb("s_win0", [128, 16, 128], BF16, ph)]
    win_sb = [win_sb[0], win_sb[0]]
    wo_sb = b.sb("s_wo", [128, 2, 512], BF16, ph)
    dtf = b.sb("s_dtf", [128, 3, 256], F32, ph)
    dtm = b.sb("s_dtm", [128, 4, PS], F32, ph)
    bch = b.sb("s_bch", [128, 2, 128], F32, ph)
    btm = b.sb("s_btm", [128, 128], BF16, ph)
    gmsk = b.sb("s_gmsk", [128, 128], BF16, ph)
    marg = b.sb("s_marg", [128, 128], F32, ph)
    sph = b.sb("s_sph", [128, 2, 128], BF16, ph)
    u3 = b.sb("s_u3", [128, 2, 3, PS], BF16, ph)
    ysc = b.sb("s_ysc", [128, 2, 2 * PS], BF16, ph)
    szh = b.sb("s_szh", [128, PS], BF16, ph)
    xnt3 = b.sb("s_xnt3", [128, 16, 4], BF16, ph)
    gcol = b.sb("s_gcol", [128, 2], F32, ph)
    dcol = b.sb("s_dcol", [128, 2, 64], F32, ph)
    tailx = b.sb("s_tailx", [4, DM], BF16, ph)
    ones1 = b.sb("s_ones1", [1, 128], F32, ph)
    identf = b.sb("s_identf", [128, 128], F32, ph)
    tri_sb = b.sb("s_tri", [128, 128], BF16, ph)
    zeros_sb = b.sb("s_zeros", [128, 128], BF16, ph)
    convw_sb = b.sb("s_convw", [128, 136], F32, ph)
    convb_sb = b.sb("s_convb", [128, 34], F32, ph)
    ssmcol_sb = b.sb("s_ssmcol", [64, 4], F32, ph)
    mup_sb = b.sb("s_mup", [128, 67], F32, ph)
    normw_sb = b.sb("s_normw", [128, 32], F32, ph)
    ssmrow_sb = b.sb("s_ssmrow", [1, 192], F32, ph)
    ps_t = b.ps("s_pst", [128, 2, 512], BF16, ph)
    ps_f = b.ps("s_psf", [128, 512], F32, ph)
    ps_f2 = b.ps("s_psf2", [128, 512], F32, ph)
    A4 = b.ps("s_a4", [128, 2048], F32, ph)
    c = b.colA
    sq_scr = bass.AP(xbc, 0, [[34 * 256, 128], [1, DM]])
    zb = raw[:, 0, 0:256]
    sq_scr3 = bass.AP(xbc, 0, [[34 * 256, 3], [1, DM]])

    rw0 = list(b.resid_ready)
    for eng in ("sync", "gpsimd", "tensor", "vector", "scalar"):
        ops.barrier(eng, rw0)
    ln_done = _emit_ln_stats(b, "ssm", sq_scr)
    NCH = HALF // 256

    ph_iof = contextlib.ExitStack()
    iof = b.sb("s_iof", [128, 256], BF16, ph_iof)
    ops.add("gpsimd", lambda e: e.iota(iof[:, 0:128], [[1, 128]], channel_multiplier=0,
                                       allow_small_or_imprecise_dtypes=True))
    ops.add("gpsimd", lambda e: e.iota(iof[:, 128:256], [[0, 128]], channel_multiplier=1,
                                       allow_small_or_imprecise_dtypes=True),
            inc=b.sem("s_g").bump())
    sG = b.sem("s_g")
    ops.add("vector", lambda e: e.tensor_tensor(
        tri_sb[:, :], iof[:, 128:256], iof[:, 0:128], ALU.is_le),
        wait=[sG.at()], inc=sV.bump())
    ops.add("vector", lambda e: e.tensor_tensor(
        identf[:, :], iof[:, 0:128], iof[:, 128:256], ALU.is_equal),
        wait=[sG.at()], inc=sV.bump())
    ops.add("vector", lambda e: e.memset(ones1[:, :], 1.0), inc=sV.bump())
    ops.add("vector", lambda e: e.memset(zeros_sb[:, :], 0.0), inc=sV.bump())
    ops.add("sync", lambda e: e.dma_start(out=convw_sb[:, :], in_=b.in_convw[:, :]),
            inc=sSm.bump(16))
    ops.add("sync", lambda e: e.dma_start(out=convb_sb[:, :], in_=b.in_convb[:, :]),
            inc=sSm.bump(16))
    ops.add("sync", lambda e: e.dma_start(out=ssmcol_sb[:, :], in_=b.in_ssmv[:, :]),
            inc=sSm.bump(16))
    ops.add("sync", lambda e: e.dma_start(out=mup_sb[:, :], in_=b.in_mup[:, :]),
            inc=sSm.bump(16))
    ops.add("sync", lambda e: e.dma_start(out=normw_sb[:, :], in_=b.in_normw[:, :]),
            inc=sSm.bump(16))
    ops.add("sync", lambda e: e.dma_start(out=ssmrow_sb[:, :], in_=b.in_ssmrow[:, :]),
            inc=sSm.bump(16))
    sm_ld = sSm.at()
    sm_ld2 = sSm2.at()

    ops.add("tensor", lambda e: e.matmul(
        ps_f[:, 0:2], ones1[0:1, :], b.gate_sb[0:1, 0:2], start=True, stop=True),
        wait=b.setup_done + [sm_ld, sV.at()], inc=sT.bump())
    ops.add("vector", lambda e: e.tensor_copy(gcol[:, :], ps_f[:, 0:2]),
            wait=[sT.at()], inc=sV.bump())
    ops.add("tensor", lambda e: e.matmul(
        ps_f[:, 64:128], ones1[0:1, :], ssmrow_sb[0:1, 64:128], start=True, stop=True),
        wait=[sV.at()], inc=sT.bump())
    ops.add("vector", lambda e: e.tensor_copy(dcol[:, 0, :], ps_f[:, 64:128]),
            wait=[sT.at()], inc=sV.bump())
    ops.add("scalar", lambda e: e.activation(
        c[0:64, 190:191], ssmcol_sb[0:64, 0:1], AF.Exp),
        wait=[sm_ld, sm_ld2], inc=sA.bump())
    ops.add("vector", lambda e: e.tensor_scalar(
        c[0:64, 190:191], c[0:64, 190:191], -1.0, None, ALU.mult),
        wait=[sA.at()], inc=sV.bump())
    setup2 = [sV.at(), sA.at(), sT.at()]
    ph_iof.close()

    ops.add("gpsimd", lambda e: e.dma_start(
        out=b.cc_tail_in[0:3, :], in_=b.resid[125:128, 15, :]),
        wait=ln_done, inc=sCt.bump(16))
    ops.add("gpsimd", lambda e: e.collective_compute(
        "AllGather", ALU.bypass,
        replica_groups=[[0, 1], [2, 3], [4, 5], [6, 7]],
        ins=[b.cc_tail_in.ap().opt()], outs=[b.cc_tail_out.ap().opt()]),
        wait=[sCt.at()], inc=sCtC.bump(1))
    ops.add("gpsimd", lambda e: e.dma_start(
        out=tailx[0:3, :], in_=b.cc_tail_out[0:3, :]),
        wait=[sCtC.at()], inc=sCt.bump(16))
    tail_ld = sCt.at()
    ops.add("vector", lambda e: e.tensor_reduce(
        c[0:3, 160:161].rearrange("p (g o) -> p g o", o=1),
        tailx[0:3, :].rearrange("p (g o) -> p g o", o=DM), AX.X, ALU.add),
        wait=[tail_ld, sV.at()], inc=sV.bump())
    ops.add("scalar", lambda e: e.activation(
        sq_scr3, tailx[0:3, :], AF.Square,
        accum_out=c[0:3, 161:162]), wait=[tail_ld, sA.at()], inc=sA.bump())
    ops.add("vector", lambda e: e.tensor_scalar(
        c[0:3, 162:163], c[0:3, 160:161], 1.0 / DM, None, ALU.mult),
        wait=[sV.at(), sA.at()], inc=sV.bump())
    ops.add("vector", lambda e: e.tensor_scalar(
        c[0:3, 163:164], c[0:3, 161:162], 1.0 / DM, None, ALU.mult),
        wait=[sV.at()], inc=sV.bump())
    ops.add("vector", lambda e: e.scalar_tensor_tensor(
        c[0:3, 164:165], c[0:3, 162:163], 1.0, c[0:3, 162:163], ALU.mult, ALU.mult),
        wait=[sV.at()], inc=sV.bump())
    ops.add("vector", lambda e: e.tensor_sub(
        c[0:3, 163:164], c[0:3, 163:164], c[0:3, 164:165]),
        wait=[sV.at()], inc=sV.bump())
    ops.add("vector", lambda e: e.tensor_scalar(
        c[0:3, 163:164], c[0:3, 163:164], LN_EPS, None, ALU.add),
        wait=[sV.at()], inc=sV.bump())
    ops.add("scalar", lambda e: e.activation(
        c[0:3, 165:166], c[0:3, 163:164], AF.Sqrt), wait=[sV.at(), sA.at()],
        inc=sA.bump())
    ops.add("vector", lambda e: e.reciprocal(
        c[0:3, 166:167], c[0:3, 165:166]), wait=[sA.at()], inc=sV.bump())
    ops.add("vector", lambda e: e.tensor_scalar(
        sq_scr3, tailx[0:3, :], c[0:3, 162:163], c[0:3, 166:167],
        ALU.subtract, ALU.mult), wait=[sV.at()], inc=sV.bump())
    tail_xn = sV.at()
    for j in range(16):
        ops.add("tensor", lambda e, j=j: e.transpose(
            ps_t[:, 0, 0:3], bass.AP(xbc, j * 128, [[34 * 256, 3], [1, 128]]), b.ident[0:3, 0:3]),
            wait=([tail_xn] + setup2 + [sV.at()]) if j == 0 else [sV.at()],
            inc=sT.bump())
        ops.add("vector", lambda e, j=j: e.tensor_copy(
            xnt3[:, j, 0:3], ps_t[:, 0, 0:3]), wait=[sT.at()], inc=sV.bump())
    tail_done = [sV.at()]

    cache_war = []

    def chunk(ct, pass_b):
        if pass_b:
            ldw = [sT.at(), sV.at(), sA.at(), sCz.at()]
            ops.add("sync", lambda e, ct=ct: e.dma_start(
                out=bass.AP(zg, 0, [[2 * DSSM, 128], [1, 2 * DSSM]]),
                in_=b.hb_sz[ct * 128:(ct + 1) * 128, :]),
                wait=ldw, inc=sZl.bump(16))
            ops.add("sync", lambda e, ct=ct: e.dma_start(
                out=bass.AP(xbc, 0, [[34 * 256, 128], [1, 34 * 256]]),
                in_=b.hb_xbc[ct * 128:(ct + 1) * 128, :]),
                inc=sZl.bump(16))
            ops.add("sync", lambda e, ct=ct: e.dma_start(
                out=bass.AP(dtf, 0, [[3 * 256, 128], [1, 3 * 256]]),
                in_=b.hb_dtf[ct * 128:(ct + 1) * 128, :]),
                inc=sZl.bump(16))
            ld_done = sZl.at()
            xbc_done = [ld_done]
            sz_done = ld_done
            dt_done = [ld_done]
        else:
            chunk_a(ct)
            xbc_done = [sV.at(), sA.at()]
            sz_done = sA.at()
            dt_done = [sV.at(), sA.at()]
            ops.add("sync", lambda e, ct=ct: e.dma_start(
                out=b.hb_sz[ct * 128:(ct + 1) * 128, :],
                in_=bass.AP(zg, 0, [[2 * DSSM, 128], [1, 2 * DSSM]])),
                wait=[sz_done], inc=sCz.bump(16))
            ops.add("sync", lambda e, ct=ct: e.dma_start(
                out=b.hb_xbc[ct * 128:(ct + 1) * 128, :],
                in_=bass.AP(xbc, 0, [[34 * 256, 128], [1, 34 * 256]])),
                wait=xbc_done, inc=sCz.bump(16))
            ops.add("sync", lambda e, ct=ct: e.dma_start(
                out=b.hb_dtf[ct * 128:(ct + 1) * 128, :],
                in_=bass.AP(dtf, 0, [[3 * 256, 128], [1, 3 * 256]])),
                wait=dt_done, inc=sCz.bump(16))
            cache_war[:] = [sCz.at()]
        chunk_rest(ct, pass_b, xbc_done, sz_done, dt_done)

    def chunk_a(ct):
        for tc, qc in enumerate((2 * ct, 2 * ct + 1)):
            ops.add("scalar", lambda e, qc=qc: e.activation(
                sq_scr, b.resid[:, qc, :], AF.Identity,
                bias=c[:, 32 + qc:33 + qc], scale=c[:, 16 + qc:17 + qc]),
                wait=ln_done + [sA.at(), sT.at()] + list(cache_war), inc=sA.bump())
            for j0 in range(0, 16, 4):
                for j in range(j0, j0 + 4):
                    ops.add("tensor", lambda e, j=j, j0=j0: e.transpose(
                        ps_t[:, 0, (j - j0) * 128:(j - j0 + 1) * 128],
                        bass.AP(xbc, j * 128, [[34 * 256, 128], [1, 128]]), b.ident[:, :]),
                        wait=[sA.at(), sV.at()] if j == j0 else [], inc=sT.bump())
                ops.add("vector", lambda e, j0=j0, tc=tc: e.tensor_copy(
                    bass.AP(xnT, tc * 128 + j0 * 256,
                            [[16 * 256, 128], [256, 4], [1, 128]]),
                    ps_t[:, 0, 0:512]), wait=[sT.at()], inc=sV.bump())
        xnT_done = sV.at()

        prs = list(range(0, 67))
        for pr in prs:
            pb = pr % 2
            span = 128 if pr < 66 else 64
            ops.add("sync", lambda e, pr=pr, pb=pb, span=span: e.dma_start(
                out=win_sb[pb][:, :, 0:span],
                in_=bass.AP(b.hb_win, pr * 128,
                            [[PROJ, 128], [128 * PROJ, 16], [1, span]])),
                wait=[sT.at()], inc=sWi[pb].bump(16))
            wld = sWi[pb].at()
            for d in range(16):
                wt = [wld, xnT_done, sV.at()] if d == 0 else []
                ops.add("tensor", lambda e, d=d, pb=pb, span=span: e.matmul(
                    A4[0:span, 0:256], win_sb[pb][:, d, 0:span], xnT[:, d, 0:256],
                    start=(d == 0), stop=(d == 15)),
                    wait=wt, inc=sT.bump() if d == 15 else None)
            pf_done = sT.at()
            if ct == 0 and pr >= 32:
                for d in range(16):
                    ops.add("tensor", lambda e, d=d, pb=pb, span=span: e.matmul(
                        A4[0:span, 256:259], win_sb[pb][:, d, 0:span], xnt3[:, d, 0:3],
                        start=(d == 0), stop=(d == 15)),
                        inc=sT.bump() if d == 15 else None)
                pf_done = sT.at()

            if pr < 32:
                ops.add("vector", lambda e, pb=pb, pr=pr: e.tensor_scalar(
                    zb[:, :], A4[:, 0:256], mup_sb[:, pr:pr + 1], None, ALU.mult),
                    wait=[pf_done, sV.at()], inc=sV.bump())
                for s2 in range(2):
                    ops.add("tensor", lambda e, s2=s2: e.transpose(
                        ps_t[:, 1, s2 * 128:(s2 + 1) * 128],
                        zb[:, s2 * 128:(s2 + 1) * 128], b.ident[:, :]),
                        wait=[sV.at()] if s2 == 0 else [], inc=sT.bump())
                ops.add("vector", lambda e, pr=pr: e.tensor_copy(
                    bass.AP(zg, pr * 128, [[2 * DSSM, 128], [DSSM, 2], [1, 128]]),
                    ps_t[:, 1, 0:256]), wait=[sT.at()], inc=sV.bump())
            elif pr < 66:
                cr = pr - 32
                if ct == 0:
                    ops.add("vector", lambda e, pb=pb, pr=pr: e.tensor_scalar(
                        raw[:, 0, 0:3], A4[:, 256:259], gcol[:, 0:1],
                        mup_sb[:, pr:pr + 1], ALU.mult, ALU.mult),
                        wait=[pf_done, sV.at()], inc=sV.bump())
                else:
                    ops.add("vector", lambda e, pb=pb, cr=cr: e.tensor_copy(
                        raw[:, 0, 0:3], tails[:, cr, 0:3]),
                        wait=[pf_done, sV.at()], inc=sV.bump())
                ops.add("vector", lambda e, pb=pb, pr=pr: e.tensor_scalar(
                    raw[:, 0, 3:259], A4[:, 0:256], mup_sb[:, pr:pr + 1], None,
                    ALU.mult), wait=[sV.at()], inc=sV.bump())
                ops.add("vector", lambda e, pb=pb, cr=cr: e.tensor_copy(
                    tails[:, cr, 0:3], raw[:, 0, 256:259]),
                    wait=[sV.at()], inc=sV.bump())
                ops.add("vector", lambda e, pb=pb, cr=cr: e.tensor_scalar(
                    xbc[:, cr, :], raw[:, 0, 0:256],
                    convw_sb[:, 4 * cr:4 * cr + 1], convb_sb[:, cr:cr + 1],
                    ALU.mult, ALU.add), wait=[sV.at()], inc=sV.bump())
                for k in range(1, 4):
                    ops.add("vector", lambda e, pb=pb, cr=cr, k=k: e.scalar_tensor_tensor(
                        xbc[:, cr, :], raw[:, 0, k:k + 256],
                        convw_sb[:, 4 * cr + k:4 * cr + k + 1],
                        xbc[:, cr, :], ALU.mult, ALU.add),
                        wait=[sV.at()], inc=sV.bump())
                ops.add("scalar", lambda e, cr=cr: e.activation(
                    xbc[:, cr, :], xbc[:, cr, :], AF.Silu),
                    wait=[sV.at(), sA.at()], inc=sA.bump())
            else:
                ops.add("vector", lambda e: e.tensor_scalar(
                    dtf[0:64, 0, :], A4[0:64, 0:256], mup_sb[0:64, 66:67], None,
                    ALU.mult), wait=[pf_done, sV.at()], inc=sV.bump())
        xbc_done = [sV.at(), sA.at()]
        z_done = sV.at()
        for zq in range(8):
            ops.add("scalar", lambda e, zq=zq: e.activation(
                zg[:, zq // 4, (zq % 4) * 1024:(zq % 4 + 1) * 1024],
                zg[:, zq // 4, (zq % 4) * 1024:(zq % 4 + 1) * 1024], AF.Silu),
                wait=([z_done, sA.at()] if zq == 0 else []), inc=sA.bump())

        ops.add("vector", lambda e: e.tensor_scalar(
            dtf[0:64, 0, :], dtf[0:64, 0, :], ssmcol_sb[0:64, 2:3], None, ALU.add),
            wait=xbc_done, inc=sV.bump())
        ops.add("vector", lambda e: e.tensor_scalar(
            dtf[0:64, 1, :], dtf[0:64, 0, :], -1.0, None, ALU.mult),
            wait=[sV.at()], inc=sV.bump())
        ops.add("vector", lambda e: e.tensor_max(
            dtf[0:64, 1, :], dtf[0:64, 1, :], dtf[0:64, 0, :]),
            wait=[sV.at()], inc=sV.bump())
        ops.add("scalar", lambda e: e.activation(
            dtf[0:64, 2, :], dtf[0:64, 1, :], AF.Exp, scale=-1.0),
            wait=[sV.at(), sA.at()], inc=sA.bump())
        ops.add("scalar", lambda e: e.activation(
            dtf[0:64, 2, :], dtf[0:64, 2, :], AF.Ln, bias=1.0),
            wait=[sA.at()], inc=sA.bump())
        ops.add("vector", lambda e: e.tensor_scalar(
            dtf[0:64, 1, :], dtf[0:64, 0, :], 0.0, None, ALU.max),
            wait=[sA.at(), sV.at()], inc=sV.bump())
        ops.add("vector", lambda e: e.tensor_add(
            dtf[0:64, 1, :], dtf[0:64, 1, :], dtf[0:64, 2, :]),
            wait=[sV.at()], inc=sV.bump())
        ops.add("vector", lambda e: e.tensor_scalar(
            dtf[0:64, 0, :], dtf[0:64, 1, :], c[0:64, 190:191], None, ALU.mult),
            wait=[sV.at()], inc=sV.bump())
        for sub in range(2):
            ops.add("vector", lambda e, sub=sub: e.tensor_tensor_scan(
                dtf[0:64, 2, sub * 128:(sub + 1) * 128],
                dtf[0:64, 0, sub * 128:(sub + 1) * 128],
                zeros_sb[0:64, 0:128], 0.0, ALU.add, ALU.add),
                wait=[sV.at()], inc=sV.bump())
        for sub in range(2):
            ops.add("vector", lambda e, sub=sub: e.tensor_scalar(
                dtf[0:64, 0, sub * 128:(sub + 1) * 128],
                dtf[0:64, 2, sub * 128:(sub + 1) * 128],
                dtf[0:64, 2, sub * 128 + 127:sub * 128 + 128], None, ALU.subtract),
                wait=[sV.at()], inc=sV.bump())
        ops.add("scalar", lambda e: e.activation(
            dtf[0:64, 0, :], dtf[0:64, 0, :], AF.Exp, scale=-1.0),
            wait=[sV.at(), sA.at()], inc=sA.bump())

    def chunk_rest(ct, pass_b, xbc_done, sz_done, dt_done):
        for sub in range(2):
            for r, row in enumerate((1, 2, 0)):
                ops.add("tensor", lambda e, sub=sub, r=r, row=row: e.transpose(
                    ps_f[:, r * 64:(r + 1) * 64],
                    dtf[0:64, row, sub * 128:(sub + 1) * 128], identf[0:64, 0:64]),
                    wait=(dt_done + [sV.at()]) if r == 0 else [], inc=sT.bump())
            ops.add("vector", lambda e: e.tensor_copy(
                bass.AP(dtm, 0, [[4 * PS, 128], [PS, 3], [1, PS]]),
                ps_f[:, 0:192]), wait=[sT.at()], inc=sV.bump())
            ops.add("scalar", lambda e: e.activation(
                dtm[:, 3, :], dtm[:, 1, :], AF.Exp), wait=[sV.at(), sA.at()],
                inc=sA.bump())
            ops.add("vector", lambda e: e.tensor_mul(
                dtm[:, 2, :], dtm[:, 0, :], dtm[:, 2, :]),
                wait=[sA.at(), sV.at()], inc=sV.bump())
            sub_cols = [sV.at(), sA.at()]

            ops.add("tensor", lambda e, sub=sub: e.transpose(
                ps_t[:, 1, 0:128], xbc[:, 32, sub * 128:(sub + 1) * 128],
                b.ident[:, :]), wait=xbc_done + [sV.at()], inc=sT.bump())
            ops.add("vector", lambda e: e.tensor_copy(
                btm[:, :], ps_t[:, 1, 0:128]), wait=[sT.at()], inc=sV.bump())
            btm_done = sV.at()
            if pass_b:
                ops.add("tensor", lambda e, sub=sub: e.matmul(
                    ps_f2[:, 192:320], xbc[:, 32, sub * 128:(sub + 1) * 128],
                    xbc[:, 33, sub * 128:(sub + 1) * 128],
                    start=True, stop=True), wait=[sV.at()], inc=sT.bump())
                ops.add("vector", lambda e: e.tensor_tensor(
                    gmsk[:, :], ps_f2[:, 192:320], tri_sb[:, :], ALU.mult),
                    wait=[sT.at()], inc=sV.bump())
            ops.add("scalar", lambda e, sub=sub: e.activation(
                c[0:64, 176:177], dtf[0:64, 2, sub * 128 + 127:sub * 128 + 128],
                AF.Exp), wait=sub_cols, inc=sA.bump())
            ops.add("tensor", lambda e: e.transpose(
                ps_f[0:1, 320:384], c[0:64, 176:177], identf[0:64, 0:64]),
                wait=[sA.at(), sV.at()], inc=sT.bump())
            ops.add("vector", lambda e: e.tensor_copy(
                dcol[0:1, 1, :], ps_f[0:1, 320:384]), wait=[sT.at()], inc=sV.bump())
            ops.add("tensor", lambda e: e.matmul(
                ps_f[:, 384:448], ones1[0:1, :], dcol[0:1, 1, 0:64],
                start=True, stop=True), wait=[sV.at()], inc=sT.bump())
            ops.add("vector", lambda e: e.tensor_copy(
                dcol[:, 1, :], ps_f[:, 384:448]), wait=[sT.at()], inc=sV.bump())
            dec_done = sV.at()

            for h in range(64):
                hp = h % 2
                hb = h % 2
                if pass_b and hb == 0:
                    ops.add("vector", lambda e, h=h: e.tensor_copy(
                        bch[:, :, :],
                        bass.AP(dtm, PS + h, [[4 * PS, 128], [1, 2], [0, 128]])),
                        wait=[sV.at(), sT.at()], inc=sV.bump())
                hb0 = 64 * (h % 2)
                ops.add("tensor", lambda e, h=h, sub=sub, hp=hp, hb0=hb0: e.transpose(
                    ps_t[:, hp, 256:320], xbc[hb0:hb0 + 64,
                                              h // 2,
                                              sub * 128:(sub + 1) * 128],
                    b.ident[hb0:hb0 + 64, hb0:hb0 + 64]),
                    wait=[sV.at()], inc=sT.bump())
                trx = sT.at()
                ops.add("vector", lambda e, hp=hp: e.tensor_copy(
                    u3[:, hp, 0, :], ps_t[:, hp, 256:320]), wait=[trx], inc=sV.bump())
                ops.add("vector", lambda e, hp=hp, h=h: e.tensor_scalar(
                    u3[:, hp, 1, :], u3[:, hp, 0, :], dtm[:, 0, h:h + 1], None,
                    ALU.mult), wait=[sV.at()], inc=sV.bump())
                ops.add("vector", lambda e, hp=hp, h=h: e.tensor_scalar(
                    u3[:, hp, 2, :], u3[:, hp, 0, :], dtm[:, 2, h:h + 1], None,
                    ALU.mult), wait=[sV.at()], inc=sV.bump())
                u_done = sV.at()
                if pass_b:
                    psM = ps_f if hp == 0 else ps_f2
                    ops.add("tensor", lambda e, hb=hb, hp=hp, psM=psM: e.transpose(
                        psM[:, 0:128], bch[:, hb, :], identf[:, :]),
                        wait=[sV.at()], inc=sT.bump())
                    ops.add("vector", lambda e, h=h, hp=hp, psM=psM: e.tensor_scalar(
                        marg[:, :], psM[:, 0:128], dtm[:, 1, h:h + 1],
                        0.0, ALU.subtract, ALU.min),
                        wait=[sT.at(), sA.at()], inc=sV.bump())
                    ops.add("scalar", lambda e, hp=hp: e.activation(
                        sph[:, hp, :], marg[:, :], AF.Exp),
                        wait=[sV.at(), sA.at()], inc=sA.bump())
                    ops.add("vector", lambda e, hp=hp: e.tensor_tensor(
                        sph[:, hp, :], sph[:, hp, :], gmsk[:, :], ALU.mult),
                        wait=[sA.at()], inc=sV.bump())
                    sp_done = sV.at()
                    yb = 1024 + hp * 512
                    ops.add("tensor", lambda e, hp=hp, yb=yb: e.matmul(
                        A4[:, yb:yb + 64], sph[:, hp, :], u3[:, hp, 1, :],
                        start=True, stop=True), wait=[sp_done, u_done],
                        inc=sT.bump())
                    ops.add("tensor", lambda e, h=h, sub=sub, yb=yb: e.matmul(
                        A4[:, yb + 64:yb + 128],
                        xbc[:, 33, sub * 128:(sub + 1) * 128], st[:, h, :],
                        start=True, stop=True), inc=sT.bump())
                    ymm = sT.at()
                    ops.add("vector", lambda e, hp=hp, h=h, yb=yb: e.scalar_tensor_tensor(
                        ysc[:, hp, 0:PS], u3[:, hp, 0, :], dcol[:, 0, h:h + 1],
                        A4[:, yb:yb + 64], ALU.mult, ALU.add),
                        wait=[ymm, sV.at()], inc=sV.bump())
                    ops.add("vector", lambda e, hp=hp, h=h, yb=yb: e.scalar_tensor_tensor(
                        ysc[:, hp, PS:2 * PS], A4[:, yb + 64:yb + 128],
                        dtm[:, 3, h:h + 1], ysc[:, hp, 0:PS],
                        ALU.mult, ALU.add), wait=[sV.at()], inc=sV.bump())
                    ops.add("vector", lambda e, sub=sub, h=h, hp=hp: e.tensor_tensor(
                        zg[:, sub, h * PS:(h + 1) * PS], ysc[:, hp, PS:2 * PS],
                        zg[:, sub, h * PS:(h + 1) * PS], ALU.mult),
                        wait=([sz_done] if h == 0 else []) + [sV.at()], inc=sV.bump())
                cb0 = hp * 512
                ops.add("tensor", lambda e, hp=hp, cb0=cb0: e.matmul(
                    A4[:, cb0:cb0 + 64], btm[:, :], u3[:, hp, 2, :],
                    start=True, stop=True), wait=[u_done, btm_done], inc=sT.bump())
                cmm = sT.at()
                ops.add("vector", lambda e, h=h, cb0=cb0: e.scalar_tensor_tensor(
                    st[:, h, :], st[:, h, :], dcol[:, 1, h:h + 1],
                    A4[:, cb0:cb0 + 64], ALU.mult, ALU.add),
                    wait=[cmm, dec_done], inc=sV.bump())
            sub_end = [sV.at(), sA.at(), sT.at()]

        if pass_b:
            for sub in range(2):
                for hh in range(2):
                    ops.add("scalar", lambda e, sub=sub, hh=hh: e.activation(
                        sq_scr, zg[:, sub, hh * DM:(hh + 1) * DM],
                        AF.Square, accum_out=c[:, 178 + hh:179 + hh]),
                        wait=sub_end + [sA.at()], inc=sA.bump())
                ops.add("vector", lambda e: e.tensor_add(
                    c[:, 180:181], c[:, 178:179], c[:, 179:180]),
                    wait=[sA.at(), sV.at()], inc=sV.bump())
                ops.add("vector", lambda e: e.tensor_scalar(
                    c[:, 180:181], c[:, 180:181], 1.0 / DSSM, SSM_EPS,
                    ALU.mult, ALU.add), wait=[sV.at()], inc=sV.bump())
                ops.add("scalar", lambda e: e.activation(
                    c[:, 181:182], c[:, 180:181], AF.Sqrt),
                    wait=[sV.at()], inc=sA.bump())
                ops.add("vector", lambda e, sub=sub: e.reciprocal(
                    c[:, 182 + sub:183 + sub], c[:, 181:182]),
                    wait=[sA.at()], inc=sV.bump())
        rms_done = [sV.at(), sA.at()]

        if pass_b:
            for tc in range(2):
                for g in range(32):
                    gp = g % 2
                    qm = {}
                    for q in range(2):
                        ops.add("sync", lambda e, g=g, q=q: e.dma_start(
                            out=wo_sb[:, q % 2, :],
                            in_=b.hb_wout[g * 128:(g + 1) * 128,
                                          q * 512:(q + 1) * 512]),
                            wait=[sT.at()], inc=sWo[q].bump(16))
                        qm[q] = sWo[q].at()
                    ops.add("tensor", lambda e, tc=tc, g=g, gp=gp: e.transpose(
                        ps_t[:, gp, 0:128],
                        zg[:, tc, g * 128:(g + 1) * 128], b.ident[:, :]),
                        wait=rms_done + [sV.at()], inc=sT.bump())
                    ops.add("vector", lambda e, gp=gp, g=g: e.tensor_scalar(
                        sph[:, gp, :], ps_t[:, gp, 0:128],
                        normw_sb[:, g:g + 1], None, ALU.mult),
                        wait=[sT.at()], inc=sV.bump())
                    gtr = sV.at()
                    for ds in range(4):
                        wt = [qm[ds]]
                        if ds == 0:
                            wt += [gtr]
                            if g == 0:
                                wt += [sV.at()]
                        ops.add("tensor", lambda e, ds=ds, g=g, gp=gp: e.matmul(
                            A4[:, ds * 512:(ds + 1) * 512], sph[:, gp, :],
                            wo_sb[:, ds % 2, :], start=(g == 0), stop=(g == 31)),
                            wait=wt, inc=sT.bump())
                        if ds < 2:
                            ops.add("sync", lambda e, g=g, q=ds + 2: e.dma_start(
                                out=wo_sb[:, q % 2, :],
                                in_=b.hb_wout[g * 128:(g + 1) * 128,
                                              q * 512:(q + 1) * 512]),
                                wait=[sT.at()], inc=sWo[ds + 2].bump(16))
                            qm[ds + 2] = sWo[ds + 2].at()
                wmm = sT.at()
                for ds in range(4):
                    ops.add("vector", lambda e, tc=tc, ct=ct, ds=ds: e.scalar_tensor_tensor(
                        b.resid[:, 2 * ct + tc, ds * 512:(ds + 1) * 512],
                        A4[:, ds * 512:(ds + 1) * 512], c[:, 182 + tc:183 + tc],
                        b.resid[:, 2 * ct + tc, ds * 512:(ds + 1) * 512],
                        ALU.mult, ALU.add), wait=[wmm], inc=sV.bump())

    ops.add("vector", lambda e: e.memset(st[:, :, :], 0.0),
            wait=setup2 + tail_done, inc=sV.bump())
    for ct in range(NCH):
        chunk(ct, pass_b=False)
    passA_done = [sV.at(), sA.at(), sT.at()]

    ops.add("gpsimd", lambda e: e.dma_start(
        out=b.cc_st_in[:, :],
        in_=bass.AP(st, 0, [[64 * PS, 128], [1, 64 * PS]])),
        wait=passA_done, inc=sCs.bump(16))
    ops.add("gpsimd", lambda e: e.collective_compute(
        "AllGather", ALU.bypass,
        replica_groups=[[0, 1], [2, 3], [4, 5], [6, 7]],
        ins=[b.cc_st_in.ap().opt()], outs=[b.cc_st_out.ap().opt()]),
        wait=[sCs.at()], inc=sCsC.bump(1))
    ops.add("gpsimd", lambda e: e.dma_start(
        out=bass.AP(st, 0, [[64 * PS, 128], [1, 64 * PS]]),
        in_=b.cc_st_out[0:128, :]), wait=[sCsC.at()], inc=sCs.bump(16))
    ops.add("vector", lambda e: e.tensor_scalar(
        bass.AP(st, 0, [[64 * PS, 128], [1, 64 * PS]]),
        bass.AP(st, 0, [[64 * PS, 128], [1, 64 * PS]]),
        gcol[:, 0:1], None, ALU.mult), wait=[sCs.at(), sV.at()], inc=sV.bump())

    for ct in range(NCH):
        chunk(ct, pass_b=True)

    b.resid_ready = [sV.at()]
    ph.close()


def _emit_output(b):
    ops = b.ops
    sSt = b.sem("out_st")
    rw = b.resid_ready
    for qc in range(NQC):
        ops.add("sync", lambda e, qc=qc: e.dma_start(
            out=b.out_x[qc * 128:(qc + 1) * 128, :], in_=b.resid[:, qc, :]),
            wait=rw, inc=sSt.bump(16))
    ops.barrier("sync", [sSt.at()])



def _stage_inputs(inputs, stages=ALL_STAGES, x_override=None):
    x_full = np.ascontiguousarray(x_override if x_override is not None
                                  else inputs["L0_attn_a_x"])
    conv_w = np.ascontiguousarray(inputs["L0_attn_b_conv_w"])
    conv_w_fm = np.zeros((128, 34 * 4), dtype=np.float32)
    conv_b_fm = np.zeros((128, 34), dtype=np.float32)
    cw = conv_w.reshape(34, 128, 4)
    cb = inputs["L0_attn_b_conv_b"].reshape(34, 128)
    for ch in range(34):
        conv_w_fm[:, ch * 4:(ch + 1) * 4] = cw[ch]
        conv_b_fm[:, ch] = cb[ch]
    ssmv = np.zeros((HS, 4), dtype=np.float32)
    ssmv[:, 0] = inputs["L0_attn_b_A_log"]
    ssmv[:, 1] = inputs["L0_attn_b_D"]
    ssmv[:, 2] = inputs["L0_attn_b_dt_bias"]
    ssmrow = np.ascontiguousarray(ssmv[:, 0:3].T.reshape(1, 192))
    norm_w_fm = np.ascontiguousarray(
        inputs["L0_attn_b_norm_w"].reshape(32, 128).T)
    mup = inputs["L0_attn_b_mup_vector"]
    mup_fm = np.zeros((128, 67), dtype=np.float32)
    mup_pad = np.zeros(67 * 128, dtype=np.float32)
    mup_pad[:PROJ] = mup
    mup_fm[:, :] = mup_pad.reshape(67, 128).T

    bqkv = np.stack([inputs["L0_attn_a_b_Q"], inputs["L0_attn_a_b_K"],
                     inputs["L0_attn_a_b_V"]], axis=-1).transpose(1, 0, 2)
    bqkv = np.ascontiguousarray(bqkv.reshape(EH, 3 * NH).astype(np.float32))
    maps = []
    for c in range(8):
        batch, half = c // 2, c % 2
        mtiles = np.zeros((128, 6, 512), dtype=np.float32)
        mtiles[:, 0, :] = 0.0 if half == 1 else -1e5
        jj = np.arange(512)[None, :]
        ii = np.arange(128)[:, None]
        for r in range(4):
            mtiles[:, 2 + r, :] = np.where(jj > ii + 128 * r, -1e5, 0.0)
        mtiles = np.ascontiguousarray(mtiles.reshape(128, 6 * 512))
        qpos = np.zeros((128, NQC), dtype=np.int32)
        for qc in range(NQC):
            qpos[:, qc] = 2048 * half + 128 * qc + np.arange(128)
        gate = np.array([[1.0 if half == 1 else 0.0, float(half), 0.0, 0.0]],
                        dtype=np.float32)
        if "attn" in stages:
            xb = x_full[batch]
            own = xb[half * HALF:(half + 1) * HALF]
            other = xb[(1 - half) * HALF:(2 - half) * HALF]
            xs = np.concatenate([other, own], axis=0)
        else:
            xs = x_full[batch][half * HALF:(half + 1) * HALF]
            xs = np.concatenate([xs, np.zeros((LSEQ - HALF, DM), np.float32)], axis=0)
        m = {
            "x": np.ascontiguousarray(xs),
            "w_q": inputs["L0_attn_a_W_Q"], "w_k": inputs["L0_attn_a_W_K"],
            "w_v": inputs["L0_attn_a_W_V"], "w_o": inputs["L0_attn_a_W_O"],
            "b_q": inputs["L0_attn_a_b_Q"], "b_k": inputs["L0_attn_a_b_K"],
            "b_v": inputs["L0_attn_a_b_V"],
            "b_o": inputs["L0_attn_a_b_O"].reshape(1, DM),
            "w_gu": inputs["L0_ffn_w_gate_up"], "w_p": inputs["L0_ffn_w_proj"],
            "w_in": inputs["L0_attn_b_W_in"], "w_out": inputs["L0_attn_b_W_out"],
            "conv_w_fm": conv_w_fm, "conv_b_fm": conv_b_fm, "ssm_vec": ssmv,
            "ssm_row": ssmrow,
            "norm_w_fm": norm_w_fm, "mup_fm": mup_fm,
            "qpos": qpos, "gate": gate, "bqkv": bqkv, "mtiles": mtiles,
        }
        maps.append({k: np.ascontiguousarray(v) for k, v in m.items()})
    return maps


def run_device(inputs, stages=ALL_STAGES, x_override=None):
    b = build_program(stages)
    in_maps = _stage_inputs(inputs, stages, x_override)
    res = run_bass_kernel_spmd(b.nc, in_maps, core_ids=list(range(8)))
    out = np.zeros((NB, LSEQ, DM), dtype=np.float32)
    for c in range(8):
        batch, half = c // 2, c % 2
        out[batch, half * HALF:(half + 1) * HALF] = res.results[c]["xo"]
    return out


def kernel(**inputs):
    return run_device(inputs, ALL_STAGES)


# version 04
# speedup vs baseline: 1.0112x; 1.0112x over previous
import contextlib
import os
import numpy as np
import concourse.bass as bass
import concourse.mybir as mybir
from concourse.bass_utils import run_bass_kernel_spmd

F32 = mybir.dt.float32
BF16 = mybir.dt.bfloat16
I32 = mybir.dt.int32
AF = mybir.ActivationFunctionType
ALU = mybir.AluOpType
AX = mybir.AxisListType

NB, LSEQ, DM = 4, 4096, 2048
HALF, NQC = 2048, 16
NH, EH = 16, 128
FH = 8192
PROJ, DSSM, CONVD, HS, PS, SS = 8512, 4096, 4352, 64, 64, 128
LN_EPS = 1e-5
RMS_EPS = float(np.finfo(np.float32).eps)
SSM_EPS = 1e-5
INV_SQRT_EH = 1.0 / float(np.sqrt(128.0))
NEG_BIG = -100000.0
ALL_STAGES = ("attn", "ffn1", "ssm", "ffn2")
DEV_SKIP_QUANT = bool(int(os.environ.get("KDEV_SKIP_QUANT", "0")))
DEV_FFN_TILES = int(os.environ.get("KDEV_FFN_TILES", "8"))


class Sem:
    def __init__(self, nc, es, name):
        self.h = es.enter_context(nc.semaphore(name))
        self.n = 0
        self.name = name

    def bump(self, k=1):
        self.n += k
        return (self, k)

    def at(self):
        return (self, self.n)


class Ops:
    ENGS = ("sync", "tensor", "vector", "scalar", "gpsimd")

    def __init__(self):
        self.q = {e: [] for e in self.ENGS}

    def add(self, eng, fn, wait=(), inc=None):
        wait = tuple((s, v) for (s, v) in wait if v > 0)

        def run(e, fn=fn, wait=wait, inc=inc):
            for s, v in wait:
                e.wait_ge(s.h, v)
            r = fn(e)
            if inc is not None:
                r.then_inc(inc[0].h, inc[1])
        self.q[eng].append(run)

    def barrier(self, eng, waits):
        self.add(eng, lambda e: e.nop(), wait=waits)


class B:
    def __init__(self, stages=ALL_STAGES):
        self.stages = stages
        self.nc = bass.Bass("TRN2", target_bir_lowering=False)
        self.es = contextlib.ExitStack()
        self.ops = Ops()
        self.sems = {}

    def sem(self, name):
        if name not in self.sems:
            self.sems[name] = Sem(self.nc, self.es, name)
        return self.sems[name]

    def sb(self, name, shape, dtype, es=None):
        return (es or self.es).enter_context(self.nc.sbuf_tensor(name, shape, dtype))

    def ps(self, name, shape, dtype=F32, es=None):
        return (es or self.es).enter_context(self.nc.psum_tensor(name, shape, dtype))



def build_program(stages=ALL_STAGES):
    b = B(stages)
    nc = b.nc

    b.in_x = nc.dram_tensor("x", [LSEQ, DM], F32, kind="ExternalInput")
    b.in_wq = nc.dram_tensor("w_q", [NH, DM, EH], F32, kind="ExternalInput")
    b.in_wk = nc.dram_tensor("w_k", [NH, DM, EH], F32, kind="ExternalInput")
    b.in_wv = nc.dram_tensor("w_v", [NH, DM, EH], F32, kind="ExternalInput")
    b.in_wo = nc.dram_tensor("w_o", [NH, EH, DM], F32, kind="ExternalInput")
    b.in_bq = nc.dram_tensor("b_q", [NH, EH], F32, kind="ExternalInput")
    b.in_bk = nc.dram_tensor("b_k", [NH, EH], F32, kind="ExternalInput")
    b.in_bv = nc.dram_tensor("b_v", [NH, EH], F32, kind="ExternalInput")
    b.in_bo = nc.dram_tensor("b_o", [1, DM], F32, kind="ExternalInput")
    b.in_wgu = nc.dram_tensor("w_gu", [2 * FH, DM], F32, kind="ExternalInput")
    b.in_wp = nc.dram_tensor("w_p", [DM, FH], F32, kind="ExternalInput")
    b.in_win = nc.dram_tensor("w_in", [DM, PROJ], F32, kind="ExternalInput")
    b.in_wout = nc.dram_tensor("w_out", [DSSM, DM], F32, kind="ExternalInput")
    b.in_convw = nc.dram_tensor("conv_w_fm", [128, 34 * 4], F32, kind="ExternalInput")
    b.in_convb = nc.dram_tensor("conv_b_fm", [128, 34], F32, kind="ExternalInput")
    b.in_ssmv = nc.dram_tensor("ssm_vec", [HS, 4], F32, kind="ExternalInput")
    b.in_ssmrow = nc.dram_tensor("ssm_row", [1, 192], F32, kind="ExternalInput")
    b.in_normw = nc.dram_tensor("norm_w_fm", [128, 32], F32, kind="ExternalInput")
    b.in_mup = nc.dram_tensor("mup_fm", [128, 67], F32, kind="ExternalInput")
    b.in_qpos = nc.dram_tensor("qpos", [128, NQC], I32, kind="ExternalInput")
    b.in_bqkv = nc.dram_tensor("bqkv", [128, 3 * NH], F32, kind="ExternalInput")
    b.in_mtiles = nc.dram_tensor("mtiles", [128, 6 * 512], F32, kind="ExternalInput")
    b.in_gate = nc.dram_tensor("gate", [1, 4], F32, kind="ExternalInput")

    b.out_x = nc.dram_tensor("xo", [HALF, DM], F32, kind="ExternalOutput")

    b.hb_wguT = nc.dram_tensor("hb_wguT", [DM, 2 * FH], BF16, kind="Internal")
    b.hb_wpT = nc.dram_tensor("hb_wpT", [FH, DM], BF16, kind="Internal")
    b.hb_win = nc.dram_tensor("hb_win", [DM, PROJ], BF16, kind="Internal")
    b.hb_wout = nc.dram_tensor("hb_wout", [DSSM, DM], BF16, kind="Internal")
    b.hb_xn = nc.dram_tensor("hb_xn", [LSEQ, DM], BF16, kind="Internal")
    b.hb_sz = nc.dram_tensor("hb_sz", [8 * 128, 2 * DSSM], BF16, kind="Internal")
    b.hb_xbc = nc.dram_tensor("hb_xbc", [8 * 128, 34 * 256], BF16, kind="Internal")
    b.hb_dtf = nc.dram_tensor("hb_dtf", [8 * 128, 3 * 256], F32, kind="Internal")
    b.hb_wq = nc.dram_tensor("hb_wq", [NH * DM, EH], BF16, kind="Internal")
    b.hb_wk = nc.dram_tensor("hb_wk", [NH * DM, EH], BF16, kind="Internal")
    b.hb_wv = nc.dram_tensor("hb_wv", [NH * DM, EH], BF16, kind="Internal")
    b.cc_tail_in = nc.dram_tensor("cc_tail_in", [3, DM], F32, kind="Internal")
    b.cc_tail_out = nc.dram_tensor("cc_tail_out", [6, DM], F32, kind="Internal")
    b.cc_st_in = nc.dram_tensor("cc_st_in", [128, 64 * PS], BF16, kind="Internal")
    b.cc_st_out = nc.dram_tensor("cc_st_out", [256, 64 * PS], BF16, kind="Internal")

    b.ident = b.sb("g_ident", [128, 128], BF16)
    b.colA = b.sb("g_colA", [128, 192], F32)
    b.gate_sb = b.sb("g_gate", [1, 4], F32)

    _emit_setup(b)
    if DEV_SKIP_QUANT:
        b.quant_done = list(b.setup_done)
    else:
        _emit_quant(b)
    if "attn" in stages:
        _emit_attn(b)
    else:
        b.resid = b.sb("g_resid", [128, NQC, DM], F32)
        _emit_resid_init(b)
    for st in stages:
        if st == "ffn1" or st == "ffn2":
            _emit_ffn(b, st)
        elif st == "ssm":
            _emit_ssm(b)
    _emit_output(b)

    with b.es:
        with nc.Block() as block:
            @block.sync
            def _(e):
                for fn in b.ops.q["sync"]:
                    fn(e)

            @block.tensor
            def _(e):
                for fn in b.ops.q["tensor"]:
                    fn(e)

            @block.vector
            def _(e):
                for fn in b.ops.q["vector"]:
                    fn(e)

            @block.scalar
            def _(e):
                for fn in b.ops.q["scalar"]:
                    fn(e)

            @block.gpsimd
            def _(e):
                for fn in b.ops.q["gpsimd"]:
                    fn(e)
    return b



def _emit_setup(b):
    ops = b.ops
    sG = b.sem("su_g")
    sV = b.sem("su_v")
    sLd = b.sem("su_ld")
    ph = contextlib.ExitStack()
    iota_r = b.sb("su_ir", [128, 128], F32, ph)
    iota_c = b.sb("su_ic", [128, 128], F32, ph)
    ops.add("gpsimd", lambda e: e.iota(iota_r[:, :], [[1, 128]], channel_multiplier=0,
                                       allow_small_or_imprecise_dtypes=True))
    ops.add("gpsimd", lambda e: e.iota(iota_c[:, :], [[0, 128]], channel_multiplier=1,
                                       allow_small_or_imprecise_dtypes=True),
            inc=sG.bump())
    ops.add("vector", lambda e: e.tensor_tensor(b.ident[:, :], iota_r[:, :],
                                                iota_c[:, :], ALU.is_equal),
            wait=[sG.at()], inc=sV.bump())
    ops.add("sync", lambda e: e.dma_start(out=b.gate_sb[:, :], in_=b.in_gate[:, :]),
            inc=sLd.bump(16))
    b.setup_done = [sV.at(), sLd.at()]
    ph.close()



def _emit_quant(b):
    ops = b.ops
    sV = b.sem("q_v")
    sA = b.sem("q_a")
    sT = b.sem("q_t")
    sLd = [b.sem("q_ld0"), b.sem("q_ld1")]
    sSt = [b.sem("q_st0"), b.sem("q_st1")]
    sCb = [b.sem("q_cb0"), b.sem("q_cb1")]

    ph = contextlib.ExitStack()
    wbuf = [b.sb(f"q_w{i}", [128, FH], BF16, ph) for i in range(2)]
    trs = [b.sb(f"q_tr{i}", [128, FH], BF16, ph) for i in range(2)]
    absf = b.sb("q_abs", [128, FH], F32, ph)
    sums = b.sb("q_sums", [128, 128], F32, ph)
    scal = b.sb("q_scal", [128, 128], F32, ph)
    scab = b.sb("q_scab", [128, 128], BF16, ph)
    rcp = b.sb("q_rcp", [128, 128], F32, ph)
    rb = b.sb("q_rb", [128, FH], BF16, ph)
    pst = b.ps("q_pst", [128, 512], BF16, ph)

    war_w = [None, None]
    war_tr = [None, None]

    jobs = [
        (b.in_wgu, DM, 2 * FH // 128, b.hb_wguT, 2 * FH),
        (b.in_wp, FH, DM // 128, b.hb_wpT, DM),
    ]
    gi = 0
    for src_t, C, nch, dst, dstw in jobs:
        G = C // 64
        nj = C // 128
        for i in range(nch):
            pb = gi % 2
            gi += 1
            wb_t, tr_t = wbuf[pb], trs[pb]
            w = [b.setup_done[0]] if gi == 1 else []
            if war_w[pb] is not None:
                w.append(war_w[pb])
            ops.add("gpsimd", lambda e, t=wb_t, s=src_t, i=i, C=C: e.dma_start(
                out=t[:, 0:C], in_=s[i * 128:(i + 1) * 128, :]),
                wait=w, inc=sLd[pb].bump(16))
            ld = sLd[pb].at()
            ops.add("scalar", lambda e, t=wb_t, C=C: e.activation(
                absf[:, 0:C], t[:, 0:C], AF.Abs), wait=[ld, sV.at()], inc=sA.bump())
            ops.add("vector", lambda e, C=C, G=G: e.tensor_reduce(
                sums[:, 0:G].rearrange("p (g o) -> p g o", o=1),
                absf[:, 0:C].rearrange("p (g o) -> p g o", o=64),
                AX.X, ALU.add), wait=[sA.at()], inc=sV.bump())
            ops.add("vector", lambda e, G=G: e.tensor_scalar(
                scal[:, 0:G], sums[:, 0:G], 1.0 / 64.0, 1e-8, ALU.mult, ALU.max),
                wait=[sV.at()], inc=sV.bump())
            ops.add("vector", lambda e, G=G: e.tensor_copy(
                scab[:, 0:G], scal[:, 0:G]), wait=[sV.at()], inc=sV.bump())
            ops.add("vector", lambda e, G=G: e.reciprocal(
                rcp[:, 0:G], scab[:, 0:G]), wait=[sV.at()], inc=sV.bump())

            def _r(e, t=wb_t, C=C, G=G):
                bcast = bass.AP(rcp, 0, [[128, 128], [1, G], [0, 64]])
                return e.tensor_tensor(
                    rb[:, 0:C].rearrange("p (g o) -> p g o", o=64),
                    t[:, 0:C].rearrange("p (g o) -> p g o", o=64), bcast, ALU.mult)
            ops.add("vector", _r, wait=[sV.at()], inc=sV.bump())
            ops.add("vector", lambda e, C=C: e.tensor_scalar(
                rb[:, 0:C], rb[:, 0:C], 192.0, None, ALU.add), wait=[sV.at()], inc=sV.bump())
            ops.add("vector", lambda e, C=C: e.tensor_scalar(
                rb[:, 0:C], rb[:, 0:C], -192.0, None, ALU.add), wait=[sV.at()], inc=sV.bump())
            ops.add("vector", lambda e, C=C: e.tensor_scalar(
                rb[:, 0:C], rb[:, 0:C], 1.0, -1.0, ALU.min, ALU.max), wait=[sV.at()], inc=sV.bump())

            def _wh(e, t=wb_t, C=C, G=G):
                bcast = bass.AP(scab, 0, [[128, 128], [1, G], [0, 64]])
                return e.tensor_tensor(
                    t[:, 0:C].rearrange("p (g o) -> p g o", o=64),
                    rb[:, 0:C].rearrange("p (g o) -> p g o", o=64), bcast, ALU.mult)
            ops.add("vector", _wh, wait=[sV.at()], inc=sV.bump())
            wh_done = sV.at()

            trw = [war_tr[pb]] if war_tr[pb] is not None else []
            for j0 in range(0, nj, 4):
                jn = min(j0 + 4, nj)
                for j in range(j0, jn):
                    w2 = ([wh_done, sV.at()] + trw) if j == j0 else []
                    ops.add("tensor", lambda e, t=wb_t, j=j, j0=j0: e.transpose(
                        pst[:, (j - j0) * 128:(j - j0 + 1) * 128],
                        t[:, j * 128:(j + 1) * 128], b.ident[:, :]),
                        wait=w2, inc=sT.bump())
                ops.add("vector", lambda e, t=tr_t, j0=j0, jn=jn: e.tensor_copy(
                    t[:, j0 * 128:jn * 128], pst[:, 0:(jn - j0) * 128]),
                    wait=[sT.at()], inc=sV.bump())
            war_w[pb] = sT.at()

            def _st(e, t=tr_t, dst=dst, dstw=dstw, i=i, nj=nj):
                out_ap = bass.AP(dst, i * 128, [[dstw, 128], [128 * dstw, nj], [1, 128]])
                in_ap = bass.AP(t, 0, [[FH, 128], [128, nj], [1, 128]])
                return e.dma_start(out=out_ap, in_=in_ap)
            ops.add("gpsimd", _st, wait=[sV.at()], inc=sSt[pb].bump(16))
            war_tr[pb] = sSt[pb].at()

    cbuf = [b.sb(f"q_cb{i}", [128, PROJ], BF16, ph) for i in range(2)]
    first_c = [True, True]
    flat_jobs = []
    for wsrc, wdst in [(b.in_wq, b.hb_wq), (b.in_wk, b.hb_wk), (b.in_wv, b.hb_wv)]:
        flat_jobs.append((bass.AP(wsrc, 0, [[4096, 1024], [1, 4096]]),
                          bass.AP(wdst, 0, [[4096, 1024], [1, 4096]])))
    for fsrc, fdst in flat_jobs:
        for i in range(8):
            pb = gi % 2
            gi += 1
            w = [] if first_c[pb] else [sCb[pb].at()]
            first_c[pb] = False
            ops.add("gpsimd", lambda e, t=cbuf[pb], fsrc=fsrc, i=i: e.dma_start(
                out=t[:, 0:4096],
                in_=bass.AP(fsrc.tensor, i * 128 * 4096, [[4096, 128], [1, 4096]])),
                wait=w, inc=sCb[pb].bump(16))
            ops.add("gpsimd", lambda e, t=cbuf[pb], fdst=fdst, i=i: e.dma_start(
                out=bass.AP(fdst.tensor, i * 128 * 4096, [[4096, 128], [1, 4096]]),
                in_=t[:, 0:4096]), wait=[sCb[pb].at()], inc=sCb[pb].bump(16))
    for src_t, nch, C, dst in [(b.in_win, DM // 128, PROJ, b.hb_win),
                               (b.in_wout, DSSM // 128, DM, b.hb_wout)]:
        for i in range(nch):
            pb = gi % 2
            gi += 1
            w = [] if first_c[pb] else [sCb[pb].at()]
            first_c[pb] = False
            ops.add("gpsimd", lambda e, t=cbuf[pb], s=src_t, i=i, C=C: e.dma_start(
                out=t[:, 0:C], in_=s[i * 128:(i + 1) * 128, :]), wait=w,
                inc=sCb[pb].bump(16))
            ops.add("gpsimd", lambda e, t=cbuf[pb], d=dst, i=i, C=C: e.dma_start(
                out=d[i * 128:(i + 1) * 128, :], in_=t[:, 0:C]),
                wait=[sCb[pb].at()], inc=sCb[pb].bump(16))

    b.quant_done = [sSt[0].at(), sSt[1].at(), sCb[0].at(), sCb[1].at(),
                    sV.at(), sA.at(), sT.at()]
    ph.close()



def _emit_resid_init(b):
    ops = b.ops
    sLd = b.sem("ri_ld")
    qd = list(b.quant_done)
    half_rows = bass.AP(b.in_x, 0, [[1, 1]])
    for qc in range(NQC):
        ops.add("sync", lambda e, qc=qc: e.dma_start(
            out=b.resid[:, qc, :], in_=b.in_x[qc * 128:(qc + 1) * 128, :]),
            wait=qd if qc == 0 else [], inc=sLd.bump(16))
    b.resid_ready = [sLd.at()]



def _emit_ln_stats(b, tag, sq_scr):
    ops = b.ops
    c = b.colA
    sV = b.sem(f"{tag}_lv")
    sA = b.sem(f"{tag}_la")
    rw = b.resid_ready
    for qc in range(NQC):
        ops.add("scalar", lambda e, qc=qc: e.activation(
            sq_scr, b.resid[:, qc, :], AF.Square,
            accum_out=c[:, 48 + qc:49 + qc]), wait=rw + [sA.at()], inc=sA.bump())
    ops.add("vector", lambda e: e.tensor_reduce(
        c[:, 64:80].rearrange("p (g o) -> p g o", o=1),
        bass.AP(b.resid, 0, [[NQC * DM, 128], [DM, NQC], [1, DM]]),
        AX.X, ALU.add), wait=rw, inc=sV.bump())
    ops.add("vector", lambda e: e.tensor_scalar(
        c[:, 0:16], c[:, 64:80], 1.0 / DM, None, ALU.mult),
        wait=[sV.at()], inc=sV.bump())
    ops.add("vector", lambda e: e.tensor_scalar(
        c[:, 80:96], c[:, 48:64], 1.0 / DM, None, ALU.mult),
        wait=[sA.at(), sV.at()], inc=sV.bump())
    ops.add("vector", lambda e: e.scalar_tensor_tensor(
        c[:, 16:32], c[:, 0:16], 1.0, c[:, 0:16], ALU.mult, ALU.mult),
        wait=[sV.at()], inc=sV.bump())
    ops.add("vector", lambda e: e.tensor_sub(
        c[:, 80:96], c[:, 80:96], c[:, 16:32]), wait=[sV.at()], inc=sV.bump())
    ops.add("vector", lambda e: e.tensor_scalar(
        c[:, 80:96], c[:, 80:96], LN_EPS, None, ALU.add), wait=[sV.at()], inc=sV.bump())
    ops.add("scalar", lambda e: e.activation(
        c[:, 96:112], c[:, 80:96], AF.Sqrt), wait=[sV.at()], inc=sA.bump())
    ops.add("vector", lambda e: e.reciprocal(
        c[:, 16:32], c[:, 96:112]), wait=[sA.at()], inc=sV.bump())
    ops.add("vector", lambda e: e.scalar_tensor_tensor(
        c[:, 32:48], c[:, 0:16], -1.0, c[:, 16:32], ALU.mult, ALU.mult),
        wait=[sV.at()], inc=sV.bump())
    return [sV.at()]



def _emit_ffn(b, tag):
    ops = b.ops
    sW = b.sem(f"{tag}_w")
    sV = b.sem(f"{tag}_v")
    sA = b.sem(f"{tag}_a")
    sT = b.sem(f"{tag}_t")
    qd = b.quant_done

    ph = contextlib.ExitStack()
    hbuf = b.sb(f"{tag}_h", [128, 2, FH], BF16, ph)
    xnT = b.sb(f"{tag}_xnT", [128, 16, 256], BF16, ph)
    wg = [b.sb(f"{tag}_wg{i}", [128, 8, 512], BF16, ph) for i in range(2)]
    wu = [b.sb(f"{tag}_wu{i}", [128, 8, 512], BF16, ph) for i in range(2)]
    wpq = b.sb(f"{tag}_wpq", [128, 4, 512], BF16, ph)
    htf = [b.sb(f"{tag}_htf{i}", [128, 256], BF16, ph) for i in range(2)]
    sg = b.sb(f"{tag}_sg", [128, 512], BF16, ph)
    ssqp = b.sb(f"{tag}_ssqp", [128, 2, 16], F32, ph)
    inv_c = b.sb(f"{tag}_inv", [128, 8], F32, ph)
    ps_t = b.ps(f"{tag}_pst", [128, 1024], BF16, ph)
    A4 = b.ps(f"{tag}_a4", [128, 2048], F32, ph)
    c = b.colA
    NT = DEV_FFN_TILES
    NHG = FH // 512

    sq_scr = hbuf[:, 1, 0:DM]
    rw0 = list(b.resid_ready)
    for eng in ("sync", "gpsimd", "tensor", "vector", "scalar"):
        ops.barrier(eng, rw0)
    ln_done = _emit_ln_stats(b, tag, sq_scr)

    war_wg = [None, None]
    war_wp = [None, None]
    prev_tile_done = []

    for tt in range(NT):
        xnT_marks = []
        for tc, qc in enumerate((2 * tt, 2 * tt + 1)):
            xv = hbuf[:, 0, 0:DM]
            ops.add("scalar", lambda e, qc=qc, xv=xv: e.activation(
                xv, b.resid[:, qc, :], AF.Identity,
                bias=c[:, 32 + qc:33 + qc], scale=c[:, 16 + qc:17 + qc]),
                wait=ln_done + [sA.at(), sT.at()] + prev_tile_done, inc=sA.bump())
            for j0 in range(0, 16, 4):
                for j in range(j0, j0 + 4):
                    ops.add("tensor", lambda e, j=j, j0=j0, xv=xv: e.transpose(
                        ps_t[:, (j - j0) * 128:(j - j0 + 1) * 128],
                        xv[:, j * 128:(j + 1) * 128], b.ident[:, :]),
                        wait=[sA.at(), sV.at()] if j == j0 else [], inc=sT.bump())
                ops.add("vector", lambda e, j0=j0, tc=tc: e.tensor_copy(
                    bass.AP(xnT, tc * 128 + j0 * 256,
                            [[16 * 256, 128], [256, 4], [1, 128]]),
                    ps_t[:, 0:512]), wait=[sT.at()], inc=sV.bump())
        xnT_done = sV.at()

        for hg in range(NHG):
            slotm = []
            for half_i in range(2):

                def _ldg(e, hi=half_i, hg=hg):
                    out_ap = bass.AP(wg[hi], 0, [[8 * 512, 128], [512, 8], [1, 512]])
                    in_ap = bass.AP(b.hb_wguT, hg * 512 + hi * 8 * 128 * 2 * FH,
                                    [[2 * FH, 128], [128 * 2 * FH, 8], [1, 512]])
                    return e.dma_start(out=out_ap, in_=in_ap)

                def _ldu(e, hi=half_i, hg=hg):
                    out_ap = bass.AP(wu[hi], 0, [[8 * 512, 128], [512, 8], [1, 512]])
                    in_ap = bass.AP(b.hb_wguT, FH + hg * 512 + hi * 8 * 128 * 2 * FH,
                                    [[2 * FH, 128], [128 * 2 * FH, 8], [1, 512]])
                    return e.dma_start(out=out_ap, in_=in_ap)
                w = [qd[0]] if (tt == 0 and hg == 0) else []
                if war_wg[half_i] is not None:
                    w = w + [war_wg[half_i]]
                sg_s = b.sem(f"{tag}_wg{half_i}")
                su_s = b.sem(f"{tag}_wu{half_i}")
                ops.add("sync", _ldg, wait=w, inc=sg_s.bump(16))
                ops.add("sync", _ldu, inc=su_s.bump(16))
                slotm.append((sg_s.at(), su_s.at()))

            pb = hg % 2
            g_ps = A4[:, pb * 512:(pb + 1) * 512]
            u_ps = A4[:, 1024 + pb * 512:1024 + (pb + 1) * 512]
            for tc in range(2):
                for d in range(16):
                    hi, dd = (0, d) if d < 8 else (1, d - 8)
                    wt = []
                    if d == 0:
                        wt = [xnT_done, sA.at(), sV.at()]
                    if tc == 0 and (d == 0 or d == 8):
                        wt = wt + [slotm[hi][0], slotm[hi][1]]
                    ops.add("tensor", lambda e, dd=dd, hi=hi, tc=tc, d=d, g_ps=g_ps: e.matmul(
                        g_ps, xnT[:, d, tc * 128:(tc + 1) * 128],
                        wg[hi][:, dd, :], start=(d == 0), stop=(d == 15)),
                        wait=wt)
                    ops.add("tensor", lambda e, dd=dd, hi=hi, tc=tc, d=d, u_ps=u_ps: e.matmul(
                        u_ps, xnT[:, d, tc * 128:(tc + 1) * 128],
                        wu[hi][:, dd, :], start=(d == 0), stop=(d == 15)),
                        inc=sT.bump() if d == 15 else None)
                mm_done = sT.at()
                ops.add("scalar", lambda e, g_ps=g_ps: e.activation(
                    sg[:, :], g_ps, AF.Silu),
                    wait=[mm_done, sV.at()], inc=sA.bump())
                ops.add("vector", lambda e, u_ps=u_ps, tc=tc, hg=hg: e.tensor_tensor(
                    hbuf[:, tc, hg * 512:(hg + 1) * 512], sg[:, :],
                    u_ps, ALU.mult), wait=[sA.at()], inc=sV.bump())
                ops.add("scalar", lambda e, tc=tc, hg=hg: e.activation(
                    sg[:, :], hbuf[:, tc, hg * 512:(hg + 1) * 512], AF.Square,
                    accum_out=ssqp[:, tc, hg:hg + 1]), wait=[sV.at()], inc=sA.bump())
            war_wg[0] = war_wg[1] = sT.at()
        h_done_V = sV.at()
        h_done_A = sA.at()

        ops.add("vector", lambda e: e.tensor_reduce(
            inv_c[:, 0:2].rearrange("p (g o) -> p g o", o=1),
            ssqp[:, :, :], AX.X, ALU.add), wait=[h_done_A, sV.at()], inc=sV.bump())
        ops.add("vector", lambda e: e.tensor_scalar(
            inv_c[:, 2:4], inv_c[:, 0:2], 1.0 / FH, RMS_EPS, ALU.mult, ALU.add),
            wait=[sV.at()], inc=sV.bump())
        ops.add("scalar", lambda e: e.activation(
            inv_c[:, 4:6], inv_c[:, 2:4], AF.Sqrt), wait=[sV.at()], inc=sA.bump())
        ops.add("vector", lambda e: e.reciprocal(
            inv_c[:, 6:8], inv_c[:, 4:6]), wait=[sA.at()], inc=sV.bump())
        inv_done = sV.at()

        NHC = FH // 128
        for dh in range(2):
            for hc in range(NHC):
                pb = hc % 2
                qmarks = []
                for q in range(2):
                    slot = pb * 2 + q
                    w = []
                    if tt == 0 and dh == 0 and hc == 0 and q == 0:
                        w = [qd[0]]
                    if war_wp[pb] is not None and q == 0:
                        w = w + [war_wp[pb]]
                    sq_s = b.sem(f"{tag}_wq{slot}")
                    ops.add("sync", lambda e, hc=hc, dh=dh, q=q, slot=slot: e.dma_start(
                        out=wpq[:, slot, :],
                        in_=b.hb_wpT[hc * 128:(hc + 1) * 128,
                                     dh * 1024 + q * 512:dh * 1024 + (q + 1) * 512]),
                        wait=w, inc=sq_s.bump(16))
                    qmarks.append(sq_s.at())
                for tc in range(2):
                    tw = [sV.at()]
                    if hc == 0 and dh == 0:
                        tw += [h_done_V]
                    ops.add("tensor", lambda e, tc=tc, hc=hc: e.transpose(
                        ps_t[:, tc * 512:tc * 512 + 128],
                        hbuf[:, tc, hc * 128:(hc + 1) * 128], b.ident[:, :]),
                        wait=tw, inc=sT.bump())
                    ops.add("vector", lambda e, tc=tc: e.tensor_copy(
                        htf[tc][:, 0:128], ps_t[:, tc * 512:tc * 512 + 128]),
                        wait=[sT.at()], inc=sV.bump())
                    tr_done = sV.at()
                    for q in range(2):
                        wt = [qmarks[q]] if tc == 0 else []
                        if q == 0:
                            wt += [tr_done]
                            if hc == 0 and tc == 0:
                                wt += [h_done_A, sV.at()]
                                if dh == 1 or tt > 0:
                                    wt += [prev_evac_done]
                        ops.add("tensor", lambda e, q=q, hc=hc, tc=tc,
                                slot=pb * 2 + q: e.matmul(
                            A4[:, tc * 1024 + q * 512:tc * 1024 + (q + 1) * 512],
                            htf[tc][:, 0:128], wpq[:, slot, :],
                            start=(hc == 0), stop=(hc == NHC - 1)),
                            wait=wt, inc=sT.bump() if (tc == 1 and q == 1) else None)
                war_wp[pb] = sT.at()
            mm2_done = sT.at()
            for tc in range(2):
                for q in range(2):
                    ops.add("vector", lambda e, tc=tc, tt=tt, dh=dh, q=q: e.scalar_tensor_tensor(
                        b.resid[:, 2 * tt + tc,
                                dh * 1024 + q * 512:dh * 1024 + (q + 1) * 512],
                        A4[:, tc * 1024 + q * 512:tc * 1024 + (q + 1) * 512],
                        inv_c[:, 6 + tc:7 + tc],
                        b.resid[:, 2 * tt + tc,
                                dh * 1024 + q * 512:dh * 1024 + (q + 1) * 512],
                        ALU.mult, ALU.add),
                        wait=[mm2_done, inv_done] if (tc == 0 and q == 0) else [],
                        inc=sV.bump())
            prev_evac_done = sV.at()
        prev_tile_done = [sV.at()]
    b.resid_ready = [sV.at()]
    ph.close()



def _emit_attn(b):
    ops = b.ops
    sV = b.sem("a_v")
    sA = b.sem("a_a")
    sT = b.sem("a_t")
    sG = b.sem("a_g")
    sX = [b.sem("a_x0"), b.sem("a_x1")]
    sP2 = [b.sem("a_p20"), b.sem("a_p21")]
    sI0 = [b.sem("a_i00"), b.sem("a_i01")]
    sXq = [b.sem("a_xq0"), b.sem("a_xq1")]
    sQs = [b.sem("a_qs0"), b.sem("a_qs1")]
    sWc = b.sem("a_wc")
    sO = b.sem("a_wo")
    sXn = b.sem("a_xn")
    sKp = b.sem("a_kp")
    c = b.colA

    ph1 = contextlib.ExitStack()
    xA = b.sb("a_xA", [128, 2, DM], F32, ph1)
    xnO = b.sb("a_xnO", [128, 2, DM], BF16, ph1)
    sq1 = b.sb("a_sq1", [128, DM], BF16, ph1)
    ac = b.sb("a_ac", [128, 6, 32], F32, ph1)
    NCK = LSEQ // 128
    for i in range(NCK):
        pb = i % 2
        w = (list(b.setup_done) + list(b.quant_done)) if i < 2 else [sV.at(), sA.at()]
        ops.add("sync", lambda e, i=i, pb=pb: e.dma_start(
            out=xA[:, pb, :], in_=b.in_x[i * 128:(i + 1) * 128, :]),
            wait=w, inc=sX[pb].bump(16))
        ld = sX[pb].at()
        ops.add("vector", lambda e, i=i, pb=pb: e.tensor_reduce(
            ac[:, 0, i:i + 1].rearrange("p (g o) -> p g o", o=1),
            xA[:, pb, :].rearrange("p (g o) -> p g o", o=DM), AX.X, ALU.add),
            wait=[ld], inc=sV.bump())
        ops.add("scalar", lambda e, i=i, pb=pb: e.activation(
            sq1[:, :], xA[:, pb, :], AF.Square, accum_out=ac[:, 1, i:i + 1]),
            wait=[ld, sA.at()], inc=sA.bump())
    st1 = [sV.at(), sA.at()]
    ops.add("vector", lambda e: e.tensor_scalar(
        ac[:, 2, :], ac[:, 0, :], 1.0 / DM, None, ALU.mult), wait=st1, inc=sV.bump())
    ops.add("vector", lambda e: e.tensor_scalar(
        ac[:, 4, :], ac[:, 1, :], 1.0 / DM, None, ALU.mult), wait=st1, inc=sV.bump())
    ops.add("vector", lambda e: e.scalar_tensor_tensor(
        ac[:, 5, :], ac[:, 2, :], 1.0, ac[:, 2, :], ALU.mult, ALU.mult),
        wait=[sV.at()], inc=sV.bump())
    ops.add("vector", lambda e: e.tensor_sub(
        ac[:, 4, :], ac[:, 4, :], ac[:, 5, :]), wait=[sV.at()], inc=sV.bump())
    ops.add("vector", lambda e: e.tensor_scalar(
        ac[:, 4, :], ac[:, 4, :], LN_EPS, None, ALU.add), wait=[sV.at()], inc=sV.bump())
    ops.add("scalar", lambda e: e.activation(
        ac[:, 5, :], ac[:, 4, :], AF.Sqrt), wait=[sV.at(), sA.at()], inc=sA.bump())
    ops.add("vector", lambda e: e.reciprocal(
        ac[:, 3, :], ac[:, 5, :]), wait=[sA.at()], inc=sV.bump())
    ops.add("vector", lambda e: e.scalar_tensor_tensor(
        ac[:, 5, :], ac[:, 2, :], -1.0, ac[:, 3, :], ALU.mult, ALU.mult),
        wait=[sV.at()], inc=sV.bump())
    stats_done = [sV.at()]
    for i in range(NCK):
        pb = i % 2
        ops.add("sync", lambda e, i=i, pb=pb: e.dma_start(
            out=xA[:, pb, :], in_=b.in_x[i * 128:(i + 1) * 128, :]),
            wait=[sA.at()] if i >= 2 else stats_done, inc=sP2[pb].bump(16))
        ld = sP2[pb].at()
        ops.add("scalar", lambda e, i=i, pb=pb: e.activation(
            xnO[:, pb, :], xA[:, pb, :], AF.Identity,
            bias=ac[:, 5, i:i + 1], scale=ac[:, 3, i:i + 1]),
            wait=[ld] + stats_done + [sXn.at()], inc=sA.bump())
        ops.add("gpsimd", lambda e, i=i, pb=pb: e.dma_start(
            out=b.hb_xn[i * 128:(i + 1) * 128, :], in_=xnO[:, pb, :]),
            wait=[sA.at()], inc=sXn.bump(16))
    ph1.close()
    xn_done = [sXn.at()]

    b.resid = b.sb("g_resid", [128, NQC, DM], F32)
    php = contextlib.ExitStack()
    psA = b.ps("a_psA", [128, 3072], F32, php)
    ps_t = b.ps("a_pst", [128, 1024], BF16, php)
    ph0 = contextlib.ExitStack()
    onesb = b.sb("a_onesb", [1, 128], BF16, ph0)
    bor = b.sb("a_bor", [1, DM], BF16, ph0)
    x0b = b.sb("a_x0b", [128, 2, DM], F32, ph0)
    ops.add("vector", lambda e: e.memset(onesb[:, :], 1.0), inc=sV.bump())
    sBo = b.sem("a_bo")
    ops.add("gpsimd", lambda e: e.dma_start(out=bor[0:1, :], in_=b.in_bo[0:1, :]),
            inc=sBo.bump(16))
    for s4 in range(4):
        ops.add("tensor", lambda e, s4=s4: e.matmul(
            psA[:, s4 * 512:(s4 + 1) * 512],
            onesb[0:1, :], bor[0:1, s4 * 512:(s4 + 1) * 512],
            start=True, stop=True), wait=[sV.at(), sBo.at()] if s4 == 0 else [],
            inc=sT.bump() if s4 == 3 else None)
    bob_done = sT.at()
    for qc in range(NQC):
        pb = qc % 2
        ops.add("sync", lambda e, qc=qc, pb=pb: e.dma_start(
            out=x0b[:, pb, :],
            in_=b.in_x[HALF + qc * 128:HALF + (qc + 1) * 128, :]),
            wait=(xn_done + [sA.at(), sV.at()]) if qc < 2 else [sV.at()],
            inc=sI0[pb].bump(16))
        x0ld = sI0[pb].at()
        for ds in range(4):
            ops.add("vector", lambda e, qc=qc, ds=ds, pb=pb: e.tensor_add(
                b.resid[:, qc, ds * 512:(ds + 1) * 512],
                x0b[:, pb, ds * 512:(ds + 1) * 512],
                psA[:, ds * 512:(ds + 1) * 512]),
                wait=[x0ld, bob_done] if ds == 0 else [], inc=sV.bump())
    ph0.close()
    init_done = [sV.at()]

    ph2 = contextlib.ExitStack()
    K_sb = b.sb("a_K", [128, LSEQ], BF16, ph2)
    Vt = b.sb("a_Vt", [128, 32, EH], BF16, ph2)
    Q_sb = b.sb("a_Q", [128, HALF], BF16, ph2)
    zh = b.sb("a_zh", [128, HALF], BF16, ph2)
    xq = b.sb("a_xq", [128, 2, 1024], BF16, ph2)
    wqkv = b.sb("a_wqkv", [128, 16, 2, EH], BF16, ph2)
    wq_s = b.sb("a_wqs", [128, 2, EH], BF16, ph2)
    wo_sb = b.sb("a_wosb", [128, DM], BF16, ph2)
    scs = b.sb("a_scs", [128, LSEQ], F32, ph2)
    prs = b.sb("a_prs", [128, 8, 512], BF16, ph2)
    prT = b.sb("a_prT", [128, 32, 128], BF16, ph2)
    mtiles = b.sb("a_mtiles", [128, 6, 512], BF16, ph2)
    bqkv = b.sb("a_bqkv", [128, 3 * NH], F32, ph2)
    lparts = b.sb("a_lp", [128, 16], F32, ph2)
    linv = b.sb("a_linv", [128, NQC], F32, ph2)
    ops.add("sync", lambda e: e.dma_start(out=bqkv[:, :], in_=b.in_bqkv[:, :]),
            inc=sKp.bump(16))
    ops.add("sync", lambda e: e.dma_start(out=scs[:, 0:3072], in_=b.in_mtiles[:, :]),
            inc=sKp.bump(16))
    ops.add("vector", lambda e: e.tensor_copy(
        bass.AP(mtiles, 0, [[6 * 512, 128], [1, 3072]]), scs[:, 0:3072]),
        wait=[sKp.at()], inc=sV.bump())
    kp_done = [sKp.at(), sV.at()]

    NQG = 4
    for h in range(NH):
        for wi, wsrc in ((0, b.hb_wk), (1, b.hb_wv)):
            ops.add("sync", lambda e, h=h, wi=wi, wsrc=wsrc: e.dma_start(
                out=bass.AP(wqkv, wi * EH, [[16 * 2 * EH, 128], [2 * EH, 16], [1, EH]]),
                in_=bass.AP(wsrc, h * DM * EH, [[EH, 128], [128 * EH, 16], [1, EH]])),
                wait=[sT.at()], inc=sWc.bump(16))
        ops.add("gpsimd", lambda e, h=h: e.dma_start(
            out=wo_sb[:, :], in_=b.in_wo[h, :, :]),
            wait=[sT.at()], inc=sO.bump(16))
        wld = sWc.at()
        for qt in range(4):
            for ci in range(16):
                pb = ci % 2
                ops.add("sync", lambda e, h=h, qt=qt, ci=ci, pb=pb: e.dma_start(
                    out=xq[:, pb, :],
                    in_=b.hb_xn[qt * 1024:(qt + 1) * 1024, ci * 128:(ci + 1) * 128],
                    transpose=True),
                    wait=(xn_done + init_done if (h == 0 and qt == 0 and ci < 2)
                          else [sT.at()]),
                    inc=sXq[pb].bump(16))
                xld = sXq[pb].at()
                wt0 = [xld, wld]
                if ci == 0:
                    wt0 = wt0 + [sV.at()]
                if qt >= 2:
                    ops.add("sync", lambda e, h=h, ci=ci, pb=pb: e.dma_start(
                        out=wq_s[:, pb, :],
                        in_=b.hb_wq[h * DM + ci * 128:h * DM + (ci + 1) * 128, :]),
                        wait=[sT.at()], inc=sQs[pb].bump(16))
                    wt0 = wt0 + [sQs[pb].at()]
                for nn in range(2):
                    ops.add("tensor", lambda e, h=h, ci=ci, nn=nn, pb=pb: e.matmul(
                        psA[:, nn * 512:(nn + 1) * 512],
                        wqkv[:, ci, 0, :], xq[:, pb, nn * 512:(nn + 1) * 512],
                        start=(ci == 0), stop=(ci == 15)),
                        wait=wt0 if nn == 0 else [])
                    ops.add("tensor", lambda e, h=h, ci=ci, nn=nn, pb=pb: e.matmul(
                        psA[:, 1024 + nn * 512:1024 + (nn + 1) * 512],
                        wqkv[:, ci, 1, :], xq[:, pb, nn * 512:(nn + 1) * 512],
                        start=(ci == 0), stop=(ci == 15)),
                        inc=None if qt >= 2 else (sT.bump() if nn == 1 else None))
                    if qt >= 2:
                        ops.add("tensor", lambda e, h=h, ci=ci, nn=nn, pb=pb: e.matmul(
                            psA[:, 2048 + nn * 512:2048 + (nn + 1) * 512],
                            wq_s[:, pb, :], xq[:, pb, nn * 512:(nn + 1) * 512],
                            start=(ci == 0), stop=(ci == 15)),
                            inc=sT.bump() if nn == 1 else None)
            mm_done = sT.at()
            ops.add("vector", lambda e, qt=qt, h=h: e.tensor_scalar(
                K_sb[:, qt * 1024:(qt + 1) * 1024], psA[:, 0:1024],
                bqkv[:, 3 * h + 1:3 * h + 2], None, ALU.add),
                wait=[mm_done] + kp_done + [sV.at()], inc=sV.bump())
            ops.add("vector", lambda e, qt=qt, h=h: e.tensor_scalar(
                bass.AP(prs, 6 * 512, [[8 * 512, 128], [1, 1024]]),
                psA[:, 1024:2048],
                bqkv[:, 3 * h + 2:3 * h + 3], None, ALU.add),
                wait=[sV.at()], inc=sV.bump())
            if qt >= 2:
                ops.add("vector", lambda e, qt=qt, h=h: e.tensor_scalar(
                    Q_sb[:, (qt - 2) * 1024:(qt - 1) * 1024], psA[:, 2048:3072],
                    bqkv[:, 3 * h:3 * h + 1], None, ALU.add),
                    wait=[sV.at()], inc=sV.bump())
            ev_done = sV.at()
            for v4 in range(2):
                for vj in range(4):
                    ops.add("tensor", lambda e, v4=v4, vj=vj: e.transpose(
                        ps_t[:, vj * 128:(vj + 1) * 128],
                        bass.AP(prs, 6 * 512 + v4 * 512 + vj * 128,
                                [[8 * 512, 128], [1, 128]]),
                        b.ident[:, :]),
                        wait=[ev_done, sV.at()] if vj == 0 else [], inc=sT.bump())
                ops.add("vector", lambda e, qt=qt, v4=v4: e.tensor_copy(
                    bass.AP(Vt, (qt * 8 + v4 * 4) * EH,
                            [[32 * EH, 128], [EH, 4], [1, 128]]),
                    ps_t[:, 0:512]), wait=[sT.at()], inc=sV.bump())
        kvq_done = [sV.at(), sT.at()]

        qcs = [(g4, g4 * 4 + qi) for g4 in range(NQG) for qi in range(4)]
        tr_mark, zh_mark, exp_mark, mx_mark = {}, {}, {}, {}

        def emit_scores(qc, g4):
            nseg = 5 + g4
            for sg in range(nseg):
                sp = sg % 2
                if sg < 4:
                    msel = 0
                elif sg == nseg - 1:
                    msel = 2 + (qc % 4)
                else:
                    msel = 1
                wt = (kvq_done + kp_done + [sV.at()]) if sg < 2 else [sV.at()]
                ops.add("tensor", lambda e, qc=qc, sg=sg, sp=sp: e.matmul(
                    psA[:, sp * 512:(sp + 1) * 512],
                    Q_sb[:, qc * 128:(qc + 1) * 128],
                    K_sb[:, sg * 512:(sg + 1) * 512],
                    start=True, stop=True), wait=wt, inc=sT.bump())
                ew = [sT.at()]
                if sg == 0 and (qc - 1) in exp_mark:
                    ew += [exp_mark[qc - 1]]
                ops.add("vector", lambda e, sg=sg, sp=sp, msel=msel: e.scalar_tensor_tensor(
                    scs[:, sg * 512:(sg + 1) * 512],
                    psA[:, sp * 512:(sp + 1) * 512],
                    INV_SQRT_EH, mtiles[:, msel, :],
                    ALU.mult, ALU.add), wait=ew, inc=sV.bump())
            kext = 512 * nseg
            mxc = 176 + (qc % 2)
            ops.add("vector", lambda e, kext=kext, mxc=mxc: e.reduce_max(
                c[:, mxc:mxc + 1], scs[:, 0:kext], AX.X, negate=True),
                wait=[sV.at()], inc=sV.bump())
            mx_mark[qc] = sV.at()

        def emit_exp(qc, g4):
            nseg = 5 + g4
            mxc = 176 + (qc % 2)
            lb = 8 * (qc % 2)
            for sg in range(nseg):
                ew = []
                if sg == 0:
                    ew = [mx_mark[qc], sA.at()]
                    if (qc - 1) in tr_mark:
                        ew += [tr_mark[qc - 1]]
                ops.add("scalar", lambda e, sg=sg, mxc=mxc, lb=lb: e.activation(
                    prs[:, sg, :], scs[:, sg * 512:(sg + 1) * 512],
                    AF.Exp, bias=c[:, mxc:mxc + 1],
                    accum_out=lparts[:, lb + sg:lb + sg + 1]),
                    wait=ew, inc=sA.bump())
            exp_mark[qc] = sA.at()

        def emit_tail(qc, g4):
            nseg = 5 + g4
            lb = 8 * (qc % 2)
            ops.add("vector", lambda e, nseg=nseg, lb=lb: e.tensor_reduce(
                c[:, 178:179].rearrange("p (g o) -> p g o", o=1),
                lparts[:, lb:lb + nseg].rearrange("p (g o) -> p g o", o=nseg),
                AX.X, ALU.add), wait=[exp_mark[qc], sV.at()], inc=sV.bump())
            ops.add("vector", lambda e, qc=qc: e.reciprocal(
                linv[:, qc:qc + 1], c[:, 178:179]),
                wait=[sV.at()], inc=sV.bump())
            for sg in range(nseg):
                for kc4 in range(4):
                    ew = []
                    if sg == 0 and kc4 == 0:
                        ew = [exp_mark[qc], sV.at()]
                    elif kc4 == 0:
                        ew = [sV.at()]
                    ops.add("tensor", lambda e, sg=sg, kc4=kc4: e.transpose(
                        ps_t[:, kc4 * 128:(kc4 + 1) * 128],
                        prs[:, sg, kc4 * 128:(kc4 + 1) * 128],
                        b.ident[:, :]), wait=ew, inc=sT.bump())
                ops.add("vector", lambda e, sg=sg: e.tensor_copy(
                    bass.AP(prT, sg * 4 * 128,
                            [[32 * 128, 128], [128, 4], [1, 128]]),
                    ps_t[:, 0:512]), wait=[sT.at()], inc=sV.bump())
            tr_mark[qc] = sT.at()
            trp_done = sV.at()
            avb = 1024 + 512 * (qc % 2)
            for kc in range(4 * nseg):
                wt = []
                if kc == 0:
                    wt = [trp_done, sV.at()]
                    if (qc - 2) in zh_mark:
                        wt += [zh_mark[qc - 2]]
                ops.add("tensor", lambda e, kc=kc, nseg=nseg, avb=avb: e.matmul(
                    psA[:, avb:avb + 128], Vt[:, kc, :], prT[:, kc, :],
                    start=(kc == 0), stop=(kc == 4 * nseg - 1)),
                    wait=wt, inc=sT.bump() if kc == 4 * nseg - 1 else None)
            ops.add("vector", lambda e, qc=qc, avb=avb: e.tensor_copy(
                zh[:, qc * 128:(qc + 1) * 128],
                psA[:, avb:avb + 128]), wait=[sT.at()], inc=sV.bump())
            zh_mark[qc] = sV.at()

        for i, (g4, qc) in enumerate(qcs):
            emit_scores(qc, g4)
            if i > 0:
                pg4, pqc = qcs[i - 1]
                emit_tail(pqc, pg4)
            emit_exp(qc, g4)
        emit_tail(qcs[-1][1], qcs[-1][0])
        tr_mark.clear(); zh_mark.clear(); exp_mark.clear(); mx_mark.clear()
        z_done = [sV.at(), sT.at()]

        wo_ld = sO.at()
        for qc in range(NQC):
            for ds in range(4):
                wt = [sV.at()] if ds == 0 else []
                if qc == 0 and ds == 0:
                    wt = z_done + [wo_ld, sV.at()]
                ops.add("tensor", lambda e, h=h, qc=qc, ds=ds: e.matmul(
                    psA[:, 1024 + ds * 512:1024 + (ds + 1) * 512],
                    zh[:, qc * 128:(qc + 1) * 128],
                    wo_sb[:, ds * 512:(ds + 1) * 512],
                    start=True, stop=True), wait=wt,
                    inc=sT.bump() if ds == 3 else None)
            wmm = sT.at()
            for ds in range(4):
                ops.add("vector", lambda e, qc=qc, ds=ds: e.scalar_tensor_tensor(
                    b.resid[:, qc, ds * 512:(ds + 1) * 512],
                    psA[:, 1024 + ds * 512:1024 + (ds + 1) * 512],
                    linv[:, qc:qc + 1],
                    b.resid[:, qc, ds * 512:(ds + 1) * 512],
                    ALU.mult, ALU.add), wait=[wmm], inc=sV.bump())
    ph2.close()
    php.close()
    b.resid_ready = [sV.at()]
def _emit_ln_stats(b, tag, sq_scr):
    ops = b.ops
    c = b.colA
    sV = b.sem(f"{tag}_lv")
    sA = b.sem(f"{tag}_la")
    rw = b.resid_ready
    for qc in range(NQC):
        ops.add("scalar", lambda e, qc=qc: e.activation(
            sq_scr, b.resid[:, qc, :], AF.Square,
            accum_out=c[:, 48 + qc:49 + qc]), wait=rw + [sA.at()], inc=sA.bump())
    ops.add("vector", lambda e: e.tensor_reduce(
        c[:, 64:80].rearrange("p (g o) -> p g o", o=1),
        bass.AP(b.resid, 0, [[NQC * DM, 128], [DM, NQC], [1, DM]]),
        AX.X, ALU.add), wait=rw, inc=sV.bump())
    ops.add("vector", lambda e: e.tensor_scalar(
        c[:, 0:16], c[:, 64:80], 1.0 / DM, None, ALU.mult),
        wait=[sV.at()], inc=sV.bump())
    ops.add("vector", lambda e: e.tensor_scalar(
        c[:, 80:96], c[:, 48:64], 1.0 / DM, None, ALU.mult),
        wait=[sA.at(), sV.at()], inc=sV.bump())
    ops.add("vector", lambda e: e.scalar_tensor_tensor(
        c[:, 16:32], c[:, 0:16], 1.0, c[:, 0:16], ALU.mult, ALU.mult),
        wait=[sV.at()], inc=sV.bump())
    ops.add("vector", lambda e: e.tensor_sub(
        c[:, 80:96], c[:, 80:96], c[:, 16:32]), wait=[sV.at()], inc=sV.bump())
    ops.add("vector", lambda e: e.tensor_scalar(
        c[:, 80:96], c[:, 80:96], LN_EPS, None, ALU.add), wait=[sV.at()], inc=sV.bump())
    ops.add("scalar", lambda e: e.activation(
        c[:, 96:112], c[:, 80:96], AF.Sqrt), wait=[sV.at()], inc=sA.bump())
    ops.add("vector", lambda e: e.reciprocal(
        c[:, 16:32], c[:, 96:112]), wait=[sA.at()], inc=sV.bump())
    ops.add("vector", lambda e: e.scalar_tensor_tensor(
        c[:, 32:48], c[:, 0:16], -1.0, c[:, 16:32], ALU.mult, ALU.mult),
        wait=[sV.at()], inc=sV.bump())
    return [sV.at()]



def _emit_ffn(b, tag):
    ops = b.ops
    sW = b.sem(f"{tag}_w")
    sV = b.sem(f"{tag}_v")
    sA = b.sem(f"{tag}_a")
    sT = b.sem(f"{tag}_t")
    qd = b.quant_done

    ph = contextlib.ExitStack()
    hbuf = b.sb(f"{tag}_h", [128, 2, FH], BF16, ph)
    xnT = b.sb(f"{tag}_xnT", [128, 16, 256], BF16, ph)
    wg = [b.sb(f"{tag}_wg{i}", [128, 8, 512], BF16, ph) for i in range(2)]
    wu = [b.sb(f"{tag}_wu{i}", [128, 8, 512], BF16, ph) for i in range(2)]
    wpq = b.sb(f"{tag}_wpq", [128, 4, 512], BF16, ph)
    htf = [b.sb(f"{tag}_htf{i}", [128, 256], BF16, ph) for i in range(2)]
    sg = b.sb(f"{tag}_sg", [128, 512], BF16, ph)
    ssqp = b.sb(f"{tag}_ssqp", [128, 2, 16], F32, ph)
    inv_c = b.sb(f"{tag}_inv", [128, 8], F32, ph)
    ps_t = b.ps(f"{tag}_pst", [128, 1024], BF16, ph)
    A4 = b.ps(f"{tag}_a4", [128, 2048], F32, ph)
    c = b.colA
    NT = DEV_FFN_TILES
    NHG = FH // 512

    sq_scr = hbuf[:, 1, 0:DM]
    rw0 = list(b.resid_ready)
    for eng in ("sync", "gpsimd", "tensor", "vector", "scalar"):
        ops.barrier(eng, rw0)
    ln_done = _emit_ln_stats(b, tag, sq_scr)

    war_wg = [None, None]
    war_wp = [None, None]
    prev_tile_done = []

    for tt in range(NT):
        xnT_marks = []
        for tc, qc in enumerate((2 * tt, 2 * tt + 1)):
            xv = hbuf[:, 0, 0:DM]
            ops.add("scalar", lambda e, qc=qc, xv=xv: e.activation(
                xv, b.resid[:, qc, :], AF.Identity,
                bias=c[:, 32 + qc:33 + qc], scale=c[:, 16 + qc:17 + qc]),
                wait=ln_done + [sA.at(), sT.at()] + prev_tile_done, inc=sA.bump())
            for j0 in range(0, 16, 4):
                for j in range(j0, j0 + 4):
                    ops.add("tensor", lambda e, j=j, j0=j0, xv=xv: e.transpose(
                        ps_t[:, (j - j0) * 128:(j - j0 + 1) * 128],
                        xv[:, j * 128:(j + 1) * 128], b.ident[:, :]),
                        wait=[sA.at(), sV.at()] if j == j0 else [], inc=sT.bump())
                ops.add("vector", lambda e, j0=j0, tc=tc: e.tensor_copy(
                    bass.AP(xnT, tc * 128 + j0 * 256,
                            [[16 * 256, 128], [256, 4], [1, 128]]),
                    ps_t[:, 0:512]), wait=[sT.at()], inc=sV.bump())
        xnT_done = sV.at()

        for hg in range(NHG):
            slotm = []
            for half_i in range(2):

                def _ldg(e, hi=half_i, hg=hg):
                    out_ap = bass.AP(wg[hi], 0, [[8 * 512, 128], [512, 8], [1, 512]])
                    in_ap = bass.AP(b.hb_wguT, hg * 512 + hi * 8 * 128 * 2 * FH,
                                    [[2 * FH, 128], [128 * 2 * FH, 8], [1, 512]])
                    return e.dma_start(out=out_ap, in_=in_ap)

                def _ldu(e, hi=half_i, hg=hg):
                    out_ap = bass.AP(wu[hi], 0, [[8 * 512, 128], [512, 8], [1, 512]])
                    in_ap = bass.AP(b.hb_wguT, FH + hg * 512 + hi * 8 * 128 * 2 * FH,
                                    [[2 * FH, 128], [128 * 2 * FH, 8], [1, 512]])
                    return e.dma_start(out=out_ap, in_=in_ap)
                w = [qd[0]] if (tt == 0 and hg == 0) else []
                if war_wg[half_i] is not None:
                    w = w + [war_wg[half_i]]
                sg_s = b.sem(f"{tag}_wg{half_i}")
                su_s = b.sem(f"{tag}_wu{half_i}")
                ops.add("sync", _ldg, wait=w, inc=sg_s.bump(16))
                ops.add("sync", _ldu, inc=su_s.bump(16))
                slotm.append((sg_s.at(), su_s.at()))

            pb = hg % 2
            g_ps = A4[:, pb * 512:(pb + 1) * 512]
            u_ps = A4[:, 1024 + pb * 512:1024 + (pb + 1) * 512]
            for tc in range(2):
                for d in range(16):
                    hi, dd = (0, d) if d < 8 else (1, d - 8)
                    wt = []
                    if d == 0:
                        wt = [xnT_done, sA.at(), sV.at()]
                    if tc == 0 and (d == 0 or d == 8):
                        wt = wt + [slotm[hi][0], slotm[hi][1]]
                    ops.add("tensor", lambda e, dd=dd, hi=hi, tc=tc, d=d, g_ps=g_ps: e.matmul(
                        g_ps, xnT[:, d, tc * 128:(tc + 1) * 128],
                        wg[hi][:, dd, :], start=(d == 0), stop=(d == 15)),
                        wait=wt)
                    ops.add("tensor", lambda e, dd=dd, hi=hi, tc=tc, d=d, u_ps=u_ps: e.matmul(
                        u_ps, xnT[:, d, tc * 128:(tc + 1) * 128],
                        wu[hi][:, dd, :], start=(d == 0), stop=(d == 15)),
                        inc=sT.bump() if d == 15 else None)
                mm_done = sT.at()
                ops.add("scalar", lambda e, g_ps=g_ps: e.activation(
                    sg[:, :], g_ps, AF.Silu),
                    wait=[mm_done, sV.at()], inc=sA.bump())
                ops.add("vector", lambda e, u_ps=u_ps, tc=tc, hg=hg: e.tensor_tensor(
                    hbuf[:, tc, hg * 512:(hg + 1) * 512], sg[:, :],
                    u_ps, ALU.mult), wait=[sA.at()], inc=sV.bump())
                ops.add("scalar", lambda e, tc=tc, hg=hg: e.activation(
                    sg[:, :], hbuf[:, tc, hg * 512:(hg + 1) * 512], AF.Square,
                    accum_out=ssqp[:, tc, hg:hg + 1]), wait=[sV.at()], inc=sA.bump())
            war_wg[0] = war_wg[1] = sT.at()
        h_done_V = sV.at()
        h_done_A = sA.at()

        ops.add("vector", lambda e: e.tensor_reduce(
            inv_c[:, 0:2].rearrange("p (g o) -> p g o", o=1),
            ssqp[:, :, :], AX.X, ALU.add), wait=[h_done_A, sV.at()], inc=sV.bump())
        ops.add("vector", lambda e: e.tensor_scalar(
            inv_c[:, 2:4], inv_c[:, 0:2], 1.0 / FH, RMS_EPS, ALU.mult, ALU.add),
            wait=[sV.at()], inc=sV.bump())
        ops.add("scalar", lambda e: e.activation(
            inv_c[:, 4:6], inv_c[:, 2:4], AF.Sqrt), wait=[sV.at()], inc=sA.bump())
        ops.add("vector", lambda e: e.reciprocal(
            inv_c[:, 6:8], inv_c[:, 4:6]), wait=[sA.at()], inc=sV.bump())
        inv_done = sV.at()

        NHC = FH // 128
        for dh in range(2):
            for hc in range(NHC):
                pb = hc % 2
                qmarks = []
                for q in range(2):
                    slot = pb * 2 + q
                    w = []
                    if tt == 0 and dh == 0 and hc == 0 and q == 0:
                        w = [qd[0]]
                    if war_wp[pb] is not None and q == 0:
                        w = w + [war_wp[pb]]
                    sq_s = b.sem(f"{tag}_wq{slot}")
                    ops.add("sync", lambda e, hc=hc, dh=dh, q=q, slot=slot: e.dma_start(
                        out=wpq[:, slot, :],
                        in_=b.hb_wpT[hc * 128:(hc + 1) * 128,
                                     dh * 1024 + q * 512:dh * 1024 + (q + 1) * 512]),
                        wait=w, inc=sq_s.bump(16))
                    qmarks.append(sq_s.at())
                for tc in range(2):
                    tw = [sV.at()]
                    if hc == 0 and dh == 0:
                        tw += [h_done_V]
                    ops.add("tensor", lambda e, tc=tc, hc=hc: e.transpose(
                        ps_t[:, tc * 512:tc * 512 + 128],
                        hbuf[:, tc, hc * 128:(hc + 1) * 128], b.ident[:, :]),
                        wait=tw, inc=sT.bump())
                    ops.add("vector", lambda e, tc=tc: e.tensor_copy(
                        htf[tc][:, 0:128], ps_t[:, tc * 512:tc * 512 + 128]),
                        wait=[sT.at()], inc=sV.bump())
                    tr_done = sV.at()
                    for q in range(2):
                        wt = [qmarks[q]] if tc == 0 else []
                        if q == 0:
                            wt += [tr_done]
                            if hc == 0 and tc == 0:
                                wt += [h_done_A, sV.at()]
                                if dh == 1 or tt > 0:
                                    wt += [prev_evac_done]
                        ops.add("tensor", lambda e, q=q, hc=hc, tc=tc,
                                slot=pb * 2 + q: e.matmul(
                            A4[:, tc * 1024 + q * 512:tc * 1024 + (q + 1) * 512],
                            htf[tc][:, 0:128], wpq[:, slot, :],
                            start=(hc == 0), stop=(hc == NHC - 1)),
                            wait=wt, inc=sT.bump() if (tc == 1 and q == 1) else None)
                war_wp[pb] = sT.at()
            mm2_done = sT.at()
            for tc in range(2):
                for q in range(2):
                    ops.add("vector", lambda e, tc=tc, tt=tt, dh=dh, q=q: e.scalar_tensor_tensor(
                        b.resid[:, 2 * tt + tc,
                                dh * 1024 + q * 512:dh * 1024 + (q + 1) * 512],
                        A4[:, tc * 1024 + q * 512:tc * 1024 + (q + 1) * 512],
                        inv_c[:, 6 + tc:7 + tc],
                        b.resid[:, 2 * tt + tc,
                                dh * 1024 + q * 512:dh * 1024 + (q + 1) * 512],
                        ALU.mult, ALU.add),
                        wait=[mm2_done, inv_done] if (tc == 0 and q == 0) else [],
                        inc=sV.bump())
            prev_evac_done = sV.at()
        prev_tile_done = [sV.at()]
    b.resid_ready = [sV.at()]
    ph.close()



def _emit_ssm(b):
    ops = b.ops
    sV = b.sem("s_v")
    sA = b.sem("s_a")
    sT = b.sem("s_t")
    sWi = [b.sem("s_wi0"), b.sem("s_wi1")]
    sWo = [b.sem(f"s_wo{i}") for i in range(4)]
    sCt = b.sem("s_ctail")
    sCtC = b.sem("s_ctailc")
    sCs = b.sem("s_cstate")
    sCsC = b.sem("s_cstatec")
    sSm = b.sem("s_sm")
    sCz = b.sem("s_czst")
    sZl = b.sem("s_czld")
    sSm2 = b.sem("s_sm2")

    ph = contextlib.ExitStack()
    zg = b.sb("s_zg", [128, 2, DSSM], BF16, ph)
    xbc = b.sb("s_xbc", [128, 34, 256], BF16, ph)
    xnT = b.sb("s_xnT", [128, 16, 256], BF16, ph)
    tails = b.sb("s_tails", [128, 34, 4], BF16, ph)
    raw = b.sb("s_raw", [128, 1, 260], BF16, ph)
    st = b.sb("s_st", [128, 64, PS], BF16, ph)
    win_sb = [b.sb("s_win0", [128, 16, 128], BF16, ph)]
    win_sb = [win_sb[0], win_sb[0]]
    wo_sb = b.sb("s_wo", [128, 2, 512], BF16, ph)
    dtf = b.sb("s_dtf", [128, 3, 256], F32, ph)
    dtm = b.sb("s_dtm", [128, 4, PS], F32, ph)
    bch = b.sb("s_bch", [128, 2, 128], F32, ph)
    btm = b.sb("s_btm", [128, 128], BF16, ph)
    gmsk = b.sb("s_gmsk", [128, 128], BF16, ph)
    marg = b.sb("s_marg", [128, 128], F32, ph)
    sph = b.sb("s_sph", [128, 2, 128], BF16, ph)
    u3 = b.sb("s_u3", [128, 2, 3, PS], BF16, ph)
    ysc = b.sb("s_ysc", [128, 2, 2 * PS], BF16, ph)
    szh = b.sb("s_szh", [128, PS], BF16, ph)
    xnt3 = b.sb("s_xnt3", [128, 16, 4], BF16, ph)
    gcol = b.sb("s_gcol", [128, 2], F32, ph)
    dcol = b.sb("s_dcol", [128, 2, 64], F32, ph)
    tailx = b.sb("s_tailx", [4, DM], BF16, ph)
    ones1 = b.sb("s_ones1", [1, 128], F32, ph)
    identf = b.sb("s_identf", [128, 128], F32, ph)
    tri_sb = b.sb("s_tri", [128, 128], BF16, ph)
    zeros_sb = b.sb("s_zeros", [128, 128], BF16, ph)
    convw_sb = b.sb("s_convw", [128, 136], F32, ph)
    convb_sb = b.sb("s_convb", [128, 34], F32, ph)
    ssmcol_sb = b.sb("s_ssmcol", [64, 4], F32, ph)
    mup_sb = b.sb("s_mup", [128, 67], F32, ph)
    normw_sb = b.sb("s_normw", [128, 32], F32, ph)
    ssmrow_sb = b.sb("s_ssmrow", [1, 192], F32, ph)
    ps_t = b.ps("s_pst", [128, 2, 512], BF16, ph)
    ps_f = b.ps("s_psf", [128, 512], F32, ph)
    ps_f2 = b.ps("s_psf2", [128, 512], F32, ph)
    A4 = b.ps("s_a4", [128, 2048], F32, ph)
    c = b.colA
    sq_scr = bass.AP(xbc, 0, [[34 * 256, 128], [1, DM]])
    zb = raw[:, 0, 0:256]
    sq_scr3 = bass.AP(xbc, 0, [[34 * 256, 3], [1, DM]])

    rw0 = list(b.resid_ready)
    for eng in ("sync", "gpsimd", "tensor", "vector", "scalar"):
        ops.barrier(eng, rw0)
    ln_done = _emit_ln_stats(b, "ssm", sq_scr)
    NCH = HALF // 256

    ph_iof = contextlib.ExitStack()
    iof = b.sb("s_iof", [128, 256], BF16, ph_iof)
    ops.add("gpsimd", lambda e: e.iota(iof[:, 0:128], [[1, 128]], channel_multiplier=0,
                                       allow_small_or_imprecise_dtypes=True))
    ops.add("gpsimd", lambda e: e.iota(iof[:, 128:256], [[0, 128]], channel_multiplier=1,
                                       allow_small_or_imprecise_dtypes=True),
            inc=b.sem("s_g").bump())
    sG = b.sem("s_g")
    ops.add("vector", lambda e: e.tensor_tensor(
        tri_sb[:, :], iof[:, 128:256], iof[:, 0:128], ALU.is_le),
        wait=[sG.at()], inc=sV.bump())
    ops.add("vector", lambda e: e.tensor_tensor(
        identf[:, :], iof[:, 0:128], iof[:, 128:256], ALU.is_equal),
        wait=[sG.at()], inc=sV.bump())
    ops.add("vector", lambda e: e.memset(ones1[:, :], 1.0), inc=sV.bump())
    ops.add("vector", lambda e: e.memset(zeros_sb[:, :], 0.0), inc=sV.bump())
    ops.add("sync", lambda e: e.dma_start(out=convw_sb[:, :], in_=b.in_convw[:, :]),
            inc=sSm.bump(16))
    ops.add("sync", lambda e: e.dma_start(out=convb_sb[:, :], in_=b.in_convb[:, :]),
            inc=sSm.bump(16))
    ops.add("sync", lambda e: e.dma_start(out=ssmcol_sb[:, :], in_=b.in_ssmv[:, :]),
            inc=sSm.bump(16))
    ops.add("sync", lambda e: e.dma_start(out=mup_sb[:, :], in_=b.in_mup[:, :]),
            inc=sSm.bump(16))
    ops.add("sync", lambda e: e.dma_start(out=normw_sb[:, :], in_=b.in_normw[:, :]),
            inc=sSm.bump(16))
    ops.add("sync", lambda e: e.dma_start(out=ssmrow_sb[:, :], in_=b.in_ssmrow[:, :]),
            inc=sSm.bump(16))
    sm_ld = sSm.at()
    sm_ld2 = sSm2.at()

    ops.add("tensor", lambda e: e.matmul(
        ps_f[:, 0:2], ones1[0:1, :], b.gate_sb[0:1, 0:2], start=True, stop=True),
        wait=b.setup_done + [sm_ld, sV.at()], inc=sT.bump())
    ops.add("vector", lambda e: e.tensor_copy(gcol[:, :], ps_f[:, 0:2]),
            wait=[sT.at()], inc=sV.bump())
    ops.add("tensor", lambda e: e.matmul(
        ps_f[:, 64:128], ones1[0:1, :], ssmrow_sb[0:1, 64:128], start=True, stop=True),
        wait=[sV.at()], inc=sT.bump())
    ops.add("vector", lambda e: e.tensor_copy(dcol[:, 0, :], ps_f[:, 64:128]),
            wait=[sT.at()], inc=sV.bump())
    ops.add("scalar", lambda e: e.activation(
        c[0:64, 190:191], ssmcol_sb[0:64, 0:1], AF.Exp),
        wait=[sm_ld, sm_ld2], inc=sA.bump())
    ops.add("vector", lambda e: e.tensor_scalar(
        c[0:64, 190:191], c[0:64, 190:191], -1.0, None, ALU.mult),
        wait=[sA.at()], inc=sV.bump())
    setup2 = [sV.at(), sA.at(), sT.at()]
    ph_iof.close()

    ops.add("gpsimd", lambda e: e.dma_start(
        out=b.cc_tail_in[0:3, :], in_=b.resid[125:128, 15, :]),
        wait=ln_done, inc=sCt.bump(16))
    ops.add("gpsimd", lambda e: e.collective_compute(
        "AllGather", ALU.bypass,
        replica_groups=[[0, 1], [2, 3], [4, 5], [6, 7]],
        ins=[b.cc_tail_in.ap().opt()], outs=[b.cc_tail_out.ap().opt()]),
        wait=[sCt.at()], inc=sCtC.bump(1))
    ops.add("gpsimd", lambda e: e.dma_start(
        out=tailx[0:3, :], in_=b.cc_tail_out[0:3, :]),
        wait=[sCtC.at()], inc=sCt.bump(16))
    tail_ld = sCt.at()
    ops.add("vector", lambda e: e.tensor_reduce(
        c[0:3, 160:161].rearrange("p (g o) -> p g o", o=1),
        tailx[0:3, :].rearrange("p (g o) -> p g o", o=DM), AX.X, ALU.add),
        wait=[tail_ld, sV.at()], inc=sV.bump())
    ops.add("scalar", lambda e: e.activation(
        sq_scr3, tailx[0:3, :], AF.Square,
        accum_out=c[0:3, 161:162]), wait=[tail_ld, sA.at()], inc=sA.bump())
    ops.add("vector", lambda e: e.tensor_scalar(
        c[0:3, 162:163], c[0:3, 160:161], 1.0 / DM, None, ALU.mult),
        wait=[sV.at(), sA.at()], inc=sV.bump())
    ops.add("vector", lambda e: e.tensor_scalar(
        c[0:3, 163:164], c[0:3, 161:162], 1.0 / DM, None, ALU.mult),
        wait=[sV.at()], inc=sV.bump())
    ops.add("vector", lambda e: e.scalar_tensor_tensor(
        c[0:3, 164:165], c[0:3, 162:163], 1.0, c[0:3, 162:163], ALU.mult, ALU.mult),
        wait=[sV.at()], inc=sV.bump())
    ops.add("vector", lambda e: e.tensor_sub(
        c[0:3, 163:164], c[0:3, 163:164], c[0:3, 164:165]),
        wait=[sV.at()], inc=sV.bump())
    ops.add("vector", lambda e: e.tensor_scalar(
        c[0:3, 163:164], c[0:3, 163:164], LN_EPS, None, ALU.add),
        wait=[sV.at()], inc=sV.bump())
    ops.add("scalar", lambda e: e.activation(
        c[0:3, 165:166], c[0:3, 163:164], AF.Sqrt), wait=[sV.at(), sA.at()],
        inc=sA.bump())
    ops.add("vector", lambda e: e.reciprocal(
        c[0:3, 166:167], c[0:3, 165:166]), wait=[sA.at()], inc=sV.bump())
    ops.add("vector", lambda e: e.tensor_scalar(
        sq_scr3, tailx[0:3, :], c[0:3, 162:163], c[0:3, 166:167],
        ALU.subtract, ALU.mult), wait=[sV.at()], inc=sV.bump())
    tail_xn = sV.at()
    for j in range(16):
        ops.add("tensor", lambda e, j=j: e.transpose(
            ps_t[:, 0, 0:3], bass.AP(xbc, j * 128, [[34 * 256, 3], [1, 128]]), b.ident[0:3, 0:3]),
            wait=([tail_xn] + setup2 + [sV.at()]) if j == 0 else [sV.at()],
            inc=sT.bump())
        ops.add("vector", lambda e, j=j: e.tensor_copy(
            xnt3[:, j, 0:3], ps_t[:, 0, 0:3]), wait=[sT.at()], inc=sV.bump())
    tail_done = [sV.at()]

    cache_war = []

    def chunk(ct, pass_b):
        if pass_b:
            ldw = [sT.at(), sV.at(), sA.at(), sCz.at()]
            ops.add("sync", lambda e, ct=ct: e.dma_start(
                out=bass.AP(zg, 0, [[2 * DSSM, 128], [1, 2 * DSSM]]),
                in_=b.hb_sz[ct * 128:(ct + 1) * 128, :]),
                wait=ldw, inc=sZl.bump(16))
            ops.add("sync", lambda e, ct=ct: e.dma_start(
                out=bass.AP(xbc, 0, [[34 * 256, 128], [1, 34 * 256]]),
                in_=b.hb_xbc[ct * 128:(ct + 1) * 128, :]),
                inc=sZl.bump(16))
            ops.add("sync", lambda e, ct=ct: e.dma_start(
                out=bass.AP(dtf, 0, [[3 * 256, 128], [1, 3 * 256]]),
                in_=b.hb_dtf[ct * 128:(ct + 1) * 128, :]),
                inc=sZl.bump(16))
            ld_done = sZl.at()
            xbc_done = [ld_done]
            sz_done = ld_done
            dt_done = [ld_done]
        else:
            chunk_a(ct)
            xbc_done = [sV.at(), sA.at()]
            sz_done = sA.at()
            dt_done = [sV.at(), sA.at()]
            ops.add("sync", lambda e, ct=ct: e.dma_start(
                out=b.hb_sz[ct * 128:(ct + 1) * 128, :],
                in_=bass.AP(zg, 0, [[2 * DSSM, 128], [1, 2 * DSSM]])),
                wait=[sz_done], inc=sCz.bump(16))
            ops.add("sync", lambda e, ct=ct: e.dma_start(
                out=b.hb_xbc[ct * 128:(ct + 1) * 128, :],
                in_=bass.AP(xbc, 0, [[34 * 256, 128], [1, 34 * 256]])),
                wait=xbc_done, inc=sCz.bump(16))
            ops.add("sync", lambda e, ct=ct: e.dma_start(
                out=b.hb_dtf[ct * 128:(ct + 1) * 128, :],
                in_=bass.AP(dtf, 0, [[3 * 256, 128], [1, 3 * 256]])),
                wait=dt_done, inc=sCz.bump(16))
            cache_war[:] = [sCz.at()]
        chunk_rest(ct, pass_b, xbc_done, sz_done, dt_done)

    def chunk_a(ct):
        for tc, qc in enumerate((2 * ct, 2 * ct + 1)):
            ops.add("scalar", lambda e, qc=qc: e.activation(
                sq_scr, b.resid[:, qc, :], AF.Identity,
                bias=c[:, 32 + qc:33 + qc], scale=c[:, 16 + qc:17 + qc]),
                wait=ln_done + [sA.at(), sT.at()] + list(cache_war), inc=sA.bump())
            for j0 in range(0, 16, 4):
                for j in range(j0, j0 + 4):
                    ops.add("tensor", lambda e, j=j, j0=j0: e.transpose(
                        ps_t[:, 0, (j - j0) * 128:(j - j0 + 1) * 128],
                        bass.AP(xbc, j * 128, [[34 * 256, 128], [1, 128]]), b.ident[:, :]),
                        wait=[sA.at(), sV.at()] if j == j0 else [], inc=sT.bump())
                ops.add("vector", lambda e, j0=j0, tc=tc: e.tensor_copy(
                    bass.AP(xnT, tc * 128 + j0 * 256,
                            [[16 * 256, 128], [256, 4], [1, 128]]),
                    ps_t[:, 0, 0:512]), wait=[sT.at()], inc=sV.bump())
        xnT_done = sV.at()

        prs = list(range(0, 67))
        for pr in prs:
            pb = pr % 2
            span = 128 if pr < 66 else 64
            ops.add("sync", lambda e, pr=pr, pb=pb, span=span: e.dma_start(
                out=win_sb[pb][:, :, 0:span],
                in_=bass.AP(b.hb_win, pr * 128,
                            [[PROJ, 128], [128 * PROJ, 16], [1, span]])),
                wait=[sT.at()], inc=sWi[pb].bump(16))
            wld = sWi[pb].at()
            for d in range(16):
                wt = [wld, xnT_done, sV.at()] if d == 0 else []
                ops.add("tensor", lambda e, d=d, pb=pb, span=span: e.matmul(
                    A4[0:span, 0:256], win_sb[pb][:, d, 0:span], xnT[:, d, 0:256],
                    start=(d == 0), stop=(d == 15)),
                    wait=wt, inc=sT.bump() if d == 15 else None)
            pf_done = sT.at()
            if ct == 0 and pr >= 32:
                for d in range(16):
                    ops.add("tensor", lambda e, d=d, pb=pb, span=span: e.matmul(
                        A4[0:span, 256:259], win_sb[pb][:, d, 0:span], xnt3[:, d, 0:3],
                        start=(d == 0), stop=(d == 15)),
                        inc=sT.bump() if d == 15 else None)
                pf_done = sT.at()

            if pr < 32:
                ops.add("vector", lambda e, pb=pb, pr=pr: e.tensor_scalar(
                    zb[:, :], A4[:, 0:256], mup_sb[:, pr:pr + 1], None, ALU.mult),
                    wait=[pf_done, sV.at()], inc=sV.bump())
                for s2 in range(2):
                    ops.add("tensor", lambda e, s2=s2: e.transpose(
                        ps_t[:, 1, s2 * 128:(s2 + 1) * 128],
                        zb[:, s2 * 128:(s2 + 1) * 128], b.ident[:, :]),
                        wait=[sV.at()] if s2 == 0 else [], inc=sT.bump())
                ops.add("vector", lambda e, pr=pr: e.tensor_copy(
                    bass.AP(zg, pr * 128, [[2 * DSSM, 128], [DSSM, 2], [1, 128]]),
                    ps_t[:, 1, 0:256]), wait=[sT.at()], inc=sV.bump())
            elif pr < 66:
                cr = pr - 32
                if ct == 0:
                    ops.add("vector", lambda e, pb=pb, pr=pr: e.tensor_scalar(
                        raw[:, 0, 0:3], A4[:, 256:259], gcol[:, 0:1],
                        mup_sb[:, pr:pr + 1], ALU.mult, ALU.mult),
                        wait=[pf_done, sV.at()], inc=sV.bump())
                else:
                    ops.add("vector", lambda e, pb=pb, cr=cr: e.tensor_copy(
                        raw[:, 0, 0:3], tails[:, cr, 0:3]),
                        wait=[pf_done, sV.at()], inc=sV.bump())
                ops.add("vector", lambda e, pb=pb, pr=pr: e.tensor_scalar(
                    raw[:, 0, 3:259], A4[:, 0:256], mup_sb[:, pr:pr + 1], None,
                    ALU.mult), wait=[sV.at()], inc=sV.bump())
                ops.add("vector", lambda e, pb=pb, cr=cr: e.tensor_copy(
                    tails[:, cr, 0:3], raw[:, 0, 256:259]),
                    wait=[sV.at()], inc=sV.bump())
                ops.add("vector", lambda e, pb=pb, cr=cr: e.tensor_scalar(
                    xbc[:, cr, :], raw[:, 0, 0:256],
                    convw_sb[:, 4 * cr:4 * cr + 1], convb_sb[:, cr:cr + 1],
                    ALU.mult, ALU.add), wait=[sV.at()], inc=sV.bump())
                for k in range(1, 4):
                    ops.add("vector", lambda e, pb=pb, cr=cr, k=k: e.scalar_tensor_tensor(
                        xbc[:, cr, :], raw[:, 0, k:k + 256],
                        convw_sb[:, 4 * cr + k:4 * cr + k + 1],
                        xbc[:, cr, :], ALU.mult, ALU.add),
                        wait=[sV.at()], inc=sV.bump())
                ops.add("scalar", lambda e, cr=cr: e.activation(
                    xbc[:, cr, :], xbc[:, cr, :], AF.Silu),
                    wait=[sV.at(), sA.at()], inc=sA.bump())
            else:
                ops.add("vector", lambda e: e.tensor_scalar(
                    dtf[0:64, 0, :], A4[0:64, 0:256], mup_sb[0:64, 66:67], None,
                    ALU.mult), wait=[pf_done, sV.at()], inc=sV.bump())
        xbc_done = [sV.at(), sA.at()]
        z_done = sV.at()
        for zq in range(8):
            ops.add("scalar", lambda e, zq=zq: e.activation(
                zg[:, zq // 4, (zq % 4) * 1024:(zq % 4 + 1) * 1024],
                zg[:, zq // 4, (zq % 4) * 1024:(zq % 4 + 1) * 1024], AF.Silu),
                wait=([z_done, sA.at()] if zq == 0 else []), inc=sA.bump())

        ops.add("vector", lambda e: e.tensor_scalar(
            dtf[0:64, 0, :], dtf[0:64, 0, :], ssmcol_sb[0:64, 2:3], None, ALU.add),
            wait=xbc_done, inc=sV.bump())
        ops.add("vector", lambda e: e.tensor_scalar(
            dtf[0:64, 1, :], dtf[0:64, 0, :], -1.0, None, ALU.mult),
            wait=[sV.at()], inc=sV.bump())
        ops.add("vector", lambda e: e.tensor_max(
            dtf[0:64, 1, :], dtf[0:64, 1, :], dtf[0:64, 0, :]),
            wait=[sV.at()], inc=sV.bump())
        ops.add("scalar", lambda e: e.activation(
            dtf[0:64, 2, :], dtf[0:64, 1, :], AF.Exp, scale=-1.0),
            wait=[sV.at(), sA.at()], inc=sA.bump())
        ops.add("scalar", lambda e: e.activation(
            dtf[0:64, 2, :], dtf[0:64, 2, :], AF.Ln, bias=1.0),
            wait=[sA.at()], inc=sA.bump())
        ops.add("vector", lambda e: e.tensor_scalar(
            dtf[0:64, 1, :], dtf[0:64, 0, :], 0.0, None, ALU.max),
            wait=[sA.at(), sV.at()], inc=sV.bump())
        ops.add("vector", lambda e: e.tensor_add(
            dtf[0:64, 1, :], dtf[0:64, 1, :], dtf[0:64, 2, :]),
            wait=[sV.at()], inc=sV.bump())
        ops.add("vector", lambda e: e.tensor_scalar(
            dtf[0:64, 0, :], dtf[0:64, 1, :], c[0:64, 190:191], None, ALU.mult),
            wait=[sV.at()], inc=sV.bump())
        for sub in range(2):
            ops.add("vector", lambda e, sub=sub: e.tensor_tensor_scan(
                dtf[0:64, 2, sub * 128:(sub + 1) * 128],
                dtf[0:64, 0, sub * 128:(sub + 1) * 128],
                zeros_sb[0:64, 0:128], 0.0, ALU.add, ALU.add),
                wait=[sV.at()], inc=sV.bump())
        for sub in range(2):
            ops.add("vector", lambda e, sub=sub: e.tensor_scalar(
                dtf[0:64, 0, sub * 128:(sub + 1) * 128],
                dtf[0:64, 2, sub * 128:(sub + 1) * 128],
                dtf[0:64, 2, sub * 128 + 127:sub * 128 + 128], None, ALU.subtract),
                wait=[sV.at()], inc=sV.bump())
        ops.add("scalar", lambda e: e.activation(
            dtf[0:64, 0, :], dtf[0:64, 0, :], AF.Exp, scale=-1.0),
            wait=[sV.at(), sA.at()], inc=sA.bump())

    def chunk_rest(ct, pass_b, xbc_done, sz_done, dt_done):
        for sub in range(2):
            for r, row in enumerate((1, 2, 0)):
                ops.add("tensor", lambda e, sub=sub, r=r, row=row: e.transpose(
                    ps_f[:, r * 64:(r + 1) * 64],
                    dtf[0:64, row, sub * 128:(sub + 1) * 128], identf[0:64, 0:64]),
                    wait=(dt_done + [sV.at()]) if r == 0 else [], inc=sT.bump())
            ops.add("vector", lambda e: e.tensor_copy(
                bass.AP(dtm, 0, [[4 * PS, 128], [PS, 3], [1, PS]]),
                ps_f[:, 0:192]), wait=[sT.at()], inc=sV.bump())
            ops.add("scalar", lambda e: e.activation(
                dtm[:, 3, :], dtm[:, 1, :], AF.Exp), wait=[sV.at(), sA.at()],
                inc=sA.bump())
            ops.add("vector", lambda e: e.tensor_mul(
                dtm[:, 2, :], dtm[:, 0, :], dtm[:, 2, :]),
                wait=[sA.at(), sV.at()], inc=sV.bump())
            sub_cols = [sV.at(), sA.at()]

            ops.add("tensor", lambda e, sub=sub: e.transpose(
                ps_t[:, 1, 0:128], xbc[:, 32, sub * 128:(sub + 1) * 128],
                b.ident[:, :]), wait=xbc_done + [sV.at()], inc=sT.bump())
            ops.add("vector", lambda e: e.tensor_copy(
                btm[:, :], ps_t[:, 1, 0:128]), wait=[sT.at()], inc=sV.bump())
            btm_done = sV.at()
            if pass_b:
                ops.add("tensor", lambda e, sub=sub: e.matmul(
                    ps_f2[:, 192:320], xbc[:, 32, sub * 128:(sub + 1) * 128],
                    xbc[:, 33, sub * 128:(sub + 1) * 128],
                    start=True, stop=True), wait=[sV.at()], inc=sT.bump())
                ops.add("vector", lambda e: e.tensor_tensor(
                    gmsk[:, :], ps_f2[:, 192:320], tri_sb[:, :], ALU.mult),
                    wait=[sT.at()], inc=sV.bump())
            ops.add("scalar", lambda e, sub=sub: e.activation(
                c[0:64, 176:177], dtf[0:64, 2, sub * 128 + 127:sub * 128 + 128],
                AF.Exp), wait=sub_cols, inc=sA.bump())
            ops.add("tensor", lambda e: e.transpose(
                ps_f[0:1, 320:384], c[0:64, 176:177], identf[0:64, 0:64]),
                wait=[sA.at(), sV.at()], inc=sT.bump())
            ops.add("vector", lambda e: e.tensor_copy(
                dcol[0:1, 1, :], ps_f[0:1, 320:384]), wait=[sT.at()], inc=sV.bump())
            ops.add("tensor", lambda e: e.matmul(
                ps_f[:, 384:448], ones1[0:1, :], dcol[0:1, 1, 0:64],
                start=True, stop=True), wait=[sV.at()], inc=sT.bump())
            ops.add("vector", lambda e: e.tensor_copy(
                dcol[:, 1, :], ps_f[:, 384:448]), wait=[sT.at()], inc=sV.bump())
            dec_done = sV.at()

            for h in range(64):
                hp = h % 2
                hb = h % 2
                if pass_b and hb == 0:
                    ops.add("vector", lambda e, h=h: e.tensor_copy(
                        bch[:, :, :],
                        bass.AP(dtm, PS + h, [[4 * PS, 128], [1, 2], [0, 128]])),
                        wait=[sV.at(), sT.at()], inc=sV.bump())
                hb0 = 64 * (h % 2)
                ops.add("tensor", lambda e, h=h, sub=sub, hp=hp, hb0=hb0: e.transpose(
                    ps_t[:, hp, 256:320], xbc[hb0:hb0 + 64,
                                              h // 2,
                                              sub * 128:(sub + 1) * 128],
                    b.ident[hb0:hb0 + 64, hb0:hb0 + 64]),
                    wait=[sV.at()], inc=sT.bump())
                trx = sT.at()
                ops.add("vector", lambda e, hp=hp: e.tensor_copy(
                    u3[:, hp, 0, :], ps_t[:, hp, 256:320]), wait=[trx], inc=sV.bump())
                ops.add("vector", lambda e, hp=hp, h=h: e.tensor_scalar(
                    u3[:, hp, 1, :], u3[:, hp, 0, :], dtm[:, 0, h:h + 1], None,
                    ALU.mult), wait=[sV.at()], inc=sV.bump())
                ops.add("vector", lambda e, hp=hp, h=h: e.tensor_scalar(
                    u3[:, hp, 2, :], u3[:, hp, 0, :], dtm[:, 2, h:h + 1], None,
                    ALU.mult), wait=[sV.at()], inc=sV.bump())
                u_done = sV.at()
                if pass_b:
                    psM = ps_f if hp == 0 else ps_f2
                    ops.add("tensor", lambda e, hb=hb, hp=hp, psM=psM: e.transpose(
                        psM[:, 0:128], bch[:, hb, :], identf[:, :]),
                        wait=[sV.at()], inc=sT.bump())
                    ops.add("vector", lambda e, h=h, hp=hp, psM=psM: e.tensor_scalar(
                        marg[:, :], psM[:, 0:128], dtm[:, 1, h:h + 1],
                        0.0, ALU.subtract, ALU.min),
                        wait=[sT.at(), sA.at()], inc=sV.bump())
                    ops.add("scalar", lambda e, hp=hp: e.activation(
                        sph[:, hp, :], marg[:, :], AF.Exp),
                        wait=[sV.at(), sA.at()], inc=sA.bump())
                    ops.add("vector", lambda e, hp=hp: e.tensor_tensor(
                        sph[:, hp, :], sph[:, hp, :], gmsk[:, :], ALU.mult),
                        wait=[sA.at()], inc=sV.bump())
                    sp_done = sV.at()
                    yb = 1024 + hp * 512
                    ops.add("tensor", lambda e, hp=hp, yb=yb: e.matmul(
                        A4[:, yb:yb + 64], sph[:, hp, :], u3[:, hp, 1, :],
                        start=True, stop=True), wait=[sp_done, u_done],
                        inc=sT.bump())
                    ops.add("tensor", lambda e, h=h, sub=sub, yb=yb: e.matmul(
                        A4[:, yb + 64:yb + 128],
                        xbc[:, 33, sub * 128:(sub + 1) * 128], st[:, h, :],
                        start=True, stop=True), inc=sT.bump())
                    ymm = sT.at()
                    ops.add("vector", lambda e, hp=hp, h=h, yb=yb: e.scalar_tensor_tensor(
                        ysc[:, hp, 0:PS], u3[:, hp, 0, :], dcol[:, 0, h:h + 1],
                        A4[:, yb:yb + 64], ALU.mult, ALU.add),
                        wait=[ymm, sV.at()], inc=sV.bump())
                    ops.add("vector", lambda e, hp=hp, h=h, yb=yb: e.scalar_tensor_tensor(
                        ysc[:, hp, PS:2 * PS], A4[:, yb + 64:yb + 128],
                        dtm[:, 3, h:h + 1], ysc[:, hp, 0:PS],
                        ALU.mult, ALU.add), wait=[sV.at()], inc=sV.bump())
                    ops.add("vector", lambda e, sub=sub, h=h, hp=hp: e.tensor_tensor(
                        zg[:, sub, h * PS:(h + 1) * PS], ysc[:, hp, PS:2 * PS],
                        zg[:, sub, h * PS:(h + 1) * PS], ALU.mult),
                        wait=([sz_done] if h == 0 else []) + [sV.at()], inc=sV.bump())
                cb0 = hp * 512
                ops.add("tensor", lambda e, hp=hp, cb0=cb0: e.matmul(
                    A4[:, cb0:cb0 + 64], btm[:, :], u3[:, hp, 2, :],
                    start=True, stop=True), wait=[u_done, btm_done], inc=sT.bump())
                cmm = sT.at()
                ops.add("vector", lambda e, h=h, cb0=cb0: e.scalar_tensor_tensor(
                    st[:, h, :], st[:, h, :], dcol[:, 1, h:h + 1],
                    A4[:, cb0:cb0 + 64], ALU.mult, ALU.add),
                    wait=[cmm, dec_done], inc=sV.bump())
            sub_end = [sV.at(), sA.at(), sT.at()]

        if pass_b:
            for sub in range(2):
                for hh in range(2):
                    ops.add("scalar", lambda e, sub=sub, hh=hh: e.activation(
                        sq_scr, zg[:, sub, hh * DM:(hh + 1) * DM],
                        AF.Square, accum_out=c[:, 178 + hh:179 + hh]),
                        wait=sub_end + [sA.at()], inc=sA.bump())
                ops.add("vector", lambda e: e.tensor_add(
                    c[:, 180:181], c[:, 178:179], c[:, 179:180]),
                    wait=[sA.at(), sV.at()], inc=sV.bump())
                ops.add("vector", lambda e: e.tensor_scalar(
                    c[:, 180:181], c[:, 180:181], 1.0 / DSSM, SSM_EPS,
                    ALU.mult, ALU.add), wait=[sV.at()], inc=sV.bump())
                ops.add("scalar", lambda e: e.activation(
                    c[:, 181:182], c[:, 180:181], AF.Sqrt),
                    wait=[sV.at()], inc=sA.bump())
                ops.add("vector", lambda e, sub=sub: e.reciprocal(
                    c[:, 182 + sub:183 + sub], c[:, 181:182]),
                    wait=[sA.at()], inc=sV.bump())
        rms_done = [sV.at(), sA.at()]

        if pass_b:
            for tc in range(2):
                for g in range(32):
                    gp = g % 2
                    qm = {}
                    for q in range(2):
                        ops.add("sync", lambda e, g=g, q=q: e.dma_start(
                            out=wo_sb[:, q % 2, :],
                            in_=b.hb_wout[g * 128:(g + 1) * 128,
                                          q * 512:(q + 1) * 512]),
                            wait=[sT.at()], inc=sWo[q].bump(16))
                        qm[q] = sWo[q].at()
                    ops.add("tensor", lambda e, tc=tc, g=g, gp=gp: e.transpose(
                        ps_t[:, gp, 0:128],
                        zg[:, tc, g * 128:(g + 1) * 128], b.ident[:, :]),
                        wait=rms_done + [sV.at()], inc=sT.bump())
                    ops.add("vector", lambda e, gp=gp, g=g: e.tensor_scalar(
                        sph[:, gp, :], ps_t[:, gp, 0:128],
                        normw_sb[:, g:g + 1], None, ALU.mult),
                        wait=[sT.at()], inc=sV.bump())
                    gtr = sV.at()
                    for ds in range(4):
                        wt = [qm[ds]]
                        if ds == 0:
                            wt += [gtr]
                            if g == 0:
                                wt += [sV.at()]
                        ops.add("tensor", lambda e, ds=ds, g=g, gp=gp: e.matmul(
                            A4[:, ds * 512:(ds + 1) * 512], sph[:, gp, :],
                            wo_sb[:, ds % 2, :], start=(g == 0), stop=(g == 31)),
                            wait=wt, inc=sT.bump())
                        if ds < 2:
                            ops.add("sync", lambda e, g=g, q=ds + 2: e.dma_start(
                                out=wo_sb[:, q % 2, :],
                                in_=b.hb_wout[g * 128:(g + 1) * 128,
                                              q * 512:(q + 1) * 512]),
                                wait=[sT.at()], inc=sWo[ds + 2].bump(16))
                            qm[ds + 2] = sWo[ds + 2].at()
                wmm = sT.at()
                for ds in range(4):
                    ops.add("vector", lambda e, tc=tc, ct=ct, ds=ds: e.scalar_tensor_tensor(
                        b.resid[:, 2 * ct + tc, ds * 512:(ds + 1) * 512],
                        A4[:, ds * 512:(ds + 1) * 512], c[:, 182 + tc:183 + tc],
                        b.resid[:, 2 * ct + tc, ds * 512:(ds + 1) * 512],
                        ALU.mult, ALU.add), wait=[wmm], inc=sV.bump())

    ops.add("vector", lambda e: e.memset(st[:, :, :], 0.0),
            wait=setup2 + tail_done, inc=sV.bump())
    for ct in range(NCH):
        chunk(ct, pass_b=False)
    passA_done = [sV.at(), sA.at(), sT.at()]

    ops.add("gpsimd", lambda e: e.dma_start(
        out=b.cc_st_in[:, :],
        in_=bass.AP(st, 0, [[64 * PS, 128], [1, 64 * PS]])),
        wait=passA_done, inc=sCs.bump(16))
    ops.add("gpsimd", lambda e: e.collective_compute(
        "AllGather", ALU.bypass,
        replica_groups=[[0, 1], [2, 3], [4, 5], [6, 7]],
        ins=[b.cc_st_in.ap().opt()], outs=[b.cc_st_out.ap().opt()]),
        wait=[sCs.at()], inc=sCsC.bump(1))
    ops.add("gpsimd", lambda e: e.dma_start(
        out=bass.AP(st, 0, [[64 * PS, 128], [1, 64 * PS]]),
        in_=b.cc_st_out[0:128, :]), wait=[sCsC.at()], inc=sCs.bump(16))
    ops.add("vector", lambda e: e.tensor_scalar(
        bass.AP(st, 0, [[64 * PS, 128], [1, 64 * PS]]),
        bass.AP(st, 0, [[64 * PS, 128], [1, 64 * PS]]),
        gcol[:, 0:1], None, ALU.mult), wait=[sCs.at(), sV.at()], inc=sV.bump())

    for ct in range(NCH):
        chunk(ct, pass_b=True)

    b.resid_ready = [sV.at()]
    ph.close()


def _emit_output(b):
    ops = b.ops
    sSt = b.sem("out_st")
    rw = b.resid_ready
    for qc in range(NQC):
        ops.add("sync", lambda e, qc=qc: e.dma_start(
            out=b.out_x[qc * 128:(qc + 1) * 128, :], in_=b.resid[:, qc, :]),
            wait=rw, inc=sSt.bump(16))
    ops.barrier("sync", [sSt.at()])



def _stage_inputs(inputs, stages=ALL_STAGES, x_override=None):
    x_full = np.ascontiguousarray(x_override if x_override is not None
                                  else inputs["L0_attn_a_x"])
    conv_w = np.ascontiguousarray(inputs["L0_attn_b_conv_w"])
    conv_w_fm = np.zeros((128, 34 * 4), dtype=np.float32)
    conv_b_fm = np.zeros((128, 34), dtype=np.float32)
    cw = conv_w.reshape(34, 128, 4)
    cb = inputs["L0_attn_b_conv_b"].reshape(34, 128)
    for ch in range(34):
        conv_w_fm[:, ch * 4:(ch + 1) * 4] = cw[ch]
        conv_b_fm[:, ch] = cb[ch]
    ssmv = np.zeros((HS, 4), dtype=np.float32)
    ssmv[:, 0] = inputs["L0_attn_b_A_log"]
    ssmv[:, 1] = inputs["L0_attn_b_D"]
    ssmv[:, 2] = inputs["L0_attn_b_dt_bias"]
    ssmrow = np.ascontiguousarray(ssmv[:, 0:3].T.reshape(1, 192))
    norm_w_fm = np.ascontiguousarray(
        inputs["L0_attn_b_norm_w"].reshape(32, 128).T)
    mup = inputs["L0_attn_b_mup_vector"]
    mup_fm = np.zeros((128, 67), dtype=np.float32)
    mup_pad = np.zeros(67 * 128, dtype=np.float32)
    mup_pad[:PROJ] = mup
    mup_fm[:, :] = mup_pad.reshape(67, 128).T

    bqkv = np.stack([inputs["L0_attn_a_b_Q"], inputs["L0_attn_a_b_K"],
                     inputs["L0_attn_a_b_V"]], axis=-1).transpose(1, 0, 2)
    bqkv = np.ascontiguousarray(bqkv.reshape(EH, 3 * NH).astype(np.float32))
    maps = []
    for c in range(8):
        batch, half = c // 2, c % 2
        mtiles = np.zeros((128, 6, 512), dtype=np.float32)
        mtiles[:, 0, :] = 0.0 if half == 1 else -1e5
        jj = np.arange(512)[None, :]
        ii = np.arange(128)[:, None]
        for r in range(4):
            mtiles[:, 2 + r, :] = np.where(jj > ii + 128 * r, -1e5, 0.0)
        mtiles = np.ascontiguousarray(mtiles.reshape(128, 6 * 512))
        qpos = np.zeros((128, NQC), dtype=np.int32)
        for qc in range(NQC):
            qpos[:, qc] = 2048 * half + 128 * qc + np.arange(128)
        gate = np.array([[1.0 if half == 1 else 0.0, float(half), 0.0, 0.0]],
                        dtype=np.float32)
        if "attn" in stages:
            xb = x_full[batch]
            own = xb[half * HALF:(half + 1) * HALF]
            other = xb[(1 - half) * HALF:(2 - half) * HALF]
            xs = np.concatenate([other, own], axis=0)
        else:
            xs = x_full[batch][half * HALF:(half + 1) * HALF]
            xs = np.concatenate([xs, np.zeros((LSEQ - HALF, DM), np.float32)], axis=0)
        m = {
            "x": np.ascontiguousarray(xs),
            "w_q": inputs["L0_attn_a_W_Q"], "w_k": inputs["L0_attn_a_W_K"],
            "w_v": inputs["L0_attn_a_W_V"], "w_o": inputs["L0_attn_a_W_O"],
            "b_q": inputs["L0_attn_a_b_Q"], "b_k": inputs["L0_attn_a_b_K"],
            "b_v": inputs["L0_attn_a_b_V"],
            "b_o": inputs["L0_attn_a_b_O"].reshape(1, DM),
            "w_gu": inputs["L0_ffn_w_gate_up"], "w_p": inputs["L0_ffn_w_proj"],
            "w_in": inputs["L0_attn_b_W_in"], "w_out": inputs["L0_attn_b_W_out"],
            "conv_w_fm": conv_w_fm, "conv_b_fm": conv_b_fm, "ssm_vec": ssmv,
            "ssm_row": ssmrow,
            "norm_w_fm": norm_w_fm, "mup_fm": mup_fm,
            "qpos": qpos, "gate": gate, "bqkv": bqkv, "mtiles": mtiles,
        }
        maps.append({k: np.ascontiguousarray(v) for k, v in m.items()})
    return maps


def run_device(inputs, stages=ALL_STAGES, x_override=None):
    b = build_program(stages)
    in_maps = _stage_inputs(inputs, stages, x_override)
    res = run_bass_kernel_spmd(b.nc, in_maps, core_ids=list(range(8)))
    out = np.zeros((NB, LSEQ, DM), dtype=np.float32)
    for c in range(8):
        batch, half = c // 2, c % 2
        out[batch, half * HALF:(half + 1) * HALF] = res.results[c]["xo"]
    return out


def kernel(**inputs):
    return run_device(inputs, ALL_STAGES)


# version 05
# speedup vs baseline: 1.0443x; 1.0327x over previous
import contextlib
import os
import numpy as np
import concourse.bass as bass
import concourse.mybir as mybir
from concourse.bass_utils import run_bass_kernel_spmd

F32 = mybir.dt.float32
BF16 = mybir.dt.bfloat16
I32 = mybir.dt.int32
AF = mybir.ActivationFunctionType
ALU = mybir.AluOpType
AX = mybir.AxisListType

NB, LSEQ, DM = 4, 4096, 2048
HALF, NQC = 2048, 16
NH, EH = 16, 128
FH = 8192
PROJ, DSSM, CONVD, HS, PS, SS = 8512, 4096, 4352, 64, 64, 128
LN_EPS = 1e-5
RMS_EPS = float(np.finfo(np.float32).eps)
SSM_EPS = 1e-5
INV_SQRT_EH = 1.0 / float(np.sqrt(128.0))
NEG_BIG = -100000.0
ALL_STAGES = ("attn", "ffn1", "ssm", "ffn2")
DEV_SKIP_QUANT = bool(int(os.environ.get("KDEV_SKIP_QUANT", "0")))
DEV_FFN_TILES = int(os.environ.get("KDEV_FFN_TILES", "8"))


class Sem:
    def __init__(self, nc, es, name):
        self.h = es.enter_context(nc.semaphore(name))
        self.n = 0
        self.name = name

    def bump(self, k=1):
        self.n += k
        return (self, k)

    def at(self):
        return (self, self.n)


class Ops:
    ENGS = ("sync", "tensor", "vector", "scalar", "gpsimd")

    def __init__(self):
        self.q = {e: [] for e in self.ENGS}

    def add(self, eng, fn, wait=(), inc=None):
        wait = tuple((s, v) for (s, v) in wait if v > 0)

        def run(e, fn=fn, wait=wait, inc=inc):
            for s, v in wait:
                e.wait_ge(s.h, v)
            r = fn(e)
            if inc is not None:
                r.then_inc(inc[0].h, inc[1])
        self.q[eng].append(run)

    def barrier(self, eng, waits):
        self.add(eng, lambda e: e.nop(), wait=waits)


class B:
    def __init__(self, stages=ALL_STAGES):
        self.stages = stages
        self.nc = bass.Bass("TRN2", target_bir_lowering=False)
        self.es = contextlib.ExitStack()
        self.ops = Ops()
        self.sems = {}

    def sem(self, name):
        if name not in self.sems:
            self.sems[name] = Sem(self.nc, self.es, name)
        return self.sems[name]

    def sb(self, name, shape, dtype, es=None):
        return (es or self.es).enter_context(self.nc.sbuf_tensor(name, shape, dtype))

    def ps(self, name, shape, dtype=F32, es=None):
        return (es or self.es).enter_context(self.nc.psum_tensor(name, shape, dtype))



def build_program(stages=ALL_STAGES):
    b = B(stages)
    nc = b.nc

    b.in_x = nc.dram_tensor("x", [LSEQ, DM], F32, kind="ExternalInput")
    b.in_wq = nc.dram_tensor("w_q", [NH, DM, EH], F32, kind="ExternalInput")
    b.in_wk = nc.dram_tensor("w_k", [NH, DM, EH], F32, kind="ExternalInput")
    b.in_wv = nc.dram_tensor("w_v", [NH, DM, EH], F32, kind="ExternalInput")
    b.in_wo = nc.dram_tensor("w_o", [NH, EH, DM], F32, kind="ExternalInput")
    b.in_bq = nc.dram_tensor("b_q", [NH, EH], F32, kind="ExternalInput")
    b.in_bk = nc.dram_tensor("b_k", [NH, EH], F32, kind="ExternalInput")
    b.in_bv = nc.dram_tensor("b_v", [NH, EH], F32, kind="ExternalInput")
    b.in_bo = nc.dram_tensor("b_o", [1, DM], F32, kind="ExternalInput")
    b.in_wgu = nc.dram_tensor("w_gu", [2 * FH, DM], F32, kind="ExternalInput")
    b.in_wp = nc.dram_tensor("w_p", [DM, FH], F32, kind="ExternalInput")
    b.in_win = nc.dram_tensor("w_in", [DM, PROJ], F32, kind="ExternalInput")
    b.in_wout = nc.dram_tensor("w_out", [DSSM, DM], F32, kind="ExternalInput")
    b.in_convw = nc.dram_tensor("conv_w_fm", [128, 34 * 4], F32, kind="ExternalInput")
    b.in_convb = nc.dram_tensor("conv_b_fm", [128, 34], F32, kind="ExternalInput")
    b.in_ssmv = nc.dram_tensor("ssm_vec", [HS, 4], F32, kind="ExternalInput")
    b.in_ssmrow = nc.dram_tensor("ssm_row", [1, 192], F32, kind="ExternalInput")
    b.in_normw = nc.dram_tensor("norm_w_fm", [128, 32], F32, kind="ExternalInput")
    b.in_mup = nc.dram_tensor("mup_fm", [128, 67], F32, kind="ExternalInput")
    b.in_qpos = nc.dram_tensor("qpos", [128, NQC], I32, kind="ExternalInput")
    b.in_bqkv = nc.dram_tensor("bqkv", [128, 3 * NH], F32, kind="ExternalInput")
    b.in_mtiles = nc.dram_tensor("mtiles", [128, 6 * 512], F32, kind="ExternalInput")
    b.in_gate = nc.dram_tensor("gate", [1, 4], F32, kind="ExternalInput")

    b.out_x = nc.dram_tensor("xo", [HALF, DM], F32, kind="ExternalOutput")

    b.hb_wguT = nc.dram_tensor("hb_wguT", [DM, 2 * FH], BF16, kind="Internal")
    b.hb_wpT = nc.dram_tensor("hb_wpT", [FH, DM], BF16, kind="Internal")
    b.hb_win = nc.dram_tensor("hb_win", [DM, PROJ], BF16, kind="Internal")
    b.hb_wout = nc.dram_tensor("hb_wout", [DSSM, DM], BF16, kind="Internal")
    b.hb_xn = nc.dram_tensor("hb_xn", [LSEQ, DM], BF16, kind="Internal")
    b.hb_sz = nc.dram_tensor("hb_sz", [8 * 128, 2 * DSSM], BF16, kind="Internal")
    b.hb_xbc = nc.dram_tensor("hb_xbc", [8 * 128, 34 * 256], BF16, kind="Internal")
    b.hb_dtf = nc.dram_tensor("hb_dtf", [8 * 128, 3 * 256], F32, kind="Internal")
    b.hb_wq = nc.dram_tensor("hb_wq", [NH * DM, EH], BF16, kind="Internal")
    b.hb_wk = nc.dram_tensor("hb_wk", [NH * DM, EH], BF16, kind="Internal")
    b.hb_wv = nc.dram_tensor("hb_wv", [NH * DM, EH], BF16, kind="Internal")
    b.cc_tail_in = nc.dram_tensor("cc_tail_in", [3, DM], F32, kind="Internal")
    b.cc_tail_out = nc.dram_tensor("cc_tail_out", [6, DM], F32, kind="Internal")
    b.cc_st_in = nc.dram_tensor("cc_st_in", [128, 64 * PS], BF16, kind="Internal")
    b.cc_st_out = nc.dram_tensor("cc_st_out", [256, 64 * PS], BF16, kind="Internal")

    b.ident = b.sb("g_ident", [128, 128], BF16)
    b.colA = b.sb("g_colA", [128, 192], F32)
    b.gate_sb = b.sb("g_gate", [1, 4], F32)

    _emit_setup(b)
    if DEV_SKIP_QUANT:
        b.quant_done = list(b.setup_done)
    else:
        _emit_quant(b)
    if "attn" in stages:
        _emit_attn(b)
    else:
        b.resid = b.sb("g_resid", [128, NQC, DM], F32)
        _emit_resid_init(b)
    for st in stages:
        if st == "ffn1" or st == "ffn2":
            _emit_ffn(b, st)
        elif st == "ssm":
            _emit_ssm(b)
    _emit_output(b)

    with b.es:
        with nc.Block() as block:
            @block.sync
            def _(e):
                for fn in b.ops.q["sync"]:
                    fn(e)

            @block.tensor
            def _(e):
                for fn in b.ops.q["tensor"]:
                    fn(e)

            @block.vector
            def _(e):
                for fn in b.ops.q["vector"]:
                    fn(e)

            @block.scalar
            def _(e):
                for fn in b.ops.q["scalar"]:
                    fn(e)

            @block.gpsimd
            def _(e):
                for fn in b.ops.q["gpsimd"]:
                    fn(e)
    return b



def _emit_setup(b):
    ops = b.ops
    sG = b.sem("su_g")
    sV = b.sem("su_v")
    sLd = b.sem("su_ld")
    ph = contextlib.ExitStack()
    iota_r = b.sb("su_ir", [128, 128], F32, ph)
    iota_c = b.sb("su_ic", [128, 128], F32, ph)
    ops.add("gpsimd", lambda e: e.iota(iota_r[:, :], [[1, 128]], channel_multiplier=0,
                                       allow_small_or_imprecise_dtypes=True))
    ops.add("gpsimd", lambda e: e.iota(iota_c[:, :], [[0, 128]], channel_multiplier=1,
                                       allow_small_or_imprecise_dtypes=True),
            inc=sG.bump())
    ops.add("vector", lambda e: e.tensor_tensor(b.ident[:, :], iota_r[:, :],
                                                iota_c[:, :], ALU.is_equal),
            wait=[sG.at()], inc=sV.bump())
    ops.add("sync", lambda e: e.dma_start(out=b.gate_sb[:, :], in_=b.in_gate[:, :]),
            inc=sLd.bump(16))
    b.setup_done = [sV.at(), sLd.at()]
    ph.close()



def _emit_quant(b):
    ops = b.ops
    sV = b.sem("q_v")
    sA = b.sem("q_a")
    sT = b.sem("q_t")
    sLd = [b.sem("q_ld0"), b.sem("q_ld1")]
    sSt = [b.sem("q_st0"), b.sem("q_st1")]
    sCb = [b.sem("q_cb0"), b.sem("q_cb1")]

    ph = contextlib.ExitStack()
    wbuf = [b.sb(f"q_w{i}", [128, FH], BF16, ph) for i in range(2)]
    trs = [b.sb(f"q_tr{i}", [128, FH], BF16, ph) for i in range(2)]
    absf = b.sb("q_abs", [128, FH], F32, ph)
    sums = b.sb("q_sums", [128, 128], F32, ph)
    scal = b.sb("q_scal", [128, 128], F32, ph)
    scab = b.sb("q_scab", [128, 128], BF16, ph)
    rcp = b.sb("q_rcp", [128, 128], F32, ph)
    rb = b.sb("q_rb", [128, FH], BF16, ph)
    pst = b.ps("q_pst", [128, 512], BF16, ph)

    war_w = [None, None]
    war_tr = [None, None]

    jobs = [
        (b.in_wgu, DM, 2 * FH // 128, b.hb_wguT, 2 * FH),
        (b.in_wp, FH, DM // 128, b.hb_wpT, DM),
    ]
    gi = 0
    for src_t, C, nch, dst, dstw in jobs:
        G = C // 64
        nj = C // 128
        for i in range(nch):
            pb = gi % 2
            gi += 1
            wb_t, tr_t = wbuf[pb], trs[pb]
            w = [b.setup_done[0]] if gi == 1 else []
            if war_w[pb] is not None:
                w.append(war_w[pb])
            ops.add("gpsimd", lambda e, t=wb_t, s=src_t, i=i, C=C: e.dma_start(
                out=t[:, 0:C], in_=s[i * 128:(i + 1) * 128, :]),
                wait=w, inc=sLd[pb].bump(16))
            ld = sLd[pb].at()
            ops.add("scalar", lambda e, t=wb_t, C=C: e.activation(
                absf[:, 0:C], t[:, 0:C], AF.Abs), wait=[ld, sV.at()], inc=sA.bump())
            ops.add("vector", lambda e, C=C, G=G: e.tensor_reduce(
                sums[:, 0:G].rearrange("p (g o) -> p g o", o=1),
                absf[:, 0:C].rearrange("p (g o) -> p g o", o=64),
                AX.X, ALU.add), wait=[sA.at()], inc=sV.bump())
            ops.add("vector", lambda e, G=G: e.tensor_scalar(
                scal[:, 0:G], sums[:, 0:G], 1.0 / 64.0, 1e-8, ALU.mult, ALU.max),
                wait=[sV.at()], inc=sV.bump())
            ops.add("vector", lambda e, G=G: e.tensor_copy(
                scab[:, 0:G], scal[:, 0:G]), wait=[sV.at()], inc=sV.bump())
            ops.add("vector", lambda e, G=G: e.reciprocal(
                rcp[:, 0:G], scab[:, 0:G]), wait=[sV.at()], inc=sV.bump())

            def _r(e, t=wb_t, C=C, G=G):
                bcast = bass.AP(rcp, 0, [[128, 128], [1, G], [0, 64]])
                return e.tensor_tensor(
                    rb[:, 0:C].rearrange("p (g o) -> p g o", o=64),
                    t[:, 0:C].rearrange("p (g o) -> p g o", o=64), bcast, ALU.mult)
            ops.add("vector", _r, wait=[sV.at()], inc=sV.bump())
            ops.add("vector", lambda e, C=C: e.tensor_scalar(
                rb[:, 0:C], rb[:, 0:C], 192.0, None, ALU.add), wait=[sV.at()], inc=sV.bump())
            ops.add("vector", lambda e, C=C: e.tensor_scalar(
                rb[:, 0:C], rb[:, 0:C], -192.0, None, ALU.add), wait=[sV.at()], inc=sV.bump())
            ops.add("vector", lambda e, C=C: e.tensor_scalar(
                rb[:, 0:C], rb[:, 0:C], 1.0, -1.0, ALU.min, ALU.max), wait=[sV.at()], inc=sV.bump())

            def _wh(e, t=wb_t, C=C, G=G):
                bcast = bass.AP(scab, 0, [[128, 128], [1, G], [0, 64]])
                return e.tensor_tensor(
                    t[:, 0:C].rearrange("p (g o) -> p g o", o=64),
                    rb[:, 0:C].rearrange("p (g o) -> p g o", o=64), bcast, ALU.mult)
            ops.add("vector", _wh, wait=[sV.at()], inc=sV.bump())
            wh_done = sV.at()

            trw = [war_tr[pb]] if war_tr[pb] is not None else []
            for j0 in range(0, nj, 4):
                jn = min(j0 + 4, nj)
                for j in range(j0, jn):
                    w2 = ([wh_done, sV.at()] + trw) if j == j0 else []
                    ops.add("tensor", lambda e, t=wb_t, j=j, j0=j0: e.transpose(
                        pst[:, (j - j0) * 128:(j - j0 + 1) * 128],
                        t[:, j * 128:(j + 1) * 128], b.ident[:, :]),
                        wait=w2, inc=sT.bump())
                ops.add("vector", lambda e, t=tr_t, j0=j0, jn=jn: e.tensor_copy(
                    t[:, j0 * 128:jn * 128], pst[:, 0:(jn - j0) * 128]),
                    wait=[sT.at()], inc=sV.bump())
            war_w[pb] = sT.at()

            def _st(e, t=tr_t, dst=dst, dstw=dstw, i=i, nj=nj):
                out_ap = bass.AP(dst, i * 128, [[dstw, 128], [128 * dstw, nj], [1, 128]])
                in_ap = bass.AP(t, 0, [[FH, 128], [128, nj], [1, 128]])
                return e.dma_start(out=out_ap, in_=in_ap)
            ops.add("gpsimd", _st, wait=[sV.at()], inc=sSt[pb].bump(16))
            war_tr[pb] = sSt[pb].at()

    cbuf = [b.sb(f"q_cb{i}", [128, PROJ], BF16, ph) for i in range(2)]
    first_c = [True, True]
    flat_jobs = []
    for wsrc, wdst in [(b.in_wq, b.hb_wq), (b.in_wk, b.hb_wk), (b.in_wv, b.hb_wv)]:
        flat_jobs.append((bass.AP(wsrc, 0, [[4096, 1024], [1, 4096]]),
                          bass.AP(wdst, 0, [[4096, 1024], [1, 4096]])))
    for fsrc, fdst in flat_jobs:
        for i in range(8):
            pb = gi % 2
            gi += 1
            w = [] if first_c[pb] else [sCb[pb].at()]
            first_c[pb] = False
            ops.add("gpsimd", lambda e, t=cbuf[pb], fsrc=fsrc, i=i: e.dma_start(
                out=t[:, 0:4096],
                in_=bass.AP(fsrc.tensor, i * 128 * 4096, [[4096, 128], [1, 4096]])),
                wait=w, inc=sCb[pb].bump(16))
            ops.add("gpsimd", lambda e, t=cbuf[pb], fdst=fdst, i=i: e.dma_start(
                out=bass.AP(fdst.tensor, i * 128 * 4096, [[4096, 128], [1, 4096]]),
                in_=t[:, 0:4096]), wait=[sCb[pb].at()], inc=sCb[pb].bump(16))
    sNw = b.sem("q_nw")
    nwq = b.sb("q_nw", [128, 32], F32, ph)
    ops.add("sync", lambda e: e.dma_start(out=nwq[:, :], in_=b.in_normw[:, :]),
            inc=sNw.bump(16))
    for src_t, nch, C, dst in [(b.in_win, DM // 128, PROJ, b.hb_win),
                               (b.in_wout, DSSM // 128, DM, b.hb_wout)]:
        scale_nw = dst is b.hb_wout
        for i in range(nch):
            pb = gi % 2
            gi += 1
            w = [] if first_c[pb] else [sCb[pb].at()]
            first_c[pb] = False
            ops.add("gpsimd", lambda e, t=cbuf[pb], s=src_t, i=i, C=C: e.dma_start(
                out=t[:, 0:C], in_=s[i * 128:(i + 1) * 128, :]), wait=w,
                inc=sCb[pb].bump(16))
            stw = [sCb[pb].at()]
            if scale_nw:
                ops.add("vector", lambda e, t=cbuf[pb], i=i, C=C: e.tensor_scalar(
                    t[:, 0:C], t[:, 0:C], nwq[:, i:i + 1], None, ALU.mult),
                    wait=[sCb[pb].at(), sNw.at(), sV.at()], inc=sV.bump())
                stw = [sV.at()]
            ops.add("gpsimd", lambda e, t=cbuf[pb], d=dst, i=i, C=C: e.dma_start(
                out=d[i * 128:(i + 1) * 128, :], in_=t[:, 0:C]),
                wait=stw, inc=sCb[pb].bump(16))

    b.quant_done = [sSt[0].at(), sSt[1].at(), sCb[0].at(), sCb[1].at(),
                    sV.at(), sA.at(), sT.at()]
    ph.close()



def _emit_resid_init(b):
    ops = b.ops
    sLd = b.sem("ri_ld")
    qd = list(b.quant_done)
    half_rows = bass.AP(b.in_x, 0, [[1, 1]])
    for qc in range(NQC):
        ops.add("sync", lambda e, qc=qc: e.dma_start(
            out=b.resid[:, qc, :], in_=b.in_x[qc * 128:(qc + 1) * 128, :]),
            wait=qd if qc == 0 else [], inc=sLd.bump(16))
    b.resid_ready = [sLd.at()]



def _emit_ln_stats(b, tag, sq_scr):
    ops = b.ops
    c = b.colA
    sV = b.sem(f"{tag}_lv")
    sA = b.sem(f"{tag}_la")
    rw = b.resid_ready
    for qc in range(NQC):
        ops.add("scalar", lambda e, qc=qc: e.activation(
            sq_scr, b.resid[:, qc, :], AF.Square,
            accum_out=c[:, 48 + qc:49 + qc]), wait=rw + [sA.at()], inc=sA.bump())
    ops.add("vector", lambda e: e.tensor_reduce(
        c[:, 64:80].rearrange("p (g o) -> p g o", o=1),
        bass.AP(b.resid, 0, [[NQC * DM, 128], [DM, NQC], [1, DM]]),
        AX.X, ALU.add), wait=rw, inc=sV.bump())
    ops.add("vector", lambda e: e.tensor_scalar(
        c[:, 0:16], c[:, 64:80], 1.0 / DM, None, ALU.mult),
        wait=[sV.at()], inc=sV.bump())
    ops.add("vector", lambda e: e.tensor_scalar(
        c[:, 80:96], c[:, 48:64], 1.0 / DM, None, ALU.mult),
        wait=[sA.at(), sV.at()], inc=sV.bump())
    ops.add("vector", lambda e: e.scalar_tensor_tensor(
        c[:, 16:32], c[:, 0:16], 1.0, c[:, 0:16], ALU.mult, ALU.mult),
        wait=[sV.at()], inc=sV.bump())
    ops.add("vector", lambda e: e.tensor_sub(
        c[:, 80:96], c[:, 80:96], c[:, 16:32]), wait=[sV.at()], inc=sV.bump())
    ops.add("vector", lambda e: e.tensor_scalar(
        c[:, 80:96], c[:, 80:96], LN_EPS, None, ALU.add), wait=[sV.at()], inc=sV.bump())
    ops.add("scalar", lambda e: e.activation(
        c[:, 96:112], c[:, 80:96], AF.Sqrt), wait=[sV.at()], inc=sA.bump())
    ops.add("vector", lambda e: e.reciprocal(
        c[:, 16:32], c[:, 96:112]), wait=[sA.at()], inc=sV.bump())
    ops.add("vector", lambda e: e.scalar_tensor_tensor(
        c[:, 32:48], c[:, 0:16], -1.0, c[:, 16:32], ALU.mult, ALU.mult),
        wait=[sV.at()], inc=sV.bump())
    return [sV.at()]



def _emit_ffn(b, tag):
    ops = b.ops
    sW = b.sem(f"{tag}_w")
    sV = b.sem(f"{tag}_v")
    sA = b.sem(f"{tag}_a")
    sT = b.sem(f"{tag}_t")
    qd = b.quant_done

    ph = contextlib.ExitStack()
    hbuf = b.sb(f"{tag}_h", [128, 2, FH], BF16, ph)
    xnT = b.sb(f"{tag}_xnT", [128, 16, 256], BF16, ph)
    wg = [b.sb(f"{tag}_wg{i}", [128, 8, 512], BF16, ph) for i in range(2)]
    wu = [b.sb(f"{tag}_wu{i}", [128, 8, 512], BF16, ph) for i in range(2)]
    wpq = b.sb(f"{tag}_wpq", [128, 4, 512], BF16, ph)
    htf = [b.sb(f"{tag}_htf{i}", [128, 256], BF16, ph) for i in range(2)]
    sg = b.sb(f"{tag}_sg", [128, 512], BF16, ph)
    ssqp = b.sb(f"{tag}_ssqp", [128, 2, 16], F32, ph)
    inv_c = b.sb(f"{tag}_inv", [128, 8], F32, ph)
    ps_t = b.ps(f"{tag}_pst", [128, 1024], BF16, ph)
    A4 = b.ps(f"{tag}_a4", [128, 2048], F32, ph)
    c = b.colA
    NT = DEV_FFN_TILES
    NHG = FH // 512

    sq_scr = hbuf[:, 1, 0:DM]
    rw0 = list(b.resid_ready)
    for eng in ("sync", "gpsimd", "tensor", "vector", "scalar"):
        ops.barrier(eng, rw0)
    ln_done = _emit_ln_stats(b, tag, sq_scr)

    war_wg = [None, None]
    war_wp = [None, None]
    prev_tile_done = []

    sXt = b.sem(f"{tag}_xt")
    for tt in range(NT):
        for tc, qc in enumerate((2 * tt, 2 * tt + 1)):
            xv = hbuf[:, 0, 0:DM]
            ops.add("scalar", lambda e, qc=qc, xv=xv: e.activation(
                xv, b.resid[:, qc, :], AF.Identity,
                bias=c[:, 32 + qc:33 + qc], scale=c[:, 16 + qc:17 + qc]),
                wait=ln_done + [sA.at(), sT.at(), sXt.at()] + prev_tile_done,
                inc=sA.bump())
            ops.add("sync", lambda e, tc=tc, xv=xv: e.dma_start_transpose(
                bass.AP(xnT, tc * 128, [[16 * 256, 128], [256, 16], [1, 128]]),
                xv[:, :]), wait=[sA.at(), sT.at()], inc=sXt.bump(16))
        xnT_done = sXt.at()

        for hg in range(NHG):
            slotm = []
            for half_i in range(2):

                def _ldg(e, hi=half_i, hg=hg):
                    out_ap = bass.AP(wg[hi], 0, [[8 * 512, 128], [512, 8], [1, 512]])
                    in_ap = bass.AP(b.hb_wguT, hg * 512 + hi * 8 * 128 * 2 * FH,
                                    [[2 * FH, 128], [128 * 2 * FH, 8], [1, 512]])
                    return e.dma_start(out=out_ap, in_=in_ap)

                def _ldu(e, hi=half_i, hg=hg):
                    out_ap = bass.AP(wu[hi], 0, [[8 * 512, 128], [512, 8], [1, 512]])
                    in_ap = bass.AP(b.hb_wguT, FH + hg * 512 + hi * 8 * 128 * 2 * FH,
                                    [[2 * FH, 128], [128 * 2 * FH, 8], [1, 512]])
                    return e.dma_start(out=out_ap, in_=in_ap)
                w = [qd[0]] if (tt == 0 and hg == 0) else []
                if war_wg[half_i] is not None:
                    w = w + [war_wg[half_i]]
                sg_s = b.sem(f"{tag}_wg{half_i}")
                su_s = b.sem(f"{tag}_wu{half_i}")
                ops.add("sync", _ldg, wait=w, inc=sg_s.bump(16))
                ops.add("sync", _ldu, inc=su_s.bump(16))
                slotm.append((sg_s.at(), su_s.at()))

            pb = hg % 2
            g_ps = A4[:, pb * 512:(pb + 1) * 512]
            u_ps = A4[:, 1024 + pb * 512:1024 + (pb + 1) * 512]
            for tc in range(2):
                for d in range(16):
                    hi, dd = (0, d) if d < 8 else (1, d - 8)
                    wt = []
                    if d == 0:
                        wt = [xnT_done, sA.at(), sV.at()]
                    if tc == 0 and (d == 0 or d == 8):
                        wt = wt + [slotm[hi][0], slotm[hi][1]]
                    ops.add("tensor", lambda e, dd=dd, hi=hi, tc=tc, d=d, g_ps=g_ps: e.matmul(
                        g_ps, xnT[:, d, tc * 128:(tc + 1) * 128],
                        wg[hi][:, dd, :], start=(d == 0), stop=(d == 15)),
                        wait=wt)
                    ops.add("tensor", lambda e, dd=dd, hi=hi, tc=tc, d=d, u_ps=u_ps: e.matmul(
                        u_ps, xnT[:, d, tc * 128:(tc + 1) * 128],
                        wu[hi][:, dd, :], start=(d == 0), stop=(d == 15)),
                        inc=sT.bump() if d == 15 else None)
                mm_done = sT.at()
                ops.add("scalar", lambda e, g_ps=g_ps: e.activation(
                    sg[:, :], g_ps, AF.Silu),
                    wait=[mm_done, sV.at()], inc=sA.bump())
                ops.add("vector", lambda e, u_ps=u_ps, tc=tc, hg=hg: e.tensor_tensor(
                    hbuf[:, tc, hg * 512:(hg + 1) * 512], sg[:, :],
                    u_ps, ALU.mult), wait=[sA.at()], inc=sV.bump())
                ops.add("scalar", lambda e, tc=tc, hg=hg: e.activation(
                    sg[:, :], hbuf[:, tc, hg * 512:(hg + 1) * 512], AF.Square,
                    accum_out=ssqp[:, tc, hg:hg + 1]), wait=[sV.at()], inc=sA.bump())
            war_wg[0] = war_wg[1] = sT.at()
        h_done_V = sV.at()
        h_done_A = sA.at()

        ops.add("vector", lambda e: e.tensor_reduce(
            inv_c[:, 0:2].rearrange("p (g o) -> p g o", o=1),
            ssqp[:, :, :], AX.X, ALU.add), wait=[h_done_A, sV.at()], inc=sV.bump())
        ops.add("vector", lambda e: e.tensor_scalar(
            inv_c[:, 2:4], inv_c[:, 0:2], 1.0 / FH, RMS_EPS, ALU.mult, ALU.add),
            wait=[sV.at()], inc=sV.bump())
        ops.add("scalar", lambda e: e.activation(
            inv_c[:, 4:6], inv_c[:, 2:4], AF.Sqrt), wait=[sV.at()], inc=sA.bump())
        ops.add("vector", lambda e: e.reciprocal(
            inv_c[:, 6:8], inv_c[:, 4:6]), wait=[sA.at()], inc=sV.bump())
        inv_done = sV.at()

        NHC = FH // 128
        for dh in range(2):
            for hc in range(NHC):
                pb = hc % 2
                qmarks = []
                for q in range(2):
                    slot = pb * 2 + q
                    w = []
                    if tt == 0 and dh == 0 and hc == 0 and q == 0:
                        w = [qd[0]]
                    if war_wp[pb] is not None and q == 0:
                        w = w + [war_wp[pb]]
                    sq_s = b.sem(f"{tag}_wq{slot}")
                    ops.add("sync", lambda e, hc=hc, dh=dh, q=q, slot=slot: e.dma_start(
                        out=wpq[:, slot, :],
                        in_=b.hb_wpT[hc * 128:(hc + 1) * 128,
                                     dh * 1024 + q * 512:dh * 1024 + (q + 1) * 512]),
                        wait=w, inc=sq_s.bump(16))
                    qmarks.append(sq_s.at())
                for tc in range(2):
                    tw = [sV.at()]
                    if hc == 0 and dh == 0:
                        tw += [h_done_V]
                    ops.add("tensor", lambda e, tc=tc, hc=hc: e.transpose(
                        ps_t[:, tc * 512:tc * 512 + 128],
                        hbuf[:, tc, hc * 128:(hc + 1) * 128], b.ident[:, :]),
                        wait=tw, inc=sT.bump())
                    ops.add("vector", lambda e, tc=tc: e.tensor_copy(
                        htf[tc][:, 0:128], ps_t[:, tc * 512:tc * 512 + 128]),
                        wait=[sT.at()], inc=sV.bump())
                    tr_done = sV.at()
                    for q in range(2):
                        wt = [qmarks[q]] if tc == 0 else []
                        if q == 0:
                            wt += [tr_done]
                            if hc == 0 and tc == 0:
                                wt += [h_done_A, sV.at()]
                                if dh == 1 or tt > 0:
                                    wt += [prev_evac_done]
                        ops.add("tensor", lambda e, q=q, hc=hc, tc=tc,
                                slot=pb * 2 + q: e.matmul(
                            A4[:, tc * 1024 + q * 512:tc * 1024 + (q + 1) * 512],
                            htf[tc][:, 0:128], wpq[:, slot, :],
                            start=(hc == 0), stop=(hc == NHC - 1)),
                            wait=wt, inc=sT.bump() if (tc == 1 and q == 1) else None)
                war_wp[pb] = sT.at()
            mm2_done = sT.at()
            for tc in range(2):
                for q in range(2):
                    ops.add("vector", lambda e, tc=tc, tt=tt, dh=dh, q=q: e.scalar_tensor_tensor(
                        b.resid[:, 2 * tt + tc,
                                dh * 1024 + q * 512:dh * 1024 + (q + 1) * 512],
                        A4[:, tc * 1024 + q * 512:tc * 1024 + (q + 1) * 512],
                        inv_c[:, 6 + tc:7 + tc],
                        b.resid[:, 2 * tt + tc,
                                dh * 1024 + q * 512:dh * 1024 + (q + 1) * 512],
                        ALU.mult, ALU.add),
                        wait=[mm2_done, inv_done] if (tc == 0 and q == 0) else [],
                        inc=sV.bump())
            prev_evac_done = sV.at()
        prev_tile_done = [sV.at()]
    b.resid_ready = [sV.at()]
    ph.close()



def _emit_attn(b):
    ops = b.ops
    sV = b.sem("a_v")
    sA = b.sem("a_a")
    sT = b.sem("a_t")
    sG = b.sem("a_g")
    sX = [b.sem("a_x0"), b.sem("a_x1")]
    sPt = b.sem("a_pt")
    sVt = b.sem("a_vt")
    sP2 = [b.sem("a_p20"), b.sem("a_p21")]
    sI0 = [b.sem("a_i00"), b.sem("a_i01")]
    sXq = [b.sem("a_xq0"), b.sem("a_xq1")]
    sQs = [b.sem("a_qs0"), b.sem("a_qs1")]
    sWc = b.sem("a_wc")
    sO = b.sem("a_wo")
    sXn = b.sem("a_xn")
    sKp = b.sem("a_kp")
    c = b.colA

    ph1 = contextlib.ExitStack()
    xA = b.sb("a_xA", [128, 2, DM], F32, ph1)
    xnO = b.sb("a_xnO", [128, 2, DM], BF16, ph1)
    sq1 = b.sb("a_sq1", [128, DM], BF16, ph1)
    ac = b.sb("a_ac", [128, 6, 32], F32, ph1)
    NCK = LSEQ // 128
    for i in range(NCK):
        pb = i % 2
        w = (list(b.setup_done) + list(b.quant_done)) if i < 2 else [sV.at(), sA.at()]
        ops.add("sync", lambda e, i=i, pb=pb: e.dma_start(
            out=xA[:, pb, :], in_=b.in_x[i * 128:(i + 1) * 128, :]),
            wait=w, inc=sX[pb].bump(16))
        ld = sX[pb].at()
        ops.add("vector", lambda e, i=i, pb=pb: e.tensor_reduce(
            ac[:, 0, i:i + 1].rearrange("p (g o) -> p g o", o=1),
            xA[:, pb, :].rearrange("p (g o) -> p g o", o=DM), AX.X, ALU.add),
            wait=[ld], inc=sV.bump())
        ops.add("scalar", lambda e, i=i, pb=pb: e.activation(
            sq1[:, :], xA[:, pb, :], AF.Square, accum_out=ac[:, 1, i:i + 1]),
            wait=[ld, sA.at()], inc=sA.bump())
    st1 = [sV.at(), sA.at()]
    ops.add("vector", lambda e: e.tensor_scalar(
        ac[:, 2, :], ac[:, 0, :], 1.0 / DM, None, ALU.mult), wait=st1, inc=sV.bump())
    ops.add("vector", lambda e: e.tensor_scalar(
        ac[:, 4, :], ac[:, 1, :], 1.0 / DM, None, ALU.mult), wait=st1, inc=sV.bump())
    ops.add("vector", lambda e: e.scalar_tensor_tensor(
        ac[:, 5, :], ac[:, 2, :], 1.0, ac[:, 2, :], ALU.mult, ALU.mult),
        wait=[sV.at()], inc=sV.bump())
    ops.add("vector", lambda e: e.tensor_sub(
        ac[:, 4, :], ac[:, 4, :], ac[:, 5, :]), wait=[sV.at()], inc=sV.bump())
    ops.add("vector", lambda e: e.tensor_scalar(
        ac[:, 4, :], ac[:, 4, :], LN_EPS, None, ALU.add), wait=[sV.at()], inc=sV.bump())
    ops.add("scalar", lambda e: e.activation(
        ac[:, 5, :], ac[:, 4, :], AF.Sqrt), wait=[sV.at(), sA.at()], inc=sA.bump())
    ops.add("vector", lambda e: e.reciprocal(
        ac[:, 3, :], ac[:, 5, :]), wait=[sA.at()], inc=sV.bump())
    ops.add("vector", lambda e: e.scalar_tensor_tensor(
        ac[:, 5, :], ac[:, 2, :], -1.0, ac[:, 3, :], ALU.mult, ALU.mult),
        wait=[sV.at()], inc=sV.bump())
    stats_done = [sV.at()]
    for i in range(NCK):
        pb = i % 2
        ops.add("sync", lambda e, i=i, pb=pb: e.dma_start(
            out=xA[:, pb, :], in_=b.in_x[i * 128:(i + 1) * 128, :]),
            wait=[sA.at()] if i >= 2 else stats_done, inc=sP2[pb].bump(16))
        ld = sP2[pb].at()
        ops.add("scalar", lambda e, i=i, pb=pb: e.activation(
            xnO[:, pb, :], xA[:, pb, :], AF.Identity,
            bias=ac[:, 5, i:i + 1], scale=ac[:, 3, i:i + 1]),
            wait=[ld] + stats_done + [sXn.at()], inc=sA.bump())
        ops.add("gpsimd", lambda e, i=i, pb=pb: e.dma_start(
            out=b.hb_xn[i * 128:(i + 1) * 128, :], in_=xnO[:, pb, :]),
            wait=[sA.at()], inc=sXn.bump(16))
    ph1.close()
    xn_done = [sXn.at()]

    b.resid = b.sb("g_resid", [128, NQC, DM], F32)
    php = contextlib.ExitStack()
    psA = b.ps("a_psA", [128, 3072], F32, php)
    ps_t = b.ps("a_pst", [128, 1024], BF16, php)
    ph0 = contextlib.ExitStack()
    onesb = b.sb("a_onesb", [1, 128], BF16, ph0)
    bor = b.sb("a_bor", [1, DM], BF16, ph0)
    x0b = b.sb("a_x0b", [128, 2, DM], F32, ph0)
    ops.add("vector", lambda e: e.memset(onesb[:, :], 1.0), inc=sV.bump())
    sBo = b.sem("a_bo")
    ops.add("gpsimd", lambda e: e.dma_start(out=bor[0:1, :], in_=b.in_bo[0:1, :]),
            inc=sBo.bump(16))
    for s4 in range(4):
        ops.add("tensor", lambda e, s4=s4: e.matmul(
            psA[:, s4 * 512:(s4 + 1) * 512],
            onesb[0:1, :], bor[0:1, s4 * 512:(s4 + 1) * 512],
            start=True, stop=True), wait=[sV.at(), sBo.at()] if s4 == 0 else [],
            inc=sT.bump() if s4 == 3 else None)
    bob_done = sT.at()
    for qc in range(NQC):
        pb = qc % 2
        ops.add("sync", lambda e, qc=qc, pb=pb: e.dma_start(
            out=x0b[:, pb, :],
            in_=b.in_x[HALF + qc * 128:HALF + (qc + 1) * 128, :]),
            wait=(xn_done + [sA.at(), sV.at()]) if qc < 2 else [sV.at()],
            inc=sI0[pb].bump(16))
        x0ld = sI0[pb].at()
        for ds in range(4):
            ops.add("vector", lambda e, qc=qc, ds=ds, pb=pb: e.tensor_add(
                b.resid[:, qc, ds * 512:(ds + 1) * 512],
                x0b[:, pb, ds * 512:(ds + 1) * 512],
                psA[:, ds * 512:(ds + 1) * 512]),
                wait=[x0ld, bob_done] if ds == 0 else [], inc=sV.bump())
    ph0.close()
    init_done = [sV.at()]

    ph2 = contextlib.ExitStack()
    K_sb = b.sb("a_K", [128, LSEQ], BF16, ph2)
    Vt = b.sb("a_Vt", [128, 32, EH], BF16, ph2)
    Q_sb = b.sb("a_Q", [128, HALF], BF16, ph2)
    zh = b.sb("a_zh", [128, HALF], BF16, ph2)
    xq = b.sb("a_xq", [128, 2, 1024], BF16, ph2)
    wqkv = b.sb("a_wqkv", [128, 16, 2, EH], BF16, ph2)
    wq_s = b.sb("a_wqs", [128, 2, EH], BF16, ph2)
    wo_sb = b.sb("a_wosb", [128, DM], BF16, ph2)
    scs = b.sb("a_scs", [128, LSEQ], F32, ph2)
    prs = b.sb("a_prs", [128, 8, 512], BF16, ph2)
    prT = b.sb("a_prT", [128, 32, 128], BF16, ph2)
    mtiles = b.sb("a_mtiles", [128, 6, 512], BF16, ph2)
    bqkv = b.sb("a_bqkv", [128, 3 * NH], F32, ph2)
    lparts = b.sb("a_lp", [128, 16], F32, ph2)
    linv = b.sb("a_linv", [128, NQC], F32, ph2)
    ops.add("sync", lambda e: e.dma_start(out=bqkv[:, :], in_=b.in_bqkv[:, :]),
            inc=sKp.bump(16))
    ops.add("sync", lambda e: e.dma_start(out=scs[:, 0:3072], in_=b.in_mtiles[:, :]),
            inc=sKp.bump(16))
    ops.add("vector", lambda e: e.tensor_copy(
        bass.AP(mtiles, 0, [[6 * 512, 128], [1, 3072]]), scs[:, 0:3072]),
        wait=[sKp.at()], inc=sV.bump())
    kp_done = [sKp.at(), sV.at()]

    NQG = 4
    for h in range(NH):
        for wi, wsrc in ((0, b.hb_wk), (1, b.hb_wv)):
            ops.add("sync", lambda e, h=h, wi=wi, wsrc=wsrc: e.dma_start(
                out=bass.AP(wqkv, wi * EH, [[16 * 2 * EH, 128], [2 * EH, 16], [1, EH]]),
                in_=bass.AP(wsrc, h * DM * EH, [[EH, 128], [128 * EH, 16], [1, EH]])),
                wait=[sT.at()], inc=sWc.bump(16))
        ops.add("gpsimd", lambda e, h=h: e.dma_start(
            out=wo_sb[:, :], in_=b.in_wo[h, :, :]),
            wait=[sT.at()], inc=sO.bump(16))
        wld = sWc.at()
        for qt in range(4):
            for ci in range(16):
                pb = ci % 2
                ops.add("sync", lambda e, h=h, qt=qt, ci=ci, pb=pb: e.dma_start(
                    out=xq[:, pb, :],
                    in_=b.hb_xn[qt * 1024:(qt + 1) * 1024, ci * 128:(ci + 1) * 128],
                    transpose=True),
                    wait=(xn_done + init_done if (h == 0 and qt == 0 and ci < 2)
                          else [sT.at()]),
                    inc=sXq[pb].bump(16))
                xld = sXq[pb].at()
                wt0 = [xld, wld]
                if ci == 0:
                    wt0 = wt0 + [sV.at()]
                if qt >= 2:
                    ops.add("sync", lambda e, h=h, ci=ci, pb=pb: e.dma_start(
                        out=wq_s[:, pb, :],
                        in_=b.hb_wq[h * DM + ci * 128:h * DM + (ci + 1) * 128, :]),
                        wait=[sT.at()], inc=sQs[pb].bump(16))
                    wt0 = wt0 + [sQs[pb].at()]
                for nn in range(2):
                    ops.add("tensor", lambda e, h=h, ci=ci, nn=nn, pb=pb: e.matmul(
                        psA[:, nn * 512:(nn + 1) * 512],
                        wqkv[:, ci, 0, :], xq[:, pb, nn * 512:(nn + 1) * 512],
                        start=(ci == 0), stop=(ci == 15)),
                        wait=wt0 if nn == 0 else [])
                    ops.add("tensor", lambda e, h=h, ci=ci, nn=nn, pb=pb: e.matmul(
                        psA[:, 1024 + nn * 512:1024 + (nn + 1) * 512],
                        wqkv[:, ci, 1, :], xq[:, pb, nn * 512:(nn + 1) * 512],
                        start=(ci == 0), stop=(ci == 15)),
                        inc=None if qt >= 2 else (sT.bump() if nn == 1 else None))
                    if qt >= 2:
                        ops.add("tensor", lambda e, h=h, ci=ci, nn=nn, pb=pb: e.matmul(
                            psA[:, 2048 + nn * 512:2048 + (nn + 1) * 512],
                            wq_s[:, pb, :], xq[:, pb, nn * 512:(nn + 1) * 512],
                            start=(ci == 0), stop=(ci == 15)),
                            inc=sT.bump() if nn == 1 else None)
            mm_done = sT.at()
            ops.add("vector", lambda e, qt=qt, h=h: e.tensor_scalar(
                K_sb[:, qt * 1024:(qt + 1) * 1024], psA[:, 0:1024],
                bqkv[:, 3 * h + 1:3 * h + 2], None, ALU.add),
                wait=[mm_done] + kp_done + [sV.at()], inc=sV.bump())
            ops.add("vector", lambda e, qt=qt, h=h: e.tensor_scalar(
                bass.AP(prs, 6 * 512, [[8 * 512, 128], [1, 1024]]),
                psA[:, 1024:2048],
                bqkv[:, 3 * h + 2:3 * h + 3], None, ALU.add),
                wait=[sV.at()], inc=sV.bump())
            if qt >= 2:
                ops.add("vector", lambda e, qt=qt, h=h: e.tensor_scalar(
                    Q_sb[:, (qt - 2) * 1024:(qt - 1) * 1024], psA[:, 2048:3072],
                    bqkv[:, 3 * h:3 * h + 1], None, ALU.add),
                    wait=[sV.at()], inc=sV.bump())
            ev_done = sV.at()
            ops.add("sync", lambda e, qt=qt: e.dma_start_transpose(
                bass.AP(Vt, qt * 8 * EH, [[32 * EH, 128], [EH, 8], [1, 128]]),
                bass.AP(prs, 6 * 512, [[8 * 512, 128], [1, 1024]])),
                wait=[ev_done, sT.at()], inc=sVt.bump(16))
        kvq_done = [sV.at(), sT.at(), sVt.at()]

        qcs = [(g4, g4 * 4 + qi) for g4 in range(NQG) for qi in range(4)]
        tr_mark, zh_mark, exp_mark, mx_mark = {}, {}, {}, {}

        def emit_scores(qc, g4):
            nseg = 5 + g4
            for sg in range(nseg):
                sp = sg % 2
                if sg < 4:
                    msel = 0
                elif sg == nseg - 1:
                    msel = 2 + (qc % 4)
                else:
                    msel = 1
                wt = (kvq_done + kp_done + [sV.at()]) if sg < 2 else [sV.at()]
                ops.add("tensor", lambda e, qc=qc, sg=sg, sp=sp: e.matmul(
                    psA[:, sp * 512:(sp + 1) * 512],
                    Q_sb[:, qc * 128:(qc + 1) * 128],
                    K_sb[:, sg * 512:(sg + 1) * 512],
                    start=True, stop=True), wait=wt, inc=sT.bump())
                ew = [sT.at()]
                if sg == 0 and (qc - 1) in exp_mark:
                    ew += [exp_mark[qc - 1]]
                ops.add("vector", lambda e, sg=sg, sp=sp, msel=msel: e.scalar_tensor_tensor(
                    scs[:, sg * 512:(sg + 1) * 512],
                    psA[:, sp * 512:(sp + 1) * 512],
                    INV_SQRT_EH, mtiles[:, msel, :],
                    ALU.mult, ALU.add), wait=ew, inc=sV.bump())
            kext = 512 * nseg
            mxc = 176 + (qc % 2)
            ops.add("vector", lambda e, kext=kext, mxc=mxc: e.reduce_max(
                c[:, mxc:mxc + 1], scs[:, 0:kext], AX.X, negate=True),
                wait=[sV.at()], inc=sV.bump())
            mx_mark[qc] = sV.at()

        def emit_exp(qc, g4):
            nseg = 5 + g4
            mxc = 176 + (qc % 2)
            lb = 8 * (qc % 2)
            for sg in range(nseg):
                ew = []
                if sg == 0:
                    ew = [mx_mark[qc], sA.at()]
                    if qc == 8:
                        ew += [sVt.at()]
                    if (qc - 1) in tr_mark:
                        ew += [tr_mark[qc - 1]]
                ops.add("scalar", lambda e, sg=sg, mxc=mxc, lb=lb: e.activation(
                    prs[:, sg, :], scs[:, sg * 512:(sg + 1) * 512],
                    AF.Exp, bias=c[:, mxc:mxc + 1],
                    accum_out=lparts[:, lb + sg:lb + sg + 1]),
                    wait=ew, inc=sA.bump())
            exp_mark[qc] = sA.at()

        def emit_tail(qc, g4):
            nseg = 5 + g4
            lb = 8 * (qc % 2)
            ops.add("vector", lambda e, nseg=nseg, lb=lb: e.tensor_reduce(
                c[:, 178:179].rearrange("p (g o) -> p g o", o=1),
                lparts[:, lb:lb + nseg].rearrange("p (g o) -> p g o", o=nseg),
                AX.X, ALU.add), wait=[exp_mark[qc], sV.at()], inc=sV.bump())
            ops.add("vector", lambda e, qc=qc: e.reciprocal(
                linv[:, qc:qc + 1], c[:, 178:179]),
                wait=[sV.at()], inc=sV.bump())
            ops.add("sync", lambda e, nseg=nseg: e.dma_start_transpose(
                bass.AP(prT, 0, [[32 * 128, 128], [128, 4 * nseg], [1, 128]]),
                bass.AP(prs, 0, [[8 * 512, 128], [1, nseg * 512]])),
                wait=[exp_mark[qc], sT.at()], inc=sPt.bump(16))
            tr_mark[qc] = sPt.at()
            avb = 1024 + 512 * (qc % 2)
            for kc in range(4 * nseg):
                wt = []
                if kc == 0:
                    wt = [tr_mark[qc], sV.at()]
                    if (qc - 2) in zh_mark:
                        wt += [zh_mark[qc - 2]]
                ops.add("tensor", lambda e, kc=kc, nseg=nseg, avb=avb: e.matmul(
                    psA[:, avb:avb + 128], Vt[:, kc, :], prT[:, kc, :],
                    start=(kc == 0), stop=(kc == 4 * nseg - 1)),
                    wait=wt, inc=sT.bump() if kc == 4 * nseg - 1 else None)
            ops.add("vector", lambda e, qc=qc, avb=avb: e.tensor_copy(
                zh[:, qc * 128:(qc + 1) * 128],
                psA[:, avb:avb + 128]), wait=[sT.at()], inc=sV.bump())
            zh_mark[qc] = sV.at()

        for i, (g4, qc) in enumerate(qcs):
            emit_scores(qc, g4)
            if i > 0:
                pg4, pqc = qcs[i - 1]
                emit_tail(pqc, pg4)
            emit_exp(qc, g4)
        emit_tail(qcs[-1][1], qcs[-1][0])
        tr_mark.clear(); zh_mark.clear(); exp_mark.clear(); mx_mark.clear()
        z_done = [sV.at(), sT.at()]

        wo_ld = sO.at()
        for qc in range(NQC):
            for ds in range(4):
                wt = [sV.at()] if ds == 0 else []
                if qc == 0 and ds == 0:
                    wt = z_done + [wo_ld, sV.at()]
                ops.add("tensor", lambda e, h=h, qc=qc, ds=ds: e.matmul(
                    psA[:, 1024 + ds * 512:1024 + (ds + 1) * 512],
                    zh[:, qc * 128:(qc + 1) * 128],
                    wo_sb[:, ds * 512:(ds + 1) * 512],
                    start=True, stop=True), wait=wt,
                    inc=sT.bump() if ds == 3 else None)
            wmm = sT.at()
            for ds in range(4):
                ops.add("vector", lambda e, qc=qc, ds=ds: e.scalar_tensor_tensor(
                    b.resid[:, qc, ds * 512:(ds + 1) * 512],
                    psA[:, 1024 + ds * 512:1024 + (ds + 1) * 512],
                    linv[:, qc:qc + 1],
                    b.resid[:, qc, ds * 512:(ds + 1) * 512],
                    ALU.mult, ALU.add), wait=[wmm], inc=sV.bump())
    ph2.close()
    php.close()
    b.resid_ready = [sV.at()]
def _emit_ln_stats(b, tag, sq_scr):
    ops = b.ops
    c = b.colA
    sV = b.sem(f"{tag}_lv")
    sA = b.sem(f"{tag}_la")
    rw = b.resid_ready
    for qc in range(NQC):
        ops.add("scalar", lambda e, qc=qc: e.activation(
            sq_scr, b.resid[:, qc, :], AF.Square,
            accum_out=c[:, 48 + qc:49 + qc]), wait=rw + [sA.at()], inc=sA.bump())
    ops.add("vector", lambda e: e.tensor_reduce(
        c[:, 64:80].rearrange("p (g o) -> p g o", o=1),
        bass.AP(b.resid, 0, [[NQC * DM, 128], [DM, NQC], [1, DM]]),
        AX.X, ALU.add), wait=rw, inc=sV.bump())
    ops.add("vector", lambda e: e.tensor_scalar(
        c[:, 0:16], c[:, 64:80], 1.0 / DM, None, ALU.mult),
        wait=[sV.at()], inc=sV.bump())
    ops.add("vector", lambda e: e.tensor_scalar(
        c[:, 80:96], c[:, 48:64], 1.0 / DM, None, ALU.mult),
        wait=[sA.at(), sV.at()], inc=sV.bump())
    ops.add("vector", lambda e: e.scalar_tensor_tensor(
        c[:, 16:32], c[:, 0:16], 1.0, c[:, 0:16], ALU.mult, ALU.mult),
        wait=[sV.at()], inc=sV.bump())
    ops.add("vector", lambda e: e.tensor_sub(
        c[:, 80:96], c[:, 80:96], c[:, 16:32]), wait=[sV.at()], inc=sV.bump())
    ops.add("vector", lambda e: e.tensor_scalar(
        c[:, 80:96], c[:, 80:96], LN_EPS, None, ALU.add), wait=[sV.at()], inc=sV.bump())
    ops.add("scalar", lambda e: e.activation(
        c[:, 96:112], c[:, 80:96], AF.Sqrt), wait=[sV.at()], inc=sA.bump())
    ops.add("vector", lambda e: e.reciprocal(
        c[:, 16:32], c[:, 96:112]), wait=[sA.at()], inc=sV.bump())
    ops.add("vector", lambda e: e.scalar_tensor_tensor(
        c[:, 32:48], c[:, 0:16], -1.0, c[:, 16:32], ALU.mult, ALU.mult),
        wait=[sV.at()], inc=sV.bump())
    return [sV.at()]



def _emit_ffn(b, tag):
    ops = b.ops
    sW = b.sem(f"{tag}_w")
    sV = b.sem(f"{tag}_v")
    sA = b.sem(f"{tag}_a")
    sT = b.sem(f"{tag}_t")
    qd = b.quant_done

    ph = contextlib.ExitStack()
    hbuf = b.sb(f"{tag}_h", [128, 2, FH], BF16, ph)
    xnT = b.sb(f"{tag}_xnT", [128, 16, 256], BF16, ph)
    wg = [b.sb(f"{tag}_wg{i}", [128, 8, 512], BF16, ph) for i in range(2)]
    wu = [b.sb(f"{tag}_wu{i}", [128, 8, 512], BF16, ph) for i in range(2)]
    wpq = b.sb(f"{tag}_wpq", [128, 4, 512], BF16, ph)
    htf = [b.sb(f"{tag}_htf{i}", [128, 256], BF16, ph) for i in range(2)]
    sg = b.sb(f"{tag}_sg", [128, 512], BF16, ph)
    ssqp = b.sb(f"{tag}_ssqp", [128, 2, 16], F32, ph)
    inv_c = b.sb(f"{tag}_inv", [128, 8], F32, ph)
    ps_t = b.ps(f"{tag}_pst", [128, 1024], BF16, ph)
    A4 = b.ps(f"{tag}_a4", [128, 2048], F32, ph)
    c = b.colA
    NT = DEV_FFN_TILES
    NHG = FH // 512

    sq_scr = hbuf[:, 1, 0:DM]
    rw0 = list(b.resid_ready)
    for eng in ("sync", "gpsimd", "tensor", "vector", "scalar"):
        ops.barrier(eng, rw0)
    ln_done = _emit_ln_stats(b, tag, sq_scr)

    war_wg = [None, None]
    war_wp = [None, None]
    prev_tile_done = []

    sXt = b.sem(f"{tag}_xt")
    for tt in range(NT):
        for tc, qc in enumerate((2 * tt, 2 * tt + 1)):
            xv = hbuf[:, 0, 0:DM]
            ops.add("scalar", lambda e, qc=qc, xv=xv: e.activation(
                xv, b.resid[:, qc, :], AF.Identity,
                bias=c[:, 32 + qc:33 + qc], scale=c[:, 16 + qc:17 + qc]),
                wait=ln_done + [sA.at(), sT.at(), sXt.at()] + prev_tile_done,
                inc=sA.bump())
            ops.add("sync", lambda e, tc=tc, xv=xv: e.dma_start_transpose(
                bass.AP(xnT, tc * 128, [[16 * 256, 128], [256, 16], [1, 128]]),
                xv[:, :]), wait=[sA.at(), sT.at()], inc=sXt.bump(16))
        xnT_done = sXt.at()

        for hg in range(NHG):
            slotm = []
            for half_i in range(2):

                def _ldg(e, hi=half_i, hg=hg):
                    out_ap = bass.AP(wg[hi], 0, [[8 * 512, 128], [512, 8], [1, 512]])
                    in_ap = bass.AP(b.hb_wguT, hg * 512 + hi * 8 * 128 * 2 * FH,
                                    [[2 * FH, 128], [128 * 2 * FH, 8], [1, 512]])
                    return e.dma_start(out=out_ap, in_=in_ap)

                def _ldu(e, hi=half_i, hg=hg):
                    out_ap = bass.AP(wu[hi], 0, [[8 * 512, 128], [512, 8], [1, 512]])
                    in_ap = bass.AP(b.hb_wguT, FH + hg * 512 + hi * 8 * 128 * 2 * FH,
                                    [[2 * FH, 128], [128 * 2 * FH, 8], [1, 512]])
                    return e.dma_start(out=out_ap, in_=in_ap)
                w = [qd[0]] if (tt == 0 and hg == 0) else []
                if war_wg[half_i] is not None:
                    w = w + [war_wg[half_i]]
                sg_s = b.sem(f"{tag}_wg{half_i}")
                su_s = b.sem(f"{tag}_wu{half_i}")
                ops.add("sync", _ldg, wait=w, inc=sg_s.bump(16))
                ops.add("sync", _ldu, inc=su_s.bump(16))
                slotm.append((sg_s.at(), su_s.at()))

            pb = hg % 2
            g_ps = A4[:, pb * 512:(pb + 1) * 512]
            u_ps = A4[:, 1024 + pb * 512:1024 + (pb + 1) * 512]
            for tc in range(2):
                for d in range(16):
                    hi, dd = (0, d) if d < 8 else (1, d - 8)
                    wt = []
                    if d == 0:
                        wt = [xnT_done, sA.at(), sV.at()]
                    if tc == 0 and (d == 0 or d == 8):
                        wt = wt + [slotm[hi][0], slotm[hi][1]]
                    ops.add("tensor", lambda e, dd=dd, hi=hi, tc=tc, d=d, g_ps=g_ps: e.matmul(
                        g_ps, xnT[:, d, tc * 128:(tc + 1) * 128],
                        wg[hi][:, dd, :], start=(d == 0), stop=(d == 15)),
                        wait=wt)
                    ops.add("tensor", lambda e, dd=dd, hi=hi, tc=tc, d=d, u_ps=u_ps: e.matmul(
                        u_ps, xnT[:, d, tc * 128:(tc + 1) * 128],
                        wu[hi][:, dd, :], start=(d == 0), stop=(d == 15)),
                        inc=sT.bump() if d == 15 else None)
                mm_done = sT.at()
                ops.add("scalar", lambda e, g_ps=g_ps: e.activation(
                    sg[:, :], g_ps, AF.Silu),
                    wait=[mm_done, sV.at()], inc=sA.bump())
                ops.add("vector", lambda e, u_ps=u_ps, tc=tc, hg=hg: e.tensor_tensor(
                    hbuf[:, tc, hg * 512:(hg + 1) * 512], sg[:, :],
                    u_ps, ALU.mult), wait=[sA.at()], inc=sV.bump())
                ops.add("scalar", lambda e, tc=tc, hg=hg: e.activation(
                    sg[:, :], hbuf[:, tc, hg * 512:(hg + 1) * 512], AF.Square,
                    accum_out=ssqp[:, tc, hg:hg + 1]), wait=[sV.at()], inc=sA.bump())
            war_wg[0] = war_wg[1] = sT.at()
        h_done_V = sV.at()
        h_done_A = sA.at()

        ops.add("vector", lambda e: e.tensor_reduce(
            inv_c[:, 0:2].rearrange("p (g o) -> p g o", o=1),
            ssqp[:, :, :], AX.X, ALU.add), wait=[h_done_A, sV.at()], inc=sV.bump())
        ops.add("vector", lambda e: e.tensor_scalar(
            inv_c[:, 2:4], inv_c[:, 0:2], 1.0 / FH, RMS_EPS, ALU.mult, ALU.add),
            wait=[sV.at()], inc=sV.bump())
        ops.add("scalar", lambda e: e.activation(
            inv_c[:, 4:6], inv_c[:, 2:4], AF.Sqrt), wait=[sV.at()], inc=sA.bump())
        ops.add("vector", lambda e: e.reciprocal(
            inv_c[:, 6:8], inv_c[:, 4:6]), wait=[sA.at()], inc=sV.bump())
        inv_done = sV.at()

        NHC = FH // 128
        for dh in range(2):
            for hc in range(NHC):
                pb = hc % 2
                qmarks = []
                for q in range(2):
                    slot = pb * 2 + q
                    w = []
                    if tt == 0 and dh == 0 and hc == 0 and q == 0:
                        w = [qd[0]]
                    if war_wp[pb] is not None and q == 0:
                        w = w + [war_wp[pb]]
                    sq_s = b.sem(f"{tag}_wq{slot}")
                    ops.add("sync", lambda e, hc=hc, dh=dh, q=q, slot=slot: e.dma_start(
                        out=wpq[:, slot, :],
                        in_=b.hb_wpT[hc * 128:(hc + 1) * 128,
                                     dh * 1024 + q * 512:dh * 1024 + (q + 1) * 512]),
                        wait=w, inc=sq_s.bump(16))
                    qmarks.append(sq_s.at())
                for tc in range(2):
                    tw = [sV.at()]
                    if hc == 0 and dh == 0:
                        tw += [h_done_V]
                    ops.add("tensor", lambda e, tc=tc, hc=hc: e.transpose(
                        ps_t[:, tc * 512:tc * 512 + 128],
                        hbuf[:, tc, hc * 128:(hc + 1) * 128], b.ident[:, :]),
                        wait=tw, inc=sT.bump())
                    ops.add("vector", lambda e, tc=tc: e.tensor_copy(
                        htf[tc][:, 0:128], ps_t[:, tc * 512:tc * 512 + 128]),
                        wait=[sT.at()], inc=sV.bump())
                    tr_done = sV.at()
                    for q in range(2):
                        wt = [qmarks[q]] if tc == 0 else []
                        if q == 0:
                            wt += [tr_done]
                            if hc == 0 and tc == 0:
                                wt += [h_done_A, sV.at()]
                                if dh == 1 or tt > 0:
                                    wt += [prev_evac_done]
                        ops.add("tensor", lambda e, q=q, hc=hc, tc=tc,
                                slot=pb * 2 + q: e.matmul(
                            A4[:, tc * 1024 + q * 512:tc * 1024 + (q + 1) * 512],
                            htf[tc][:, 0:128], wpq[:, slot, :],
                            start=(hc == 0), stop=(hc == NHC - 1)),
                            wait=wt, inc=sT.bump() if (tc == 1 and q == 1) else None)
                war_wp[pb] = sT.at()
            mm2_done = sT.at()
            for tc in range(2):
                for q in range(2):
                    ops.add("vector", lambda e, tc=tc, tt=tt, dh=dh, q=q: e.scalar_tensor_tensor(
                        b.resid[:, 2 * tt + tc,
                                dh * 1024 + q * 512:dh * 1024 + (q + 1) * 512],
                        A4[:, tc * 1024 + q * 512:tc * 1024 + (q + 1) * 512],
                        inv_c[:, 6 + tc:7 + tc],
                        b.resid[:, 2 * tt + tc,
                                dh * 1024 + q * 512:dh * 1024 + (q + 1) * 512],
                        ALU.mult, ALU.add),
                        wait=[mm2_done, inv_done] if (tc == 0 and q == 0) else [],
                        inc=sV.bump())
            prev_evac_done = sV.at()
        prev_tile_done = [sV.at()]
    b.resid_ready = [sV.at()]
    ph.close()



def _emit_ssm(b):
    ops = b.ops
    sV = b.sem("s_v")
    sA = b.sem("s_a")
    sT = b.sem("s_t")
    sWi = [b.sem("s_wi0"), b.sem("s_wi1")]
    sWo = [b.sem(f"s_wo{i}") for i in range(4)]
    sCt = b.sem("s_ctail")
    sCtC = b.sem("s_ctailc")
    sCs = b.sem("s_cstate")
    sCsC = b.sem("s_cstatec")
    sSm = b.sem("s_sm")
    sCz = b.sem("s_czst")
    sXnt = b.sem("s_xnt")
    sGt = b.sem("s_gt")
    sZl = b.sem("s_czld")
    sSm2 = b.sem("s_sm2")

    ph = contextlib.ExitStack()
    zg = b.sb("s_zg", [128, 2, DSSM], BF16, ph)
    xbc = b.sb("s_xbc", [128, 34, 256], BF16, ph)
    xnT = b.sb("s_xnT", [128, 16, 256], BF16, ph)
    tails = b.sb("s_tails", [128, 34, 4], BF16, ph)
    raw = b.sb("s_raw", [128, 1, 260], BF16, ph)
    st = b.sb("s_st", [128, 64, PS], BF16, ph)
    win_sb = [b.sb("s_win0", [128, 16, 128], BF16, ph)]
    win_sb = [win_sb[0], win_sb[0]]
    wo_sb = b.sb("s_wo", [128, 2, 512], BF16, ph)
    dtf = b.sb("s_dtf", [128, 3, 256], F32, ph)
    dtm = b.sb("s_dtm", [128, 4, PS], F32, ph)
    bch = b.sb("s_bch", [128, 2, 128], F32, ph)
    btm = b.sb("s_btm", [128, 128], BF16, ph)
    gmsk = b.sb("s_gmsk", [128, 128], BF16, ph)
    marg = b.sb("s_marg", [128, 128], F32, ph)
    sph = b.sb("s_sph", [128, 2, 128], BF16, ph)
    u3 = b.sb("s_u3", [128, 2, 3, PS], BF16, ph)
    ysc = b.sb("s_ysc", [128, 2, 2 * PS], BF16, ph)
    szh = b.sb("s_szh", [128, PS], BF16, ph)
    xnt3 = b.sb("s_xnt3", [128, 16, 4], BF16, ph)
    gcol = b.sb("s_gcol", [128, 2], F32, ph)
    dcol = b.sb("s_dcol", [128, 2, 64], F32, ph)
    tailx = b.sb("s_tailx", [4, DM], BF16, ph)
    ones1 = b.sb("s_ones1", [1, 128], F32, ph)
    identf = b.sb("s_identf", [128, 128], F32, ph)
    tri_sb = b.sb("s_tri", [128, 128], BF16, ph)
    zeros_sb = b.sb("s_zeros", [128, 128], BF16, ph)
    convw_sb = b.sb("s_convw", [128, 136], F32, ph)
    convb_sb = b.sb("s_convb", [128, 34], F32, ph)
    ssmcol_sb = b.sb("s_ssmcol", [64, 4], F32, ph)
    mup_sb = b.sb("s_mup", [128, 67], F32, ph)
    normw_sb = b.sb("s_normw", [128, 32], F32, ph)
    ssmrow_sb = b.sb("s_ssmrow", [1, 192], F32, ph)
    ps_t = b.ps("s_pst", [128, 2, 512], BF16, ph)
    ps_f = b.ps("s_psf", [128, 512], F32, ph)
    ps_f2 = b.ps("s_psf2", [128, 512], F32, ph)
    A4 = b.ps("s_a4", [128, 2048], F32, ph)
    c = b.colA
    sq_scr = bass.AP(xbc, 0, [[34 * 256, 128], [1, DM]])
    zb = raw[:, 0, 0:256]
    sq_scr3 = bass.AP(xbc, 0, [[34 * 256, 3], [1, DM]])

    rw0 = list(b.resid_ready)
    for eng in ("sync", "gpsimd", "tensor", "vector", "scalar"):
        ops.barrier(eng, rw0)
    ln_done = _emit_ln_stats(b, "ssm", sq_scr)
    NCH = HALF // 256

    ph_iof = contextlib.ExitStack()
    iof = b.sb("s_iof", [128, 256], BF16, ph_iof)
    ops.add("gpsimd", lambda e: e.iota(iof[:, 0:128], [[1, 128]], channel_multiplier=0,
                                       allow_small_or_imprecise_dtypes=True))
    ops.add("gpsimd", lambda e: e.iota(iof[:, 128:256], [[0, 128]], channel_multiplier=1,
                                       allow_small_or_imprecise_dtypes=True),
            inc=b.sem("s_g").bump())
    sG = b.sem("s_g")
    ops.add("vector", lambda e: e.tensor_tensor(
        tri_sb[:, :], iof[:, 128:256], iof[:, 0:128], ALU.is_le),
        wait=[sG.at()], inc=sV.bump())
    ops.add("vector", lambda e: e.tensor_tensor(
        identf[:, :], iof[:, 0:128], iof[:, 128:256], ALU.is_equal),
        wait=[sG.at()], inc=sV.bump())
    ops.add("vector", lambda e: e.memset(ones1[:, :], 1.0), inc=sV.bump())
    ops.add("vector", lambda e: e.memset(zeros_sb[:, :], 0.0), inc=sV.bump())
    ops.add("sync", lambda e: e.dma_start(out=convw_sb[:, :], in_=b.in_convw[:, :]),
            inc=sSm.bump(16))
    ops.add("sync", lambda e: e.dma_start(out=convb_sb[:, :], in_=b.in_convb[:, :]),
            inc=sSm.bump(16))
    ops.add("sync", lambda e: e.dma_start(out=ssmcol_sb[:, :], in_=b.in_ssmv[:, :]),
            inc=sSm.bump(16))
    ops.add("sync", lambda e: e.dma_start(out=mup_sb[:, :], in_=b.in_mup[:, :]),
            inc=sSm.bump(16))
    ops.add("sync", lambda e: e.dma_start(out=normw_sb[:, :], in_=b.in_normw[:, :]),
            inc=sSm.bump(16))
    ops.add("sync", lambda e: e.dma_start(out=ssmrow_sb[:, :], in_=b.in_ssmrow[:, :]),
            inc=sSm.bump(16))
    sm_ld = sSm.at()
    sm_ld2 = sSm2.at()

    ops.add("tensor", lambda e: e.matmul(
        ps_f[:, 0:2], ones1[0:1, :], b.gate_sb[0:1, 0:2], start=True, stop=True),
        wait=b.setup_done + [sm_ld, sV.at()], inc=sT.bump())
    ops.add("vector", lambda e: e.tensor_copy(gcol[:, :], ps_f[:, 0:2]),
            wait=[sT.at()], inc=sV.bump())
    ops.add("tensor", lambda e: e.matmul(
        ps_f[:, 64:128], ones1[0:1, :], ssmrow_sb[0:1, 64:128], start=True, stop=True),
        wait=[sV.at()], inc=sT.bump())
    ops.add("vector", lambda e: e.tensor_copy(dcol[:, 0, :], ps_f[:, 64:128]),
            wait=[sT.at()], inc=sV.bump())
    ops.add("scalar", lambda e: e.activation(
        c[0:64, 190:191], ssmcol_sb[0:64, 0:1], AF.Exp),
        wait=[sm_ld, sm_ld2], inc=sA.bump())
    ops.add("vector", lambda e: e.tensor_scalar(
        c[0:64, 190:191], c[0:64, 190:191], -1.0, None, ALU.mult),
        wait=[sA.at()], inc=sV.bump())
    setup2 = [sV.at(), sA.at(), sT.at()]
    ph_iof.close()

    ops.add("gpsimd", lambda e: e.dma_start(
        out=b.cc_tail_in[0:3, :], in_=b.resid[125:128, 15, :]),
        wait=ln_done, inc=sCt.bump(16))
    ops.add("gpsimd", lambda e: e.collective_compute(
        "AllGather", ALU.bypass,
        replica_groups=[[0, 1], [2, 3], [4, 5], [6, 7]],
        ins=[b.cc_tail_in.ap().opt()], outs=[b.cc_tail_out.ap().opt()]),
        wait=[sCt.at()], inc=sCtC.bump(1))
    ops.add("gpsimd", lambda e: e.dma_start(
        out=tailx[0:3, :], in_=b.cc_tail_out[0:3, :]),
        wait=[sCtC.at()], inc=sCt.bump(16))
    tail_ld = sCt.at()
    ops.add("vector", lambda e: e.tensor_reduce(
        c[0:3, 160:161].rearrange("p (g o) -> p g o", o=1),
        tailx[0:3, :].rearrange("p (g o) -> p g o", o=DM), AX.X, ALU.add),
        wait=[tail_ld, sV.at()], inc=sV.bump())
    ops.add("scalar", lambda e: e.activation(
        sq_scr3, tailx[0:3, :], AF.Square,
        accum_out=c[0:3, 161:162]), wait=[tail_ld, sA.at()], inc=sA.bump())
    ops.add("vector", lambda e: e.tensor_scalar(
        c[0:3, 162:163], c[0:3, 160:161], 1.0 / DM, None, ALU.mult),
        wait=[sV.at(), sA.at()], inc=sV.bump())
    ops.add("vector", lambda e: e.tensor_scalar(
        c[0:3, 163:164], c[0:3, 161:162], 1.0 / DM, None, ALU.mult),
        wait=[sV.at()], inc=sV.bump())
    ops.add("vector", lambda e: e.scalar_tensor_tensor(
        c[0:3, 164:165], c[0:3, 162:163], 1.0, c[0:3, 162:163], ALU.mult, ALU.mult),
        wait=[sV.at()], inc=sV.bump())
    ops.add("vector", lambda e: e.tensor_sub(
        c[0:3, 163:164], c[0:3, 163:164], c[0:3, 164:165]),
        wait=[sV.at()], inc=sV.bump())
    ops.add("vector", lambda e: e.tensor_scalar(
        c[0:3, 163:164], c[0:3, 163:164], LN_EPS, None, ALU.add),
        wait=[sV.at()], inc=sV.bump())
    ops.add("scalar", lambda e: e.activation(
        c[0:3, 165:166], c[0:3, 163:164], AF.Sqrt), wait=[sV.at(), sA.at()],
        inc=sA.bump())
    ops.add("vector", lambda e: e.reciprocal(
        c[0:3, 166:167], c[0:3, 165:166]), wait=[sA.at()], inc=sV.bump())
    ops.add("vector", lambda e: e.tensor_scalar(
        sq_scr3, tailx[0:3, :], c[0:3, 162:163], c[0:3, 166:167],
        ALU.subtract, ALU.mult), wait=[sV.at()], inc=sV.bump())
    tail_xn = sV.at()
    for j in range(16):
        ops.add("tensor", lambda e, j=j: e.transpose(
            ps_t[:, 0, 0:3], bass.AP(xbc, j * 128, [[34 * 256, 3], [1, 128]]), b.ident[0:3, 0:3]),
            wait=([tail_xn] + setup2 + [sV.at()]) if j == 0 else [sV.at()],
            inc=sT.bump())
        ops.add("vector", lambda e, j=j: e.tensor_copy(
            xnt3[:, j, 0:3], ps_t[:, 0, 0:3]), wait=[sT.at()], inc=sV.bump())
    tail_done = [sV.at()]

    cache_war = []

    def chunk(ct, pass_b):
        if pass_b:
            ldw = [sT.at(), sV.at(), sA.at(), sCz.at()]
            ops.add("sync", lambda e, ct=ct: e.dma_start(
                out=bass.AP(zg, 0, [[2 * DSSM, 128], [1, 2 * DSSM]]),
                in_=b.hb_sz[ct * 128:(ct + 1) * 128, :]),
                wait=ldw, inc=sZl.bump(16))
            ops.add("sync", lambda e, ct=ct: e.dma_start(
                out=bass.AP(xbc, 0, [[34 * 256, 128], [1, 34 * 256]]),
                in_=b.hb_xbc[ct * 128:(ct + 1) * 128, :]),
                inc=sZl.bump(16))
            ops.add("sync", lambda e, ct=ct: e.dma_start(
                out=bass.AP(dtf, 0, [[3 * 256, 128], [1, 3 * 256]]),
                in_=b.hb_dtf[ct * 128:(ct + 1) * 128, :]),
                inc=sZl.bump(16))
            ld_done = sZl.at()
            xbc_done = [ld_done]
            sz_done = ld_done
            dt_done = [ld_done]
        else:
            chunk_a(ct)
            xbc_done = [sV.at(), sA.at()]
            sz_done = sA.at()
            dt_done = [sV.at(), sA.at()]
            ops.add("sync", lambda e, ct=ct: e.dma_start(
                out=b.hb_sz[ct * 128:(ct + 1) * 128, :],
                in_=bass.AP(zg, 0, [[2 * DSSM, 128], [1, 2 * DSSM]])),
                wait=[sz_done], inc=sCz.bump(16))
            ops.add("sync", lambda e, ct=ct: e.dma_start(
                out=b.hb_xbc[ct * 128:(ct + 1) * 128, :],
                in_=bass.AP(xbc, 0, [[34 * 256, 128], [1, 34 * 256]])),
                wait=xbc_done, inc=sCz.bump(16))
            ops.add("sync", lambda e, ct=ct: e.dma_start(
                out=b.hb_dtf[ct * 128:(ct + 1) * 128, :],
                in_=bass.AP(dtf, 0, [[3 * 256, 128], [1, 3 * 256]])),
                wait=dt_done, inc=sCz.bump(16))
            cache_war[:] = [sCz.at()]
        chunk_rest(ct, pass_b, xbc_done, sz_done, dt_done)

    def chunk_a(ct):
        for tc, qc in enumerate((2 * ct, 2 * ct + 1)):
            ops.add("scalar", lambda e, qc=qc: e.activation(
                sq_scr, b.resid[:, qc, :], AF.Identity,
                bias=c[:, 32 + qc:33 + qc], scale=c[:, 16 + qc:17 + qc]),
                wait=ln_done + [sA.at(), sT.at(), sXnt.at()] + list(cache_war),
                inc=sA.bump())
            ops.add("sync", lambda e, tc=tc: e.dma_start_transpose(
                bass.AP(xnT, tc * 128, [[16 * 256, 128], [256, 16], [1, 128]]),
                bass.AP(xbc, 0, [[34 * 256, 128], [1, DM]])),
                wait=[sA.at(), sT.at()], inc=sXnt.bump(16))
        xnT_done = sXnt.at()

        prs = list(range(0, 67))
        for pr in prs:
            pb = pr % 2
            span = 128 if pr < 66 else 64
            ops.add("sync", lambda e, pr=pr, pb=pb, span=span: e.dma_start(
                out=win_sb[pb][:, :, 0:span],
                in_=bass.AP(b.hb_win, pr * 128,
                            [[PROJ, 128], [128 * PROJ, 16], [1, span]])),
                wait=[sT.at()], inc=sWi[pb].bump(16))
            wld = sWi[pb].at()
            for d in range(16):
                wt = [wld, xnT_done, sV.at()] if d == 0 else []
                ops.add("tensor", lambda e, d=d, pb=pb, span=span: e.matmul(
                    A4[0:span, 0:256], win_sb[pb][:, d, 0:span], xnT[:, d, 0:256],
                    start=(d == 0), stop=(d == 15)),
                    wait=wt, inc=sT.bump() if d == 15 else None)
            pf_done = sT.at()
            if ct == 0 and pr >= 32:
                for d in range(16):
                    ops.add("tensor", lambda e, d=d, pb=pb, span=span: e.matmul(
                        A4[0:span, 256:259], win_sb[pb][:, d, 0:span], xnt3[:, d, 0:3],
                        start=(d == 0), stop=(d == 15)),
                        inc=sT.bump() if d == 15 else None)
                pf_done = sT.at()

            if pr < 32:
                ops.add("vector", lambda e, pb=pb, pr=pr: e.tensor_scalar(
                    zb[:, :], A4[:, 0:256], mup_sb[:, pr:pr + 1], None, ALU.mult),
                    wait=[pf_done, sV.at()], inc=sV.bump())
                for s2 in range(2):
                    ops.add("tensor", lambda e, s2=s2: e.transpose(
                        ps_t[:, 1, s2 * 128:(s2 + 1) * 128],
                        zb[:, s2 * 128:(s2 + 1) * 128], b.ident[:, :]),
                        wait=[sV.at()] if s2 == 0 else [], inc=sT.bump())
                ops.add("vector", lambda e, pr=pr: e.tensor_copy(
                    bass.AP(zg, pr * 128, [[2 * DSSM, 128], [DSSM, 2], [1, 128]]),
                    ps_t[:, 1, 0:256]), wait=[sT.at()], inc=sV.bump())
            elif pr < 66:
                cr = pr - 32
                if ct == 0:
                    ops.add("vector", lambda e, pb=pb, pr=pr: e.tensor_scalar(
                        raw[:, 0, 0:3], A4[:, 256:259], gcol[:, 0:1],
                        mup_sb[:, pr:pr + 1], ALU.mult, ALU.mult),
                        wait=[pf_done, sV.at()], inc=sV.bump())
                else:
                    ops.add("vector", lambda e, pb=pb, cr=cr: e.tensor_copy(
                        raw[:, 0, 0:3], tails[:, cr, 0:3]),
                        wait=[pf_done, sV.at()], inc=sV.bump())
                ops.add("vector", lambda e, pb=pb, pr=pr: e.tensor_scalar(
                    raw[:, 0, 3:259], A4[:, 0:256], mup_sb[:, pr:pr + 1], None,
                    ALU.mult), wait=[sV.at()], inc=sV.bump())
                ops.add("vector", lambda e, pb=pb, cr=cr: e.tensor_copy(
                    tails[:, cr, 0:3], raw[:, 0, 256:259]),
                    wait=[sV.at()], inc=sV.bump())
                ops.add("vector", lambda e, pb=pb, cr=cr: e.tensor_scalar(
                    xbc[:, cr, :], raw[:, 0, 0:256],
                    convw_sb[:, 4 * cr:4 * cr + 1], convb_sb[:, cr:cr + 1],
                    ALU.mult, ALU.add), wait=[sV.at()], inc=sV.bump())
                for k in range(1, 4):
                    ops.add("vector", lambda e, pb=pb, cr=cr, k=k: e.scalar_tensor_tensor(
                        xbc[:, cr, :], raw[:, 0, k:k + 256],
                        convw_sb[:, 4 * cr + k:4 * cr + k + 1],
                        xbc[:, cr, :], ALU.mult, ALU.add),
                        wait=[sV.at()], inc=sV.bump())
                ops.add("scalar", lambda e, cr=cr: e.activation(
                    xbc[:, cr, :], xbc[:, cr, :], AF.Silu),
                    wait=[sV.at(), sA.at()], inc=sA.bump())
            else:
                ops.add("vector", lambda e: e.tensor_scalar(
                    dtf[0:64, 0, :], A4[0:64, 0:256], mup_sb[0:64, 66:67], None,
                    ALU.mult), wait=[pf_done, sV.at()], inc=sV.bump())
        xbc_done = [sV.at(), sA.at()]
        z_done = sV.at()
        for zq in range(8):
            ops.add("scalar", lambda e, zq=zq: e.activation(
                zg[:, zq // 4, (zq % 4) * 1024:(zq % 4 + 1) * 1024],
                zg[:, zq // 4, (zq % 4) * 1024:(zq % 4 + 1) * 1024], AF.Silu),
                wait=([z_done, sA.at()] if zq == 0 else []), inc=sA.bump())

        ops.add("vector", lambda e: e.tensor_scalar(
            dtf[0:64, 0, :], dtf[0:64, 0, :], ssmcol_sb[0:64, 2:3], None, ALU.add),
            wait=xbc_done, inc=sV.bump())
        ops.add("vector", lambda e: e.tensor_scalar(
            dtf[0:64, 1, :], dtf[0:64, 0, :], -1.0, None, ALU.mult),
            wait=[sV.at()], inc=sV.bump())
        ops.add("vector", lambda e: e.tensor_max(
            dtf[0:64, 1, :], dtf[0:64, 1, :], dtf[0:64, 0, :]),
            wait=[sV.at()], inc=sV.bump())
        ops.add("scalar", lambda e: e.activation(
            dtf[0:64, 2, :], dtf[0:64, 1, :], AF.Exp, scale=-1.0),
            wait=[sV.at(), sA.at()], inc=sA.bump())
        ops.add("scalar", lambda e: e.activation(
            dtf[0:64, 2, :], dtf[0:64, 2, :], AF.Ln, bias=1.0),
            wait=[sA.at()], inc=sA.bump())
        ops.add("vector", lambda e: e.tensor_scalar(
            dtf[0:64, 1, :], dtf[0:64, 0, :], 0.0, None, ALU.max),
            wait=[sA.at(), sV.at()], inc=sV.bump())
        ops.add("vector", lambda e: e.tensor_add(
            dtf[0:64, 1, :], dtf[0:64, 1, :], dtf[0:64, 2, :]),
            wait=[sV.at()], inc=sV.bump())
        ops.add("vector", lambda e: e.tensor_scalar(
            dtf[0:64, 0, :], dtf[0:64, 1, :], c[0:64, 190:191], None, ALU.mult),
            wait=[sV.at()], inc=sV.bump())
        for sub in range(2):
            ops.add("vector", lambda e, sub=sub: e.tensor_tensor_scan(
                dtf[0:64, 2, sub * 128:(sub + 1) * 128],
                dtf[0:64, 0, sub * 128:(sub + 1) * 128],
                zeros_sb[0:64, 0:128], 0.0, ALU.add, ALU.add),
                wait=[sV.at()], inc=sV.bump())
        for sub in range(2):
            ops.add("vector", lambda e, sub=sub: e.tensor_scalar(
                dtf[0:64, 0, sub * 128:(sub + 1) * 128],
                dtf[0:64, 2, sub * 128:(sub + 1) * 128],
                dtf[0:64, 2, sub * 128 + 127:sub * 128 + 128], None, ALU.subtract),
                wait=[sV.at()], inc=sV.bump())
        ops.add("scalar", lambda e: e.activation(
            dtf[0:64, 0, :], dtf[0:64, 0, :], AF.Exp, scale=-1.0),
            wait=[sV.at(), sA.at()], inc=sA.bump())

    def chunk_rest(ct, pass_b, xbc_done, sz_done, dt_done):
        for sub in range(2):
            for r, row in enumerate((1, 2, 0)):
                ops.add("tensor", lambda e, sub=sub, r=r, row=row: e.transpose(
                    ps_f[:, r * 64:(r + 1) * 64],
                    dtf[0:64, row, sub * 128:(sub + 1) * 128], identf[0:64, 0:64]),
                    wait=(dt_done + [sV.at()]) if r == 0 else [], inc=sT.bump())
            ops.add("vector", lambda e: e.tensor_copy(
                bass.AP(dtm, 0, [[4 * PS, 128], [PS, 3], [1, PS]]),
                ps_f[:, 0:192]), wait=[sT.at()], inc=sV.bump())
            ops.add("scalar", lambda e: e.activation(
                dtm[:, 3, :], dtm[:, 1, :], AF.Exp), wait=[sV.at(), sA.at()],
                inc=sA.bump())
            ops.add("vector", lambda e: e.tensor_mul(
                dtm[:, 2, :], dtm[:, 0, :], dtm[:, 2, :]),
                wait=[sA.at(), sV.at()], inc=sV.bump())
            sub_cols = [sV.at(), sA.at()]

            ops.add("tensor", lambda e, sub=sub: e.transpose(
                ps_t[:, 1, 0:128], xbc[:, 32, sub * 128:(sub + 1) * 128],
                b.ident[:, :]), wait=xbc_done + [sV.at()], inc=sT.bump())
            ops.add("vector", lambda e: e.tensor_copy(
                btm[:, :], ps_t[:, 1, 0:128]), wait=[sT.at()], inc=sV.bump())
            btm_done = sV.at()
            if pass_b:
                ops.add("tensor", lambda e, sub=sub: e.matmul(
                    ps_f2[:, 192:320], xbc[:, 32, sub * 128:(sub + 1) * 128],
                    xbc[:, 33, sub * 128:(sub + 1) * 128],
                    start=True, stop=True), wait=[sV.at()], inc=sT.bump())
                ops.add("vector", lambda e: e.tensor_tensor(
                    gmsk[:, :], ps_f2[:, 192:320], tri_sb[:, :], ALU.mult),
                    wait=[sT.at()], inc=sV.bump())
            ops.add("scalar", lambda e, sub=sub: e.activation(
                c[0:64, 176:177], dtf[0:64, 2, sub * 128 + 127:sub * 128 + 128],
                AF.Exp), wait=sub_cols, inc=sA.bump())
            ops.add("tensor", lambda e: e.transpose(
                ps_f[0:1, 320:384], c[0:64, 176:177], identf[0:64, 0:64]),
                wait=[sA.at(), sV.at()], inc=sT.bump())
            ops.add("vector", lambda e: e.tensor_copy(
                dcol[0:1, 1, :], ps_f[0:1, 320:384]), wait=[sT.at()], inc=sV.bump())
            ops.add("tensor", lambda e: e.matmul(
                ps_f[:, 384:448], ones1[0:1, :], dcol[0:1, 1, 0:64],
                start=True, stop=True), wait=[sV.at()], inc=sT.bump())
            ops.add("vector", lambda e: e.tensor_copy(
                dcol[:, 1, :], ps_f[:, 384:448]), wait=[sT.at()], inc=sV.bump())
            dec_done = sV.at()

            for h in range(64):
                hp = h % 2
                hb = h % 2
                if pass_b and hb == 0:
                    ops.add("vector", lambda e, h=h: e.tensor_copy(
                        bch[:, :, :],
                        bass.AP(dtm, PS + h, [[4 * PS, 128], [1, 2], [0, 128]])),
                        wait=[sV.at(), sT.at()], inc=sV.bump())
                hb0 = 64 * (h % 2)
                ops.add("tensor", lambda e, h=h, sub=sub, hp=hp, hb0=hb0: e.transpose(
                    ps_t[:, hp, 256:320], xbc[hb0:hb0 + 64,
                                              h // 2,
                                              sub * 128:(sub + 1) * 128],
                    b.ident[hb0:hb0 + 64, hb0:hb0 + 64]),
                    wait=[sV.at()], inc=sT.bump())
                trx = sT.at()
                ops.add("vector", lambda e, hp=hp: e.tensor_copy(
                    u3[:, hp, 0, :], ps_t[:, hp, 256:320]), wait=[trx], inc=sV.bump())
                ops.add("vector", lambda e, hp=hp, h=h: e.tensor_scalar(
                    u3[:, hp, 1, :], u3[:, hp, 0, :], dtm[:, 0, h:h + 1], None,
                    ALU.mult), wait=[sV.at()], inc=sV.bump())
                ops.add("vector", lambda e, hp=hp, h=h: e.tensor_scalar(
                    u3[:, hp, 2, :], u3[:, hp, 0, :], dtm[:, 2, h:h + 1], None,
                    ALU.mult), wait=[sV.at()], inc=sV.bump())
                u_done = sV.at()
                if pass_b:
                    psM = ps_f if hp == 0 else ps_f2
                    ops.add("tensor", lambda e, hb=hb, hp=hp, psM=psM: e.transpose(
                        psM[:, 0:128], bch[:, hb, :], identf[:, :]),
                        wait=[sV.at()], inc=sT.bump())
                    ops.add("vector", lambda e, h=h, hp=hp, psM=psM: e.tensor_scalar(
                        marg[:, :], psM[:, 0:128], dtm[:, 1, h:h + 1],
                        0.0, ALU.subtract, ALU.min),
                        wait=[sT.at(), sA.at()], inc=sV.bump())
                    ops.add("scalar", lambda e, hp=hp: e.activation(
                        sph[:, hp, :], marg[:, :], AF.Exp),
                        wait=[sV.at(), sA.at()], inc=sA.bump())
                    ops.add("vector", lambda e, hp=hp: e.tensor_tensor(
                        sph[:, hp, :], sph[:, hp, :], gmsk[:, :], ALU.mult),
                        wait=[sA.at()], inc=sV.bump())
                    sp_done = sV.at()
                    yb = 1024 + hp * 512
                    ops.add("tensor", lambda e, hp=hp, yb=yb: e.matmul(
                        A4[:, yb:yb + 64], sph[:, hp, :], u3[:, hp, 1, :],
                        start=True, stop=True), wait=[sp_done, u_done],
                        inc=sT.bump())
                    ops.add("tensor", lambda e, h=h, sub=sub, yb=yb: e.matmul(
                        A4[:, yb + 64:yb + 128],
                        xbc[:, 33, sub * 128:(sub + 1) * 128], st[:, h, :],
                        start=True, stop=True), inc=sT.bump())
                    ymm = sT.at()
                    ops.add("vector", lambda e, hp=hp, h=h, yb=yb: e.scalar_tensor_tensor(
                        ysc[:, hp, 0:PS], u3[:, hp, 0, :], dcol[:, 0, h:h + 1],
                        A4[:, yb:yb + 64], ALU.mult, ALU.add),
                        wait=[ymm, sV.at()], inc=sV.bump())
                    ops.add("vector", lambda e, hp=hp, h=h, yb=yb: e.scalar_tensor_tensor(
                        ysc[:, hp, PS:2 * PS], A4[:, yb + 64:yb + 128],
                        dtm[:, 3, h:h + 1], ysc[:, hp, 0:PS],
                        ALU.mult, ALU.add), wait=[sV.at()], inc=sV.bump())
                    ops.add("vector", lambda e, sub=sub, h=h, hp=hp: e.tensor_tensor(
                        zg[:, sub, h * PS:(h + 1) * PS], ysc[:, hp, PS:2 * PS],
                        zg[:, sub, h * PS:(h + 1) * PS], ALU.mult),
                        wait=([sz_done] if h == 0 else []) + [sV.at()], inc=sV.bump())
                cb0 = hp * 512
                ops.add("tensor", lambda e, hp=hp, cb0=cb0: e.matmul(
                    A4[:, cb0:cb0 + 64], btm[:, :], u3[:, hp, 2, :],
                    start=True, stop=True), wait=[u_done, btm_done], inc=sT.bump())
                cmm = sT.at()
                ops.add("vector", lambda e, h=h, cb0=cb0: e.scalar_tensor_tensor(
                    st[:, h, :], st[:, h, :], dcol[:, 1, h:h + 1],
                    A4[:, cb0:cb0 + 64], ALU.mult, ALU.add),
                    wait=[cmm, dec_done], inc=sV.bump())
            sub_end = [sV.at(), sA.at(), sT.at()]

        if pass_b:
            for sub in range(2):
                for hh in range(2):
                    ops.add("scalar", lambda e, sub=sub, hh=hh: e.activation(
                        sq_scr, zg[:, sub, hh * DM:(hh + 1) * DM],
                        AF.Square, accum_out=c[:, 178 + hh:179 + hh]),
                        wait=sub_end + [sA.at()], inc=sA.bump())
                ops.add("vector", lambda e: e.tensor_add(
                    c[:, 180:181], c[:, 178:179], c[:, 179:180]),
                    wait=[sA.at(), sV.at()], inc=sV.bump())
                ops.add("vector", lambda e: e.tensor_scalar(
                    c[:, 180:181], c[:, 180:181], 1.0 / DSSM, SSM_EPS,
                    ALU.mult, ALU.add), wait=[sV.at()], inc=sV.bump())
                ops.add("scalar", lambda e: e.activation(
                    c[:, 181:182], c[:, 180:181], AF.Sqrt),
                    wait=[sV.at()], inc=sA.bump())
                ops.add("vector", lambda e, sub=sub: e.reciprocal(
                    c[:, 182 + sub:183 + sub], c[:, 181:182]),
                    wait=[sA.at()], inc=sV.bump())
        rms_done = [sV.at(), sA.at()]

        if pass_b:
            for tc in range(2):
                ops.add("sync", lambda e, tc=tc: e.dma_start_transpose(
                    bass.AP(xnT, 0, [[16 * 256, 128], [128, 32], [1, 128]]),
                    bass.AP(zg, tc * DSSM, [[2 * DSSM, 128], [1, DSSM]])),
                    wait=[sV.at(), sT.at()], inc=sGt.bump(16))
                gt_ld = sGt.at()
                for g in range(32):
                    qm = {}
                    for q in range(2):
                        ops.add("sync", lambda e, g=g, q=q: e.dma_start(
                            out=wo_sb[:, q % 2, :],
                            in_=b.hb_wout[g * 128:(g + 1) * 128,
                                          q * 512:(q + 1) * 512]),
                            wait=[sT.at()], inc=sWo[q].bump(16))
                        qm[q] = sWo[q].at()
                    for ds in range(4):
                        wt = [qm[ds]]
                        if ds == 0:
                            wt += [gt_ld]
                            if g == 0:
                                wt += [sV.at()]
                        ops.add("tensor", lambda e, ds=ds, g=g: e.matmul(
                            A4[:, ds * 512:(ds + 1) * 512],
                            bass.AP(xnT, g * 128, [[16 * 256, 128], [1, 128]]),
                            wo_sb[:, ds % 2, :], start=(g == 0), stop=(g == 31)),
                            wait=wt, inc=sT.bump())
                        if ds < 2:
                            ops.add("sync", lambda e, g=g, q=ds + 2: e.dma_start(
                                out=wo_sb[:, q % 2, :],
                                in_=b.hb_wout[g * 128:(g + 1) * 128,
                                              q * 512:(q + 1) * 512]),
                                wait=[sT.at()], inc=sWo[ds + 2].bump(16))
                            qm[ds + 2] = sWo[ds + 2].at()
                wmm = sT.at()
                for ds in range(4):
                    ops.add("vector", lambda e, tc=tc, ct=ct, ds=ds: e.scalar_tensor_tensor(
                        b.resid[:, 2 * ct + tc, ds * 512:(ds + 1) * 512],
                        A4[:, ds * 512:(ds + 1) * 512], c[:, 182 + tc:183 + tc],
                        b.resid[:, 2 * ct + tc, ds * 512:(ds + 1) * 512],
                        ALU.mult, ALU.add),
                        wait=([wmm] + rms_done) if ds == 0 else [wmm],
                        inc=sV.bump())

    ops.add("vector", lambda e: e.memset(st[:, :, :], 0.0),
            wait=setup2 + tail_done, inc=sV.bump())
    for ct in range(NCH):
        chunk(ct, pass_b=False)
    passA_done = [sV.at(), sA.at(), sT.at()]

    ops.add("gpsimd", lambda e: e.dma_start(
        out=b.cc_st_in[:, :],
        in_=bass.AP(st, 0, [[64 * PS, 128], [1, 64 * PS]])),
        wait=passA_done, inc=sCs.bump(16))
    ops.add("gpsimd", lambda e: e.collective_compute(
        "AllGather", ALU.bypass,
        replica_groups=[[0, 1], [2, 3], [4, 5], [6, 7]],
        ins=[b.cc_st_in.ap().opt()], outs=[b.cc_st_out.ap().opt()]),
        wait=[sCs.at()], inc=sCsC.bump(1))
    ops.add("gpsimd", lambda e: e.dma_start(
        out=bass.AP(st, 0, [[64 * PS, 128], [1, 64 * PS]]),
        in_=b.cc_st_out[0:128, :]), wait=[sCsC.at()], inc=sCs.bump(16))
    ops.add("vector", lambda e: e.tensor_scalar(
        bass.AP(st, 0, [[64 * PS, 128], [1, 64 * PS]]),
        bass.AP(st, 0, [[64 * PS, 128], [1, 64 * PS]]),
        gcol[:, 0:1], None, ALU.mult), wait=[sCs.at(), sV.at()], inc=sV.bump())

    for ct in range(NCH):
        chunk(ct, pass_b=True)

    b.resid_ready = [sV.at()]
    ph.close()


def _emit_output(b):
    ops = b.ops
    sSt = b.sem("out_st")
    rw = b.resid_ready
    for qc in range(NQC):
        ops.add("sync", lambda e, qc=qc: e.dma_start(
            out=b.out_x[qc * 128:(qc + 1) * 128, :], in_=b.resid[:, qc, :]),
            wait=rw, inc=sSt.bump(16))
    ops.barrier("sync", [sSt.at()])



def _stage_inputs(inputs, stages=ALL_STAGES, x_override=None):
    x_full = np.ascontiguousarray(x_override if x_override is not None
                                  else inputs["L0_attn_a_x"])
    conv_w = np.ascontiguousarray(inputs["L0_attn_b_conv_w"])
    conv_w_fm = np.zeros((128, 34 * 4), dtype=np.float32)
    conv_b_fm = np.zeros((128, 34), dtype=np.float32)
    cw = conv_w.reshape(34, 128, 4)
    cb = inputs["L0_attn_b_conv_b"].reshape(34, 128)
    for ch in range(34):
        conv_w_fm[:, ch * 4:(ch + 1) * 4] = cw[ch]
        conv_b_fm[:, ch] = cb[ch]
    ssmv = np.zeros((HS, 4), dtype=np.float32)
    ssmv[:, 0] = inputs["L0_attn_b_A_log"]
    ssmv[:, 1] = inputs["L0_attn_b_D"]
    ssmv[:, 2] = inputs["L0_attn_b_dt_bias"]
    ssmrow = np.ascontiguousarray(ssmv[:, 0:3].T.reshape(1, 192))
    norm_w_fm = np.ascontiguousarray(
        inputs["L0_attn_b_norm_w"].reshape(32, 128).T)
    mup = inputs["L0_attn_b_mup_vector"]
    mup_fm = np.zeros((128, 67), dtype=np.float32)
    mup_pad = np.zeros(67 * 128, dtype=np.float32)
    mup_pad[:PROJ] = mup
    mup_fm[:, :] = mup_pad.reshape(67, 128).T

    bqkv = np.stack([inputs["L0_attn_a_b_Q"], inputs["L0_attn_a_b_K"],
                     inputs["L0_attn_a_b_V"]], axis=-1).transpose(1, 0, 2)
    bqkv = np.ascontiguousarray(bqkv.reshape(EH, 3 * NH).astype(np.float32))
    maps = []
    for c in range(8):
        batch, half = c // 2, c % 2
        mtiles = np.zeros((128, 6, 512), dtype=np.float32)
        mtiles[:, 0, :] = 0.0 if half == 1 else -1e5
        jj = np.arange(512)[None, :]
        ii = np.arange(128)[:, None]
        for r in range(4):
            mtiles[:, 2 + r, :] = np.where(jj > ii + 128 * r, -1e5, 0.0)
        mtiles = np.ascontiguousarray(mtiles.reshape(128, 6 * 512))
        qpos = np.zeros((128, NQC), dtype=np.int32)
        for qc in range(NQC):
            qpos[:, qc] = 2048 * half + 128 * qc + np.arange(128)
        gate = np.array([[1.0 if half == 1 else 0.0, float(half), 0.0, 0.0]],
                        dtype=np.float32)
        if "attn" in stages:
            xb = x_full[batch]
            own = xb[half * HALF:(half + 1) * HALF]
            other = xb[(1 - half) * HALF:(2 - half) * HALF]
            xs = np.concatenate([other, own], axis=0)
        else:
            xs = x_full[batch][half * HALF:(half + 1) * HALF]
            xs = np.concatenate([xs, np.zeros((LSEQ - HALF, DM), np.float32)], axis=0)
        m = {
            "x": np.ascontiguousarray(xs),
            "w_q": inputs["L0_attn_a_W_Q"], "w_k": inputs["L0_attn_a_W_K"],
            "w_v": inputs["L0_attn_a_W_V"], "w_o": inputs["L0_attn_a_W_O"],
            "b_q": inputs["L0_attn_a_b_Q"], "b_k": inputs["L0_attn_a_b_K"],
            "b_v": inputs["L0_attn_a_b_V"],
            "b_o": inputs["L0_attn_a_b_O"].reshape(1, DM),
            "w_gu": inputs["L0_ffn_w_gate_up"], "w_p": inputs["L0_ffn_w_proj"],
            "w_in": inputs["L0_attn_b_W_in"], "w_out": inputs["L0_attn_b_W_out"],
            "conv_w_fm": conv_w_fm, "conv_b_fm": conv_b_fm, "ssm_vec": ssmv,
            "ssm_row": ssmrow,
            "norm_w_fm": norm_w_fm, "mup_fm": mup_fm,
            "qpos": qpos, "gate": gate, "bqkv": bqkv, "mtiles": mtiles,
        }
        maps.append({k: np.ascontiguousarray(v) for k, v in m.items()})
    return maps


def run_device(inputs, stages=ALL_STAGES, x_override=None):
    b = build_program(stages)
    in_maps = _stage_inputs(inputs, stages, x_override)
    res = run_bass_kernel_spmd(b.nc, in_maps, core_ids=list(range(8)))
    out = np.zeros((NB, LSEQ, DM), dtype=np.float32)
    for c in range(8):
        batch, half = c // 2, c % 2
        out[batch, half * HALF:(half + 1) * HALF] = res.results[c]["xo"]
    return out


def kernel(**inputs):
    return run_device(inputs, ALL_STAGES)


# version 10 (final)
# speedup vs baseline: 1.0838x; 1.0379x over previous
import contextlib
import os
import numpy as np
import concourse.bass as bass
import concourse.mybir as mybir
from concourse.bass_utils import run_bass_kernel_spmd

F32 = mybir.dt.float32
BF16 = mybir.dt.bfloat16
I32 = mybir.dt.int32
AF = mybir.ActivationFunctionType
ALU = mybir.AluOpType
AX = mybir.AxisListType

NB, LSEQ, DM = 4, 4096, 2048
HALF, NQC = 2048, 16
NH, EH = 16, 128
FH = 8192
PROJ, DSSM, CONVD, HS, PS, SS = 8512, 4096, 4352, 64, 64, 128
LN_EPS = 1e-5
RMS_EPS = float(np.finfo(np.float32).eps)
SSM_EPS = 1e-5
INV_SQRT_EH = 1.0 / float(np.sqrt(128.0))
NEG_BIG = -100000.0
ALL_STAGES = ("attn", "ffn1", "ssm", "ffn2")
DEV_SKIP_QUANT = bool(int(os.environ.get("KDEV_SKIP_QUANT", "0")))
DEV_FFN_TILES = int(os.environ.get("KDEV_FFN_TILES", "8"))


class Sem:
    def __init__(self, nc, es, name):
        self.h = es.enter_context(nc.semaphore(name))
        self.n = 0
        self.name = name

    def bump(self, k=1):
        self.n += k
        return (self, k)

    def at(self):
        return (self, self.n)


class Ops:
    ENGS = ("sync", "tensor", "vector", "scalar", "gpsimd")

    def __init__(self):
        self.q = {e: [] for e in self.ENGS}

    def add(self, eng, fn, wait=(), inc=None):
        wait = tuple((s, v) for (s, v) in wait if v > 0)

        def run(e, fn=fn, wait=wait, inc=inc):
            for s, v in wait:
                e.wait_ge(s.h, v)
            r = fn(e)
            if inc is not None:
                r.then_inc(inc[0].h, inc[1])
        self.q[eng].append(run)

    def barrier(self, eng, waits):
        self.add(eng, lambda e: e.nop(), wait=waits)


class B:
    def __init__(self, stages=ALL_STAGES):
        self.stages = stages
        self.nc = bass.Bass("TRN2", target_bir_lowering=False)
        self.es = contextlib.ExitStack()
        self.ops = Ops()
        self.sems = {}

    def sem(self, name):
        if name not in self.sems:
            self.sems[name] = Sem(self.nc, self.es, name)
        return self.sems[name]

    def sb(self, name, shape, dtype, es=None):
        return (es or self.es).enter_context(self.nc.sbuf_tensor(name, shape, dtype))

    def ps(self, name, shape, dtype=F32, es=None):
        return (es or self.es).enter_context(self.nc.psum_tensor(name, shape, dtype))



def build_program(stages=ALL_STAGES):
    b = B(stages)
    nc = b.nc

    b.in_x = nc.dram_tensor("x", [LSEQ, DM], F32, kind="ExternalInput")
    b.in_wq = nc.dram_tensor("w_q", [NH, DM, EH], F32, kind="ExternalInput")
    b.in_wk = nc.dram_tensor("w_k", [NH, DM, EH], F32, kind="ExternalInput")
    b.in_wv = nc.dram_tensor("w_v", [NH, DM, EH], F32, kind="ExternalInput")
    b.in_wo = nc.dram_tensor("w_o", [NH, EH, DM], F32, kind="ExternalInput")
    b.in_bq = nc.dram_tensor("b_q", [NH, EH], F32, kind="ExternalInput")
    b.in_bk = nc.dram_tensor("b_k", [NH, EH], F32, kind="ExternalInput")
    b.in_bv = nc.dram_tensor("b_v", [NH, EH], F32, kind="ExternalInput")
    b.in_bo = nc.dram_tensor("b_o", [1, DM], F32, kind="ExternalInput")
    b.in_wgu = nc.dram_tensor("w_gu", [2 * FH, DM], F32, kind="ExternalInput")
    b.in_wp = nc.dram_tensor("w_p", [DM, FH], F32, kind="ExternalInput")
    b.in_win = nc.dram_tensor("w_in", [DM, PROJ], F32, kind="ExternalInput")
    b.in_wout = nc.dram_tensor("w_out", [DSSM, DM], F32, kind="ExternalInput")
    b.in_convw = nc.dram_tensor("conv_w_fm", [128, 34 * 4], F32, kind="ExternalInput")
    b.in_convb = nc.dram_tensor("conv_b_fm", [128, 34], F32, kind="ExternalInput")
    b.in_ssmv = nc.dram_tensor("ssm_vec", [HS, 4], F32, kind="ExternalInput")
    b.in_ssmrow = nc.dram_tensor("ssm_row", [1, 192], F32, kind="ExternalInput")
    b.in_normw = nc.dram_tensor("norm_w_fm", [128, 32], F32, kind="ExternalInput")
    b.in_mup = nc.dram_tensor("mup_fm", [128, 67], F32, kind="ExternalInput")
    b.in_qpos = nc.dram_tensor("qpos", [128, NQC], I32, kind="ExternalInput")
    b.in_bqkv = nc.dram_tensor("bqkv", [128, 3 * NH], F32, kind="ExternalInput")
    b.in_mtiles = nc.dram_tensor("mtiles", [128, 6 * 512], F32, kind="ExternalInput")
    b.in_gate = nc.dram_tensor("gate", [1, 4], F32, kind="ExternalInput")

    b.out_x = nc.dram_tensor("xo", [HALF, DM], F32, kind="ExternalOutput")

    b.hb_wguT = nc.dram_tensor("hb_wguT", [DM, 2 * FH], BF16, kind="Internal")
    b.hb_wpT = nc.dram_tensor("hb_wpT", [FH, DM], BF16, kind="Internal")
    b.hb_win = nc.dram_tensor("hb_win", [DM, PROJ], BF16, kind="Internal")
    b.hb_wout = nc.dram_tensor("hb_wout", [DSSM, DM], BF16, kind="Internal")
    b.hb_xn = nc.dram_tensor("hb_xn", [LSEQ, DM], BF16, kind="Internal")
    b.hb_sz = nc.dram_tensor("hb_sz", [8 * 128, 2 * DSSM], BF16, kind="Internal")
    b.hb_xbc = nc.dram_tensor("hb_xbc", [8 * 128, 34 * 256], BF16, kind="Internal")
    b.hb_dtf = nc.dram_tensor("hb_dtf", [8 * 128, 3 * 256], F32, kind="Internal")
    b.hb_wq = nc.dram_tensor("hb_wq", [NH * DM, EH], BF16, kind="Internal")
    b.hb_wk = nc.dram_tensor("hb_wk", [NH * DM, EH], BF16, kind="Internal")
    b.hb_wv = nc.dram_tensor("hb_wv", [NH * DM, EH], BF16, kind="Internal")
    b.cc_tail_in = nc.dram_tensor("cc_tail_in", [3, DM], F32, kind="Internal")
    b.cc_tail_out = nc.dram_tensor("cc_tail_out", [6, DM], F32, kind="Internal")
    b.cc_st_in = nc.dram_tensor("cc_st_in", [128, 64 * PS], BF16, kind="Internal")
    b.cc_st_out = nc.dram_tensor("cc_st_out", [256, 64 * PS], BF16, kind="Internal")

    b.ident = b.sb("g_ident", [128, 128], BF16)
    b.colA = b.sb("g_colA", [128, 192], F32)
    b.gate_sb = b.sb("g_gate", [1, 4], F32)

    _emit_setup(b)
    if DEV_SKIP_QUANT:
        b.quant_done = list(b.setup_done)
    else:
        _emit_quant(b)
    if "attn" in stages:
        _emit_attn(b)
    else:
        b.resid = b.sb("g_resid", [128, NQC, DM], F32)
        _emit_resid_init(b)
    for st in stages:
        if st == "ffn1" or st == "ffn2":
            _emit_ffn(b, st)
        elif st == "ssm":
            _emit_ssm(b)
    _emit_output(b)

    with b.es:
        with nc.Block() as block:
            @block.sync
            def _(e):
                for fn in b.ops.q["sync"]:
                    fn(e)

            @block.tensor
            def _(e):
                for fn in b.ops.q["tensor"]:
                    fn(e)

            @block.vector
            def _(e):
                for fn in b.ops.q["vector"]:
                    fn(e)

            @block.scalar
            def _(e):
                for fn in b.ops.q["scalar"]:
                    fn(e)

            @block.gpsimd
            def _(e):
                for fn in b.ops.q["gpsimd"]:
                    fn(e)
    return b



def _emit_setup(b):
    ops = b.ops
    sG = b.sem("su_g")
    sV = b.sem("su_v")
    sLd = b.sem("su_ld")
    ph = contextlib.ExitStack()
    iota_r = b.sb("su_ir", [128, 128], F32, ph)
    iota_c = b.sb("su_ic", [128, 128], F32, ph)
    ops.add("gpsimd", lambda e: e.iota(iota_r[:, :], [[1, 128]], channel_multiplier=0,
                                       allow_small_or_imprecise_dtypes=True))
    ops.add("gpsimd", lambda e: e.iota(iota_c[:, :], [[0, 128]], channel_multiplier=1,
                                       allow_small_or_imprecise_dtypes=True),
            inc=sG.bump())
    ops.add("vector", lambda e: e.tensor_tensor(b.ident[:, :], iota_r[:, :],
                                                iota_c[:, :], ALU.is_equal),
            wait=[sG.at()], inc=sV.bump())
    ops.add("sync", lambda e: e.dma_start(out=b.gate_sb[:, :], in_=b.in_gate[:, :]),
            inc=sLd.bump(16))
    b.setup_done = [sV.at(), sLd.at()]
    ph.close()



def _emit_quant(b):
    ops = b.ops
    sV = b.sem("q_v")
    sA = b.sem("q_a")
    sT = b.sem("q_t")
    sLd = [b.sem("q_ld0"), b.sem("q_ld1")]
    sSt = [b.sem("q_st0"), b.sem("q_st1")]
    sCb = [b.sem("q_cb0"), b.sem("q_cb1")]

    ph = contextlib.ExitStack()
    wbuf = [b.sb(f"q_w{i}", [128, FH], BF16, ph) for i in range(2)]
    trs = [b.sb(f"q_tr{i}", [128, FH], BF16, ph) for i in range(2)]
    fbuf = [b.sb(f"q_f{i}", [128, PROJ], F32, ph) for i in range(2)]
    absf = b.sb("q_abs", [128, FH], BF16, ph)
    sums = b.sb("q_sums", [128, 128], F32, ph)
    scal = b.sb("q_scal", [128, 128], F32, ph)
    scab = b.sb("q_scab", [128, 128], BF16, ph)
    rcp = b.sb("q_rcp", [128, 128], F32, ph)
    rb = b.sb("q_rb", [128, FH], BF16, ph)
    pst = b.ps("q_pst", [128, 512], BF16, ph)

    sNw = b.sem("q_nw")
    nwq = b.sb("q_nw", [128, 32], F32, ph)
    ops.add("sync", lambda e: e.dma_start(out=nwq[:, :], in_=b.in_normw[:, :]),
            inc=sNw.bump(16))
    war_w = [None, None]
    war_tr = [None, None]
    war_abs = None
    war_f = [None, None]
    sTt = [b.sem("q_tt0"), b.sem("q_tt1")]

    jobs = [
        (b.in_wgu, DM, 2 * FH // 128, b.hb_wguT, 2 * FH),
        (b.in_wp, FH, DM // 128, b.hb_wpT, DM),
    ]
    gi = 0
    for src_t, C, nch, dst, dstw in jobs:
        G = C // 64
        nj = C // 128
        for i in range(nch):
            pb = gi % 2
            gi += 1
            wb_t, tr_t = wbuf[pb], trs[pb]
            w = [b.setup_done[0]] if gi == 1 else []
            if war_f[pb] is not None:
                w.append(war_f[pb])
            ops.add("sync", lambda e, t=fbuf[pb], s=src_t, i=i, C=C: e.dma_start(
                out=t[:, 0:C], in_=s[i * 128:(i + 1) * 128, :]),
                wait=w, inc=sLd[pb].bump(16))
            ld = sLd[pb].at()
            aw = [ld]
            if war_abs is not None:
                aw.append(war_abs)
            ops.add("scalar", lambda e, f=fbuf[pb], C=C: e.activation(
                absf[:, 0:C], f[:, 0:C], AF.Abs), wait=aw, inc=sA.bump())
            cw = [ld, sV.at()]
            if war_w[pb] is not None:
                cw.append(war_w[pb])
            ops.add("vector", lambda e, t=wb_t, f=fbuf[pb], C=C: e.tensor_copy(
                t[:, 0:C], f[:, 0:C]), wait=cw, inc=sV.bump())
            war_f[pb] = sV.at()
            ops.add("vector", lambda e, C=C, G=G: e.tensor_reduce(
                sums[:, 0:G].rearrange("p (g o) -> p g o", o=1),
                absf[:, 0:C].rearrange("p (g o) -> p g o", o=64),
                AX.X, ALU.add), wait=[sA.at(), sV.at()], inc=sV.bump())
            war_abs = sV.at()
            ops.add("vector", lambda e, G=G: e.tensor_scalar(
                scal[:, 0:G], sums[:, 0:G], 1.0 / 64.0, 1e-8, ALU.mult, ALU.max),
                wait=[sV.at()], inc=sV.bump())
            ops.add("vector", lambda e, G=G: e.tensor_copy(
                scab[:, 0:G], scal[:, 0:G]), wait=[sV.at()], inc=sV.bump())
            ops.add("vector", lambda e, G=G: e.reciprocal(
                rcp[:, 0:G], scab[:, 0:G]), wait=[sV.at()], inc=sV.bump())

            def _r(e, t=wb_t, C=C, G=G):
                bcast = bass.AP(rcp, 0, [[128, 128], [1, G], [0, 64]])
                return e.tensor_tensor(
                    rb[:, 0:C].rearrange("p (g o) -> p g o", o=64),
                    t[:, 0:C].rearrange("p (g o) -> p g o", o=64), bcast, ALU.mult)
            ops.add("vector", _r, wait=[sV.at()], inc=sV.bump())
            ops.add("vector", lambda e, C=C: e.tensor_scalar(
                rb[:, 0:C], rb[:, 0:C], 192.0, None, ALU.add), wait=[sV.at()], inc=sV.bump())
            ops.add("vector", lambda e, C=C: e.tensor_scalar(
                rb[:, 0:C], rb[:, 0:C], -192.0, None, ALU.add), wait=[sV.at()], inc=sV.bump())
            ops.add("vector", lambda e, C=C: e.tensor_scalar(
                rb[:, 0:C], rb[:, 0:C], 1.0, -1.0, ALU.min, ALU.max), wait=[sV.at()], inc=sV.bump())

            def _wh(e, t=wb_t, C=C, G=G):
                bcast = bass.AP(scab, 0, [[128, 128], [1, G], [0, 64]])
                return e.tensor_tensor(
                    t[:, 0:C].rearrange("p (g o) -> p g o", o=64),
                    rb[:, 0:C].rearrange("p (g o) -> p g o", o=64), bcast, ALU.mult)
            ops.add("vector", _wh, wait=[sV.at()], inc=sV.bump())
            wh_done = sV.at()

            trw = [wh_done]
            if war_tr[pb] is not None:
                trw.append(war_tr[pb])
            ops.add("sync", lambda e, t=wb_t, tr_t=tr_t, C=C, nj=nj: e.dma_start_transpose(
                bass.AP(tr_t, 0, [[FH, 128], [128, nj], [1, 128]]),
                t[:, 0:C]), wait=trw, inc=sTt[pb].bump(16))
            war_w[pb] = sTt[pb].at()

            def _st(e, t=tr_t, dst=dst, dstw=dstw, i=i, nj=nj):
                out_ap = bass.AP(dst, i * 128, [[dstw, 128], [128 * dstw, nj], [1, 128]])
                in_ap = bass.AP(t, 0, [[FH, 128], [128, nj], [1, 128]])
                return e.dma_start(out=out_ap, in_=in_ap)
            ops.add("gpsimd", _st, wait=[war_w[pb]], inc=sSt[pb].bump(16))
            war_tr[pb] = sSt[pb].at()

    cbuf = [b.sb(f"q_cb{i}", [128, PROJ], BF16, ph) for i in range(2)]
    first_c = [True, True]
    flat_jobs = []
    for wsrc, wdst in [(b.in_wq, b.hb_wq), (b.in_wk, b.hb_wk), (b.in_wv, b.hb_wv)]:
        flat_jobs.append((bass.AP(wsrc, 0, [[4096, 1024], [1, 4096]]),
                          bass.AP(wdst, 0, [[4096, 1024], [1, 4096]])))
    sCv = b.sem("q_cv")
    war_cb = [None, None]

    def _flat(load_fn, store_fn, C, pb, nw_i=None):
        w = [war_f[pb]] if war_f[pb] is not None else []
        ops.add("sync", load_fn, wait=w, inc=sLd[pb].bump(16))
        cw = [sLd[pb].at(), sV.at()]
        if war_cb[pb] is not None:
            cw.append(war_cb[pb])
        ops.add("vector", lambda e, t=cbuf[pb], f=fbuf[pb], C=C: e.tensor_copy(
            t[:, 0:C], f[:, 0:C]), wait=cw, inc=sV.bump())
        war_f[pb] = sV.at()
        if nw_i is not None:
            ops.add("vector", lambda e, t=cbuf[pb], i=nw_i, C=C: e.tensor_scalar(
                t[:, 0:C], t[:, 0:C], nwq[:, i:i + 1], None, ALU.mult),
                wait=[sNw.at(), sV.at()], inc=sV.bump())
        st_w = [sV.at()]
        ops.add("gpsimd", store_fn, wait=st_w, inc=sCb[pb].bump(16))
        war_cb[pb] = sCb[pb].at()

    for fsrc, fdst in flat_jobs:
        for i in range(8):
            pb = gi % 2
            gi += 1
            _flat(
                lambda e, f=None, fsrc=fsrc, i=i, pb=pb: e.dma_start(
                    out=fbuf[pb][:, 0:4096],
                    in_=bass.AP(fsrc.tensor, i * 128 * 4096, [[4096, 128], [1, 4096]])),
                lambda e, t=None, fdst=fdst, i=i, pb=pb: e.dma_start(
                    out=bass.AP(fdst.tensor, i * 128 * 4096, [[4096, 128], [1, 4096]]),
                    in_=cbuf[pb][:, 0:4096]),
                4096, pb)
    for src_t, nch, C, dst in [(b.in_win, DM // 128, PROJ, b.hb_win),
                               (b.in_wout, DSSM // 128, DM, b.hb_wout)]:
        scale_nw = dst is b.hb_wout
        for i in range(nch):
            pb = gi % 2
            gi += 1
            _flat(
                lambda e, s=src_t, i=i, C=C, pb=pb: e.dma_start(
                    out=fbuf[pb][:, 0:C], in_=s[i * 128:(i + 1) * 128, :]),
                lambda e, d=dst, i=i, C=C, pb=pb: e.dma_start(
                    out=d[i * 128:(i + 1) * 128, :], in_=cbuf[pb][:, 0:C]),
                C, pb, nw_i=(i if scale_nw else None))

    b.quant_done = [sSt[0].at(), sSt[1].at(), sCb[0].at(), sCb[1].at(),
                    sV.at(), sA.at(), sT.at()]
    ph.close()



def _emit_resid_init(b):
    ops = b.ops
    sLd = b.sem("ri_ld")
    qd = list(b.quant_done)
    half_rows = bass.AP(b.in_x, 0, [[1, 1]])
    for qc in range(NQC):
        ops.add("sync", lambda e, qc=qc: e.dma_start(
            out=b.resid[:, qc, :], in_=b.in_x[qc * 128:(qc + 1) * 128, :]),
            wait=qd if qc == 0 else [], inc=sLd.bump(16))
    b.resid_ready = [sLd.at()]



def _emit_ln_stats(b, tag, sq_scr):
    ops = b.ops
    c = b.colA
    sV = b.sem(f"{tag}_lv")
    sA = b.sem(f"{tag}_la")
    rw = b.resid_ready
    for qc in range(NQC):
        ops.add("scalar", lambda e, qc=qc: e.activation(
            sq_scr, b.resid[:, qc, :], AF.Square,
            accum_out=c[:, 48 + qc:49 + qc]), wait=rw + [sA.at()], inc=sA.bump())
    ops.add("vector", lambda e: e.tensor_reduce(
        c[:, 64:80].rearrange("p (g o) -> p g o", o=1),
        bass.AP(b.resid, 0, [[NQC * DM, 128], [DM, NQC], [1, DM]]),
        AX.X, ALU.add), wait=rw, inc=sV.bump())
    ops.add("vector", lambda e: e.tensor_scalar(
        c[:, 0:16], c[:, 64:80], 1.0 / DM, None, ALU.mult),
        wait=[sV.at()], inc=sV.bump())
    ops.add("vector", lambda e: e.tensor_scalar(
        c[:, 80:96], c[:, 48:64], 1.0 / DM, None, ALU.mult),
        wait=[sA.at(), sV.at()], inc=sV.bump())
    ops.add("vector", lambda e: e.scalar_tensor_tensor(
        c[:, 16:32], c[:, 0:16], 1.0, c[:, 0:16], ALU.mult, ALU.mult),
        wait=[sV.at()], inc=sV.bump())
    ops.add("vector", lambda e: e.tensor_sub(
        c[:, 80:96], c[:, 80:96], c[:, 16:32]), wait=[sV.at()], inc=sV.bump())
    ops.add("vector", lambda e: e.tensor_scalar(
        c[:, 80:96], c[:, 80:96], LN_EPS, None, ALU.add), wait=[sV.at()], inc=sV.bump())
    ops.add("scalar", lambda e: e.activation(
        c[:, 96:112], c[:, 80:96], AF.Sqrt), wait=[sV.at()], inc=sA.bump())
    ops.add("vector", lambda e: e.reciprocal(
        c[:, 16:32], c[:, 96:112]), wait=[sA.at()], inc=sV.bump())
    ops.add("vector", lambda e: e.scalar_tensor_tensor(
        c[:, 32:48], c[:, 0:16], -1.0, c[:, 16:32], ALU.mult, ALU.mult),
        wait=[sV.at()], inc=sV.bump())
    return [sV.at()]



def _emit_ffn(b, tag):
    ops = b.ops
    sW = b.sem(f"{tag}_w")
    sV = b.sem(f"{tag}_v")
    sA = b.sem(f"{tag}_a")
    sT = b.sem(f"{tag}_t")
    qd = b.quant_done

    ph = contextlib.ExitStack()
    hbuf = b.sb(f"{tag}_h", [128, 2, FH], BF16, ph)
    xnT = b.sb(f"{tag}_xnT", [128, 16, 256], BF16, ph)
    wg = [b.sb(f"{tag}_wg{i}", [128, 8, 512], BF16, ph) for i in range(2)]
    wu = [b.sb(f"{tag}_wu{i}", [128, 8, 512], BF16, ph) for i in range(2)]
    wpq = b.sb(f"{tag}_wpq", [128, 4, 512], BF16, ph)
    htf = [b.sb(f"{tag}_htf{i}", [128, 256], BF16, ph) for i in range(2)]
    sg = b.sb(f"{tag}_sg", [128, 512], BF16, ph)
    ssqp = b.sb(f"{tag}_ssqp", [128, 2, 16], F32, ph)
    inv_c = b.sb(f"{tag}_inv", [128, 8], F32, ph)
    ps_t = b.ps(f"{tag}_pst", [128, 1024], BF16, ph)
    A4 = b.ps(f"{tag}_a4", [128, 2048], F32, ph)
    c = b.colA
    NT = DEV_FFN_TILES
    NHG = FH // 512

    sq_scr = hbuf[:, 1, 0:DM]
    rw0 = list(b.resid_ready)
    for eng in ("sync", "gpsimd", "tensor", "vector", "scalar"):
        ops.barrier(eng, rw0)
    ln_done = _emit_ln_stats(b, tag, sq_scr)

    war_wg = [None, None]
    war_wp = [None, None]
    prev_tile_done = []

    sXt = b.sem(f"{tag}_xt")
    for tt in range(NT):
        for tc, qc in enumerate((2 * tt, 2 * tt + 1)):
            xv = hbuf[:, 0, 0:DM]
            ops.add("scalar", lambda e, qc=qc, xv=xv: e.activation(
                xv, b.resid[:, qc, :], AF.Identity,
                bias=c[:, 32 + qc:33 + qc], scale=c[:, 16 + qc:17 + qc]),
                wait=ln_done + [sA.at(), sT.at(), sXt.at()] + prev_tile_done,
                inc=sA.bump())
            ops.add("sync", lambda e, tc=tc, xv=xv: e.dma_start_transpose(
                bass.AP(xnT, tc * 128, [[16 * 256, 128], [256, 16], [1, 128]]),
                xv[:, :]), wait=[sA.at(), sT.at()], inc=sXt.bump(16))
        xnT_done = sXt.at()

        for hg in range(NHG):
            slotm = []
            for half_i in range(2):

                def _ldg(e, hi=half_i, hg=hg):
                    out_ap = bass.AP(wg[hi], 0, [[8 * 512, 128], [512, 8], [1, 512]])
                    in_ap = bass.AP(b.hb_wguT, hg * 512 + hi * 8 * 128 * 2 * FH,
                                    [[2 * FH, 128], [128 * 2 * FH, 8], [1, 512]])
                    return e.dma_start(out=out_ap, in_=in_ap)

                def _ldu(e, hi=half_i, hg=hg):
                    out_ap = bass.AP(wu[hi], 0, [[8 * 512, 128], [512, 8], [1, 512]])
                    in_ap = bass.AP(b.hb_wguT, FH + hg * 512 + hi * 8 * 128 * 2 * FH,
                                    [[2 * FH, 128], [128 * 2 * FH, 8], [1, 512]])
                    return e.dma_start(out=out_ap, in_=in_ap)
                w = [qd[0]] if (tt == 0 and hg == 0) else []
                if war_wg[half_i] is not None:
                    w = w + [war_wg[half_i]]
                sg_s = b.sem(f"{tag}_wg{half_i}")
                su_s = b.sem(f"{tag}_wu{half_i}")
                ops.add("sync", _ldg, wait=w, inc=sg_s.bump(16))
                ops.add("sync", _ldu, inc=su_s.bump(16))
                slotm.append((sg_s.at(), su_s.at()))

            pb = hg % 2
            g_ps = A4[:, pb * 512:(pb + 1) * 512]
            u_ps = A4[:, 1024 + pb * 512:1024 + (pb + 1) * 512]
            for tc in range(2):
                for d in range(16):
                    hi, dd = (0, d) if d < 8 else (1, d - 8)
                    wt = []
                    if d == 0:
                        wt = [xnT_done, sA.at(), sV.at()]
                    if tc == 0 and (d == 0 or d == 8):
                        wt = wt + [slotm[hi][0], slotm[hi][1]]
                    ops.add("tensor", lambda e, dd=dd, hi=hi, tc=tc, d=d, g_ps=g_ps: e.matmul(
                        g_ps, xnT[:, d, tc * 128:(tc + 1) * 128],
                        wg[hi][:, dd, :], start=(d == 0), stop=(d == 15)),
                        wait=wt)
                    ops.add("tensor", lambda e, dd=dd, hi=hi, tc=tc, d=d, u_ps=u_ps: e.matmul(
                        u_ps, xnT[:, d, tc * 128:(tc + 1) * 128],
                        wu[hi][:, dd, :], start=(d == 0), stop=(d == 15)),
                        inc=sT.bump() if d == 15 else None)
                mm_done = sT.at()
                ops.add("scalar", lambda e, g_ps=g_ps: e.activation(
                    sg[:, :], g_ps, AF.Silu),
                    wait=[mm_done, sV.at()], inc=sA.bump())
                ops.add("vector", lambda e, u_ps=u_ps, tc=tc, hg=hg: e.tensor_tensor(
                    hbuf[:, tc, hg * 512:(hg + 1) * 512], sg[:, :],
                    u_ps, ALU.mult), wait=[sA.at()], inc=sV.bump())
                ops.add("scalar", lambda e, tc=tc, hg=hg: e.activation(
                    sg[:, :], hbuf[:, tc, hg * 512:(hg + 1) * 512], AF.Square,
                    accum_out=ssqp[:, tc, hg:hg + 1]), wait=[sV.at()], inc=sA.bump())
            war_wg[0] = war_wg[1] = sT.at()
        h_done_V = sV.at()
        h_done_A = sA.at()

        ops.add("vector", lambda e: e.tensor_reduce(
            inv_c[:, 0:2].rearrange("p (g o) -> p g o", o=1),
            ssqp[:, :, :], AX.X, ALU.add), wait=[h_done_A, sV.at()], inc=sV.bump())
        ops.add("vector", lambda e: e.tensor_scalar(
            inv_c[:, 2:4], inv_c[:, 0:2], 1.0 / FH, RMS_EPS, ALU.mult, ALU.add),
            wait=[sV.at()], inc=sV.bump())
        ops.add("scalar", lambda e: e.activation(
            inv_c[:, 4:6], inv_c[:, 2:4], AF.Sqrt), wait=[sV.at()], inc=sA.bump())
        ops.add("vector", lambda e: e.reciprocal(
            inv_c[:, 6:8], inv_c[:, 4:6]), wait=[sA.at()], inc=sV.bump())
        inv_done = sV.at()

        NHC = FH // 128
        for dh in range(2):
            for hc in range(NHC):
                pb = hc % 2
                qmarks = []
                for q in range(2):
                    slot = pb * 2 + q
                    w = []
                    if tt == 0 and dh == 0 and hc == 0 and q == 0:
                        w = [qd[0]]
                    if war_wp[pb] is not None and q == 0:
                        w = w + [war_wp[pb]]
                    sq_s = b.sem(f"{tag}_wq{slot}")
                    ops.add("sync", lambda e, hc=hc, dh=dh, q=q, slot=slot: e.dma_start(
                        out=wpq[:, slot, :],
                        in_=b.hb_wpT[hc * 128:(hc + 1) * 128,
                                     dh * 1024 + q * 512:dh * 1024 + (q + 1) * 512]),
                        wait=w, inc=sq_s.bump(16))
                    qmarks.append(sq_s.at())
                for tc in range(2):
                    tw = [sV.at()]
                    if hc == 0 and dh == 0:
                        tw += [h_done_V]
                    ops.add("tensor", lambda e, tc=tc, hc=hc: e.transpose(
                        ps_t[:, tc * 512:tc * 512 + 128],
                        hbuf[:, tc, hc * 128:(hc + 1) * 128], b.ident[:, :]),
                        wait=tw, inc=sT.bump())
                    ops.add("vector", lambda e, tc=tc: e.tensor_copy(
                        htf[tc][:, 0:128], ps_t[:, tc * 512:tc * 512 + 128]),
                        wait=[sT.at()], inc=sV.bump())
                    tr_done = sV.at()
                    for q in range(2):
                        wt = [qmarks[q]] if tc == 0 else []
                        if q == 0:
                            wt += [tr_done]
                            if hc == 0 and tc == 0:
                                wt += [h_done_A, sV.at()]
                                if dh == 1 or tt > 0:
                                    wt += [prev_evac_done]
                        ops.add("tensor", lambda e, q=q, hc=hc, tc=tc,
                                slot=pb * 2 + q: e.matmul(
                            A4[:, tc * 1024 + q * 512:tc * 1024 + (q + 1) * 512],
                            htf[tc][:, 0:128], wpq[:, slot, :],
                            start=(hc == 0), stop=(hc == NHC - 1)),
                            wait=wt, inc=sT.bump() if (tc == 1 and q == 1) else None)
                war_wp[pb] = sT.at()
            mm2_done = sT.at()
            for tc in range(2):
                for q in range(2):
                    ops.add("vector", lambda e, tc=tc, tt=tt, dh=dh, q=q: e.scalar_tensor_tensor(
                        b.resid[:, 2 * tt + tc,
                                dh * 1024 + q * 512:dh * 1024 + (q + 1) * 512],
                        A4[:, tc * 1024 + q * 512:tc * 1024 + (q + 1) * 512],
                        inv_c[:, 6 + tc:7 + tc],
                        b.resid[:, 2 * tt + tc,
                                dh * 1024 + q * 512:dh * 1024 + (q + 1) * 512],
                        ALU.mult, ALU.add),
                        wait=[mm2_done, inv_done] if (tc == 0 and q == 0) else [],
                        inc=sV.bump())
            prev_evac_done = sV.at()
        prev_tile_done = [sV.at()]
    b.resid_ready = [sV.at()]
    ph.close()



def _emit_attn(b):
    ops = b.ops
    sV = b.sem("a_v")
    sA = b.sem("a_a")
    sT = b.sem("a_t")
    sG = b.sem("a_g")
    sX = [b.sem("a_x0"), b.sem("a_x1")]
    sPt = b.sem("a_pt")
    sVt = b.sem("a_vt")
    sP2 = [b.sem("a_p20"), b.sem("a_p21")]
    sI0 = [b.sem("a_i00"), b.sem("a_i01")]
    sXq = [b.sem("a_xq0"), b.sem("a_xq1")]
    sQs = [b.sem("a_qs0"), b.sem("a_qs1")]
    sWc = b.sem("a_wc")
    sO = b.sem("a_wo")
    sXn = b.sem("a_xn")
    sKp = b.sem("a_kp")
    c = b.colA

    ph1 = contextlib.ExitStack()
    xA = b.sb("a_xA", [128, 2, DM], F32, ph1)
    xnO = b.sb("a_xnO", [128, 2, DM], BF16, ph1)
    sq1 = b.sb("a_sq1", [128, DM], BF16, ph1)
    ac = b.sb("a_ac", [128, 6, 32], F32, ph1)
    NCK = LSEQ // 128
    for i in range(NCK):
        pb = i % 2
        w = (list(b.setup_done) + list(b.quant_done)) if i < 2 else [sV.at(), sA.at()]
        ops.add("sync", lambda e, i=i, pb=pb: e.dma_start(
            out=xA[:, pb, :], in_=b.in_x[i * 128:(i + 1) * 128, :]),
            wait=w, inc=sX[pb].bump(16))
        ld = sX[pb].at()
        ops.add("vector", lambda e, i=i, pb=pb: e.tensor_reduce(
            ac[:, 0, i:i + 1].rearrange("p (g o) -> p g o", o=1),
            xA[:, pb, :].rearrange("p (g o) -> p g o", o=DM), AX.X, ALU.add),
            wait=[ld], inc=sV.bump())
        ops.add("scalar", lambda e, i=i, pb=pb: e.activation(
            sq1[:, :], xA[:, pb, :], AF.Square, accum_out=ac[:, 1, i:i + 1]),
            wait=[ld, sA.at()], inc=sA.bump())
    st1 = [sV.at(), sA.at()]
    ops.add("vector", lambda e: e.tensor_scalar(
        ac[:, 2, :], ac[:, 0, :], 1.0 / DM, None, ALU.mult), wait=st1, inc=sV.bump())
    ops.add("vector", lambda e: e.tensor_scalar(
        ac[:, 4, :], ac[:, 1, :], 1.0 / DM, None, ALU.mult), wait=st1, inc=sV.bump())
    ops.add("vector", lambda e: e.scalar_tensor_tensor(
        ac[:, 5, :], ac[:, 2, :], 1.0, ac[:, 2, :], ALU.mult, ALU.mult),
        wait=[sV.at()], inc=sV.bump())
    ops.add("vector", lambda e: e.tensor_sub(
        ac[:, 4, :], ac[:, 4, :], ac[:, 5, :]), wait=[sV.at()], inc=sV.bump())
    ops.add("vector", lambda e: e.tensor_scalar(
        ac[:, 4, :], ac[:, 4, :], LN_EPS, None, ALU.add), wait=[sV.at()], inc=sV.bump())
    ops.add("scalar", lambda e: e.activation(
        ac[:, 5, :], ac[:, 4, :], AF.Sqrt), wait=[sV.at(), sA.at()], inc=sA.bump())
    ops.add("vector", lambda e: e.reciprocal(
        ac[:, 3, :], ac[:, 5, :]), wait=[sA.at()], inc=sV.bump())
    ops.add("vector", lambda e: e.scalar_tensor_tensor(
        ac[:, 5, :], ac[:, 2, :], -1.0, ac[:, 3, :], ALU.mult, ALU.mult),
        wait=[sV.at()], inc=sV.bump())
    stats_done = [sV.at()]
    for i in range(NCK):
        pb = i % 2
        ops.add("sync", lambda e, i=i, pb=pb: e.dma_start(
            out=xA[:, pb, :], in_=b.in_x[i * 128:(i + 1) * 128, :]),
            wait=[sA.at()] if i >= 2 else stats_done, inc=sP2[pb].bump(16))
        ld = sP2[pb].at()
        ops.add("scalar", lambda e, i=i, pb=pb: e.activation(
            xnO[:, pb, :], xA[:, pb, :], AF.Identity,
            bias=ac[:, 5, i:i + 1], scale=ac[:, 3, i:i + 1]),
            wait=[ld] + stats_done + [sXn.at()], inc=sA.bump())
        ops.add("gpsimd", lambda e, i=i, pb=pb: e.dma_start(
            out=b.hb_xn[i * 128:(i + 1) * 128, :], in_=xnO[:, pb, :]),
            wait=[sA.at()], inc=sXn.bump(16))
    ph1.close()
    xn_done = [sXn.at()]

    b.resid = b.sb("g_resid", [128, NQC, DM], F32)
    php = contextlib.ExitStack()
    psA = b.ps("a_psA", [128, 3072], F32, php)
    ps_t = b.ps("a_pst", [128, 1024], BF16, php)
    ph0 = contextlib.ExitStack()
    onesb = b.sb("a_onesb", [1, 128], BF16, ph0)
    bor = b.sb("a_bor", [1, DM], BF16, ph0)
    x0b = b.sb("a_x0b", [128, 2, DM], F32, ph0)
    ops.add("vector", lambda e: e.memset(onesb[:, :], 1.0), inc=sV.bump())
    sBo = b.sem("a_bo")
    ops.add("gpsimd", lambda e: e.dma_start(out=bor[0:1, :], in_=b.in_bo[0:1, :]),
            inc=sBo.bump(16))
    for s4 in range(4):
        ops.add("tensor", lambda e, s4=s4: e.matmul(
            psA[:, s4 * 512:(s4 + 1) * 512],
            onesb[0:1, :], bor[0:1, s4 * 512:(s4 + 1) * 512],
            start=True, stop=True), wait=[sV.at(), sBo.at()] if s4 == 0 else [],
            inc=sT.bump() if s4 == 3 else None)
    bob_done = sT.at()
    for qc in range(NQC):
        pb = qc % 2
        ops.add("sync", lambda e, qc=qc, pb=pb: e.dma_start(
            out=x0b[:, pb, :],
            in_=b.in_x[HALF + qc * 128:HALF + (qc + 1) * 128, :]),
            wait=(xn_done + [sA.at(), sV.at()]) if qc < 2 else [sV.at()],
            inc=sI0[pb].bump(16))
        x0ld = sI0[pb].at()
        for ds in range(4):
            ops.add("vector", lambda e, qc=qc, ds=ds, pb=pb: e.tensor_add(
                b.resid[:, qc, ds * 512:(ds + 1) * 512],
                x0b[:, pb, ds * 512:(ds + 1) * 512],
                psA[:, ds * 512:(ds + 1) * 512]),
                wait=[x0ld, bob_done] if ds == 0 else [], inc=sV.bump())
    ph0.close()
    init_done = [sV.at()]

    ph2 = contextlib.ExitStack()
    K_sb = b.sb("a_K", [128, LSEQ], BF16, ph2)
    Vt = b.sb("a_Vt", [128, 32, EH], BF16, ph2)
    Q_sb = b.sb("a_Q", [128, HALF], BF16, ph2)
    zh = b.sb("a_zh", [128, HALF], BF16, ph2)
    xq = b.sb("a_xq", [128, 2, 1024], BF16, ph2)
    wqkv = b.sb("a_wqkv", [128, 16, 2, EH], BF16, ph2)
    wq_s = b.sb("a_wqs", [128, 2, EH], BF16, ph2)
    wo_sb = b.sb("a_wosb", [128, DM], BF16, ph2)
    scs = b.sb("a_scs", [128, LSEQ], F32, ph2)
    prs = b.sb("a_prs", [128, 8, 512], BF16, ph2)
    prT = b.sb("a_prT", [128, 32, 128], BF16, ph2)
    mtiles = b.sb("a_mtiles", [128, 6, 512], BF16, ph2)
    bqkv = b.sb("a_bqkv", [128, 3 * NH], F32, ph2)
    lparts = b.sb("a_lp", [128, 16], F32, ph2)
    linv = b.sb("a_linv", [128, NQC], F32, ph2)
    ops.add("sync", lambda e: e.dma_start(out=bqkv[:, :], in_=b.in_bqkv[:, :]),
            inc=sKp.bump(16))
    ops.add("sync", lambda e: e.dma_start(out=scs[:, 0:3072], in_=b.in_mtiles[:, :]),
            inc=sKp.bump(16))
    ops.add("vector", lambda e: e.tensor_copy(
        bass.AP(mtiles, 0, [[6 * 512, 128], [1, 3072]]), scs[:, 0:3072]),
        wait=[sKp.at()], inc=sV.bump())
    kp_done = [sKp.at(), sV.at()]

    NQG = 4
    for h in range(NH):
        for wi, wsrc in ((0, b.hb_wk), (1, b.hb_wv)):
            ops.add("sync", lambda e, h=h, wi=wi, wsrc=wsrc: e.dma_start(
                out=bass.AP(wqkv, wi * EH, [[16 * 2 * EH, 128], [2 * EH, 16], [1, EH]]),
                in_=bass.AP(wsrc, h * DM * EH, [[EH, 128], [128 * EH, 16], [1, EH]])),
                wait=[sT.at()], inc=sWc.bump(16))
        ops.add("gpsimd", lambda e, h=h: e.dma_start(
            out=wo_sb[:, :], in_=b.in_wo[h, :, :]),
            wait=[sT.at()], inc=sO.bump(16))
        wld = sWc.at()
        for qt in range(4):
            for ci in range(16):
                pb = ci % 2
                ops.add("sync", lambda e, h=h, qt=qt, ci=ci, pb=pb: e.dma_start(
                    out=xq[:, pb, :],
                    in_=b.hb_xn[qt * 1024:(qt + 1) * 1024, ci * 128:(ci + 1) * 128],
                    transpose=True),
                    wait=(xn_done + init_done if (h == 0 and qt == 0 and ci < 2)
                          else [sT.at()]),
                    inc=sXq[pb].bump(16))
                xld = sXq[pb].at()
                wt0 = [xld, wld]
                if ci == 0:
                    wt0 = wt0 + [sV.at()]
                if qt >= 2:
                    ops.add("sync", lambda e, h=h, ci=ci, pb=pb: e.dma_start(
                        out=wq_s[:, pb, :],
                        in_=b.hb_wq[h * DM + ci * 128:h * DM + (ci + 1) * 128, :]),
                        wait=[sT.at()], inc=sQs[pb].bump(16))
                    wt0 = wt0 + [sQs[pb].at()]
                for nn in range(2):
                    ops.add("tensor", lambda e, h=h, ci=ci, nn=nn, pb=pb: e.matmul(
                        psA[:, nn * 512:(nn + 1) * 512],
                        wqkv[:, ci, 0, :], xq[:, pb, nn * 512:(nn + 1) * 512],
                        start=(ci == 0), stop=(ci == 15)),
                        wait=wt0 if nn == 0 else [])
                    ops.add("tensor", lambda e, h=h, ci=ci, nn=nn, pb=pb: e.matmul(
                        psA[:, 1024 + nn * 512:1024 + (nn + 1) * 512],
                        wqkv[:, ci, 1, :], xq[:, pb, nn * 512:(nn + 1) * 512],
                        start=(ci == 0), stop=(ci == 15)),
                        inc=None if qt >= 2 else (sT.bump() if nn == 1 else None))
                    if qt >= 2:
                        ops.add("tensor", lambda e, h=h, ci=ci, nn=nn, pb=pb: e.matmul(
                            psA[:, 2048 + nn * 512:2048 + (nn + 1) * 512],
                            wq_s[:, pb, :], xq[:, pb, nn * 512:(nn + 1) * 512],
                            start=(ci == 0), stop=(ci == 15)),
                            inc=sT.bump() if nn == 1 else None)
            mm_done = sT.at()
            ops.add("vector", lambda e, qt=qt, h=h: e.tensor_scalar(
                K_sb[:, qt * 1024:(qt + 1) * 1024], psA[:, 0:1024],
                bqkv[:, 3 * h + 1:3 * h + 2], None, ALU.add),
                wait=[mm_done] + kp_done + [sV.at()], inc=sV.bump())
            ops.add("vector", lambda e, qt=qt, h=h: e.tensor_scalar(
                bass.AP(prs, 6 * 512, [[8 * 512, 128], [1, 1024]]),
                psA[:, 1024:2048],
                bqkv[:, 3 * h + 2:3 * h + 3], None, ALU.add),
                wait=[sV.at(), sVt.at()], inc=sV.bump())
            if qt >= 2:
                ops.add("vector", lambda e, qt=qt, h=h: e.tensor_scalar(
                    Q_sb[:, (qt - 2) * 1024:(qt - 1) * 1024], psA[:, 2048:3072],
                    bqkv[:, 3 * h:3 * h + 1], None, ALU.add),
                    wait=[sV.at()], inc=sV.bump())
            ev_done = sV.at()
            ops.add("sync", lambda e, qt=qt: e.dma_start_transpose(
                bass.AP(Vt, qt * 8 * EH, [[32 * EH, 128], [EH, 8], [1, 128]]),
                bass.AP(prs, 6 * 512, [[8 * 512, 128], [1, 1024]])),
                wait=[ev_done, sT.at()], inc=sVt.bump(16))
        kvq_done = [sV.at(), sT.at(), sVt.at()]

        qcs = [(g4, g4 * 4 + qi) for g4 in range(NQG) for qi in range(4)]
        tr_mark, zh_mark, exp_mark, mx_mark = {}, {}, {}, {}

        def emit_scores(qc, g4):
            nseg = 5 + g4
            for sg in range(nseg):
                sp = sg % 2
                if sg < 4:
                    msel = 0
                elif sg == nseg - 1:
                    msel = 2 + (qc % 4)
                else:
                    msel = 1
                wt = (kvq_done + kp_done + [sV.at()]) if sg < 2 else [sV.at()]
                ops.add("tensor", lambda e, qc=qc, sg=sg, sp=sp: e.matmul(
                    psA[:, sp * 512:(sp + 1) * 512],
                    Q_sb[:, qc * 128:(qc + 1) * 128],
                    K_sb[:, sg * 512:(sg + 1) * 512],
                    start=True, stop=True), wait=wt, inc=sT.bump())
                ew = [sT.at()]
                if sg == 0 and (qc - 1) in exp_mark:
                    ew += [exp_mark[qc - 1]]
                ops.add("vector", lambda e, sg=sg, sp=sp, msel=msel: e.scalar_tensor_tensor(
                    scs[:, sg * 512:(sg + 1) * 512],
                    psA[:, sp * 512:(sp + 1) * 512],
                    INV_SQRT_EH, mtiles[:, msel, :],
                    ALU.mult, ALU.add), wait=ew, inc=sV.bump())
            kext = 512 * nseg
            mxc = 176 + (qc % 2)
            ops.add("vector", lambda e, kext=kext, mxc=mxc: e.reduce_max(
                c[:, mxc:mxc + 1], scs[:, 0:kext], AX.X, negate=True),
                wait=[sV.at()], inc=sV.bump())
            mx_mark[qc] = sV.at()

        def emit_exp(qc, g4):
            nseg = 5 + g4
            mxc = 176 + (qc % 2)
            lb = 8 * (qc % 2)
            for sg in range(nseg):
                ew = []
                if sg == 0:
                    ew = [mx_mark[qc], sA.at()]
                    if qc == 8:
                        ew += [sVt.at()]
                    if (qc - 1) in tr_mark:
                        ew += [tr_mark[qc - 1]]
                ops.add("scalar", lambda e, sg=sg, mxc=mxc, lb=lb: e.activation(
                    prs[:, sg, :], scs[:, sg * 512:(sg + 1) * 512],
                    AF.Exp, bias=c[:, mxc:mxc + 1],
                    accum_out=lparts[:, lb + sg:lb + sg + 1]),
                    wait=ew, inc=sA.bump())
            exp_mark[qc] = sA.at()

        def emit_tail(qc, g4):
            nseg = 5 + g4
            lb = 8 * (qc % 2)
            ops.add("vector", lambda e, nseg=nseg, lb=lb: e.tensor_reduce(
                c[:, 178:179].rearrange("p (g o) -> p g o", o=1),
                lparts[:, lb:lb + nseg].rearrange("p (g o) -> p g o", o=nseg),
                AX.X, ALU.add), wait=[exp_mark[qc], sV.at()], inc=sV.bump())
            ops.add("vector", lambda e, qc=qc: e.reciprocal(
                linv[:, qc:qc + 1], c[:, 178:179]),
                wait=[sV.at()], inc=sV.bump())
            ops.add("sync", lambda e, nseg=nseg: e.dma_start_transpose(
                bass.AP(prT, 0, [[32 * 128, 128], [128, 4 * nseg], [1, 128]]),
                bass.AP(prs, 0, [[8 * 512, 128], [1, nseg * 512]])),
                wait=[exp_mark[qc], sT.at()], inc=sPt.bump(16))
            tr_mark[qc] = sPt.at()
            avb = 1024 + 512 * (qc % 2)
            for kc in range(4 * nseg):
                wt = []
                if kc == 0:
                    wt = [tr_mark[qc], sV.at()]
                    if (qc - 2) in zh_mark:
                        wt += [zh_mark[qc - 2]]
                ops.add("tensor", lambda e, kc=kc, nseg=nseg, avb=avb: e.matmul(
                    psA[:, avb:avb + 128], Vt[:, kc, :], prT[:, kc, :],
                    start=(kc == 0), stop=(kc == 4 * nseg - 1)),
                    wait=wt, inc=sT.bump() if kc == 4 * nseg - 1 else None)
            ops.add("vector", lambda e, qc=qc, avb=avb: e.tensor_copy(
                zh[:, qc * 128:(qc + 1) * 128],
                psA[:, avb:avb + 128]), wait=[sT.at()], inc=sV.bump())
            zh_mark[qc] = sV.at()

        for i, (g4, qc) in enumerate(qcs):
            emit_scores(qc, g4)
            if i > 0:
                pg4, pqc = qcs[i - 1]
                emit_tail(pqc, pg4)
            emit_exp(qc, g4)
        emit_tail(qcs[-1][1], qcs[-1][0])
        tr_mark.clear(); zh_mark.clear(); exp_mark.clear(); mx_mark.clear()
        z_done = [sV.at(), sT.at()]

        wo_ld = sO.at()
        for qc in range(NQC):
            for ds in range(4):
                wt = [sV.at()] if ds == 0 else []
                if qc == 0 and ds == 0:
                    wt = z_done + [wo_ld, sV.at()]
                ops.add("tensor", lambda e, h=h, qc=qc, ds=ds: e.matmul(
                    psA[:, 1024 + ds * 512:1024 + (ds + 1) * 512],
                    zh[:, qc * 128:(qc + 1) * 128],
                    wo_sb[:, ds * 512:(ds + 1) * 512],
                    start=True, stop=True), wait=wt,
                    inc=sT.bump() if ds == 3 else None)
            wmm = sT.at()
            for ds in range(4):
                ops.add("vector", lambda e, qc=qc, ds=ds: e.scalar_tensor_tensor(
                    b.resid[:, qc, ds * 512:(ds + 1) * 512],
                    psA[:, 1024 + ds * 512:1024 + (ds + 1) * 512],
                    linv[:, qc:qc + 1],
                    b.resid[:, qc, ds * 512:(ds + 1) * 512],
                    ALU.mult, ALU.add), wait=[wmm], inc=sV.bump())
    ph2.close()
    php.close()
    b.resid_ready = [sV.at()]
def _emit_ln_stats(b, tag, sq_scr):
    ops = b.ops
    c = b.colA
    sV = b.sem(f"{tag}_lv")
    sA = b.sem(f"{tag}_la")
    rw = b.resid_ready
    for qc in range(NQC):
        ops.add("scalar", lambda e, qc=qc: e.activation(
            sq_scr, b.resid[:, qc, :], AF.Square,
            accum_out=c[:, 48 + qc:49 + qc]), wait=rw + [sA.at()], inc=sA.bump())
    ops.add("vector", lambda e: e.tensor_reduce(
        c[:, 64:80].rearrange("p (g o) -> p g o", o=1),
        bass.AP(b.resid, 0, [[NQC * DM, 128], [DM, NQC], [1, DM]]),
        AX.X, ALU.add), wait=rw, inc=sV.bump())
    ops.add("vector", lambda e: e.tensor_scalar(
        c[:, 0:16], c[:, 64:80], 1.0 / DM, None, ALU.mult),
        wait=[sV.at()], inc=sV.bump())
    ops.add("vector", lambda e: e.tensor_scalar(
        c[:, 80:96], c[:, 48:64], 1.0 / DM, None, ALU.mult),
        wait=[sA.at(), sV.at()], inc=sV.bump())
    ops.add("vector", lambda e: e.scalar_tensor_tensor(
        c[:, 16:32], c[:, 0:16], 1.0, c[:, 0:16], ALU.mult, ALU.mult),
        wait=[sV.at()], inc=sV.bump())
    ops.add("vector", lambda e: e.tensor_sub(
        c[:, 80:96], c[:, 80:96], c[:, 16:32]), wait=[sV.at()], inc=sV.bump())
    ops.add("vector", lambda e: e.tensor_scalar(
        c[:, 80:96], c[:, 80:96], LN_EPS, None, ALU.add), wait=[sV.at()], inc=sV.bump())
    ops.add("scalar", lambda e: e.activation(
        c[:, 96:112], c[:, 80:96], AF.Sqrt), wait=[sV.at()], inc=sA.bump())
    ops.add("vector", lambda e: e.reciprocal(
        c[:, 16:32], c[:, 96:112]), wait=[sA.at()], inc=sV.bump())
    ops.add("vector", lambda e: e.scalar_tensor_tensor(
        c[:, 32:48], c[:, 0:16], -1.0, c[:, 16:32], ALU.mult, ALU.mult),
        wait=[sV.at()], inc=sV.bump())
    return [sV.at()]



def _emit_ffn(b, tag):
    ops = b.ops
    sW = b.sem(f"{tag}_w")
    sV = b.sem(f"{tag}_v")
    sA = b.sem(f"{tag}_a")
    sT = b.sem(f"{tag}_t")
    qd = b.quant_done

    ph = contextlib.ExitStack()
    hbuf = b.sb(f"{tag}_h", [128, 2, FH], BF16, ph)
    xnT = b.sb(f"{tag}_xnT", [128, 16, 256], BF16, ph)
    wg = [b.sb(f"{tag}_wg{i}", [128, 8, 512], BF16, ph) for i in range(2)]
    wu = [b.sb(f"{tag}_wu{i}", [128, 8, 512], BF16, ph) for i in range(2)]
    wpq = b.sb(f"{tag}_wpq", [128, 4, 512], BF16, ph)
    htf = [b.sb(f"{tag}_htf{i}", [128, 256], BF16, ph) for i in range(2)]
    sg = b.sb(f"{tag}_sg", [128, 512], BF16, ph)
    ssqp = b.sb(f"{tag}_ssqp", [128, 2, 16], F32, ph)
    inv_c = b.sb(f"{tag}_inv", [128, 8], F32, ph)
    ps_t = b.ps(f"{tag}_pst", [128, 1024], BF16, ph)
    A4 = b.ps(f"{tag}_a4", [128, 2048], F32, ph)
    c = b.colA
    NT = DEV_FFN_TILES
    NHG = FH // 512

    sq_scr = hbuf[:, 1, 0:DM]
    rw0 = list(b.resid_ready)
    for eng in ("sync", "gpsimd", "tensor", "vector", "scalar"):
        ops.barrier(eng, rw0)
    ln_done = _emit_ln_stats(b, tag, sq_scr)

    war_wg = [None, None]
    war_wp = [None, None]
    prev_tile_done = []

    sXt = b.sem(f"{tag}_xt")
    for tt in range(NT):
        for tc, qc in enumerate((2 * tt, 2 * tt + 1)):
            xv = hbuf[:, 0, 0:DM]
            ops.add("scalar", lambda e, qc=qc, xv=xv: e.activation(
                xv, b.resid[:, qc, :], AF.Identity,
                bias=c[:, 32 + qc:33 + qc], scale=c[:, 16 + qc:17 + qc]),
                wait=ln_done + [sA.at(), sT.at(), sXt.at()] + prev_tile_done,
                inc=sA.bump())
            ops.add("sync", lambda e, tc=tc, xv=xv: e.dma_start_transpose(
                bass.AP(xnT, tc * 128, [[16 * 256, 128], [256, 16], [1, 128]]),
                xv[:, :]), wait=[sA.at(), sT.at()], inc=sXt.bump(16))
        xnT_done = sXt.at()

        for hg in range(NHG):
            slotm = []
            for half_i in range(2):

                def _ldg(e, hi=half_i, hg=hg):
                    out_ap = bass.AP(wg[hi], 0, [[8 * 512, 128], [512, 8], [1, 512]])
                    in_ap = bass.AP(b.hb_wguT, hg * 512 + hi * 8 * 128 * 2 * FH,
                                    [[2 * FH, 128], [128 * 2 * FH, 8], [1, 512]])
                    return e.dma_start(out=out_ap, in_=in_ap)

                def _ldu(e, hi=half_i, hg=hg):
                    out_ap = bass.AP(wu[hi], 0, [[8 * 512, 128], [512, 8], [1, 512]])
                    in_ap = bass.AP(b.hb_wguT, FH + hg * 512 + hi * 8 * 128 * 2 * FH,
                                    [[2 * FH, 128], [128 * 2 * FH, 8], [1, 512]])
                    return e.dma_start(out=out_ap, in_=in_ap)
                w = [qd[0]] if (tt == 0 and hg == 0) else []
                if war_wg[half_i] is not None:
                    w = w + [war_wg[half_i]]
                sg_s = b.sem(f"{tag}_wg{half_i}")
                su_s = b.sem(f"{tag}_wu{half_i}")
                ops.add("sync", _ldg, wait=w, inc=sg_s.bump(16))
                ops.add("sync", _ldu, inc=su_s.bump(16))
                slotm.append((sg_s.at(), su_s.at()))

            pb = hg % 2
            g_ps = A4[:, pb * 512:(pb + 1) * 512]
            u_ps = A4[:, 1024 + pb * 512:1024 + (pb + 1) * 512]
            for tc in range(2):
                for d in range(16):
                    hi, dd = (0, d) if d < 8 else (1, d - 8)
                    wt = []
                    if d == 0:
                        wt = [xnT_done, sA.at(), sV.at()]
                    if tc == 0 and (d == 0 or d == 8):
                        wt = wt + [slotm[hi][0], slotm[hi][1]]
                    ops.add("tensor", lambda e, dd=dd, hi=hi, tc=tc, d=d, g_ps=g_ps: e.matmul(
                        g_ps, xnT[:, d, tc * 128:(tc + 1) * 128],
                        wg[hi][:, dd, :], start=(d == 0), stop=(d == 15)),
                        wait=wt)
                    ops.add("tensor", lambda e, dd=dd, hi=hi, tc=tc, d=d, u_ps=u_ps: e.matmul(
                        u_ps, xnT[:, d, tc * 128:(tc + 1) * 128],
                        wu[hi][:, dd, :], start=(d == 0), stop=(d == 15)),
                        inc=sT.bump() if d == 15 else None)
                mm_done = sT.at()
                ops.add("scalar", lambda e, g_ps=g_ps: e.activation(
                    sg[:, :], g_ps, AF.Silu),
                    wait=[mm_done, sV.at()], inc=sA.bump())
                ops.add("vector", lambda e, u_ps=u_ps, tc=tc, hg=hg: e.tensor_tensor(
                    hbuf[:, tc, hg * 512:(hg + 1) * 512], sg[:, :],
                    u_ps, ALU.mult), wait=[sA.at()], inc=sV.bump())
                ops.add("scalar", lambda e, tc=tc, hg=hg: e.activation(
                    sg[:, :], hbuf[:, tc, hg * 512:(hg + 1) * 512], AF.Square,
                    accum_out=ssqp[:, tc, hg:hg + 1]), wait=[sV.at()], inc=sA.bump())
            war_wg[0] = war_wg[1] = sT.at()
        h_done_V = sV.at()
        h_done_A = sA.at()

        ops.add("vector", lambda e: e.tensor_reduce(
            inv_c[:, 0:2].rearrange("p (g o) -> p g o", o=1),
            ssqp[:, :, :], AX.X, ALU.add), wait=[h_done_A, sV.at()], inc=sV.bump())
        ops.add("vector", lambda e: e.tensor_scalar(
            inv_c[:, 2:4], inv_c[:, 0:2], 1.0 / FH, RMS_EPS, ALU.mult, ALU.add),
            wait=[sV.at()], inc=sV.bump())
        ops.add("scalar", lambda e: e.activation(
            inv_c[:, 4:6], inv_c[:, 2:4], AF.Sqrt), wait=[sV.at()], inc=sA.bump())
        ops.add("vector", lambda e: e.reciprocal(
            inv_c[:, 6:8], inv_c[:, 4:6]), wait=[sA.at()], inc=sV.bump())
        inv_done = sV.at()

        NHC = FH // 128
        for dh in range(2):
            for hc in range(NHC):
                pb = hc % 2
                qmarks = []
                for q in range(2):
                    slot = pb * 2 + q
                    w = []
                    if tt == 0 and dh == 0 and hc == 0 and q == 0:
                        w = [qd[0]]
                    if war_wp[pb] is not None and q == 0:
                        w = w + [war_wp[pb]]
                    sq_s = b.sem(f"{tag}_wq{slot}")
                    ops.add("sync", lambda e, hc=hc, dh=dh, q=q, slot=slot: e.dma_start(
                        out=wpq[:, slot, :],
                        in_=b.hb_wpT[hc * 128:(hc + 1) * 128,
                                     dh * 1024 + q * 512:dh * 1024 + (q + 1) * 512]),
                        wait=w, inc=sq_s.bump(16))
                    qmarks.append(sq_s.at())
                for tc in range(2):
                    tw = [sV.at()]
                    if hc == 0 and dh == 0:
                        tw += [h_done_V]
                    ops.add("tensor", lambda e, tc=tc, hc=hc: e.transpose(
                        ps_t[:, tc * 512:tc * 512 + 128],
                        hbuf[:, tc, hc * 128:(hc + 1) * 128], b.ident[:, :]),
                        wait=tw, inc=sT.bump())
                    ops.add("vector", lambda e, tc=tc: e.tensor_copy(
                        htf[tc][:, 0:128], ps_t[:, tc * 512:tc * 512 + 128]),
                        wait=[sT.at()], inc=sV.bump())
                    tr_done = sV.at()
                    for q in range(2):
                        wt = [qmarks[q]] if tc == 0 else []
                        if q == 0:
                            wt += [tr_done]
                            if hc == 0 and tc == 0:
                                wt += [h_done_A, sV.at()]
                                if dh == 1 or tt > 0:
                                    wt += [prev_evac_done]
                        ops.add("tensor", lambda e, q=q, hc=hc, tc=tc,
                                slot=pb * 2 + q: e.matmul(
                            A4[:, tc * 1024 + q * 512:tc * 1024 + (q + 1) * 512],
                            htf[tc][:, 0:128], wpq[:, slot, :],
                            start=(hc == 0), stop=(hc == NHC - 1)),
                            wait=wt, inc=sT.bump() if (tc == 1 and q == 1) else None)
                war_wp[pb] = sT.at()
            mm2_done = sT.at()
            for tc in range(2):
                for q in range(2):
                    ops.add("vector", lambda e, tc=tc, tt=tt, dh=dh, q=q: e.scalar_tensor_tensor(
                        b.resid[:, 2 * tt + tc,
                                dh * 1024 + q * 512:dh * 1024 + (q + 1) * 512],
                        A4[:, tc * 1024 + q * 512:tc * 1024 + (q + 1) * 512],
                        inv_c[:, 6 + tc:7 + tc],
                        b.resid[:, 2 * tt + tc,
                                dh * 1024 + q * 512:dh * 1024 + (q + 1) * 512],
                        ALU.mult, ALU.add),
                        wait=[mm2_done, inv_done] if (tc == 0 and q == 0) else [],
                        inc=sV.bump())
            prev_evac_done = sV.at()
        prev_tile_done = [sV.at()]
    b.resid_ready = [sV.at()]
    ph.close()



def _emit_ssm(b):
    ops = b.ops
    sV = b.sem("s_v")
    sA = b.sem("s_a")
    sT = b.sem("s_t")
    sWi = [b.sem("s_wi0"), b.sem("s_wi1")]
    sWo = [b.sem(f"s_wo{i}") for i in range(4)]
    sCt = b.sem("s_ctail")
    sCtC = b.sem("s_ctailc")
    sCs = b.sem("s_cstate")
    sCsC = b.sem("s_cstatec")
    sSm = b.sem("s_sm")
    sCz = b.sem("s_czst")
    sXnt = b.sem("s_xnt")
    sGt = b.sem("s_gt")
    sZl = b.sem("s_czld")
    sSm2 = b.sem("s_sm2")

    ph = contextlib.ExitStack()
    zg = b.sb("s_zg", [128, 2, DSSM], BF16, ph)
    xbc = b.sb("s_xbc", [128, 34, 256], BF16, ph)
    xnT = b.sb("s_xnT", [128, 16, 256], BF16, ph)
    tails = b.sb("s_tails", [128, 34, 4], BF16, ph)
    raw = b.sb("s_raw", [128, 1, 260], BF16, ph)
    st = b.sb("s_st", [128, 64, PS], BF16, ph)
    win_sb = [b.sb("s_win0", [128, 16, 128], BF16, ph)]
    win_sb = [win_sb[0], win_sb[0]]
    wo_sb = b.sb("s_wo", [128, 2, 512], BF16, ph)
    dtf = b.sb("s_dtf", [128, 3, 256], F32, ph)
    dtm = b.sb("s_dtm", [128, 4, PS], F32, ph)
    bch = b.sb("s_bch", [128, 2, 128], F32, ph)
    btm = b.sb("s_btm", [128, 128], BF16, ph)
    gmsk = b.sb("s_gmsk", [128, 128], BF16, ph)
    marg = b.sb("s_marg", [128, 2, 128], F32, ph)
    sph = b.sb("s_sph", [128, 2, 128], BF16, ph)
    u3 = b.sb("s_u3", [128, 2, 3, PS], BF16, ph)
    ysc = b.sb("s_ysc", [128, 2, 2 * PS], BF16, ph)
    szh = b.sb("s_szh", [128, PS], BF16, ph)
    xnt3 = b.sb("s_xnt3", [128, 16, 4], BF16, ph)
    gcol = b.sb("s_gcol", [128, 2], F32, ph)
    dcol = b.sb("s_dcol", [128, 2, 64], F32, ph)
    tailx = b.sb("s_tailx", [4, DM], BF16, ph)
    ones1 = b.sb("s_ones1", [1, 128], F32, ph)
    identf = b.sb("s_identf", [128, 128], F32, ph)
    tri_sb = b.sb("s_tri", [128, 128], BF16, ph)
    zeros_sb = b.sb("s_zeros", [128, 128], BF16, ph)
    convw_sb = b.sb("s_convw", [128, 136], F32, ph)
    convb_sb = b.sb("s_convb", [128, 34], F32, ph)
    ssmcol_sb = b.sb("s_ssmcol", [64, 4], F32, ph)
    mup_sb = b.sb("s_mup", [128, 67], F32, ph)
    normw_sb = b.sb("s_normw", [128, 32], F32, ph)
    ssmrow_sb = b.sb("s_ssmrow", [1, 192], F32, ph)
    ps_t = b.ps("s_pst", [128, 2, 512], BF16, ph)
    ps_f = b.ps("s_psf", [128, 512], F32, ph)
    ps_f2 = b.ps("s_psf2", [128, 512], F32, ph)
    A4 = b.ps("s_a4", [128, 2048], F32, ph)
    c = b.colA
    sq_scr = bass.AP(xbc, 0, [[34 * 256, 128], [1, DM]])
    zb = raw[:, 0, 0:256]
    sq_scr3 = bass.AP(xbc, 0, [[34 * 256, 3], [1, DM]])

    rw0 = list(b.resid_ready)
    for eng in ("sync", "gpsimd", "tensor", "vector", "scalar"):
        ops.barrier(eng, rw0)
    ln_done = _emit_ln_stats(b, "ssm", sq_scr)
    NCH = HALF // 256

    ph_iof = contextlib.ExitStack()
    iof = b.sb("s_iof", [128, 256], BF16, ph_iof)
    ops.add("gpsimd", lambda e: e.iota(iof[:, 0:128], [[1, 128]], channel_multiplier=0,
                                       allow_small_or_imprecise_dtypes=True))
    ops.add("gpsimd", lambda e: e.iota(iof[:, 128:256], [[0, 128]], channel_multiplier=1,
                                       allow_small_or_imprecise_dtypes=True),
            inc=b.sem("s_g").bump())
    sG = b.sem("s_g")
    ops.add("vector", lambda e: e.tensor_tensor(
        tri_sb[:, :], iof[:, 128:256], iof[:, 0:128], ALU.is_le),
        wait=[sG.at()], inc=sV.bump())
    ops.add("vector", lambda e: e.tensor_tensor(
        identf[:, :], iof[:, 0:128], iof[:, 128:256], ALU.is_equal),
        wait=[sG.at()], inc=sV.bump())
    ops.add("vector", lambda e: e.memset(ones1[:, :], 1.0), inc=sV.bump())
    ops.add("vector", lambda e: e.memset(zeros_sb[:, :], 0.0), inc=sV.bump())
    ops.add("sync", lambda e: e.dma_start(out=convw_sb[:, :], in_=b.in_convw[:, :]),
            inc=sSm.bump(16))
    ops.add("sync", lambda e: e.dma_start(out=convb_sb[:, :], in_=b.in_convb[:, :]),
            inc=sSm.bump(16))
    ops.add("sync", lambda e: e.dma_start(out=ssmcol_sb[:, :], in_=b.in_ssmv[:, :]),
            inc=sSm.bump(16))
    ops.add("sync", lambda e: e.dma_start(out=mup_sb[:, :], in_=b.in_mup[:, :]),
            inc=sSm.bump(16))
    ops.add("sync", lambda e: e.dma_start(out=normw_sb[:, :], in_=b.in_normw[:, :]),
            inc=sSm.bump(16))
    ops.add("sync", lambda e: e.dma_start(out=ssmrow_sb[:, :], in_=b.in_ssmrow[:, :]),
            inc=sSm.bump(16))
    sm_ld = sSm.at()
    sm_ld2 = sSm2.at()

    ops.add("tensor", lambda e: e.matmul(
        ps_f[:, 0:2], ones1[0:1, :], b.gate_sb[0:1, 0:2], start=True, stop=True),
        wait=b.setup_done + [sm_ld, sV.at()], inc=sT.bump())
    ops.add("vector", lambda e: e.tensor_copy(gcol[:, :], ps_f[:, 0:2]),
            wait=[sT.at()], inc=sV.bump())
    ops.add("tensor", lambda e: e.matmul(
        ps_f[:, 64:128], ones1[0:1, :], ssmrow_sb[0:1, 64:128], start=True, stop=True),
        wait=[sV.at()], inc=sT.bump())
    ops.add("vector", lambda e: e.tensor_copy(dcol[:, 0, :], ps_f[:, 64:128]),
            wait=[sT.at()], inc=sV.bump())
    ops.add("scalar", lambda e: e.activation(
        c[0:64, 190:191], ssmcol_sb[0:64, 0:1], AF.Exp),
        wait=[sm_ld, sm_ld2], inc=sA.bump())
    ops.add("vector", lambda e: e.tensor_scalar(
        c[0:64, 190:191], c[0:64, 190:191], -1.0, None, ALU.mult),
        wait=[sA.at()], inc=sV.bump())
    setup2 = [sV.at(), sA.at(), sT.at()]
    ph_iof.close()

    ops.add("gpsimd", lambda e: e.dma_start(
        out=b.cc_tail_in[0:3, :], in_=b.resid[125:128, 15, :]),
        wait=ln_done, inc=sCt.bump(16))
    ops.add("gpsimd", lambda e: e.collective_compute(
        "AllGather", ALU.bypass,
        replica_groups=[[0, 1], [2, 3], [4, 5], [6, 7]],
        ins=[b.cc_tail_in.ap().opt()], outs=[b.cc_tail_out.ap().opt()]),
        wait=[sCt.at()], inc=sCtC.bump(1))
    ops.add("gpsimd", lambda e: e.dma_start(
        out=tailx[0:3, :], in_=b.cc_tail_out[0:3, :]),
        wait=[sCtC.at()], inc=sCt.bump(16))
    tail_ld = sCt.at()
    ops.add("vector", lambda e: e.tensor_reduce(
        c[0:3, 160:161].rearrange("p (g o) -> p g o", o=1),
        tailx[0:3, :].rearrange("p (g o) -> p g o", o=DM), AX.X, ALU.add),
        wait=[tail_ld, sV.at()], inc=sV.bump())
    ops.add("scalar", lambda e: e.activation(
        sq_scr3, tailx[0:3, :], AF.Square,
        accum_out=c[0:3, 161:162]), wait=[tail_ld, sA.at()], inc=sA.bump())
    ops.add("vector", lambda e: e.tensor_scalar(
        c[0:3, 162:163], c[0:3, 160:161], 1.0 / DM, None, ALU.mult),
        wait=[sV.at(), sA.at()], inc=sV.bump())
    ops.add("vector", lambda e: e.tensor_scalar(
        c[0:3, 163:164], c[0:3, 161:162], 1.0 / DM, None, ALU.mult),
        wait=[sV.at()], inc=sV.bump())
    ops.add("vector", lambda e: e.scalar_tensor_tensor(
        c[0:3, 164:165], c[0:3, 162:163], 1.0, c[0:3, 162:163], ALU.mult, ALU.mult),
        wait=[sV.at()], inc=sV.bump())
    ops.add("vector", lambda e: e.tensor_sub(
        c[0:3, 163:164], c[0:3, 163:164], c[0:3, 164:165]),
        wait=[sV.at()], inc=sV.bump())
    ops.add("vector", lambda e: e.tensor_scalar(
        c[0:3, 163:164], c[0:3, 163:164], LN_EPS, None, ALU.add),
        wait=[sV.at()], inc=sV.bump())
    ops.add("scalar", lambda e: e.activation(
        c[0:3, 165:166], c[0:3, 163:164], AF.Sqrt), wait=[sV.at(), sA.at()],
        inc=sA.bump())
    ops.add("vector", lambda e: e.reciprocal(
        c[0:3, 166:167], c[0:3, 165:166]), wait=[sA.at()], inc=sV.bump())
    ops.add("vector", lambda e: e.tensor_scalar(
        sq_scr3, tailx[0:3, :], c[0:3, 162:163], c[0:3, 166:167],
        ALU.subtract, ALU.mult), wait=[sV.at()], inc=sV.bump())
    tail_xn = sV.at()
    for j in range(16):
        ops.add("tensor", lambda e, j=j: e.transpose(
            ps_t[:, 0, 0:3], bass.AP(xbc, j * 128, [[34 * 256, 3], [1, 128]]), b.ident[0:3, 0:3]),
            wait=([tail_xn] + setup2 + [sV.at()]) if j == 0 else [sV.at()],
            inc=sT.bump())
        ops.add("vector", lambda e, j=j: e.tensor_copy(
            xnt3[:, j, 0:3], ps_t[:, 0, 0:3]), wait=[sT.at()], inc=sV.bump())
    tail_done = [sV.at()]

    cache_war = []

    def chunk(ct, pass_b):
        if pass_b:
            ldw = [sT.at(), sV.at(), sA.at(), sCz.at()]
            ops.add("gpsimd", lambda e, ct=ct: e.dma_start(
                out=bass.AP(zg, 0, [[2 * DSSM, 128], [1, 2 * DSSM]]),
                in_=b.hb_sz[ct * 128:(ct + 1) * 128, :]),
                wait=ldw, inc=sZl.bump(16))
            ops.add("gpsimd", lambda e, ct=ct: e.dma_start(
                out=bass.AP(xbc, 0, [[34 * 256, 128], [1, 34 * 256]]),
                in_=b.hb_xbc[ct * 128:(ct + 1) * 128, :]),
                inc=sZl.bump(16))
            ops.add("gpsimd", lambda e, ct=ct: e.dma_start(
                out=bass.AP(dtf, 0, [[3 * 256, 64], [1, 3 * 256]]),
                in_=b.hb_dtf[ct * 128:ct * 128 + 64, :]),
                inc=sZl.bump(16))
            ld_done = sZl.at()
            xbc_done = [ld_done]
            sz_done = ld_done
            dt_done = [ld_done]
        else:
            chunk_a(ct)
            xbc_done = [sV.at(), sA.at()]
            sz_done = sA.at()
            dt_done = [sV.at(), sA.at()]
            ops.add("gpsimd", lambda e, ct=ct: e.dma_start(
                out=b.hb_sz[ct * 128:(ct + 1) * 128, :],
                in_=bass.AP(zg, 0, [[2 * DSSM, 128], [1, 2 * DSSM]])),
                wait=[sz_done], inc=sCz.bump(16))
            ops.add("gpsimd", lambda e, ct=ct: e.dma_start(
                out=b.hb_xbc[ct * 128:(ct + 1) * 128, :],
                in_=bass.AP(xbc, 0, [[34 * 256, 128], [1, 34 * 256]])),
                wait=xbc_done, inc=sCz.bump(16))
            ops.add("gpsimd", lambda e, ct=ct: e.dma_start(
                out=b.hb_dtf[ct * 128:ct * 128 + 64, :],
                in_=bass.AP(dtf, 0, [[3 * 256, 64], [1, 3 * 256]])),
                wait=dt_done, inc=sCz.bump(16))
            cache_war[:] = [sCz.at()]
        chunk_rest(ct, pass_b, xbc_done, sz_done, dt_done)

    def chunk_a(ct):
        for tc, qc in enumerate((2 * ct, 2 * ct + 1)):
            ops.add("scalar", lambda e, qc=qc: e.activation(
                sq_scr, b.resid[:, qc, :], AF.Identity,
                bias=c[:, 32 + qc:33 + qc], scale=c[:, 16 + qc:17 + qc]),
                wait=ln_done + [sA.at(), sT.at(), sXnt.at()] + list(cache_war),
                inc=sA.bump())
            ops.add("sync", lambda e, tc=tc: e.dma_start_transpose(
                bass.AP(xnT, tc * 128, [[16 * 256, 128], [256, 16], [1, 128]]),
                bass.AP(xbc, 0, [[34 * 256, 128], [1, DM]])),
                wait=[sA.at(), sT.at()], inc=sXnt.bump(16))
        xnT_done = sXnt.at()

        prs = list(range(0, 67))
        for pr in prs:
            pb = pr % 2
            span = 128 if pr < 66 else 64
            ops.add("sync", lambda e, pr=pr, pb=pb, span=span: e.dma_start(
                out=win_sb[pb][:, :, 0:span],
                in_=bass.AP(b.hb_win, pr * 128,
                            [[PROJ, 128], [128 * PROJ, 16], [1, span]])),
                wait=[sT.at()], inc=sWi[pb].bump(16))
            wld = sWi[pb].at()
            for d in range(16):
                wt = [wld, xnT_done, sV.at()] if d == 0 else []
                ops.add("tensor", lambda e, d=d, pb=pb, span=span: e.matmul(
                    A4[0:span, 0:256], win_sb[pb][:, d, 0:span], xnT[:, d, 0:256],
                    start=(d == 0), stop=(d == 15)),
                    wait=wt, inc=sT.bump() if d == 15 else None)
            pf_done = sT.at()
            if ct == 0 and pr >= 32:
                for d in range(16):
                    ops.add("tensor", lambda e, d=d, pb=pb, span=span: e.matmul(
                        A4[0:span, 256:259], win_sb[pb][:, d, 0:span], xnt3[:, d, 0:3],
                        start=(d == 0), stop=(d == 15)),
                        inc=sT.bump() if d == 15 else None)
                pf_done = sT.at()

            if pr < 32:
                ops.add("vector", lambda e, pb=pb, pr=pr: e.tensor_scalar(
                    zb[:, :], A4[:, 0:256], mup_sb[:, pr:pr + 1], None, ALU.mult),
                    wait=[pf_done, sV.at()], inc=sV.bump())
                for s2 in range(2):
                    ops.add("tensor", lambda e, s2=s2: e.transpose(
                        ps_t[:, 1, s2 * 128:(s2 + 1) * 128],
                        zb[:, s2 * 128:(s2 + 1) * 128], b.ident[:, :]),
                        wait=[sV.at()] if s2 == 0 else [], inc=sT.bump())
                ops.add("vector", lambda e, pr=pr: e.tensor_copy(
                    bass.AP(zg, pr * 128, [[2 * DSSM, 128], [DSSM, 2], [1, 128]]),
                    ps_t[:, 1, 0:256]), wait=[sT.at()], inc=sV.bump())
            elif pr < 66:
                cr = pr - 32
                if ct == 0:
                    ops.add("vector", lambda e, pb=pb, pr=pr: e.tensor_scalar(
                        raw[:, 0, 0:3], A4[:, 256:259], gcol[:, 0:1],
                        mup_sb[:, pr:pr + 1], ALU.mult, ALU.mult),
                        wait=[pf_done, sV.at()], inc=sV.bump())
                else:
                    ops.add("vector", lambda e, pb=pb, cr=cr: e.tensor_copy(
                        raw[:, 0, 0:3], tails[:, cr, 0:3]),
                        wait=[pf_done, sV.at()], inc=sV.bump())
                ops.add("vector", lambda e, pb=pb, pr=pr: e.tensor_scalar(
                    raw[:, 0, 3:259], A4[:, 0:256], mup_sb[:, pr:pr + 1], None,
                    ALU.mult), wait=[sV.at()], inc=sV.bump())
                ops.add("vector", lambda e, pb=pb, cr=cr: e.tensor_copy(
                    tails[:, cr, 0:3], raw[:, 0, 256:259]),
                    wait=[sV.at()], inc=sV.bump())
                ops.add("vector", lambda e, pb=pb, cr=cr: e.tensor_scalar(
                    xbc[:, cr, :], raw[:, 0, 0:256],
                    convw_sb[:, 4 * cr:4 * cr + 1], convb_sb[:, cr:cr + 1],
                    ALU.mult, ALU.add), wait=[sV.at()], inc=sV.bump())
                for k in range(1, 4):
                    ops.add("vector", lambda e, pb=pb, cr=cr, k=k: e.scalar_tensor_tensor(
                        xbc[:, cr, :], raw[:, 0, k:k + 256],
                        convw_sb[:, 4 * cr + k:4 * cr + k + 1],
                        xbc[:, cr, :], ALU.mult, ALU.add),
                        wait=[sV.at()], inc=sV.bump())
                ops.add("scalar", lambda e, cr=cr: e.activation(
                    xbc[:, cr, :], xbc[:, cr, :], AF.Silu),
                    wait=[sV.at(), sA.at()], inc=sA.bump())
            else:
                ops.add("vector", lambda e: e.tensor_scalar(
                    dtf[0:64, 0, :], A4[0:64, 0:256], mup_sb[0:64, 66:67], None,
                    ALU.mult), wait=[pf_done, sV.at()], inc=sV.bump())
        xbc_done = [sV.at(), sA.at()]
        z_done = sV.at()
        for zq in range(8):
            ops.add("scalar", lambda e, zq=zq: e.activation(
                zg[:, zq // 4, (zq % 4) * 1024:(zq % 4 + 1) * 1024],
                zg[:, zq // 4, (zq % 4) * 1024:(zq % 4 + 1) * 1024], AF.Silu),
                wait=([z_done, sA.at()] if zq == 0 else []), inc=sA.bump())

        ops.add("vector", lambda e: e.tensor_scalar(
            dtf[0:64, 0, :], dtf[0:64, 0, :], ssmcol_sb[0:64, 2:3], None, ALU.add),
            wait=xbc_done, inc=sV.bump())
        ops.add("vector", lambda e: e.tensor_scalar(
            dtf[0:64, 1, :], dtf[0:64, 0, :], -1.0, None, ALU.mult),
            wait=[sV.at()], inc=sV.bump())
        ops.add("vector", lambda e: e.tensor_max(
            dtf[0:64, 1, :], dtf[0:64, 1, :], dtf[0:64, 0, :]),
            wait=[sV.at()], inc=sV.bump())
        ops.add("scalar", lambda e: e.activation(
            dtf[0:64, 2, :], dtf[0:64, 1, :], AF.Exp, scale=-1.0),
            wait=[sV.at(), sA.at()], inc=sA.bump())
        ops.add("scalar", lambda e: e.activation(
            dtf[0:64, 2, :], dtf[0:64, 2, :], AF.Ln, bias=1.0),
            wait=[sA.at()], inc=sA.bump())
        ops.add("vector", lambda e: e.tensor_scalar(
            dtf[0:64, 1, :], dtf[0:64, 0, :], 0.0, None, ALU.max),
            wait=[sA.at(), sV.at()], inc=sV.bump())
        ops.add("vector", lambda e: e.tensor_add(
            dtf[0:64, 1, :], dtf[0:64, 1, :], dtf[0:64, 2, :]),
            wait=[sV.at()], inc=sV.bump())
        ops.add("vector", lambda e: e.tensor_scalar(
            dtf[0:64, 0, :], dtf[0:64, 1, :], c[0:64, 190:191], None, ALU.mult),
            wait=[sV.at()], inc=sV.bump())
        for sub in range(2):
            ops.add("vector", lambda e, sub=sub: e.tensor_tensor_scan(
                dtf[0:64, 2, sub * 128:(sub + 1) * 128],
                dtf[0:64, 0, sub * 128:(sub + 1) * 128],
                zeros_sb[0:64, 0:128], 0.0, ALU.add, ALU.add),
                wait=[sV.at()], inc=sV.bump())
        for sub in range(2):
            ops.add("vector", lambda e, sub=sub: e.tensor_scalar(
                dtf[0:64, 0, sub * 128:(sub + 1) * 128],
                dtf[0:64, 2, sub * 128:(sub + 1) * 128],
                dtf[0:64, 2, sub * 128 + 127:sub * 128 + 128], None, ALU.subtract),
                wait=[sV.at()], inc=sV.bump())
        ops.add("scalar", lambda e: e.activation(
            dtf[0:64, 0, :], dtf[0:64, 0, :], AF.Exp, scale=-1.0),
            wait=[sV.at(), sA.at()], inc=sA.bump())

    def chunk_rest(ct, pass_b, xbc_done, sz_done, dt_done):
        for sub in range(2):
            for r, row in enumerate((1, 2, 0)):
                ops.add("tensor", lambda e, sub=sub, r=r, row=row: e.transpose(
                    ps_f[:, r * 64:(r + 1) * 64],
                    dtf[0:64, row, sub * 128:(sub + 1) * 128], identf[0:64, 0:64]),
                    wait=(dt_done + [sV.at()]) if r == 0 else [], inc=sT.bump())
            ops.add("vector", lambda e: e.tensor_copy(
                bass.AP(dtm, 0, [[4 * PS, 128], [PS, 3], [1, PS]]),
                ps_f[:, 0:192]), wait=[sT.at()], inc=sV.bump())
            ops.add("scalar", lambda e: e.activation(
                dtm[:, 3, :], dtm[:, 1, :], AF.Exp), wait=[sV.at(), sA.at()],
                inc=sA.bump())
            ops.add("vector", lambda e: e.tensor_mul(
                dtm[:, 2, :], dtm[:, 0, :], dtm[:, 2, :]),
                wait=[sA.at(), sV.at()], inc=sV.bump())
            sub_cols = [sV.at(), sA.at()]

            ops.add("tensor", lambda e, sub=sub: e.transpose(
                ps_t[:, 1, 0:128], xbc[:, 32, sub * 128:(sub + 1) * 128],
                b.ident[:, :]), wait=xbc_done + [sV.at()], inc=sT.bump())
            ops.add("vector", lambda e: e.tensor_copy(
                btm[:, :], ps_t[:, 1, 0:128]), wait=[sT.at()], inc=sV.bump())
            btm_done = sV.at()
            if pass_b:
                ops.add("tensor", lambda e, sub=sub: e.matmul(
                    ps_f2[:, 192:320], xbc[:, 32, sub * 128:(sub + 1) * 128],
                    xbc[:, 33, sub * 128:(sub + 1) * 128],
                    start=True, stop=True), wait=[sV.at()], inc=sT.bump())
                ops.add("vector", lambda e: e.tensor_tensor(
                    gmsk[:, :], ps_f2[:, 192:320], tri_sb[:, :], ALU.mult),
                    wait=[sT.at()], inc=sV.bump())
            ops.add("scalar", lambda e, sub=sub: e.activation(
                c[0:64, 176:177], dtf[0:64, 2, sub * 128 + 127:sub * 128 + 128],
                AF.Exp), wait=sub_cols, inc=sA.bump())
            ops.add("tensor", lambda e: e.transpose(
                ps_f[0:1, 320:384], c[0:64, 176:177], identf[0:64, 0:64]),
                wait=[sA.at(), sV.at()], inc=sT.bump())
            ops.add("vector", lambda e: e.tensor_copy(
                dcol[0:1, 1, :], ps_f[0:1, 320:384]), wait=[sT.at()], inc=sV.bump())
            ops.add("tensor", lambda e: e.matmul(
                ps_f[:, 384:448], ones1[0:1, :], dcol[0:1, 1, 0:64],
                start=True, stop=True), wait=[sV.at()], inc=sT.bump())
            ops.add("vector", lambda e: e.tensor_copy(
                dcol[:, 1, :], ps_f[:, 384:448]), wait=[sT.at()], inc=sV.bump())
            dec_done = sV.at()

            mexp = {0: None, 1: None}
            for h in range(64):
                hp = h % 2
                hb = h % 2
                if pass_b and hb == 0:
                    ops.add("vector", lambda e, h=h: e.tensor_copy(
                        bch[:, :, :],
                        bass.AP(dtm, PS + h, [[4 * PS, 128], [1, 2], [0, 128]])),
                        wait=[sV.at(), sT.at()], inc=sV.bump())
                hb0 = 64 * (h % 2)
                ops.add("tensor", lambda e, h=h, sub=sub, hp=hp, hb0=hb0: e.transpose(
                    ps_t[:, hp, 256:320], xbc[hb0:hb0 + 64,
                                              h // 2,
                                              sub * 128:(sub + 1) * 128],
                    b.ident[hb0:hb0 + 64, hb0:hb0 + 64]),
                    wait=[sV.at()], inc=sT.bump())
                trx = sT.at()
                ops.add("vector", lambda e, hp=hp: e.tensor_copy(
                    u3[:, hp, 0, :], ps_t[:, hp, 256:320]), wait=[trx], inc=sV.bump())
                ops.add("vector", lambda e, hp=hp, h=h: e.tensor_scalar(
                    u3[:, hp, 1, :], u3[:, hp, 0, :], dtm[:, 0, h:h + 1], None,
                    ALU.mult), wait=[sV.at()], inc=sV.bump())
                ops.add("vector", lambda e, hp=hp, h=h: e.tensor_scalar(
                    u3[:, hp, 2, :], u3[:, hp, 0, :], dtm[:, 2, h:h + 1], None,
                    ALU.mult), wait=[sV.at()], inc=sV.bump())
                u_done = sV.at()
                if pass_b:
                    psM = ps_f if hp == 0 else ps_f2
                    ops.add("tensor", lambda e, hb=hb, hp=hp, psM=psM: e.transpose(
                        psM[:, 0:128], bch[:, hb, :], identf[:, :]),
                        wait=[sV.at()], inc=sT.bump())
                    mw = [sT.at()]
                    if mexp[hp] is not None:
                        mw.append(mexp[hp])
                    ops.add("vector", lambda e, h=h, hp=hp, psM=psM: e.tensor_scalar(
                        marg[:, hp, :], psM[:, 0:128], dtm[:, 1, h:h + 1],
                        0.0, ALU.subtract, ALU.min),
                        wait=mw, inc=sV.bump())
                    ops.add("scalar", lambda e, hp=hp: e.activation(
                        sph[:, hp, :], marg[:, hp, :], AF.Exp),
                        wait=[sV.at(), sA.at()], inc=sA.bump())
                    mexp[hp] = sA.at()
                    ops.add("vector", lambda e, hp=hp: e.tensor_tensor(
                        sph[:, hp, :], sph[:, hp, :], gmsk[:, :], ALU.mult),
                        wait=[sA.at()], inc=sV.bump())
                    sp_done = sV.at()
                    yb = 1024 + hp * 512
                    ops.add("tensor", lambda e, hp=hp, yb=yb: e.matmul(
                        A4[:, yb:yb + 64], sph[:, hp, :], u3[:, hp, 1, :],
                        start=True, stop=True), wait=[sp_done, u_done],
                        inc=sT.bump())
                    ops.add("tensor", lambda e, h=h, sub=sub, yb=yb: e.matmul(
                        A4[:, yb + 64:yb + 128],
                        xbc[:, 33, sub * 128:(sub + 1) * 128], st[:, h, :],
                        start=True, stop=True), inc=sT.bump())
                    ymm = sT.at()
                    ops.add("vector", lambda e, hp=hp, h=h, yb=yb: e.scalar_tensor_tensor(
                        ysc[:, hp, 0:PS], u3[:, hp, 0, :], dcol[:, 0, h:h + 1],
                        A4[:, yb:yb + 64], ALU.mult, ALU.add),
                        wait=[ymm, sV.at()], inc=sV.bump())
                    ops.add("vector", lambda e, hp=hp, h=h, yb=yb: e.scalar_tensor_tensor(
                        ysc[:, hp, PS:2 * PS], A4[:, yb + 64:yb + 128],
                        dtm[:, 3, h:h + 1], ysc[:, hp, 0:PS],
                        ALU.mult, ALU.add), wait=[sV.at()], inc=sV.bump())
                    ops.add("vector", lambda e, sub=sub, h=h, hp=hp: e.tensor_tensor(
                        zg[:, sub, h * PS:(h + 1) * PS], ysc[:, hp, PS:2 * PS],
                        zg[:, sub, h * PS:(h + 1) * PS], ALU.mult),
                        wait=([sz_done] if h == 0 else []) + [sV.at()], inc=sV.bump())
                cb0 = hp * 512
                ops.add("tensor", lambda e, hp=hp, cb0=cb0: e.matmul(
                    A4[:, cb0:cb0 + 64], btm[:, :], u3[:, hp, 2, :],
                    start=True, stop=True), wait=[u_done, btm_done], inc=sT.bump())
                cmm = sT.at()
                ops.add("vector", lambda e, h=h, cb0=cb0: e.scalar_tensor_tensor(
                    st[:, h, :], st[:, h, :], dcol[:, 1, h:h + 1],
                    A4[:, cb0:cb0 + 64], ALU.mult, ALU.add),
                    wait=[cmm, dec_done], inc=sV.bump())
            sub_end = [sV.at(), sA.at(), sT.at()]

        if pass_b:
            for sub in range(2):
                for hh in range(2):
                    ops.add("scalar", lambda e, sub=sub, hh=hh: e.activation(
                        sq_scr, zg[:, sub, hh * DM:(hh + 1) * DM],
                        AF.Square, accum_out=c[:, 178 + hh:179 + hh]),
                        wait=sub_end + [sA.at()], inc=sA.bump())
                ops.add("vector", lambda e: e.tensor_add(
                    c[:, 180:181], c[:, 178:179], c[:, 179:180]),
                    wait=[sA.at(), sV.at()], inc=sV.bump())
                ops.add("vector", lambda e: e.tensor_scalar(
                    c[:, 180:181], c[:, 180:181], 1.0 / DSSM, SSM_EPS,
                    ALU.mult, ALU.add), wait=[sV.at()], inc=sV.bump())
                ops.add("scalar", lambda e: e.activation(
                    c[:, 181:182], c[:, 180:181], AF.Sqrt),
                    wait=[sV.at()], inc=sA.bump())
                ops.add("vector", lambda e, sub=sub: e.reciprocal(
                    c[:, 182 + sub:183 + sub], c[:, 181:182]),
                    wait=[sA.at()], inc=sV.bump())
        rms_done = [sV.at(), sA.at()]

        if pass_b:
            for tc in range(2):
                ops.add("sync", lambda e, tc=tc: e.dma_start_transpose(
                    bass.AP(xnT, 0, [[16 * 256, 128], [128, 32], [1, 128]]),
                    bass.AP(zg, tc * DSSM, [[2 * DSSM, 128], [1, DSSM]])),
                    wait=[sV.at(), sT.at()], inc=sGt.bump(16))
                gt_ld = sGt.at()
                for g in range(32):
                    qm = {}
                    for q in range(2):
                        ops.add("sync", lambda e, g=g, q=q: e.dma_start(
                            out=wo_sb[:, q % 2, :],
                            in_=b.hb_wout[g * 128:(g + 1) * 128,
                                          q * 512:(q + 1) * 512]),
                            wait=[sT.at()], inc=sWo[q].bump(16))
                        qm[q] = sWo[q].at()
                    for ds in range(4):
                        wt = [qm[ds]]
                        if ds == 0:
                            wt += [gt_ld]
                            if g == 0:
                                wt += [sV.at()]
                        ops.add("tensor", lambda e, ds=ds, g=g: e.matmul(
                            A4[:, ds * 512:(ds + 1) * 512],
                            bass.AP(xnT, g * 128, [[16 * 256, 128], [1, 128]]),
                            wo_sb[:, ds % 2, :], start=(g == 0), stop=(g == 31)),
                            wait=wt, inc=sT.bump())
                        if ds < 2:
                            ops.add("sync", lambda e, g=g, q=ds + 2: e.dma_start(
                                out=wo_sb[:, q % 2, :],
                                in_=b.hb_wout[g * 128:(g + 1) * 128,
                                              q * 512:(q + 1) * 512]),
                                wait=[sT.at()], inc=sWo[ds + 2].bump(16))
                            qm[ds + 2] = sWo[ds + 2].at()
                wmm = sT.at()
                for ds in range(4):
                    ops.add("vector", lambda e, tc=tc, ct=ct, ds=ds: e.scalar_tensor_tensor(
                        b.resid[:, 2 * ct + tc, ds * 512:(ds + 1) * 512],
                        A4[:, ds * 512:(ds + 1) * 512], c[:, 182 + tc:183 + tc],
                        b.resid[:, 2 * ct + tc, ds * 512:(ds + 1) * 512],
                        ALU.mult, ALU.add),
                        wait=([wmm] + rms_done) if ds == 0 else [wmm],
                        inc=sV.bump())

    ops.add("vector", lambda e: e.memset(st[:, :, :], 0.0),
            wait=setup2 + tail_done, inc=sV.bump())
    for ct in range(NCH):
        chunk(ct, pass_b=False)
    passA_done = [sV.at(), sA.at(), sT.at()]

    ops.add("gpsimd", lambda e: e.dma_start(
        out=b.cc_st_in[:, :],
        in_=bass.AP(st, 0, [[64 * PS, 128], [1, 64 * PS]])),
        wait=passA_done, inc=sCs.bump(16))
    ops.add("gpsimd", lambda e: e.collective_compute(
        "AllGather", ALU.bypass,
        replica_groups=[[0, 1], [2, 3], [4, 5], [6, 7]],
        ins=[b.cc_st_in.ap().opt()], outs=[b.cc_st_out.ap().opt()]),
        wait=[sCs.at()], inc=sCsC.bump(1))
    ops.add("gpsimd", lambda e: e.dma_start(
        out=bass.AP(st, 0, [[64 * PS, 128], [1, 64 * PS]]),
        in_=b.cc_st_out[0:128, :]), wait=[sCsC.at()], inc=sCs.bump(16))
    ops.add("vector", lambda e: e.tensor_scalar(
        bass.AP(st, 0, [[64 * PS, 128], [1, 64 * PS]]),
        bass.AP(st, 0, [[64 * PS, 128], [1, 64 * PS]]),
        gcol[:, 0:1], None, ALU.mult), wait=[sCs.at(), sV.at()], inc=sV.bump())

    for ct in range(NCH):
        chunk(ct, pass_b=True)

    b.resid_ready = [sV.at()]
    ph.close()


def _emit_output(b):
    ops = b.ops
    sSt = b.sem("out_st")
    rw = b.resid_ready
    for qc in range(NQC):
        ops.add("sync", lambda e, qc=qc: e.dma_start(
            out=b.out_x[qc * 128:(qc + 1) * 128, :], in_=b.resid[:, qc, :]),
            wait=rw, inc=sSt.bump(16))
    ops.barrier("sync", [sSt.at()])



def _stage_inputs(inputs, stages=ALL_STAGES, x_override=None):
    x_full = np.ascontiguousarray(x_override if x_override is not None
                                  else inputs["L0_attn_a_x"])
    conv_w = np.ascontiguousarray(inputs["L0_attn_b_conv_w"])
    conv_w_fm = np.zeros((128, 34 * 4), dtype=np.float32)
    conv_b_fm = np.zeros((128, 34), dtype=np.float32)
    cw = conv_w.reshape(34, 128, 4)
    cb = inputs["L0_attn_b_conv_b"].reshape(34, 128)
    for ch in range(34):
        conv_w_fm[:, ch * 4:(ch + 1) * 4] = cw[ch]
        conv_b_fm[:, ch] = cb[ch]
    ssmv = np.zeros((HS, 4), dtype=np.float32)
    ssmv[:, 0] = inputs["L0_attn_b_A_log"]
    ssmv[:, 1] = inputs["L0_attn_b_D"]
    ssmv[:, 2] = inputs["L0_attn_b_dt_bias"]
    ssmrow = np.ascontiguousarray(ssmv[:, 0:3].T.reshape(1, 192))
    norm_w_fm = np.ascontiguousarray(
        inputs["L0_attn_b_norm_w"].reshape(32, 128).T)
    mup = inputs["L0_attn_b_mup_vector"]
    mup_fm = np.zeros((128, 67), dtype=np.float32)
    mup_pad = np.zeros(67 * 128, dtype=np.float32)
    mup_pad[:PROJ] = mup
    mup_fm[:, :] = mup_pad.reshape(67, 128).T

    bqkv = np.stack([inputs["L0_attn_a_b_Q"], inputs["L0_attn_a_b_K"],
                     inputs["L0_attn_a_b_V"]], axis=-1).transpose(1, 0, 2)
    bqkv = np.ascontiguousarray(bqkv.reshape(EH, 3 * NH).astype(np.float32))
    maps = []
    for c in range(8):
        batch, half = c // 2, c % 2
        mtiles = np.zeros((128, 6, 512), dtype=np.float32)
        mtiles[:, 0, :] = 0.0 if half == 1 else -1e5
        jj = np.arange(512)[None, :]
        ii = np.arange(128)[:, None]
        for r in range(4):
            mtiles[:, 2 + r, :] = np.where(jj > ii + 128 * r, -1e5, 0.0)
        mtiles = np.ascontiguousarray(mtiles.reshape(128, 6 * 512))
        qpos = np.zeros((128, NQC), dtype=np.int32)
        for qc in range(NQC):
            qpos[:, qc] = 2048 * half + 128 * qc + np.arange(128)
        gate = np.array([[1.0 if half == 1 else 0.0, float(half), 0.0, 0.0]],
                        dtype=np.float32)
        if "attn" in stages:
            xb = x_full[batch]
            own = xb[half * HALF:(half + 1) * HALF]
            other = xb[(1 - half) * HALF:(2 - half) * HALF]
            xs = np.concatenate([other, own], axis=0)
        else:
            xs = x_full[batch][half * HALF:(half + 1) * HALF]
            xs = np.concatenate([xs, np.zeros((LSEQ - HALF, DM), np.float32)], axis=0)
        m = {
            "x": np.ascontiguousarray(xs),
            "w_q": inputs["L0_attn_a_W_Q"], "w_k": inputs["L0_attn_a_W_K"],
            "w_v": inputs["L0_attn_a_W_V"], "w_o": inputs["L0_attn_a_W_O"],
            "b_q": inputs["L0_attn_a_b_Q"], "b_k": inputs["L0_attn_a_b_K"],
            "b_v": inputs["L0_attn_a_b_V"],
            "b_o": inputs["L0_attn_a_b_O"].reshape(1, DM),
            "w_gu": inputs["L0_ffn_w_gate_up"], "w_p": inputs["L0_ffn_w_proj"],
            "w_in": inputs["L0_attn_b_W_in"], "w_out": inputs["L0_attn_b_W_out"],
            "conv_w_fm": conv_w_fm, "conv_b_fm": conv_b_fm, "ssm_vec": ssmv,
            "ssm_row": ssmrow,
            "norm_w_fm": norm_w_fm, "mup_fm": mup_fm,
            "qpos": qpos, "gate": gate, "bqkv": bqkv, "mtiles": mtiles,
        }
        maps.append({k: np.ascontiguousarray(v) for k, v in m.items()})
    return maps


def run_device(inputs, stages=ALL_STAGES, x_override=None):
    b = build_program(stages)
    in_maps = _stage_inputs(inputs, stages, x_override)
    res = run_bass_kernel_spmd(b.nc, in_maps, core_ids=list(range(8)))
    out = np.zeros((NB, LSEQ, DM), dtype=np.float32)
    for c in range(8):
        batch, half = c // 2, c % 2
        out[batch, half * HALF:(half + 1) * HALF] = res.results[c]["xo"]
    return out


def kernel(**inputs):
    return run_device(inputs, ALL_STAGES)
